# version 03
# baseline (speedup 1.0000x reference)
import math
import numpy as np
from contextlib import ExitStack

import concourse.bass as bass
import concourse.mybir as mybir
from concourse.bass_utils import run_bass_kernel_spmd

F32 = mybir.dt.float32
BF16 = mybir.dt.bfloat16
U32 = mybir.dt.uint32
AF = mybir.ActivationFunctionType
ALU = mybir.AluOpType
AX = mybir.AxisListType

T = 4096
D = 1024
NT = D // 128
LN_EPS = 1e-5

ENGINES = ("sync", "vector", "scalar", "tensor", "gpsimd")


class Prog:

    def __init__(self, nc):
        self.nc = nc
        self.stages = []
        self.stage()

    def stage(self, dep=None):
        self.stages.append({"ops": {e: [] for e in ENGINES}, "dep": dep})
        return len(self.stages) - 1

    def on(self, eng, fn, dma=False, stage_idx=None):
        st = self.stages[-1 if stage_idx is None else stage_idx]
        st["ops"][eng].append((fn, dma))

    def t(self, fn):
        self.on("tensor", fn)

    def v(self, fn):
        self.on("vector", fn)

    def s(self, fn):
        self.on("scalar", fn)

    def g(self, fn):
        self.on("gpsimd", fn)

    def d(self, fn):
        self.on("sync", fn, dma=True)

    def dg(self, fn):
        self.on("gpsimd", fn, dma=True)

    def ds(self, fn):
        self.on("scalar", fn, dma=True)

    def emit(self, block, sem):
        Tt = []
        tot = 0
        for st in self.stages:
            n_dma = sum(1 for e in ENGINES for f, dd in st["ops"][e] if dd)
            n_ce = sum(
                1 for e in ENGINES if any(not dd for f, dd in st["ops"][e])
            )
            tot += 16 * n_dma + n_ce
            Tt.append(tot)

        def dep_of(i):
            d = self.stages[i]["dep"]
            return (i - 1) if d is None else d

        def mk_body(engine_name):
            def body(e):
                hw = 0
                for i, st in enumerate(self.stages):
                    ops = st["ops"][engine_name]
                    if not ops:
                        continue
                    dep = dep_of(i)
                    if dep >= 0 and Tt[dep] > hw:
                        e.wait_ge(sem, Tt[dep])
                        hw = Tt[dep]
                    last_comp = None
                    for j, (fn, dd) in enumerate(ops):
                        if not dd:
                            last_comp = j
                    for j, (fn, dd) in enumerate(ops):
                        inst = fn(e)
                        if dd:
                            inst.then_inc(sem, 16)
                        elif j == last_comp:
                            inst.then_inc(sem, 1)
                if engine_name == "sync":
                    e.wait_ge(sem, Tt[-1])
            return body

        block.sync(mk_body("sync"))
        block.vector(mk_body("vector"))
        block.scalar(mk_body("scalar"))
        block.tensor(mk_body("tensor"))
        block.gpsimd(mk_body("gpsimd"))


def apx(buf, off, freepat):
    if isinstance(buf, bass.AP):
        return bass.AP(buf.tensor, buf.offset + off,
                       [list(buf.ap[0])] + [list(x) for x in freepat])
    base = buf[:, :]
    return bass.AP(base.tensor, off,
                   [list(base.ap[0])] + [list(x) for x in freepat])


class Arena:

    def __init__(self, c, nelems):
        self.t = c.sb([128, nelems], F32, "arena", persist=True)
        self.n = nelems
        self.off = 0
        self.hi = 0

    def reset(self):
        self.hi = max(self.hi, self.off)
        self.off = 0

    def alloc(self, shape, dtype):
        nbytes = shape[1] * mybir.dt.size(dtype)
        cells = (nbytes + 3) // 4
        cells = (cells + 7) & ~7
        off = self.off
        self.off += cells
        assert self.off <= self.n, f"arena overflow {self.off}>{self.n}"
        if dtype == F32:
            return self.t[0:shape[0], off:off + shape[1]]
        ratio = 4 // mybir.dt.size(dtype)
        return self.t.bitcast(dtype)[0:shape[0],
                                     off * ratio:off * ratio + shape[1]]


class Ctx:

    def __init__(self, nc, stack):
        self.nc = nc
        self.stack = stack
        self.uid = 0
        self.arena = None

    def sb(self, shape, dtype, name=None, persist=False):
        if self.arena is not None and not persist:
            return self.arena.alloc(shape, dtype)
        self.uid += 1
        nm = name or f"sb{self.uid}"
        return self.stack.enter_context(self.nc.sbuf_tensor(nm, shape, dtype))

    def ps(self, shape, dtype=F32, name=None):
        self.uid += 1
        nm = name or f"ps{self.uid}"
        return self.stack.enter_context(self.nc.psum_tensor(nm, shape, dtype))

    def sem(self, name=None):
        self.uid += 1
        return self.stack.enter_context(
            self.nc.semaphore(name or f"sem{self.uid}")
        )


def setup_consts(p, c):
    c.ones_f = c.sb([1, 128], F32, "ones_f")
    c.onescol_f = c.sb([128, 1], F32, "onescol_f")
    c.invD_col = c.sb([128, 1], F32, "invD_col")
    p.g(lambda e: e.memset(c.ones_f[:, :], 1.0))
    p.g(lambda e: e.memset(c.onescol_f[:, :], 1.0))
    p.g(lambda e: e.memset(c.invD_col[:, :], 1.0 / D))
    c._constcols = {}
    c._constprog = p


def constcol(c, val, parts=1):
    key = (float(val), parts)
    if key not in c._constcols:
        tile = c.sb([parts, 1], F32, persist=True)
        c._constprog.on("gpsimd",
                        lambda e, tile=tile, val=val: e.memset(tile[:, :],
                                                               float(val)),
                        stage_idx=0)
        c._constcols[key] = tile
    return c._constcols[key]


class LNBufs:
    def __init__(self, c, tc, psA=None, psB=None):
        self.tc = tc
        self.sq = [c.sb([128, tc], F32), c.sb([128, tc], F32)]
        self.psA = c.ps([1, tc], F32) if psA is None else psA
        self.psB = c.ps([1, tc], F32) if psB is None else psB
        self.var = c.sb([1, tc], F32)


def emit_ln_stats_chunk(p, c, X, cs, b, mean_row, rstd_row, mr_row=None,
                        eps=LN_EPS, width=None):
    w = width if width is not None else b.tc
    for i in range(NT):
        buf = b.sq[i % 2]

        def f(e, i=i, buf=buf, cs=cs, w=w):
            return e.activation(buf[:, 0:w], X[i][:, cs], AF.Square)
        p.s(f)
        p.stage()

        def fA(e, i=i, cs=cs, w=w):
            return e.matmul(b.psA[:, 0:w], c.invD_col[:, :], X[i][:, cs],
                            start=(i == 0), stop=(i == NT - 1))
        p.t(fA)

        def fB(e, i=i, buf=buf, w=w):
            return e.matmul(b.psB[:, 0:w], c.invD_col[:, :], buf[:, 0:w],
                            start=(i == 0), stop=(i == NT - 1))
        p.t(fB)
    p.stage()

    def fm(e, w=w):
        return e.tensor_copy(mean_row[0:1, 0:w], b.psA[:, 0:w])
    p.v(fm)
    p.stage()

    def fv(e, w=w):
        return e.tensor_tensor(b.var[0:1, 0:w], mean_row[0:1, 0:w],
                               mean_row[0:1, 0:w], ALU.mult)
    p.v(fv)
    p.stage()

    def fv2(e, w=w):
        return e.tensor_tensor(b.var[0:1, 0:w], b.psB[:, 0:w], b.var[0:1, 0:w],
                               ALU.subtract)
    p.v(fv2)
    p.stage()

    epscol = constcol(c, eps, 1)

    def fr1(e, w=w, epscol=epscol):
        return e.activation(rstd_row[0:1, 0:w], b.var[0:1, 0:w], AF.Ln,
                            bias=epscol[:, :])
    p.s(fr1)
    p.stage()

    def fr2(e, w=w):
        return e.activation(rstd_row[0:1, 0:w], rstd_row[0:1, 0:w], AF.Exp,
                            scale=-0.5)
    p.s(fr2)
    if mr_row is not None:
        p.stage()

        def fmr(e, w=w):
            return e.tensor_tensor(mr_row[0:1, 0:w], mean_row[0:1, 0:w],
                                   rstd_row[0:1, 0:w], ALU.mult)
        p.v(fmr)


def load_lhsT_bf16(p, c, dram_ap, K, M, name=None):
    nk = (K + 127) // 128
    tiles = []
    for k in range(nk):
        kk = min(128, K - k * 128)
        tile = c.sb([kk, M], BF16, name and f"{name}_{k}")
        tiles.append(tile)

        def f(e, tile=tile, k=k, kk=kk):
            return e.dma_start(tile[:, :], dram_ap[k * 128:k * 128 + kk, :])
        p.dg(f)
    return tiles


def load_col_f32(p, c, dram_ap, n, name=None):
    tiles = []
    for i in range((n + 127) // 128):
        kk = min(128, n - i * 128)
        tile = c.sb([kk, 1], F32, name and f"{name}_{i}")
        tiles.append(tile)

        def f(e, tile=tile, i=i, kk=kk):
            return e.dma_start(tile[:, :], dram_ap[i * 128:i * 128 + kk, :])
        p.d(f)
    return tiles



def emit_ffn0(p, c, getX, putX, wrT, wkT, wvT, mur, muk):
    tc = 256
    PSG = getattr(c, "PSG", None)
    if PSG is not None:
        ps_g, ps_mr = PSG[0][:, 0:tc], PSG[1][:, 0:tc]
        ps_mm = [PSG[2][:, 0:tc], PSG[3][:, 0:tc]]
        ps_v = [PSG[4][:, 0:tc], PSG[5][:, 0:tc]]
        lnb = LNBufs(c, tc, psA=PSG[2][0:1, 0:tc], psB=PSG[3][0:1, 0:tc])
    else:
        ps_g = c.ps([128, tc], F32)
        ps_mr = c.ps([128, tc], F32)
        ps_mm = [c.ps([128, tc], F32), c.ps([128, tc], F32)]
        ps_v = [c.ps([128, tc], F32), c.ps([128, tc], F32)]
        lnb = LNBufs(c, tc)
    mean_row = c.sb([1, tc], F32)
    rstd_row = c.sb([1, tc], F32)
    mr_row = c.sb([1, tc], F32)

    om_r = [c.sb([128, 1], F32) for _ in range(NT)]
    om_k = [c.sb([128, 1], F32) for _ in range(NT)]
    p.stage()
    for i in range(NT):
        def f1(e, i=i):
            return e.tensor_scalar(om_r[i][:, :], mur[i][:, :], -1.0, 1.0,
                                   ALU.mult, op1=ALU.add)
        p.v(f1)

        def f2(e, i=i):
            return e.tensor_scalar(om_k[i][:, :], muk[i][:, :], -1.0, 1.0,
                                   ALU.mult, op1=ALU.add)
        p.v(f2)

    xn = [c.sb([128, tc], F32) for _ in range(NT)]
    stash = c.sb([128, NT], F32)
    zcol = c.sb([128, 1], F32)
    tmp = [c.sb([128, tc], F32)]
    xr_bf = [c.sb([128, tc], BF16) for _ in range(NT)]
    xk_bf = [c.sb([128, tc], BF16) for _ in range(NT)]
    r_sig1 = c.sb([128, tc], F32)
    k2_bf = [c.sb([128, tc], BF16) for _ in range(NT)]
    krelu = c.sb([128, tc], F32)
    vtmp = c.sb([128, tc], F32)
    p.stage()
    p.g(lambda e: e.memset(zcol[:, :], 0.0))
    p.g(lambda e: e.memset(stash[:, :], 0.0))

    for ci in range(T // tc):
        cs = slice(ci * tc, (ci + 1) * tc)
        p.stage()
        Xv = [getX(i, cs) for i in range(NT)]
        p.stage()
        emit_ln_stats_chunk(p, c, Xv, slice(0, tc), lnb, mean_row, rstd_row,
                            mr_row)
        p.stage()

        def fb1(e):
            return e.matmul(ps_g[:, :], c.ones_f[:, :], rstd_row[0:1, :],
                            start=True, stop=True)
        p.t(fb1)

        def fb2(e):
            return e.matmul(ps_mr[:, :], c.ones_f[:, :], mr_row[0:1, :],
                            start=True, stop=True)
        p.t(fb2)
        p.stage()
        for i in range(NT):
            def fx1(e, i=i):
                return e.tensor_tensor(xn[i][:, :], Xv[i][:, :], ps_g[:, :],
                                       ALU.mult)
            p.v(fx1)

            def fx2(e, i=i):
                return e.tensor_tensor(xn[i][:, :], xn[i][:, :], ps_mr[:, :],
                                       ALU.subtract)
            p.v(fx2)
        p.stage()
        for i in range(NT):
            for (om, mu, dst) in ((om_r, mur, xr_bf), (om_k, muk, xk_bf)):
                def fs1(e, i=i, om=om):
                    return e.tensor_scalar(tmp[0][:, 1:tc], xn[i][:, 0:tc - 1],
                                           om[i][:, :], None, ALU.mult)
                p.v(fs1)

                def fs2(e, i=i, om=om, ci=ci):
                    src = stash[:, i:i + 1] if ci > 0 else zcol[:, :]
                    return e.tensor_scalar(tmp[0][:, 0:1], src, om[i][:, :],
                                           None, ALU.mult)
                p.v(fs2)

                def fs3(e, i=i, mu=mu, dst=dst):
                    return e.scalar_tensor_tensor(dst[i][:, :], xn[i][:, :],
                                                  mu[i][:, :], tmp[0][:, :],
                                                  ALU.mult, ALU.add)
                p.v(fs3)
        p.stage()
        for i in range(NT):
            def fst(e, i=i):
                return e.tensor_copy(stash[:, i:i + 1], xn[i][:, tc - 1:tc])
            p.v(fst)

        for m in range(NT):
            p.stage()
            for k in range(NT):
                def fm(e, m=m, k=k):
                    return e.matmul(ps_mm[m % 2][:, :],
                                    wkT[k][:, m * 128:(m + 1) * 128],
                                    xk_bf[k][:, :],
                                    start=(k == 0), stop=(k == NT - 1))
                p.t(fm)
            p.stage()

            def fr(e, m=m):
                return e.activation(krelu[:, :], ps_mm[m % 2][:, :], AF.Relu)
            p.s(fr)
            p.stage()

            def fq(e, m=m):
                return e.tensor_tensor(k2_bf[m][:, :], krelu[:, :],
                                       krelu[:, :], ALU.mult)
            p.v(fq)
        for m in range(NT):
            p.stage()
            for k in range(NT):
                def fm(e, m=m, k=k):
                    return e.matmul(ps_mm[m % 2][:, :],
                                    wrT[k][:, m * 128:(m + 1) * 128],
                                    xr_bf[k][:, :],
                                    start=(k == 0), stop=(k == NT - 1))
                p.t(fm)
            p.stage()
            for k in range(NT):
                def fm(e, m=m, k=k):
                    return e.matmul(ps_v[m % 2][:, :],
                                    wvT[k][:, m * 128:(m + 1) * 128],
                                    k2_bf[k][:, :],
                                    start=(k == 0), stop=(k == NT - 1))
                p.t(fm)

            def fe(e, m=m):
                return e.activation(r_sig1[:, :], ps_mm[m % 2][:, :],
                                    AF.Sigmoid)
            p.s(fe)
            p.stage()

            def fv1(e, m=m):
                return e.scalar_tensor_tensor(vtmp[:, :], r_sig1[:, :],
                                              0.5, ps_v[m % 2][:, :],
                                              ALU.mult, ALU.mult)
            p.v(fv1)
            p.stage()

            def fv2(e, m=m):
                return e.tensor_tensor(Xv[m][:, :], Xv[m][:, :], vtmp[:, :],
                                       ALU.add)
            p.v(fv2)
        p.stage()
        for i in range(NT):
            putX(i, cs, Xv[i])
        p.stage()


def emit_ffn0_phase(p, c, xT, yT, wL):
    wr = load_lhsT_bf16(p, c, wL["wrT"], D, D)
    wk = load_lhsT_bf16(p, c, wL["wkT"], D, D)
    wv = load_lhsT_bf16(p, c, wL["wvT"], D, D)
    mur_t = load_col_f32(p, c, wL["mur"], D)
    muk_t = load_col_f32(p, c, wL["muk"], D)
    xbufs = [c.sb([128, 256], F32) for _ in range(NT)]

    def getX(i, cs):
        p.on("sync", lambda e, i=i, cs=cs: e.dma_start(
            xbufs[i][:, :], xT[i * 128:(i + 1) * 128, cs]), dma=True)
        return xbufs[i]

    def putX(i, cs, buf):
        p.on("sync", lambda e, i=i, cs=cs, buf=buf: e.dma_start(
            yT[i * 128:(i + 1) * 128, cs], buf[:, :]), dma=True)

    emit_ffn0(p, c, getX, putX, wr, wk, wv, mur_t, muk_t)


def make_ffn0_nc():
    nc = bass.Bass("TRN2", target_bir_lowering=False)
    xT = nc.dram_tensor("xT", [D, T], F32, kind="ExternalInput").ap()
    wL = {
        "wrT": nc.dram_tensor("wrT", [D, D], F32, kind="ExternalInput").ap(),
        "wkT": nc.dram_tensor("wkT", [D, D], F32, kind="ExternalInput").ap(),
        "wvT": nc.dram_tensor("wvT", [D, D], F32, kind="ExternalInput").ap(),
        "mur": nc.dram_tensor("mur", [D, 1], F32, kind="ExternalInput").ap(),
        "muk": nc.dram_tensor("muk", [D, 1], F32, kind="ExternalInput").ap(),
    }
    yT = nc.dram_tensor("yT", [D, T], F32, kind="ExternalOutput").ap()
    with ExitStack() as stack:
        c = Ctx(nc, stack)
        sem = c.sem("SS")
        block = stack.enter_context(nc.Block())
        p = Prog(nc)
        setup_consts(p, c)
        p.stage()
        emit_ffn0_phase(p, c, xT, yT, wL)
        p.emit(block, sem)
    return nc



RMS_EPS = 1.1920929e-07


def emit_attn0_prep(p, c):
    c.eye_bf = c.sb([128, 128], BF16, "eye_bf")
    c.eye_f = c.sb([128, 128], F32, "eye_f")
    c.mask_bf = c.sb([128, 128], BF16, "mask_bf")
    it = c.sb([128, 128], F32)
    p.stage()
    p.g(lambda e: e.iota(it[:, :], [[1, 128]], base=0, channel_multiplier=-1,
                         allow_small_or_imprecise_dtypes=True))
    p.stage()
    p.v(lambda e: e.tensor_scalar(c.eye_bf[:, :], it[:, :], 0.0, None,
                                  ALU.is_equal))
    p.v(lambda e: e.tensor_scalar(c.eye_f[:, :], it[:, :], 0.0, None,
                                  ALU.is_equal))
    p.v(lambda e: e.tensor_scalar(c.mask_bf[:, :], it[:, :], 0.0, None,
                                  ALU.is_ge))


def attn_bufs(c):
    B = {}
    if getattr(c, "PSG", None) is not None:
        B["PS"] = c.PSG
        B["PT"] = c.PTB
    else:
        B["PS"] = [c.ps([128, 512], F32, f"APS{i}") for i in range(7)]
        B["PT"] = c.ps([128, 256], BF16, "APT")
    B["rstd_row"] = c.sb([1, T], F32, "rstd_row")
    B["mr_row"] = c.sb([1, T], F32, "mr_row")
    B["xnh_f"] = c.sb([64, 512], F32)
    B["xnh"] = c.sb([64, T], BF16, "xnh")
    B["cq"] = [c.sb([128, 512], BF16) for _ in range(4)]
    B["ckv"] = [c.sb([128, 512], BF16) for _ in range(8)]
    B["qc_f"] = c.sb([64, 512], F32)
    B["kc_f"] = c.sb([64, 512], F32)
    B["v_bf"] = c.sb([64, 512], BF16)
    B["sq"] = c.sb([64, 512], F32)
    B["sk"] = c.sb([64, 512], F32)
    B["msq"] = c.sb([1, 512], F32)
    B["msk"] = c.sb([1, 512], F32)
    B["Q"] = c.sb([128, T], BF16, "Qt")
    B["K"] = c.sb([128, T], BF16, "Kt")
    B["v_tok"] = c.sb([128, 65 * 32], BF16, "v_tok")
    B["qs_bf"] = c.sb([128, 512], BF16)
    B["ks_bf"] = c.sb([128, 512], BF16)
    B["ones_bfc"] = c.sb([128, 1], BF16)
    B["inv64"] = c.sb([64, 1], F32)
    B["t11a"] = c.sb([1, 1], F32)
    B["t11b"] = c.sb([1, 1], F32)
    B["qmax"] = c.sb([1, 1], F32)
    B["kmax"] = c.sb([1, 1], F32)
    B["mh_col"] = c.sb([128, 1], F32)
    B["p_bf"] = c.sb([128, 128], BF16)
    B["rc"] = c.sb([128, 1], F32)
    B["ctx_bf"] = c.sb([128, 64], BF16)
    B["ctxT_bf"] = c.sb([64, 128], BF16)
    return B


def emit_attn0_head(p, c, h, xh, w, B, Xrows_add):
    tcq = 512
    PS = B["PS"]
    PT = B["PT"]
    nchunk = T // tcq
    for ci in range(nchunk):
        cs = slice(ci * tcq, (ci + 1) * tcq)
        p.stage()
        p.t(lambda e, cs=cs: e.matmul(PS[0][0:64, :], c.ones_f[0:1, 0:64],
                                      B["rstd_row"][0:1, cs], start=True,
                                      stop=True))
        p.t(lambda e, cs=cs: e.matmul(PS[1][0:64, :], c.ones_f[0:1, 0:64],
                                      B["mr_row"][0:1, cs], start=True,
                                      stop=True))
        p.stage()
        p.v(lambda e, cs=cs: e.tensor_tensor(B["xnh_f"][:, :], xh[:, cs],
                                             PS[0][0:64, :], ALU.mult))
        p.stage()
        p.v(lambda e, cs=cs: e.tensor_tensor(B["xnh"][:, cs], B["xnh_f"][:, :],
                                             PS[1][0:64, :], ALU.subtract))
        p.stage()
        for m in range(4):
            p.t(lambda e, m=m, cs=cs: e.matmul(
                PS[2][:, :], w["dq"][0:64, m * 128:(m + 1) * 128],
                B["xnh"][:, cs], start=True, stop=True))
            p.stage()
            if m % 2 == 0:
                p.s(lambda e, m=m: e.copy(B["cq"][m][:, :], PS[2][:, :]))
            else:
                p.v(lambda e, m=m: e.tensor_copy(B["cq"][m][:, :], PS[2][:, :]))
            p.stage()
        for m in range(8):
            p.t(lambda e, m=m, cs=cs: e.matmul(
                PS[3][:, :], w["dkv"][0:64, m * 128:(m + 1) * 128],
                B["xnh"][:, cs], start=True, stop=True))
            p.stage()
            if m % 2 == 0:
                p.s(lambda e, m=m: e.copy(B["ckv"][m][:, :], PS[3][:, :]))
            else:
                p.v(lambda e, m=m: e.tensor_copy(B["ckv"][m][:, :],
                                                 PS[3][:, :]))
            p.stage()
        for k in range(4):
            p.t(lambda e, k=k: e.matmul(PS[4][0:64, :], w["uq"][k][:, 0:64],
                                        B["cq"][k][:, :],
                                        start=(k == 0), stop=(k == 3)))
        for k in range(8):
            p.t(lambda e, k=k: e.matmul(PS[5][:, :], w["ukv"][k][:, :],
                                        B["ckv"][k][:, :],
                                        start=(k == 0), stop=(k == 7)))
        p.stage()
        p.s(lambda e: e.copy(B["qc_f"][:, :], PS[4][0:64, :]))
        p.s(lambda e: e.copy(B["kc_f"][:, :], PS[5][0:64, :]))
        p.v(lambda e: e.tensor_copy(B["v_bf"][:, :], PS[5][64:128, :]))
        p.stage()
        p.v(lambda e: e.tensor_tensor(B["sq"][:, :], B["qc_f"][:, :],
                                      B["qc_f"][:, :], ALU.mult))
        p.v(lambda e: e.tensor_tensor(B["sk"][:, :], B["kc_f"][:, :],
                                      B["kc_f"][:, :], ALU.mult))
        for j in range(tcq // 128):
            jt = ci * 4 + j
            p.stage()
            p.t(lambda e, j=j: e.transpose(
                PT[:, 0:64], B["v_bf"][:, j * 128:(j + 1) * 128],
                c.eye_bf[0:64, 0:64]))
            p.stage()
            p.v(lambda e, jt=jt: e.tensor_copy(
                B["v_tok"][:, jt * 65:jt * 65 + 64], PT[:, 0:64]))
        p.stage()
        p.t(lambda e: e.matmul(PS[6][0:1, :], B["inv64"][:, :],
                               B["sq"][:, :], start=True, stop=True))
        p.t(lambda e: e.matmul(PS[2][0:1, :], B["inv64"][:, :],
                               B["sk"][:, :], start=True, stop=True))
        p.stage()
        epsc = constcol(c, RMS_EPS, 1)
        p.s(lambda e, epsc=epsc: e.activation(B["msq"][0:1, :], PS[6][0:1, :],
                                              AF.Ln, bias=epsc[:, :]))
        p.s(lambda e, epsc=epsc: e.activation(B["msk"][0:1, :], PS[2][0:1, :],
                                              AF.Ln, bias=epsc[:, :]))
        p.stage()
        p.s(lambda e: e.activation(B["msq"][0:1, :], B["msq"][0:1, :],
                                   AF.Exp, scale=-0.5))
        p.s(lambda e: e.activation(B["msk"][0:1, :], B["msk"][0:1, :],
                                   AF.Exp, scale=-0.5))
        p.stage()
        p.t(lambda e: e.matmul(PS[6][0:64, :], c.ones_f[0:1, 0:64],
                               B["msq"][0:1, :], start=True, stop=True))
        p.t(lambda e: e.matmul(PS[2][0:64, :], c.ones_f[0:1, 0:64],
                               B["msk"][0:1, :], start=True, stop=True))
        p.stage()
        p.v(lambda e, cs=cs: e.scalar_tensor_tensor(
            B["Q"][0:64, cs], B["qc_f"][:, :], w["qnw"][:, :],
            PS[6][0:64, :], ALU.mult, ALU.mult))
        p.v(lambda e, cs=cs: e.scalar_tensor_tensor(
            B["K"][0:64, cs], B["kc_f"][:, :], w["knw"][:, :],
            PS[2][0:64, :], ALU.mult, ALU.mult))
        p.stage()
        p.t(lambda e, cs=cs: e.matmul(PS[4][0:64, :], w["qr"][0:64, 0:64],
                                      B["Q"][0:64, cs], start=True, stop=True))
        p.t(lambda e, cs=cs: e.matmul(PS[5][0:64, :], w["kr"][0:64, 0:64],
                                      B["xnh"][:, cs], start=True, stop=True))
        p.stage()
        p.s(lambda e, cs=cs: e.copy(B["Q"][64:128, cs], PS[4][0:64, :]))
        p.s(lambda e, cs=cs: e.copy(B["K"][64:128, cs], PS[5][0:64, :]))
        p.stage()
        p.v(lambda e, cs=cs: e.tensor_tensor(B["qs_bf"][:, :], B["Q"][:, cs],
                                             B["Q"][:, cs], ALU.mult))
        p.v(lambda e, cs=cs: e.tensor_tensor(B["ks_bf"][:, :], B["K"][:, cs],
                                             B["K"][:, cs], ALU.mult))
        p.stage()
        p.t(lambda e: e.matmul(PS[4][0:1, :], B["ones_bfc"][:, :],
                               B["qs_bf"][:, :], start=True, stop=True))
        p.t(lambda e: e.matmul(PS[5][0:1, :], B["ones_bfc"][:, :],
                               B["ks_bf"][:, :], start=True, stop=True))
        p.stage()
        p.v(lambda e: e.tensor_reduce(B["t11a"][0:1, 0:1], PS[4][0:1, :],
                                      AX.X, ALU.max))
        p.v(lambda e: e.tensor_reduce(B["t11b"][0:1, 0:1], PS[5][0:1, :],
                                      AX.X, ALU.max))
        p.stage()
        p.v(lambda e, ci=ci: e.tensor_tensor(
            B["qmax"][0:1, 0:1],
            B["qmax"][0:1, 0:1] if ci else B["t11a"][0:1, 0:1],
            B["t11a"][0:1, 0:1], ALU.max))
        p.v(lambda e, ci=ci: e.tensor_tensor(
            B["kmax"][0:1, 0:1],
            B["kmax"][0:1, 0:1] if ci else B["t11b"][0:1, 0:1],
            B["t11b"][0:1, 0:1], ALU.max))
    p.stage()
    p.v(lambda e: e.tensor_tensor(B["t11a"][0:1, 0:1], B["qmax"][0:1, 0:1],
                                  B["kmax"][0:1, 0:1], ALU.mult))
    p.stage()
    p.s(lambda e: e.activation(B["t11a"][0:1, 0:1], B["t11a"][0:1, 0:1],
                               AF.Sqrt))
    p.stage()
    p.v(lambda e: e.tensor_scalar(B["t11a"][0:1, 0:1], B["t11a"][0:1, 0:1],
                                  0.125, 80.0, ALU.mult, op1=ALU.min))
    p.stage()
    p.v(lambda e: e.tensor_scalar(B["t11a"][0:1, 0:1], B["t11a"][0:1, 0:1],
                                  -1.0, None, ALU.mult))
    p.stage()
    p.t(lambda e: e.matmul(PS[0][:, 0:1], c.ones_f[0:1, :],
                           B["t11a"][0:1, 0:1], start=True, stop=True))
    p.stage()
    p.v(lambda e: e.tensor_copy(B["mh_col"][:, :], PS[0][:, 0:1]))
    p.stage()
    for qt in range(T // 128):
        tq0 = qt * 128
        for j in range(qt + 1):
            p.stage()
            p.t(lambda e, j=j, tq0=tq0: e.matmul(
                PS[2][:, 0:128], B["K"][:, j * 128:(j + 1) * 128],
                B["Q"][:, tq0:tq0 + 128], start=True, stop=True))
            p.stage()
            p.s(lambda e: e.activation(B["p_bf"][:, :], PS[2][:, 0:128],
                                       AF.Exp, scale=0.125,
                                       bias=B["mh_col"][:, :]))
            if j == qt:
                p.stage()
                p.v(lambda e: e.tensor_tensor(B["p_bf"][:, :], B["p_bf"][:, :],
                                              c.mask_bf[:, :], ALU.mult))
            p.stage()
            p.t(lambda e, j=j, qt=qt: e.matmul(
                PS[3][:, 0:65], B["p_bf"][:, :],
                B["v_tok"][:, j * 65:(j + 1) * 65],
                start=(j == 0), stop=(j == qt)))
        p.stage()
        p.v(lambda e: e.reciprocal(B["rc"][:, :], PS[3][:, 64:65]))
        p.stage()
        p.v(lambda e: e.tensor_scalar(B["ctx_bf"][:, :], PS[3][:, 0:64],
                                      B["rc"][:, :], None, ALU.mult))
        p.stage()
        p.t(lambda e: e.transpose(PT[0:64, 0:128], B["ctx_bf"][:, :],
                                  c.eye_bf[:, :]))
        p.stage()
        p.v(lambda e: e.tensor_copy(B["ctxT_bf"][:, :], PT[0:64, 0:128]))
        p.stage()
        p.t(lambda e: e.matmul(PS[4][0:64, 0:128], w["wo"][0:64, 0:64],
                               B["ctxT_bf"][:, :], start=True, stop=True))
        p.stage()
        p.v(lambda e, tq0=tq0: e.tensor_tensor(
            Xrows_add[:, tq0:tq0 + 128], Xrows_add[:, tq0:tq0 + 128],
            PS[4][0:64, 0:128], ALU.add))


def emit_attn_weights_load(p, c, w, aps):
    (wdqT, wuqT, wdkvT, wukvT, wqrT, wkrT, woT, qnw, knw) = aps
    w["dq"] = c.sb([64, 512], BF16, "wdq")
    w["uq"] = [c.sb([128, 64], BF16, f"wuq{k}") for k in range(4)]
    w["dkv"] = c.sb([64, 1024], BF16, "wdkv")
    w["ukv"] = [c.sb([128, 128], BF16, f"wukv{k}") for k in range(8)]
    w["qr"] = c.sb([64, 64], BF16, "wqr")
    w["kr"] = c.sb([64, 64], BF16, "wkr")
    w["wo"] = c.sb([64, 64], BF16, "wwo")
    w["qnw"] = c.sb([64, 1], F32, "wqnw")
    w["knw"] = c.sb([64, 1], F32, "wknw")
    p.dg(lambda e: e.dma_start(w["dq"][:, :], wdqT))
    for k in range(4):
        p.dg(lambda e, k=k: e.dma_start(w["uq"][k][:, :],
                                        wuqT[k * 128:(k + 1) * 128, :]))
    p.dg(lambda e: e.dma_start(w["dkv"][:, :], wdkvT))
    for k in range(8):
        p.dg(lambda e, k=k: e.dma_start(w["ukv"][k][:, :],
                                        wukvT[k * 128:(k + 1) * 128, :]))
    p.dg(lambda e: e.dma_start(w["qr"][:, :], wqrT))
    p.dg(lambda e: e.dma_start(w["kr"][:, :], wkrT))
    p.dg(lambda e: e.dma_start(w["wo"][:, :], woT))
    p.d(lambda e: e.dma_start(w["qnw"][:, :], qnw))
    p.d(lambda e: e.dma_start(w["knw"][:, :], knw))


def emit_ln_stats_stream(p, c, xT, B, lnb, xbufs):
    tc = lnb.tc
    mean_row = lnb.var
    for ci in range(T // tc):
        cs = slice(ci * tc, (ci + 1) * tc)
        for i in range(NT):
            buf = xbufs[i % 2]
            p.stage()
            p.d(lambda e, buf=buf, i=i, cs=cs: e.dma_start(
                buf[:, :], xT[i * 128:(i + 1) * 128, cs]))
            p.stage()
            p.s(lambda e, buf=buf, i=i: e.activation(
                lnb.sq[i % 2][:, :], buf[:, :], AF.Square))
            p.stage()
            p.t(lambda e, i=i, buf=buf: e.matmul(
                lnb.psA[:, :], c.invD_col[:, :], buf[:, :],
                start=(i == 0), stop=(i == NT - 1)))
            p.t(lambda e, i=i: e.matmul(
                lnb.psB[:, :], c.invD_col[:, :], lnb.sq[i % 2][:, :],
                start=(i == 0), stop=(i == NT - 1)))
        p.stage()
        p.v(lambda e, cs=cs: e.tensor_copy(B["mr_row"][0:1, cs],
                                           lnb.psA[:, :]))
        p.stage()
        p.v(lambda e, cs=cs: e.tensor_tensor(lnb.var[0:1, :],
                                             B["mr_row"][0:1, cs],
                                             B["mr_row"][0:1, cs], ALU.mult))
        p.stage()
        p.v(lambda e: e.tensor_tensor(lnb.var[0:1, :], lnb.psB[:, :],
                                      lnb.var[0:1, :], ALU.subtract))
        p.stage()
        epsc = constcol(c, LN_EPS, 1)
        p.s(lambda e, cs=cs, epsc=epsc: e.activation(
            B["rstd_row"][0:1, cs], lnb.var[0:1, :], AF.Ln, bias=epsc[:, :]))
        p.stage()
        p.s(lambda e, cs=cs: e.activation(B["rstd_row"][0:1, cs],
                                          B["rstd_row"][0:1, cs],
                                          AF.Exp, scale=-0.5))
        p.stage()
        p.v(lambda e, cs=cs: e.tensor_tensor(B["mr_row"][0:1, cs],
                                             B["mr_row"][0:1, cs],
                                             B["rstd_row"][0:1, cs],
                                             ALU.mult))


def emit_attn0_phase(p, c, xT, yT, aps):
    B = attn_bufs(c)
    lnb = LNBufs(c, 512, psA=B["PS"][0][0:1, :], psB=B["PS"][1][0:1, :])
    w = {}
    emit_attn_weights_load(p, c, w, aps)
    p.g(lambda e: e.memset(B["ones_bfc"][:, :], 1.0))
    p.g(lambda e: e.memset(B["inv64"][:, :], 1.0 / 64))
    p.g(lambda e: e.memset(B["v_tok"][:, :], 1.0))
    xbufs = [c.sb([128, 512], F32), c.sb([128, 512], F32)]
    emit_ln_stats_stream(p, c, xT, B, lnb, xbufs)
    xh = c.sb([64, T], F32)
    for h in range(16):
        i, r0 = h // 2, (h % 2) * 64
        p.stage()
        p.d(lambda e, i=i, r0=r0: e.dma_start(
            xh[:, :], xT[i * 128 + r0:i * 128 + r0 + 64, :]))
        p.stage()
        emit_attn0_head(p, c, h, xh, w, B, xh)
        p.stage()
        p.d(lambda e, i=i, r0=r0: e.dma_start(
            yT[i * 128 + r0:i * 128 + r0 + 64, :], xh[:, :]))
        p.stage()


def make_attn0_nc():
    nc = bass.Bass("TRN2", target_bir_lowering=False)
    xT = nc.dram_tensor("xT", [D, T], F32, kind="ExternalInput").ap()
    aps = tuple(
        nc.dram_tensor(nm, sh, F32, kind="ExternalInput").ap()
        for nm, sh in (
            ("wdqT", [64, 512]), ("wuqT", [512, 64]), ("wdkvT", [64, 1024]),
            ("wukvT", [1024, 128]), ("wqrT", [64, 64]), ("wkrT", [64, 64]),
            ("woT", [64, 64]), ("qnw", [64, 1]), ("knw", [64, 1])))
    yT = nc.dram_tensor("yT", [D, T], F32, kind="ExternalOutput").ap()
    with ExitStack() as stack:
        c = Ctx(nc, stack)
        sem = c.sem("SS")
        block = stack.enter_context(nc.Block())
        p = Prog(nc)
        setup_consts(p, c)
        emit_attn0_prep(p, c)
        emit_attn0_phase(p, c, xT, yT, aps)
        p.emit(block, sem)
    return nc



NUM_HEAVY = 1024
LOGK = float(np.log(np.float32(1152.0)))
BIGPOS = 1.0e9


def ffn1_bufs(c):
    B = {}
    if getattr(c, "PSG", None) is not None:
        B["PS"] = c.PSG[0:6]
        B["PTf"] = c.PSG[6][:, 0:128]
        B["PTb"] = c.PTB
    else:
        B["PS"] = [c.ps([128, 512], F32, f"FPS{i}") for i in range(6)]
        B["PTf"] = c.ps([128, 128], F32, "FPTf")
        B["PTb"] = c.ps([128, 256], BF16, "FPTb")
    B["xn"] = [c.sb([128, 512], F32) for _ in range(NT)]
    B["mean"] = c.sb([1, 512], F32)
    B["rstd"] = c.sb([1, 512], F32)
    B["mr"] = c.sb([1, 512], F32)
    B["sn"] = c.sb([1, 512], F32)
    B["s_chunk"] = c.sb([1, 512], F32)
    B["li_bf"] = [c.sb([128, 512], BF16) for _ in range(NT)]
    B["h1l_bf"] = [c.sb([128, 512], BF16) for _ in range(4)]
    B["ladd"] = c.sb([128, 512], F32)
    B["big"] = c.sb([128, 1024], F32, "bigbuf")
    for nm in ("s_pm", "b_pm", "u_pm", "e_pm", "q_pm", "i_all", "cnt_gt",
               "cnt_eq", "sel_pm", "pfx_pm", "ones_pm", "d_pm", "t_pm"):
        B[nm] = c.sb([128, 32], F32, nm)
    B["colA"] = c.sb([128, 1], F32)
    B["colB"] = c.sb([128, 1], F32)
    B["t11"] = [c.sb([1, 1], F32) for _ in range(4)]
    B["trow"] = c.sb([1, 128], F32)
    B["prow"] = c.sb([1, 128], F32)
    B["orow"] = c.sb([1, 128], F32)
    B["q_rep"] = c.sb([128, T], F32, "q_rep")
    B["junk_bf"] = c.sb([128, T], BF16, "junk_bf")
    B["mask_bf"] = c.sb([128, T], BF16, "maskT")
    B["iotaT"] = c.sb([128, T], F32, "iotaT")
    B["d_u32"] = c.sb([128, 32], U32)
    B["iota_src"] = c.sb([128, 32], U32, persist=True)
    B["offs"] = c.sb([128, 1], U32, persist=True)
    B["idx"] = [c.sb([128, 1], U32, persist=True) for _ in range(8)]
    B["hin_bf"] = c.sb([128, 1024], BF16, "hin_tok")
    B["hinT"] = [c.sb([128, 1024], BF16) for _ in range(NT)]
    B["h1h_bf"] = [c.sb([128, 256], BF16) for _ in range(32)]
    B["hoT"] = [c.sb([128, 256], F32) for _ in range(NT)]
    B["w1h"] = c.sb([128, 1024], BF16, "w1hbuf")
    B["w2h"] = c.sb([128, 2048], BF16, "w2hbuf")
    B["srms_col"] = c.sb([128, 1], F32)
    B["sq_col"] = c.sb([128, 1], F32)
    return B


def emit_ffn1(p, c, B, w, getX, putX, dram):
    PS = B["PS"]
    tc = 512
    lnb = w["lnb"]
    for ci in range(T // tc):
        cs = slice(ci * tc, (ci + 1) * tc)
        p.stage()
        Xv = [getX(i, cs) for i in range(NT)]
        p.stage()
        emit_ln_stats_chunk(p, c, Xv, slice(0, tc), lnb, B["mean"], B["rstd"],
                            B["mr"])
        p.stage()
        p.t(lambda e: e.matmul(PS[0][:, :], c.ones_f[:, :], B["rstd"][0:1, :],
                               start=True, stop=True))
        p.t(lambda e: e.matmul(PS[1][:, :], c.ones_f[:, :], B["mr"][0:1, :],
                               start=True, stop=True))
        p.stage()
        for i in range(NT):
            p.v(lambda e, i=i: e.tensor_tensor(B["xn"][i][:, :], Xv[i][:, :],
                                               PS[0][:, :], ALU.mult))
        p.stage()
        for i in range(NT):
            p.v(lambda e, i=i: e.tensor_tensor(B["xn"][i][:, :],
                                               B["xn"][i][:, :], PS[1][:, :],
                                               ALU.subtract))
        p.stage()
        for i in range(NT):
            p.t(lambda e, i=i: e.matmul(PS[2][0:1, :], w["rt"][i][:, :],
                                        B["xn"][i][:, :], start=(i == 0),
                                        stop=(i == NT - 1)))
            p.s(lambda e, i=i: e.activation(lnb.sq[i % 2][:, :],
                                            B["xn"][i][:, :], AF.Square))
            p.stage()
            p.t(lambda e, i=i: e.matmul(PS[3][0:1, :], c.onescol_f[:, :],
                                        lnb.sq[i % 2][:, :], start=(i == 0),
                                        stop=(i == NT - 1)))
        p.stage()
        p.v(lambda e: e.tensor_copy(B["s_chunk"][0:1, :], PS[2][0:1, :]))
        p.s(lambda e: e.activation(B["sn"][0:1, :], PS[3][0:1, :], AF.Sqrt))
        p.stage()
        p.d(lambda e, ci=ci: e.dma_start(dram["qrow"][ci * tc:(ci + 1) * tc],
                                         B["s_chunk"][0:1, :]))
        small = constcol(c, 1e-12, 1)
        p.v(lambda e, small=small: e.tensor_scalar(
            B["sn"][0:1, :], B["sn"][0:1, :], small[:, :], None, ALU.max))
        p.stage()
        p.v(lambda e: e.reciprocal(B["sn"][0:1, :], B["sn"][0:1, :]))
        p.stage()
        p.v(lambda e: e.tensor_scalar(B["sn"][0:1, :], B["sn"][0:1, :],
                                      float(np.sqrt(1024.0)), None, ALU.mult))
        p.stage()
        p.t(lambda e: e.matmul(PS[4][:, :], c.ones_f[:, :], B["sn"][0:1, :],
                               start=True, stop=True))
        p.stage()
        for i in range(NT):
            p.v(lambda e, i=i: e.tensor_tensor(B["li_bf"][i][:, :],
                                               B["xn"][i][:, :], PS[4][:, :],
                                               ALU.mult))
        for tb in range(tc // 128):
            for i in range(NT):
                p.stage()
                p.t(lambda e, i=i, tb=tb: e.transpose(
                    B["PTf"][:, :], B["xn"][i][:, tb * 128:(tb + 1) * 128],
                    c.eye_f[:, :]))
                p.stage()
                p.v(lambda e, i=i: e.tensor_copy(
                    B["big"][:, i * 128:(i + 1) * 128], B["PTf"][:, :]))
            p.stage()
            t0 = ci * tc + tb * 128
            p.d(lambda e, t0=t0: e.dma_start(
                dram["xn_tok"][t0:t0 + 128, :], B["big"][:, :]))
        p.stage()
        for m in range(4):
            p.stage()
            for k in range(NT):
                p.t(lambda e, m=m, k=k: e.matmul(
                    PS[5][:, :], w["w1l"][k][:, m * 128:(m + 1) * 128],
                    B["li_bf"][k][:, :], start=(k == 0), stop=(k == NT - 1)))
            p.stage()
            p.s(lambda e, m=m: e.activation(B["h1l_bf"][m][:, :], PS[5][:, :],
                                            AF.Gelu, bias=w["b1l"][m][:, :]))
        for m in range(NT):
            p.stage()
            for k in range(4):
                p.t(lambda e, m=m, k=k: e.matmul(
                    PS[5][:, :], w["w2l"][k][:, m * 128:(m + 1) * 128],
                    B["h1l_bf"][k][:, :], start=(k == 0), stop=(k == 3)))
            p.stage()
            p.v(lambda e, m=m: e.tensor_scalar(
                B["ladd"][:, :], PS[5][:, :], 0.5, w["b2l_half"][m][:, :],
                ALU.mult, op1=ALU.add))
            p.stage()
            p.v(lambda e, m=m: e.tensor_tensor(Xv[m][:, :], Xv[m][:, :],
                                               B["ladd"][:, :], ALU.add))
        p.stage()
        for i in range(NT):
            putX(i, cs, Xv[i])
        p.stage()
    p.stage()
    p.d(lambda e: e.dma_start(
        B["s_pm"][:, :], bass.AP(dram["qrow"].tensor, 0, [[32, 128], [1, 32]])))
    p.stage()
    p.v(lambda e: e.tensor_copy(B["b_pm"][:, :], B["s_pm"][:, :]))
    p.stage()
    for it in range(50):
        p.stage()
        p.v(lambda e: e.tensor_tensor(B["u_pm"][:, :], B["s_pm"][:, :],
                                      B["b_pm"][:, :], ALU.subtract))
        p.stage()
        p.v(lambda e: e.tensor_reduce(B["colA"][:, :], B["u_pm"][:, :], AX.X,
                                      ALU.max))
        p.stage()
        p.t(lambda e: e.transpose(B["PTf"][0:1, :], B["colA"][:, :],
                                  c.eye_f[:, :]))
        p.stage()
        p.v(lambda e: e.tensor_reduce(B["t11"][0][0:1, :], B["PTf"][0:1, :],
                                      AX.X, ALU.max))
        p.stage()
        p.v(lambda e: e.tensor_scalar(B["t11"][1][0:1, :], B["t11"][0][0:1, :],
                                      -1.0, None, ALU.mult))
        p.stage()
        p.t(lambda e: e.matmul(PS[0][:, 0:1], c.ones_f[0:1, :],
                               B["t11"][1][0:1, :], start=True, stop=True))
        p.stage()
        p.v(lambda e: e.tensor_copy(B["colB"][:, :], PS[0][:, 0:1]))
        p.stage()
        p.s(lambda e: e.activation(B["e_pm"][:, :], B["u_pm"][:, :], AF.Exp,
                                   bias=B["colB"][:, :]))
        p.stage()
        p.v(lambda e: e.tensor_reduce(B["colA"][:, :], B["e_pm"][:, :], AX.X,
                                      ALU.add))
        p.stage()
        p.t(lambda e: e.transpose(B["PTf"][0:1, :], B["colA"][:, :],
                                  c.eye_f[:, :]))
        p.stage()
        p.v(lambda e: e.tensor_reduce(B["t11"][2][0:1, :], B["PTf"][0:1, :],
                                      AX.X, ALU.add))
        p.stage()
        p.s(lambda e: e.activation(B["t11"][2][0:1, :], B["t11"][2][0:1, :],
                                   AF.Ln))
        p.stage()
        p.v(lambda e: e.tensor_tensor(B["t11"][2][0:1, :], B["t11"][2][0:1, :],
                                      B["t11"][0][0:1, :], ALU.add))
        p.stage()
        p.v(lambda e: e.tensor_scalar(B["t11"][3][0:1, :], B["t11"][2][0:1, :],
                                      -1.0, LOGK, ALU.mult, op1=ALU.add))
        p.stage()
        p.t(lambda e: e.matmul(PS[0][:, 0:1], c.ones_f[0:1, :],
                               B["t11"][3][0:1, :], start=True, stop=True))
        p.stage()
        p.v(lambda e: e.tensor_copy(B["colA"][:, :], PS[0][:, 0:1]))
        p.stage()
        p.s(lambda e: e.activation(B["b_pm"][:, :], B["s_pm"][:, :], AF.Relu,
                                   bias=B["colA"][:, :]))
    p.stage()
    p.v(lambda e: e.tensor_scalar(B["q_pm"][:, :], B["s_pm"][:, :],
                                  B["colA"][:, :], 0.0, ALU.add, op1=ALU.min))
    p.stage()
    p.d(lambda e: e.dma_start(
        bass.AP(dram["qrow"].tensor, 0, [[32, 128], [1, 32]]), B["q_pm"][:, :]))
    p.stage()
    p.d(lambda e: e.dma_start(
        B["q_rep"][:, :], bass.AP(dram["qrow"].tensor, 0, [[0, 128], [1, T]])))
    p.g(lambda e: e.iota(B["iotaT"][:, :], [[1, T]], base=0,
                         channel_multiplier=0,
                         allow_small_or_imprecise_dtypes=True))
    p.g(lambda e: e.iota(B["i_all"][:, :], [[1, 32]], base=0,
                         channel_multiplier=32,
                         allow_small_or_imprecise_dtypes=True))
    p.g(lambda e: e.iota(B["iota_src"][:, :], [[128, 32]], base=0,
                         channel_multiplier=1))
    p.g(lambda e: e.memset(B["ones_pm"][:, :], 1.0))
    p.stage()
    for k in range(32):
        p.stage()
        p.v(lambda e, k=k: e.tensor_scalar(
            B["junk_bf"][:, :], B["q_rep"][:, :], B["q_pm"][:, k:k + 1], 0.0,
            ALU.is_gt, op1=ALU.add, accum_out=B["cnt_gt"][:, k:k + 1]))
        p.v(lambda e, k=k: e.tensor_scalar(
            B["mask_bf"][:, :], B["iotaT"][:, :], B["i_all"][:, k:k + 1], None,
            ALU.is_lt))
        p.stage()
        p.v(lambda e, k=k: e.scalar_tensor_tensor(
            B["junk_bf"][:, :], B["q_rep"][:, :], B["q_pm"][:, k:k + 1],
            B["mask_bf"][:, :], ALU.is_equal, ALU.mult,
            accum_out=B["cnt_eq"][:, k:k + 1]))
    p.stage()
    p.v(lambda e: e.tensor_tensor(B["sel_pm"][:, :], B["cnt_gt"][:, :],
                                  B["cnt_eq"][:, :], ALU.add))
    p.stage()
    p.v(lambda e: e.tensor_scalar(B["sel_pm"][:, :], B["sel_pm"][:, :],
                                  float(NUM_HEAVY), None, ALU.is_lt))
    p.stage()
    p.v(lambda e: e.tensor_tensor_scan(B["pfx_pm"][:, :], B["ones_pm"][:, :],
                                       B["sel_pm"][:, :], 0.0, ALU.mult,
                                       ALU.add))
    p.stage()
    p.t(lambda e: e.transpose(B["PTf"][0:1, :], B["pfx_pm"][:, 31:32],
                              c.eye_f[:, :]))
    p.stage()
    p.v(lambda e: e.tensor_copy(B["trow"][0:1, :], B["PTf"][0:1, :]))
    p.g(lambda e: e.memset(B["orow"][0:1, :], 1.0))
    p.stage()
    p.v(lambda e: e.tensor_tensor_scan(B["prow"][0:1, :], B["orow"][0:1, :],
                                       B["trow"][0:1, :], 0.0, ALU.mult,
                                       ALU.add))
    p.stage()
    p.v(lambda e: e.tensor_tensor(B["prow"][0:1, :], B["prow"][0:1, :],
                                  B["trow"][0:1, :], ALU.subtract))
    p.stage()
    p.t(lambda e: e.transpose(B["PTf"][:, 0:1], B["prow"][0:1, :],
                              c.eye_f[0:1, 0:1]))
    p.stage()
    p.v(lambda e: e.tensor_copy(B["colA"][:, :], B["PTf"][:, 0:1]))
    p.stage()
    p.v(lambda e: e.tensor_tensor(B["d_pm"][:, :], B["pfx_pm"][:, :],
                                  B["sel_pm"][:, :], ALU.subtract))
    p.stage()
    p.v(lambda e: e.tensor_scalar(B["d_pm"][:, :], B["d_pm"][:, :],
                                  B["colA"][:, :], None, ALU.add))
    p.stage()
    p.v(lambda e: e.tensor_tensor(B["d_pm"][:, :], B["d_pm"][:, :],
                                  B["sel_pm"][:, :], ALU.mult))
    p.stage()
    p.v(lambda e: e.tensor_scalar(B["t_pm"][:, :], B["sel_pm"][:, :],
                                  -1.0, 1.0, ALU.mult, op1=ALU.add))
    p.v(lambda e: e.tensor_scalar(B["u_pm"][:, :], B["i_all"][:, :],
                                  1.0, float(NUM_HEAVY), ALU.mult,
                                  op1=ALU.add))
    p.stage()
    p.v(lambda e: e.tensor_tensor(B["t_pm"][:, :], B["t_pm"][:, :],
                                  B["u_pm"][:, :], ALU.mult))
    p.stage()
    p.v(lambda e: e.tensor_tensor(B["d_pm"][:, :], B["d_pm"][:, :],
                                  B["t_pm"][:, :], ALU.add))
    p.stage()
    p.v(lambda e: e.tensor_copy(B["d_u32"][:, :], B["d_pm"][:, :]))
    p.stage()
    p.d(lambda e: e.dma_start(
        bass.AP(dram["drow"].tensor, 0, [[32, 128], [1, 32]]), B["d_u32"][:, :]))
    p.stage()
    p.g(lambda e: e.memset(B["big"][:, :], 0.0))
    p.stage()
    for k in range(32):
        p.stage()
        p.d(lambda e, k=k: e.dma_start(
            B["offs"][:, :],
            bass.AP(dram["drow"].tensor, k * 128, [[1, 128], [1, 1]])))
        p.stage()
        p.dg(lambda e, k=k: e.indirect_dma_start(
            dram["idx"],
            bass.IndirectOffsetOnAxis(ap=B["offs"][:, :], axis=0),
            B["iota_src"][:, k:k + 1], None))
    p.stage()
    for tb in range(T // 128):
        p.d(lambda e, tb=tb: e.dma_start(
            dram["hfull"][tb * 128:(tb + 1) * 128, :], B["big"][:, :]))
    p.stage()
    for r in range(8):
        p.d(lambda e, r=r: e.dma_start(
            B["idx"][r][:, :], dram["idx"][r * 128:(r + 1) * 128, :]))
    p.stage()
    for r in range(8):
        p.stage()
        p.dg(lambda e, r=r: e.indirect_dma_start(
            B["big"][:, :], None, dram["xn_tok"],
            bass.IndirectOffsetOnAxis(ap=B["idx"][r][:, :], axis=0)))
        p.stage()
        p.v(lambda e: e.tensor_tensor(B["junk_bf"].bitcast(F32)[:, 0:1024],
                                      B["big"][:, :], B["big"][:, :],
                                      ALU.mult))
        p.stage()
        p.v(lambda e: e.tensor_reduce(B["sq_col"][:, :],
                                      B["junk_bf"].bitcast(F32)[:, 0:1024],
                                      AX.X, ALU.add))
        p.stage()
        p.s(lambda e: e.activation(B["sq_col"][:, :], B["sq_col"][:, :],
                                   AF.Sqrt))
        p.stage()
        small = constcol(c, 1e-12, 128)
        p.v(lambda e, small=small: e.tensor_scalar(
            B["sq_col"][:, :], B["sq_col"][:, :], small[:, :], None, ALU.max))
        p.stage()
        p.v(lambda e: e.reciprocal(B["srms_col"][:, :], B["sq_col"][:, :]))
        p.stage()
        p.v(lambda e: e.tensor_scalar(B["srms_col"][:, :], B["srms_col"][:, :],
                                      float(np.sqrt(1024.0)), None, ALU.mult))
        p.stage()
        p.v(lambda e: e.tensor_scalar(B["hin_bf"][:, :], B["big"][:, :],
                                      B["srms_col"][:, :], None, ALU.mult))
        p.stage()
        for i in range(NT):
            p.t(lambda e, i=i: e.transpose(
                B["PTb"][:, 0:128], B["hin_bf"][:, i * 128:(i + 1) * 128],
                c.eye_bf[:, :]))
            p.stage()
            p.v(lambda e, i=i, r=r: e.tensor_copy(
                B["hinT"][i][:, r * 128:(r + 1) * 128], B["PTb"][:, 0:128]))
            p.stage()
    for rc in range(4):
        rs = slice(rc * 256, (rc + 1) * 256)
        for m in range(32):
            p.stage()
            p.dg(lambda e, m=m: e.dma_start(
                B["w1h"][:, :],
                bass.AP(w["w1hT"].tensor, m * 128,
                        [[4096, 128], [4096 * 128, 8], [1, 128]])))
            p.stage()
            for k in range(NT):
                p.v(lambda e, k=k: e.tensor_scalar(
                    B["w1h"][:, k * 128:(k + 1) * 128],
                    B["w1h"][:, k * 128:(k + 1) * 128],
                    w["gh"][k][:, :], None, ALU.mult))
            p.stage()
            for k in range(NT):
                p.t(lambda e, m=m, k=k, rs=rs: e.matmul(
                    PS[5][:, 0:256], B["w1h"][:, k * 128:(k + 1) * 128],
                    B["hinT"][k][:, rs], start=(k == 0), stop=(k == NT - 1)))
            p.stage()
            p.s(lambda e, m=m: e.activation(B["h1h_bf"][m][:, :], PS[5][:, 0:256],
                                            AF.Gelu, bias=w["b1h"][m][:, :]))
        for m in range(NT):
            for half in range(2):
                p.stage()
                p.dg(lambda e, m=m, half=half: e.dma_start(
                    B["w2h"][:, :],
                    bass.AP(w["w2hT"].tensor, m * 128 + half * 16 * 128 * 1024,
                            [[1024, 128], [1024 * 128, 16], [1, 128]])))
                p.stage()
                for kk in range(16):
                    k = half * 16 + kk
                    p.t(lambda e, m=m, k=k, kk=kk: e.matmul(
                        PS[5][:, 0:256], B["w2h"][:, kk * 128:(kk + 1) * 128],
                        B["h1h_bf"][k][:, :], start=(k == 0), stop=(k == 31)))
            p.stage()
            p.v(lambda e, m=m: e.tensor_scalar(
                B["hoT"][m][:, :], PS[5][:, 0:256], 0.5, w["b2h_half"][m][:, :],
                ALU.mult, op1=ALU.add))
        for rt_ in range(2):
            r = rc * 2 + rt_
            for i in range(NT):
                p.stage()
                p.t(lambda e, i=i, rt_=rt_: e.transpose(
                    B["PTf"][:, :], B["hoT"][i][:, rt_ * 128:(rt_ + 1) * 128],
                    c.eye_f[:, :]))
                p.stage()
                p.v(lambda e, i=i: e.tensor_copy(
                    B["big"][:, i * 128:(i + 1) * 128], B["PTf"][:, :]))
            p.stage()
            p.dg(lambda e, r=r: e.indirect_dma_start(
                dram["hfull"],
                bass.IndirectOffsetOnAxis(ap=B["idx"][r][:, :], axis=0),
                B["big"][:, :], None))
            p.stage()


def emit_hfull_add(p, c, B, dram, getXb, putXb):
    for g in range(T // 512):
        p.stage()
        Xv = [getXb(i, g) for i in range(NT)]
        for tb in range(4):
            t0 = g * 512 + tb * 128
            p.stage()
            p.d(lambda e, t0=t0: e.dma_start(B["big"][:, :],
                                             dram["hfull"][t0:t0 + 128, :]))
            p.stage()
            for i in range(NT):
                p.t(lambda e, i=i: e.transpose(
                    B["PTf"][:, :], B["big"][:, i * 128:(i + 1) * 128],
                    c.eye_f[:, :]))
                p.stage()
                p.v(lambda e, i=i, tb=tb: e.tensor_tensor(
                    Xv[i][:, tb * 128:(tb + 1) * 128],
                    Xv[i][:, tb * 128:(tb + 1) * 128],
                    B["PTf"][:, :], ALU.add))
                p.stage()
        p.stage()
        for i in range(NT):
            putXb(i, g, Xv[i])
        p.stage()


def emit_ffn1_weights_load(p, c, w, aps):
    (w1lT, b1l, w2lT, b2l, w1hT, b1h, w2hT, b2h, rt, gl, gh) = aps
    w["w1l"] = load_lhsT_bf16(p, c, w1lT, 1024, 512, "w1l")
    w["w2l"] = load_lhsT_bf16(p, c, w2lT, 512, 1024, "w2l")
    w["b1l"] = load_col_f32(p, c, b1l, 512, "b1l")
    b2l_t = load_col_f32(p, c, b2l, 1024, "b2l")
    w["b1h"] = load_col_f32(p, c, b1h, 4096, "b1h")
    b2h_t = load_col_f32(p, c, b2h, 1024, "b2h")
    w["rt"] = load_col_f32(p, c, rt, 1024, "rt")
    gl_t = load_col_f32(p, c, gl, 1024, "gl")
    w["gh"] = load_col_f32(p, c, gh, 1024, "gh")
    w["w1hT"] = w1hT
    w["w2hT"] = w2hT
    p.stage()
    for i in range(NT):
        p.v(lambda e, i=i: e.tensor_scalar(b2l_t[i][:, :], b2l_t[i][:, :],
                                           0.5, None, ALU.mult))
        p.v(lambda e, i=i: e.tensor_scalar(b2h_t[i][:, :], b2h_t[i][:, :],
                                           0.5, None, ALU.mult))
        p.v(lambda e, i=i: e.tensor_scalar(w["w1l"][i][:, :], w["w1l"][i][:, :],
                                           gl_t[i][:, :], None, ALU.mult))
    w["b2l_half"] = b2l_t
    w["b2h_half"] = b2h_t


def emit_ffn1_phase(p, c, xT, yT, aps, dram):
    B = ffn1_bufs(c)
    w = {"lnb": LNBufs(c, 512, psA=B["PS"][0][0:1, :],
                       psB=B["PS"][1][0:1, :])}
    emit_ffn1_weights_load(p, c, w, aps)
    xbufs = [c.sb([128, 512], F32) for _ in range(NT)]

    def getX(i, cs):
        p.on("sync", lambda e, i=i, cs=cs: e.dma_start(
            xbufs[i][:, :], xT[i * 128:(i + 1) * 128, cs]), dma=True)
        return xbufs[i]

    def putX(i, cs, buf):
        p.on("sync", lambda e, i=i, cs=cs, buf=buf: e.dma_start(
            yT[i * 128:(i + 1) * 128, cs], buf[:, :]), dma=True)

    emit_ffn1(p, c, B, w, getX, putX, dram)

    def getXb(i, g):
        cs = slice(g * 512, (g + 1) * 512)
        p.on("sync", lambda e, i=i, cs=cs: e.dma_start(
            xbufs[i][:, :], yT[i * 128:(i + 1) * 128, cs]), dma=True)
        return xbufs[i]

    def putXb(i, g, buf):
        cs = slice(g * 512, (g + 1) * 512)
        p.on("sync", lambda e, i=i, cs=cs, buf=buf: e.dma_start(
            yT[i * 128:(i + 1) * 128, cs], buf[:, :]), dma=True)

    emit_hfull_add(p, c, B, dram, getXb, putXb)


def ffn1_dram(nc, tag=""):
    return {
        "xn_tok": nc.dram_tensor(f"xn_tok{tag}", [T, 1024], F32,
                                 kind="Internal").ap(),
        "qrow": nc.dram_tensor(f"qrow{tag}", [T], F32, kind="Internal").ap(),
        "drow": nc.dram_tensor(f"drow{tag}", [T], U32, kind="Internal").ap(),
        "idx": nc.dram_tensor(f"idxl{tag}", [NUM_HEAVY + T, 1], U32,
                              kind="Internal").ap(),
        "hfull": nc.dram_tensor(f"hfull{tag}", [T, 1024], F32,
                                kind="Internal").ap(),
    }


def make_ffn1_nc():
    nc = bass.Bass("TRN2", target_bir_lowering=False)
    xT = nc.dram_tensor("xT", [D, T], F32, kind="ExternalInput").ap()
    aps = tuple(
        nc.dram_tensor(nm, sh, F32, kind="ExternalInput").ap()
        for nm, sh in (
            ("w1lT", [1024, 512]), ("b1l", [512, 1]), ("w2lT", [512, 1024]),
            ("b2l", [1024, 1]), ("w1hT", [1024, 4096]), ("b1h", [4096, 1]),
            ("w2hT", [4096, 1024]), ("b2h", [1024, 1]), ("rt", [1024, 1]),
            ("gl", [1024, 1]), ("gh", [1024, 1])))
    yT = nc.dram_tensor("yT", [D, T], F32, kind="ExternalOutput").ap()
    dram = ffn1_dram(nc)
    with ExitStack() as stack:
        c = Ctx(nc, stack)
        sem = c.sem("SS")
        block = stack.enter_context(nc.Block())
        p = Prog(nc)
        setup_consts(p, c)
        emit_attn0_prep(p, c)
        emit_ffn1_phase(p, c, xT, yT, aps, dram)
        p.emit(block, sem)
    return nc



DIN = 2048
NDT = DIN // 128
NST = 16
TCM = 128
SEG = TCM + 1


def mamba_bufs(c):
    B = {}
    if getattr(c, "PSG", None) is not None:
        B["PS"] = c.PSG
    else:
        B["PS"] = [c.ps([128, 512], F32, f"MPS{i}") for i in range(7)]
    B["mean"] = c.sb([1, 512], F32)
    B["rstd"] = c.sb([1, 512], F32)
    B["mr"] = c.sb([1, 512], F32)
    B["xtile"] = [c.sb([128, 512], F32) for _ in range(2)]
    B["xnf"] = c.sb([128, 512], F32)
    B["stg"] = [c.sb([128, 512], F32) for _ in range(2)]
    B["xnc"] = [c.sb([128, 512], BF16) for _ in range(NT)]
    B["wblk"] = [c.sb([128, 1024], BF16) for _ in range(NT)]
    B["P_x"] = [c.sb([128, TCM + 3], F32) for _ in range(NDT)]
    B["P_u"] = [c.sb([128, TCM], F32) for _ in range(NDT)]
    B["P_d"] = [c.sb([128, TCM], F32) for _ in range(NDT)]
    B["P_y"] = [c.sb([128, TCM], F32) for _ in range(NDT)]
    B["u_bf"] = [c.sb([128, TCM], BF16) for _ in range(NDT)]
    B["scanA"] = c.sb([128, NST * SEG], F32, "scanA")
    B["scanB"] = c.sb([128, NST * SEG], F32, "scanB")
    B["scanH"] = c.sb([128, NST * SEG], F32, "scanH")
    B["hC"] = c.sb([128, NST * TCM], F32, "hCil")
    B["yred"] = c.sb([128, TCM], F32)
    B["B_rep"] = c.sb([128, NST * TCM], F32, "B_rep")
    B["C_rep"] = c.sb([128, NST * TCM], F32, "C_rep")
    B["carry"] = [c.sb([128, NST], F32) for _ in range(NDT)]
    B["draw_bf"] = c.sb([64, TCM], BF16)
    B["B_t"] = c.sb([16, TCM], F32)
    B["C_t"] = c.sb([16, TCM], F32)
    B["tmp1"] = c.sb([128, TCM], F32)
    B["tmp2"] = c.sb([128, TCM], F32)
    B["y_bf"] = [c.sb([128, 256], BF16) for _ in range(NDT)]
    B["xres"] = [c.sb([128, 256], F32) for _ in range(NT)]
    B["xa_t"] = [B["scanA"][:, i * 512:(i + 1) * 512] for i in range(4)]
    B["xa_t"] += [B["scanB"][:, i * 512:(i + 1) * 512] for i in range(4)]
    return B


def emit_mamba_weights_load(p, c, w, aps):
    (ng, nb, inwT, convw, convb, xprojT, dtwT, dtb, alog, dvec, outwT) = aps
    w["ng"] = load_col_f32(p, c, ng, D, "mng")
    w["nb"] = load_col_f32(p, c, nb, D, "mnb")
    w["inwT"] = inwT
    w["convw"] = []
    for i in range(NDT):
        t = c.sb([128, 4], F32, f"cw{i}")
        w["convw"].append(t)
        p.d(lambda e, t=t, i=i: e.dma_start(t[:, :],
                                            convw[i * 128:(i + 1) * 128, :]))
    w["convb"] = load_col_f32(p, c, convb, DIN, "mcb")
    w["xproj"] = []
    for i in range(NDT):
        t = c.sb([128, 128], BF16, f"xpj{i}")
        w["xproj"].append(t)
        p.dg(lambda e, t=t, i=i: e.dma_start(t[:, :],
                                             xprojT[i * 128:(i + 1) * 128, :]))
    w["dtw"] = c.sb([64, 2048], BF16, "dtw")
    p.dg(lambda e: e.dma_start(w["dtw"][:, :], dtwT))
    w["dtb"] = load_col_f32(p, c, dtb, DIN, "mdtb")
    w["A"] = []
    for i in range(NDT):
        t = c.sb([128, NST], F32, f"mA{i}")
        w["A"].append(t)
        p.d(lambda e, t=t, i=i: e.dma_start(t[:, :],
                                            alog[i * 128:(i + 1) * 128, :]))
    w["D"] = load_col_f32(p, c, dvec, DIN, "mD")
    w["outwT"] = []
    for k in range(NDT):
        t = c.sb([128, 1024], BF16, f"mow{k}")
        w["outwT"].append(t)
        p.dg(lambda e, t=t, k=k: e.dma_start(t[:, :],
                                             outwT[k * 128:(k + 1) * 128, :]))
    p.stage()
    for i in range(NDT):
        p.s(lambda e, i=i: e.activation(w["A"][i][:, :], w["A"][i][:, :],
                                        AF.Exp))
    p.stage()
    for i in range(NDT):
        p.v(lambda e, i=i: e.tensor_scalar(w["A"][i][:, :], w["A"][i][:, :],
                                           -1.0, None, ALU.mult))


def emit_mamba(p, c, B, w, xT, yT, dram):
    PS = B["PS"]
    lnb = w["lnb"]
    epsc = constcol(c, LN_EPS, 1)
    for ci in range(T // 512):
        cs = slice(ci * 512, (ci + 1) * 512)
        p.stage()
        for i in range(NT):
            buf = B["xtile"][i % 2]
            p.stage()
            p.d(lambda e, buf=buf, i=i, cs=cs: e.dma_start(
                buf[:, :], xT[i * 128:(i + 1) * 128, cs]))
            p.stage()
            p.s(lambda e, buf=buf, i=i: e.activation(
                lnb.sq[i % 2][:, :], buf[:, :], AF.Square))
            p.stage()
            p.t(lambda e, i=i, buf=buf: e.matmul(
                lnb.psA[:, :], c.invD_col[:, :], buf[:, :],
                start=(i == 0), stop=(i == NT - 1)))
            p.t(lambda e, i=i: e.matmul(
                lnb.psB[:, :], c.invD_col[:, :], lnb.sq[i % 2][:, :],
                start=(i == 0), stop=(i == NT - 1)))
        p.stage()
        p.v(lambda e: e.tensor_copy(B["mean"][0:1, :], lnb.psA[:, :]))
        p.stage()
        p.v(lambda e: e.tensor_tensor(lnb.var[0:1, :], B["mean"][0:1, :],
                                      B["mean"][0:1, :], ALU.mult))
        p.stage()
        p.v(lambda e: e.tensor_tensor(lnb.var[0:1, :], lnb.psB[:, :],
                                      lnb.var[0:1, :], ALU.subtract))
        p.stage()
        p.s(lambda e, epsc=epsc: e.activation(B["rstd"][0:1, :],
                                              lnb.var[0:1, :], AF.Ln,
                                              bias=epsc[:, :]))
        p.stage()
        p.s(lambda e: e.activation(B["rstd"][0:1, :], B["rstd"][0:1, :],
                                   AF.Exp, scale=-0.5))
        p.stage()
        p.v(lambda e: e.tensor_tensor(B["mr"][0:1, :], B["mean"][0:1, :],
                                      B["rstd"][0:1, :], ALU.mult))
        p.stage()
        p.t(lambda e: e.matmul(PS[0][:, :], c.ones_f[:, :], B["rstd"][0:1, :],
                               start=True, stop=True))
        p.t(lambda e: e.matmul(PS[1][:, :], c.ones_f[:, :], B["mr"][0:1, :],
                               start=True, stop=True))
        p.stage()
        for i in range(NT):
            buf = B["xtile"][i % 2]
            p.stage()
            p.d(lambda e, buf=buf, i=i, cs=cs: e.dma_start(
                buf[:, :], xT[i * 128:(i + 1) * 128, cs]))
            p.stage()
            p.v(lambda e, buf=buf, i=i: e.tensor_tensor(
                B["xa_t"][i][:, :], buf[:, :], PS[0][:, :], ALU.mult))
            p.stage()
            p.v(lambda e, i=i: e.tensor_tensor(B["xa_t"][i][:, :],
                                               B["xa_t"][i][:, :],
                                               PS[1][:, :], ALU.subtract))
            p.stage()
            p.d(lambda e, i=i, cs=cs: e.dma_start(
                dram["xa"][i * 128:(i + 1) * 128, cs], B["xa_t"][i][:, :]))
        for i in range(NT):
            p.stage()
            p.s(lambda e, i=i: e.activation(lnb.sq[i % 2][:, :],
                                            B["xa_t"][i][:, :], AF.Square))
            p.stage()
            p.t(lambda e, i=i: e.matmul(
                PS[4][0:1, :], c.invD_col[:, :], B["xa_t"][i][:, :],
                start=(i == 0), stop=(i == NT - 1)))
            p.t(lambda e, i=i: e.matmul(
                PS[5][0:1, :], c.invD_col[:, :], lnb.sq[i % 2][:, :],
                start=(i == 0), stop=(i == NT - 1)))
        p.stage()
        p.v(lambda e: e.tensor_copy(B["mean"][0:1, :], PS[4][0:1, :]))
        p.stage()
        p.v(lambda e: e.tensor_tensor(lnb.var[0:1, :], B["mean"][0:1, :],
                                      B["mean"][0:1, :], ALU.mult))
        p.stage()
        p.v(lambda e: e.tensor_tensor(lnb.var[0:1, :], PS[5][0:1, :],
                                      lnb.var[0:1, :], ALU.subtract))
        p.stage()
        p.s(lambda e, epsc=epsc: e.activation(B["rstd"][0:1, :],
                                              lnb.var[0:1, :], AF.Ln,
                                              bias=epsc[:, :]))
        p.stage()
        p.s(lambda e: e.activation(B["rstd"][0:1, :], B["rstd"][0:1, :],
                                   AF.Exp, scale=-0.5))
        p.stage()
        p.v(lambda e: e.tensor_tensor(B["mr"][0:1, :], B["mean"][0:1, :],
                                      B["rstd"][0:1, :], ALU.mult))
        p.stage()
        p.t(lambda e: e.matmul(PS[0][:, :], c.ones_f[:, :], B["rstd"][0:1, :],
                               start=True, stop=True))
        p.t(lambda e: e.matmul(PS[1][:, :], c.ones_f[:, :], B["mr"][0:1, :],
                               start=True, stop=True))
        p.stage()
        for i in range(NT):
            p.stage()
            p.v(lambda e, i=i: e.tensor_tensor(B["xnf"][:, :],
                                               B["xa_t"][i][:, :],
                                               PS[0][:, :], ALU.mult))
            p.stage()
            p.v(lambda e: e.tensor_tensor(B["xnf"][:, :], B["xnf"][:, :],
                                          PS[1][:, :], ALU.subtract))
            p.stage()
            p.v(lambda e, i=i: e.tensor_scalar(
                B["xnc"][i][:, :], B["xnf"][:, :], w["ng"][i][:, :],
                w["nb"][i][:, :], ALU.mult, op1=ALU.add))
            p.stage()
            p.d(lambda e, i=i, cs=cs: e.dma_start(
                dram["xnb"][i * 128:(i + 1) * 128, cs], B["xnc"][i][:, :]))
        p.stage()
    for mb in range(4):
        p.stage()
        for k in range(NT):
            p.dg(lambda e, mb=mb, k=k: e.dma_start(
                B["wblk"][k][:, :],
                w["inwT"][k * 128:(k + 1) * 128,
                          mb * 1024:(mb + 1) * 1024]))
        p.stage()
        for tcx in range(T // 512):
            cs = slice(tcx * 512, (tcx + 1) * 512)
            p.stage()
            for i in range(NT):
                p.d(lambda e, i=i, cs=cs: e.dma_start(
                    B["xnc"][i][:, :], dram["xnb"][i * 128:(i + 1) * 128, cs]))
            p.stage()
            for m in range(8):
                par = m % 2
                p.stage()
                for k in range(NT):
                    p.t(lambda e, m=m, k=k, par=par: e.matmul(
                        PS[2 + par][:, :],
                        B["wblk"][k][:, m * 128:(m + 1) * 128],
                        B["xnc"][k][:, :], start=(k == 0), stop=(k == NT - 1)))
                p.stage()
                if par == 0:
                    p.s(lambda e, par=par: e.copy(B["stg"][par][:, :],
                                                  PS[2 + par][:, :]))
                else:
                    p.v(lambda e, par=par: e.tensor_copy(B["stg"][par][:, :],
                                                         PS[2 + par][:, :]))
                p.stage()
                row = mb * 8 + m
                p.d(lambda e, row=row, cs=cs, par=par: e.dma_start(
                    dram["xz"][row * 128:(row + 1) * 128, cs],
                    B["stg"][par][:, :]))
    p.stage()
    for i in range(NDT):
        p.g(lambda e, i=i: e.memset(B["carry"][i][:, :], 0.0))
    for ci in range(T // TCM):
        t0 = ci * TCM
        p.stage()
        for i in range(NDT):
            if ci == 0:
                p.g(lambda e, i=i: e.memset(B["P_x"][i][:, 0:3], 0.0))
                p.d(lambda e, i=i, t0=t0: e.dma_start(
                    B["P_x"][i][:, 3:TCM + 3],
                    dram["xz"][i * 128:(i + 1) * 128, t0:t0 + TCM]))
            else:
                p.d(lambda e, i=i, t0=t0: e.dma_start(
                    B["P_x"][i][:, 0:TCM + 3],
                    dram["xz"][i * 128:(i + 1) * 128, t0 - 3:t0 + TCM]))
        p.stage()
        for i in range(NDT):
            p.v(lambda e, i=i: e.tensor_scalar(
                B["tmp1"][:, :], B["P_x"][i][:, 0:TCM], w["convw"][i][:, 0:1],
                w["convb"][i][:, :], ALU.mult, op1=ALU.add))
            for kk in range(1, 4):
                p.v(lambda e, i=i, kk=kk: e.scalar_tensor_tensor(
                    B["tmp1"][:, :], B["P_x"][i][:, kk:kk + TCM],
                    w["convw"][i][:, kk:kk + 1], B["tmp1"][:, :],
                    ALU.mult, ALU.add))
            p.stage()
            p.s(lambda e, i=i: e.activation(B["P_u"][i][:, :], B["tmp1"][:, :],
                                            AF.Silu))
            p.stage()
            p.v(lambda e, i=i: e.tensor_copy(B["u_bf"][i][:, :],
                                             B["P_u"][i][:, :]))
            p.stage()
        p.stage()
        for k in range(NDT):
            p.t(lambda e, k=k: e.matmul(PS[2][:, 0:TCM], w["xproj"][k][:, :],
                                        B["u_bf"][k][:, :], start=(k == 0),
                                        stop=(k == NDT - 1)))
        p.stage()
        p.s(lambda e: e.copy(B["draw_bf"][:, :], PS[2][0:64, 0:TCM]))
        p.v(lambda e: e.tensor_copy(B["B_t"][:, :], PS[2][64:80, 0:TCM]))
        p.v(lambda e: e.tensor_copy(B["C_t"][:, :], PS[2][96:112, 0:TCM]))
        p.stage()
        p.d(lambda e: e.dma_start(
            bass.AP(dram["bcf"].tensor, 0, [[TCM, 16], [1, TCM]]),
            B["B_t"][:, :]))
        p.d(lambda e: e.dma_start(
            bass.AP(dram["bcf"].tensor, NST * TCM, [[TCM, 16], [1, TCM]]),
            B["C_t"][:, :]))
        p.stage()
        p.d(lambda e: e.dma_start(
            B["B_rep"][:, :],
            bass.AP(dram["bcf"].tensor, 0, [[0, 128], [1, NST * TCM]])))
        p.d(lambda e: e.dma_start(
            B["C_rep"][:, :],
            bass.AP(dram["bcf"].tensor, NST * TCM, [[0, 128], [1, NST * TCM]])))
        for mi in range(NDT):
            p.stage()
            p.t(lambda e, mi=mi: e.matmul(
                PS[3][:, 0:TCM], w["dtw"][:, mi * 128:(mi + 1) * 128],
                B["draw_bf"][:, :], start=True, stop=True))
            p.stage()
            p.s(lambda e, mi=mi: e.activation(B["tmp1"][:, :], PS[3][:, 0:TCM],
                                              AF.Abs, bias=w["dtb"][mi][:, :]))
            p.s(lambda e, mi=mi: e.activation(B["tmp2"][:, :], PS[3][:, 0:TCM],
                                              AF.Relu, bias=w["dtb"][mi][:, :]))
            p.stage()
            p.s(lambda e: e.activation(B["tmp1"][:, :], B["tmp1"][:, :],
                                       AF.Exp, scale=-1.0))
            p.stage()
            p.s(lambda e: e.activation(B["tmp1"][:, :], B["tmp1"][:, :],
                                       AF.Ln, bias=1.0))
            p.stage()
            p.v(lambda e, mi=mi: e.tensor_tensor(B["P_d"][mi][:, :],
                                                 B["tmp1"][:, :],
                                                 B["tmp2"][:, :], ALU.add))
        p.stage()
        for i in range(NDT):
            p.v(lambda e, i=i: e.tensor_scalar(B["P_y"][i][:, :],
                                               B["P_u"][i][:, :],
                                               w["D"][i][:, :], None, ALU.mult))
        p.stage()
        for i in range(NDT):
            p.v(lambda e, i=i: e.tensor_tensor(B["P_u"][i][:, :],
                                               B["P_u"][i][:, :],
                                               B["P_d"][i][:, :], ALU.mult))
        p.stage()
        for i in range(NDT):
            p.stage()
            p.g(lambda e: e.memset(
                apx(B["scanA"], 0, [[SEG, NST], [1, 1]]), 0.0))
            for n in range(NST):
                p.s(lambda e, i=i, n=n: e.activation(
                    B["scanA"][:, n * SEG + 1:(n + 1) * SEG],
                    B["P_d"][i][:, :], AF.Exp, scale=w["A"][i][:, n:n + 1]))
            p.v(lambda e, i=i: e.tensor_copy(
                apx(B["scanB"], 0, [[SEG, NST], [1, 1]]),
                B["carry"][i][:, :]))
            p.stage()
            p.v(lambda e, i=i: e.tensor_tensor(
                apx(B["scanB"], 1, [[SEG, NST], [1, TCM]]),
                apx(B["P_u"][i], 0, [[0, NST], [1, TCM]]),
                B["B_rep"][:, :], ALU.mult))
            p.stage()
            p.v(lambda e: e.tensor_tensor_scan(
                B["scanH"][:, :], B["scanA"][:, :], B["scanB"][:, :], 0.0,
                ALU.mult, ALU.add))
            p.stage()
            p.v(lambda e, i=i: e.tensor_copy(
                B["carry"][i][:, :],
                apx(B["scanH"], TCM, [[SEG, NST], [1, 1]])))
            p.v(lambda e: e.tensor_tensor(
                apx(B["hC"], 0, [[1, NST], [NST, TCM]]),
                apx(B["scanH"], 1, [[SEG, NST], [1, TCM]]),
                B["C_rep"][:, :], ALU.mult))
            p.stage()
            p.v(lambda e: e.tensor_reduce(
                B["yred"][:, :],
                apx(B["hC"], 0, [[NST, TCM], [1, NST]]),
                AX.X, ALU.add))
            p.stage()
            p.v(lambda e, i=i: e.tensor_tensor(B["P_y"][i][:, :],
                                               B["P_y"][i][:, :],
                                               B["yred"][:, :], ALU.add))
            p.stage()
        p.stage()
        for i in range(NDT):
            p.d(lambda e, i=i, t0=t0: e.dma_start(
                B["P_x"][i][:, 0:TCM],
                dram["xz"][(NDT + i) * 128:(NDT + i + 1) * 128, t0:t0 + TCM]))
        p.stage()
        for i in range(NDT):
            p.s(lambda e, i=i: e.activation(B["P_d"][i][:, :],
                                            B["P_x"][i][:, 0:TCM], AF.Silu))
        p.stage()
        for i in range(NDT):
            p.v(lambda e, i=i: e.tensor_tensor(B["P_y"][i][:, :],
                                               B["P_y"][i][:, :],
                                               B["P_d"][i][:, :], ALU.mult))
        p.stage()
        for i in range(NDT):
            p.d(lambda e, i=i, t0=t0: e.dma_start(
                dram["ysp"][i * 128:(i + 1) * 128, t0:t0 + TCM],
                B["P_y"][i][:, :]))
    for tcx in range(T // 256):
        cs = slice(tcx * 256, (tcx + 1) * 256)
        p.stage()
        for k in range(NDT):
            p.dg(lambda e, k=k, cs=cs: e.dma_start(
                B["y_bf"][k][:, :], dram["ysp"][k * 128:(k + 1) * 128, cs]))
        for i in range(NT):
            p.d(lambda e, i=i, cs=cs: e.dma_start(
                B["xres"][i][:, :], xT[i * 128:(i + 1) * 128, cs]))
        p.stage()
        for m in range(NT):
            par = m % 2
            p.stage()
            for k in range(NDT):
                p.t(lambda e, m=m, k=k, par=par: e.matmul(
                    PS[2 + par][:, 0:256],
                    w["outwT"][k][:, m * 128:(m + 1) * 128],
                    B["y_bf"][k][:, :], start=(k == 0), stop=(k == NDT - 1)))
            p.d(lambda e, m=m, cs=cs, par=par: e.dma_start(
                B["stg"][par][:, 0:256], dram["xa"][m * 128:(m + 1) * 128, cs]))
            p.stage()
            p.v(lambda e, m=m, par=par: e.tensor_tensor(
                B["xres"][m][:, :], B["xres"][m][:, :], PS[2 + par][:, 0:256],
                ALU.add))
            p.stage()
            p.v(lambda e, m=m, par=par: e.tensor_tensor(
                B["xres"][m][:, :], B["xres"][m][:, :], B["stg"][par][:, 0:256],
                ALU.add))
            p.stage()
            p.d(lambda e, m=m, cs=cs: e.dma_start(
                yT[m * 128:(m + 1) * 128, cs], B["xres"][m][:, :]))


def emit_mamba_phase(p, c, xT, yT, aps, dram):
    B = mamba_bufs(c)
    w = {"lnb": LNBufs(c, 512, psA=B["PS"][0][0:1, :],
                       psB=B["PS"][1][0:1, :])}
    emit_mamba_weights_load(p, c, w, aps)
    p.stage()
    emit_mamba(p, c, B, w, xT, yT, dram)


def mamba_dram(nc, dbg=False):
    kd = "ExternalOutput" if dbg else "Internal"
    return {
        "xz": nc.dram_tensor("xz_sp", [2 * DIN, T], F32, kind=kd).ap(),
        "ysp": nc.dram_tensor("y_sp", [DIN, T], F32, kind=kd).ap(),
        "xnb": nc.dram_tensor("xn_bf", [D, T], BF16, kind=kd).ap(),
        "xa": nc.dram_tensor("xa_sp", [D, T], F32, kind=kd).ap(),
        "bcf": nc.dram_tensor("bc_flat", [2 * NST * TCM], F32,
                              kind="Internal").ap(),
    }


def make_mamba_nc(dbg=False):
    nc = bass.Bass("TRN2", target_bir_lowering=False)
    xT = nc.dram_tensor("xT", [D, T], F32, kind="ExternalInput").ap()
    aps = tuple(
        nc.dram_tensor(nm, sh, F32, kind="ExternalInput").ap()
        for nm, sh in (
            ("ng", [D, 1]), ("nb", [D, 1]), ("inwT", [1024, 4096]),
            ("convw", [DIN, 4]), ("convb", [DIN, 1]), ("xprojT", [DIN, 128]),
            ("dtwT", [64, DIN]), ("dtb", [DIN, 1]), ("alog", [DIN, NST]),
            ("dvec", [DIN, 1]), ("outwT", [DIN, 1024])))
    yT = nc.dram_tensor("yT", [D, T], F32, kind="ExternalOutput").ap()
    dram = mamba_dram(nc, dbg)
    with ExitStack() as stack:
        c = Ctx(nc, stack)
        sem = c.sem("SS")
        block = stack.enter_context(nc.Block())
        p = Prog(nc)
        setup_consts(p, c)
        emit_mamba_phase(p, c, xT, yT, aps, dram)
        p.emit(block, sem)
    return nc



ARENA_N = 52224


def make_fused_nc():
    nc = bass.Bass("TRN2", target_bir_lowering=False)
    xT = nc.dram_tensor("xT", [D, T], F32, kind="ExternalInput").ap()

    def din(nm, sh):
        return nc.dram_tensor(nm, sh, F32, kind="ExternalInput").ap()

    ffn0_aps = {k: din(k, sh) for k, sh in (
        ("wrT", [D, D]), ("wkT", [D, D]), ("wvT", [D, D]),
        ("mur", [D, 1]), ("muk", [D, 1]))}
    attn_aps = tuple(din(nm, sh) for nm, sh in (
        ("wdqT", [64, 512]), ("wuqT", [512, 64]), ("wdkvT", [64, 1024]),
        ("wukvT", [1024, 128]), ("wqrT", [64, 64]), ("wkrT", [64, 64]),
        ("woT", [64, 64]), ("qnw", [64, 1]), ("knw", [64, 1])))
    ffn1_aps = tuple(din(nm, sh) for nm, sh in (
        ("w1lT", [1024, 512]), ("b1l", [512, 1]), ("w2lT", [512, 1024]),
        ("b2l", [1024, 1]), ("w1hT", [1024, 4096]), ("b1h", [4096, 1]),
        ("w2hT", [4096, 1024]), ("b2h", [1024, 1]), ("rt", [1024, 1]),
        ("gl", [1024, 1]), ("gh", [1024, 1])))
    mamba_aps = tuple(din(nm, sh) for nm, sh in (
        ("ng", [D, 1]), ("nb", [D, 1]), ("inwT", [1024, 4096]),
        ("convw", [DIN, 4]), ("convb", [DIN, 1]), ("xprojT", [DIN, 128]),
        ("dtwT", [64, DIN]), ("dtb", [DIN, 1]), ("alog", [DIN, NST]),
        ("dvec", [DIN, 1]), ("outwT", [DIN, 1024])))
    yT = nc.dram_tensor("yT", [D, T], F32, kind="ExternalOutput").ap()
    xs = [nc.dram_tensor(f"xc{i}", [D, T], F32, kind="Internal").ap()
          for i in range(5)]
    dramF = ffn1_dram(nc)
    dramM = mamba_dram(nc)
    with ExitStack() as stack:
        c = Ctx(nc, stack)
        sem = c.sem("SS")
        block = stack.enter_context(nc.Block())
        p = Prog(nc)
        setup_consts(p, c)
        emit_attn0_prep(p, c)
        c.PSG = [c.ps([128, 512], F32, f"G{i}") for i in range(7)]
        c.PTB = c.ps([128, 256], BF16, "GT")
        ar = Arena(c, ARENA_N)
        c.arena = ar
        wf0 = (ffn0_aps["wrT"], ffn0_aps["wkT"], ffn0_aps["wvT"],
               ffn0_aps["mur"], ffn0_aps["muk"])
        ar.reset()
        p.stage()
        emit_ffn0_phase(p, c, xT, xs[0], ffn0_aps)
        ar.reset()
        p.stage()
        emit_attn0_phase(p, c, xs[0], xs[1], attn_aps)
        ar.reset()
        p.stage()
        emit_ffn0_phase(p, c, xs[1], xs[2], ffn0_aps)
        ar.reset()
        p.stage()
        emit_ffn1_phase(p, c, xs[2], xs[3], ffn1_aps, dramF)
        ar.reset()
        p.stage()
        emit_mamba_phase(p, c, xs[3], xs[4], mamba_aps, dramM)
        ar.reset()
        p.stage()
        emit_ffn1_phase(p, c, xs[4], yT, ffn1_aps, dramF)
        p.emit(block, sem)
    return nc



def _ct(a):
    return np.ascontiguousarray(a)


def kernel(**inputs):
    ncores = 8
    x = inputs["L0_attn_x"]
    cur = [_ct(x[b].T) for b in range(ncores)]

    ffn0_w = {
        "wrT": _ct(inputs["L0_ffn_W_r"].T),
        "wkT": _ct(inputs["L0_ffn_W_k"].T),
        "wvT": _ct(inputs["L0_ffn_W_v"].T),
        "mur": _ct(inputs["L0_ffn_mu_r"].reshape(-1, 1)),
        "muk": _ct(inputs["L0_ffn_mu_k"].reshape(-1, 1)),
    }
    attn_w = {
        "wdqT": _ct(inputs["L0_attn_W_DQ"].T),
        "wuqT": _ct(inputs["L0_attn_W_UQ"].T),
        "wdkvT": _ct(inputs["L0_attn_W_DKV"].T),
        "wukvT": _ct(inputs["L0_attn_W_UKV"].T),
        "wqrT": _ct(inputs["L0_attn_W_QR"].T),
        "wkrT": _ct(inputs["L0_attn_W_KR"].T),
        "woT": _ct(inputs["L0_attn_W_O"].T),
        "qnw": _ct(inputs["L0_attn_q_norm_w"].reshape(-1, 1)),
        "knw": _ct(inputs["L0_attn_k_norm_w"].reshape(-1, 1)),
    }
    ffn1_w = {
        "w1lT": _ct(inputs["L1_ffn_w1_l"].T),
        "b1l": _ct(inputs["L1_ffn_b1_l"].reshape(-1, 1)),
        "w2lT": _ct(inputs["L1_ffn_w2_l"].T),
        "b2l": _ct(inputs["L1_ffn_b2_l"].reshape(-1, 1)),
        "w1hT": _ct(inputs["L1_ffn_w1_h"].T),
        "b1h": _ct(inputs["L1_ffn_b1_h"].reshape(-1, 1)),
        "w2hT": _ct(inputs["L1_ffn_w2_h"].T),
        "b2h": _ct(inputs["L1_ffn_b2_h"].reshape(-1, 1)),
        "rt": _ct(inputs["L1_ffn_routing_token"].reshape(-1, 1)),
        "gl": _ct(inputs["L1_ffn_gamma_l"].reshape(-1, 1)),
        "gh": _ct(inputs["L1_ffn_gamma_h"].reshape(-1, 1)),
    }
    xp = inputs["L1_attn_xproj_w"]
    xprojT_pad = np.concatenate(
        [_ct(xp[:64].T), _ct(xp[64:80].T), np.zeros((DIN, 16), np.float32),
         _ct(xp[80:96].T), np.zeros((DIN, 16), np.float32)], axis=1)
    mamba_w = {
        "ng": _ct(inputs["L1_attn_norm_g"].reshape(-1, 1)),
        "nb": _ct(inputs["L1_attn_norm_b"].reshape(-1, 1)),
        "inwT": _ct(inputs["L1_attn_in_w"].T),
        "convw": _ct(inputs["L1_attn_conv_w"][:, 0, :]),
        "convb": _ct(inputs["L1_attn_conv_b"].reshape(-1, 1)),
        "xprojT": _ct(xprojT_pad),
        "dtwT": _ct(inputs["L1_attn_dt_w"].T),
        "dtb": _ct(inputs["L1_attn_dt_b"].reshape(-1, 1)),
        "alog": _ct(inputs["L1_attn_A_log"]),
        "dvec": _ct(inputs["L1_attn_D"].reshape(-1, 1)),
        "outwT": _ct(inputs["L1_attn_out_w"].T),
    }

    wall = {**ffn0_w, **attn_w, **ffn1_w, **mamba_w}
    nc = make_fused_nc()
    in_maps = [{**wall, "xT": cur[b]} for b in range(ncores)]
    res = run_bass_kernel_spmd(nc, in_maps, core_ids=list(range(ncores)))
    cur = [np.asarray(res.results[b]["yT"]) for b in range(ncores)]
    out = np.stack([cur[b].T for b in range(ncores)], axis=0)
    return np.ascontiguousarray(out).astype(np.float32)


# version 04
# speedup vs baseline: 1.1963x; 1.1963x over previous
import math
import numpy as np
from contextlib import ExitStack

import concourse.bass as bass
import concourse.mybir as mybir
from concourse.bass_utils import run_bass_kernel_spmd

F32 = mybir.dt.float32
BF16 = mybir.dt.bfloat16
U32 = mybir.dt.uint32
AF = mybir.ActivationFunctionType
ALU = mybir.AluOpType
AX = mybir.AxisListType

T = 4096
D = 1024
NT = D // 128
LN_EPS = 1e-5

ENGINES = ("sync", "vector", "scalar", "tensor", "gpsimd")


class Prog:

    def __init__(self, nc):
        self.nc = nc
        self.stages = []
        self.stage()

    def stage(self, dep=None):
        self.stages.append({"ops": {e: [] for e in ENGINES}, "dep": dep})
        return len(self.stages) - 1

    def on(self, eng, fn, dma=False, stage_idx=None):
        st = self.stages[-1 if stage_idx is None else stage_idx]
        st["ops"][eng].append((fn, dma))

    def t(self, fn):
        self.on("tensor", fn)

    def v(self, fn):
        self.on("vector", fn)

    def s(self, fn):
        self.on("scalar", fn)

    def g(self, fn):
        self.on("gpsimd", fn)

    def d(self, fn):
        self.on("sync", fn, dma=True)

    def dg(self, fn):
        self.on("gpsimd", fn, dma=True)

    def ds(self, fn):
        self.on("scalar", fn, dma=True)

    def emit(self, block, sem):
        Tt = []
        tot = 0
        for st in self.stages:
            n_dma = sum(1 for e in ENGINES for f, dd in st["ops"][e] if dd)
            n_ce = sum(
                1 for e in ENGINES if any(not dd for f, dd in st["ops"][e])
            )
            tot += 16 * n_dma + n_ce
            Tt.append(tot)

        def dep_of(i):
            d = self.stages[i]["dep"]
            return (i - 1) if d is None else d

        def mk_body(engine_name):
            def body(e):
                hw = 0
                for i, st in enumerate(self.stages):
                    ops = st["ops"][engine_name]
                    if not ops:
                        continue
                    dep = dep_of(i)
                    if dep >= 0 and Tt[dep] > hw:
                        e.wait_ge(sem, Tt[dep])
                        hw = Tt[dep]
                    last_comp = None
                    for j, (fn, dd) in enumerate(ops):
                        if not dd:
                            last_comp = j
                    for j, (fn, dd) in enumerate(ops):
                        inst = fn(e)
                        if dd:
                            inst.then_inc(sem, 16)
                        elif j == last_comp:
                            inst.then_inc(sem, 1)
                if engine_name == "sync":
                    e.wait_ge(sem, Tt[-1])
            return body

        block.sync(mk_body("sync"))
        block.vector(mk_body("vector"))
        block.scalar(mk_body("scalar"))
        block.tensor(mk_body("tensor"))
        block.gpsimd(mk_body("gpsimd"))


def apx(buf, off, freepat):
    if isinstance(buf, bass.AP):
        return bass.AP(buf.tensor, buf.offset + off,
                       [list(buf.ap[0])] + [list(x) for x in freepat])
    base = buf[:, :]
    return bass.AP(base.tensor, off,
                   [list(base.ap[0])] + [list(x) for x in freepat])


class Arena:

    def __init__(self, c, nelems):
        self.t = c.sb([128, nelems], F32, "arena", persist=True)
        self.n = nelems
        self.off = 0
        self.hi = 0

    def reset(self):
        self.hi = max(self.hi, self.off)
        self.off = 0

    def alloc(self, shape, dtype):
        nbytes = shape[1] * mybir.dt.size(dtype)
        cells = (nbytes + 3) // 4
        cells = (cells + 7) & ~7
        off = self.off
        self.off += cells
        assert self.off <= self.n, f"arena overflow {self.off}>{self.n}"
        if dtype == F32:
            return self.t[0:shape[0], off:off + shape[1]]
        ratio = 4 // mybir.dt.size(dtype)
        return self.t.bitcast(dtype)[0:shape[0],
                                     off * ratio:off * ratio + shape[1]]


class Ctx:

    def __init__(self, nc, stack):
        self.nc = nc
        self.stack = stack
        self.uid = 0
        self.arena = None

    def sb(self, shape, dtype, name=None, persist=False):
        if self.arena is not None and not persist:
            return self.arena.alloc(shape, dtype)
        self.uid += 1
        nm = name or f"sb{self.uid}"
        return self.stack.enter_context(self.nc.sbuf_tensor(nm, shape, dtype))

    def ps(self, shape, dtype=F32, name=None):
        self.uid += 1
        nm = name or f"ps{self.uid}"
        return self.stack.enter_context(self.nc.psum_tensor(nm, shape, dtype))

    def sem(self, name=None):
        self.uid += 1
        return self.stack.enter_context(
            self.nc.semaphore(name or f"sem{self.uid}")
        )


def setup_consts(p, c):
    c.ones_f = c.sb([1, 128], F32, "ones_f")
    c.onescol_f = c.sb([128, 1], F32, "onescol_f")
    c.invD_col = c.sb([128, 1], F32, "invD_col")
    p.g(lambda e: e.memset(c.ones_f[:, :], 1.0))
    p.g(lambda e: e.memset(c.onescol_f[:, :], 1.0))
    p.g(lambda e: e.memset(c.invD_col[:, :], 1.0 / D))
    c._constcols = {}
    c._constprog = p


def constcol(c, val, parts=1):
    key = (float(val), parts)
    if key not in c._constcols:
        tile = c.sb([parts, 1], F32, persist=True)
        c._constprog.on("gpsimd",
                        lambda e, tile=tile, val=val: e.memset(tile[:, :],
                                                               float(val)),
                        stage_idx=0)
        c._constcols[key] = tile
    return c._constcols[key]


class LNBufs:
    def __init__(self, c, tc, psA=None, psB=None):
        self.tc = tc
        self.sq = [c.sb([128, tc], F32), c.sb([128, tc], F32)]
        self.psA = c.ps([1, tc], F32) if psA is None else psA
        self.psB = c.ps([1, tc], F32) if psB is None else psB
        self.var = c.sb([1, tc], F32)


def emit_ln_stats_chunk(p, c, X, cs, b, mean_row, rstd_row, mr_row=None,
                        eps=LN_EPS, width=None):
    w = width if width is not None else b.tc
    for i in range(NT):
        buf = b.sq[i % 2]

        def f(e, i=i, buf=buf, cs=cs, w=w):
            return e.activation(buf[:, 0:w], X[i][:, cs], AF.Square)
        p.s(f)
        p.stage()

        def fA(e, i=i, cs=cs, w=w):
            return e.matmul(b.psA[:, 0:w], c.invD_col[:, :], X[i][:, cs],
                            start=(i == 0), stop=(i == NT - 1))
        p.t(fA)

        def fB(e, i=i, buf=buf, w=w):
            return e.matmul(b.psB[:, 0:w], c.invD_col[:, :], buf[:, 0:w],
                            start=(i == 0), stop=(i == NT - 1))
        p.t(fB)
    p.stage()

    def fm(e, w=w):
        return e.tensor_copy(mean_row[0:1, 0:w], b.psA[:, 0:w])
    p.v(fm)
    p.stage()

    def fv(e, w=w):
        return e.tensor_tensor(b.var[0:1, 0:w], mean_row[0:1, 0:w],
                               mean_row[0:1, 0:w], ALU.mult)
    p.v(fv)
    p.stage()

    def fv2(e, w=w):
        return e.tensor_tensor(b.var[0:1, 0:w], b.psB[:, 0:w], b.var[0:1, 0:w],
                               ALU.subtract)
    p.v(fv2)
    p.stage()

    epscol = constcol(c, eps, 1)

    def fr1(e, w=w, epscol=epscol):
        return e.activation(rstd_row[0:1, 0:w], b.var[0:1, 0:w], AF.Ln,
                            bias=epscol[:, :])
    p.s(fr1)
    p.stage()

    def fr2(e, w=w):
        return e.activation(rstd_row[0:1, 0:w], rstd_row[0:1, 0:w], AF.Exp,
                            scale=-0.5)
    p.s(fr2)
    if mr_row is not None:
        p.stage()

        def fmr(e, w=w):
            return e.tensor_tensor(mr_row[0:1, 0:w], mean_row[0:1, 0:w],
                                   rstd_row[0:1, 0:w], ALU.mult)
        p.v(fmr)


def load_lhsT_bf16(p, c, dram_ap, K, M, name=None):
    nk = (K + 127) // 128
    tiles = []
    for k in range(nk):
        kk = min(128, K - k * 128)
        tile = c.sb([kk, M], BF16, name and f"{name}_{k}")
        tiles.append(tile)

        def f(e, tile=tile, k=k, kk=kk):
            return e.dma_start(tile[:, :], dram_ap[k * 128:k * 128 + kk, :])
        p.dg(f)
    return tiles


def load_col_f32(p, c, dram_ap, n, name=None):
    tiles = []
    for i in range((n + 127) // 128):
        kk = min(128, n - i * 128)
        tile = c.sb([kk, 1], F32, name and f"{name}_{i}")
        tiles.append(tile)

        def f(e, tile=tile, i=i, kk=kk):
            return e.dma_start(tile[:, :], dram_ap[i * 128:i * 128 + kk, :])
        p.d(f)
    return tiles



def emit_ffn0(p, c, getX, putX, wrT, wkT, wvT, mur, muk):
    tc = 256
    PSG = getattr(c, "PSG", None)
    if PSG is not None:
        ps_g, ps_mr = PSG[0][:, 0:tc], PSG[1][:, 0:tc]
        ps_mm = [PSG[2][:, 0:tc], PSG[3][:, 0:tc]]
        ps_v = [PSG[4][:, 0:tc], PSG[5][:, 0:tc]]
        lnb = LNBufs(c, tc, psA=PSG[2][0:1, 0:tc], psB=PSG[3][0:1, 0:tc])
    else:
        ps_g = c.ps([128, tc], F32)
        ps_mr = c.ps([128, tc], F32)
        ps_mm = [c.ps([128, tc], F32), c.ps([128, tc], F32)]
        ps_v = [c.ps([128, tc], F32), c.ps([128, tc], F32)]
        lnb = LNBufs(c, tc)
    mean_row = c.sb([1, tc], F32)
    rstd_row = c.sb([1, tc], F32)
    mr_row = c.sb([1, tc], F32)

    om_r = [c.sb([128, 1], F32) for _ in range(NT)]
    om_k = [c.sb([128, 1], F32) for _ in range(NT)]
    p.stage()
    for i in range(NT):
        def f1(e, i=i):
            return e.tensor_scalar(om_r[i][:, :], mur[i][:, :], -1.0, 1.0,
                                   ALU.mult, op1=ALU.add)
        p.v(f1)

        def f2(e, i=i):
            return e.tensor_scalar(om_k[i][:, :], muk[i][:, :], -1.0, 1.0,
                                   ALU.mult, op1=ALU.add)
        p.v(f2)

    xn = [c.sb([128, tc], F32) for _ in range(NT)]
    stash = c.sb([128, NT], F32)
    zcol = c.sb([128, 1], F32)
    tmp = [c.sb([128, tc], F32)]
    xr_bf = [c.sb([128, tc], BF16) for _ in range(NT)]
    xk_bf = [c.sb([128, tc], BF16) for _ in range(NT)]
    r_sig1 = c.sb([128, tc], F32)
    k2_bf = [c.sb([128, tc], BF16) for _ in range(NT)]
    krelu = c.sb([128, tc], F32)
    vtmp = c.sb([128, tc], F32)
    p.stage()
    p.g(lambda e: e.memset(zcol[:, :], 0.0))
    p.g(lambda e: e.memset(stash[:, :], 0.0))

    for ci in range(T // tc):
        cs = slice(ci * tc, (ci + 1) * tc)
        p.stage()
        Xv = [getX(i, cs) for i in range(NT)]
        p.stage()
        emit_ln_stats_chunk(p, c, Xv, slice(0, tc), lnb, mean_row, rstd_row,
                            mr_row)
        p.stage()

        def fb1(e):
            return e.matmul(ps_g[:, :], c.ones_f[:, :], rstd_row[0:1, :],
                            start=True, stop=True)
        p.t(fb1)

        def fb2(e):
            return e.matmul(ps_mr[:, :], c.ones_f[:, :], mr_row[0:1, :],
                            start=True, stop=True)
        p.t(fb2)
        p.stage()
        for i in range(NT):
            def fx1(e, i=i):
                return e.tensor_tensor(xn[i][:, :], Xv[i][:, :], ps_g[:, :],
                                       ALU.mult)
            p.v(fx1)

            def fx2(e, i=i):
                return e.tensor_tensor(xn[i][:, :], xn[i][:, :], ps_mr[:, :],
                                       ALU.subtract)
            p.v(fx2)
        p.stage()
        for i in range(NT):
            for (om, mu, dst) in ((om_r, mur, xr_bf), (om_k, muk, xk_bf)):
                def fs1(e, i=i, om=om):
                    return e.tensor_scalar(tmp[0][:, 1:tc], xn[i][:, 0:tc - 1],
                                           om[i][:, :], None, ALU.mult)
                p.v(fs1)

                def fs2(e, i=i, om=om, ci=ci):
                    src = stash[:, i:i + 1] if ci > 0 else zcol[:, :]
                    return e.tensor_scalar(tmp[0][:, 0:1], src, om[i][:, :],
                                           None, ALU.mult)
                p.v(fs2)

                def fs3(e, i=i, mu=mu, dst=dst):
                    return e.scalar_tensor_tensor(dst[i][:, :], xn[i][:, :],
                                                  mu[i][:, :], tmp[0][:, :],
                                                  ALU.mult, ALU.add)
                p.v(fs3)
        p.stage()
        for i in range(NT):
            def fst(e, i=i):
                return e.tensor_copy(stash[:, i:i + 1], xn[i][:, tc - 1:tc])
            p.v(fst)

        for m in range(NT):
            p.stage()
            for k in range(NT):
                def fm(e, m=m, k=k):
                    return e.matmul(ps_mm[m % 2][:, :],
                                    wkT[k][:, m * 128:(m + 1) * 128],
                                    xk_bf[k][:, :],
                                    start=(k == 0), stop=(k == NT - 1))
                p.t(fm)
            p.stage()

            def fr(e, m=m):
                return e.activation(krelu[:, :], ps_mm[m % 2][:, :], AF.Relu)
            p.s(fr)
            p.stage()

            def fq(e, m=m):
                return e.tensor_tensor(k2_bf[m][:, :], krelu[:, :],
                                       krelu[:, :], ALU.mult)
            p.v(fq)
        for m in range(NT):
            p.stage()
            for k in range(NT):
                def fm(e, m=m, k=k):
                    return e.matmul(ps_mm[m % 2][:, :],
                                    wrT[k][:, m * 128:(m + 1) * 128],
                                    xr_bf[k][:, :],
                                    start=(k == 0), stop=(k == NT - 1))
                p.t(fm)
            p.stage()
            for k in range(NT):
                def fm(e, m=m, k=k):
                    return e.matmul(ps_v[m % 2][:, :],
                                    wvT[k][:, m * 128:(m + 1) * 128],
                                    k2_bf[k][:, :],
                                    start=(k == 0), stop=(k == NT - 1))
                p.t(fm)

            def fe(e, m=m):
                return e.activation(r_sig1[:, :], ps_mm[m % 2][:, :],
                                    AF.Sigmoid)
            p.s(fe)
            p.stage()

            def fv1(e, m=m):
                return e.scalar_tensor_tensor(vtmp[:, :], r_sig1[:, :],
                                              0.5, ps_v[m % 2][:, :],
                                              ALU.mult, ALU.mult)
            p.v(fv1)
            p.stage()

            def fv2(e, m=m):
                return e.tensor_tensor(Xv[m][:, :], Xv[m][:, :], vtmp[:, :],
                                       ALU.add)
            p.v(fv2)
        p.stage()
        for i in range(NT):
            putX(i, cs, Xv[i])
        p.stage()


def emit_ffn0_phase(p, c, xT, yT, wL):
    wr = load_lhsT_bf16(p, c, wL["wrT"], D, D)
    wk = load_lhsT_bf16(p, c, wL["wkT"], D, D)
    wv = load_lhsT_bf16(p, c, wL["wvT"], D, D)
    mur_t = load_col_f32(p, c, wL["mur"], D)
    muk_t = load_col_f32(p, c, wL["muk"], D)
    xbufs = [c.sb([128, 256], F32) for _ in range(NT)]

    def getX(i, cs):
        p.on("sync", lambda e, i=i, cs=cs: e.dma_start(
            xbufs[i][:, :], xT[i * 128:(i + 1) * 128, cs]), dma=True)
        return xbufs[i]

    def putX(i, cs, buf):
        p.on("sync", lambda e, i=i, cs=cs, buf=buf: e.dma_start(
            yT[i * 128:(i + 1) * 128, cs], buf[:, :]), dma=True)

    emit_ffn0(p, c, getX, putX, wr, wk, wv, mur_t, muk_t)


def make_ffn0_nc():
    nc = bass.Bass("TRN2", target_bir_lowering=False)
    xT = nc.dram_tensor("xT", [D, T], F32, kind="ExternalInput").ap()
    wL = {
        "wrT": nc.dram_tensor("wrT", [D, D], F32, kind="ExternalInput").ap(),
        "wkT": nc.dram_tensor("wkT", [D, D], F32, kind="ExternalInput").ap(),
        "wvT": nc.dram_tensor("wvT", [D, D], F32, kind="ExternalInput").ap(),
        "mur": nc.dram_tensor("mur", [D, 1], F32, kind="ExternalInput").ap(),
        "muk": nc.dram_tensor("muk", [D, 1], F32, kind="ExternalInput").ap(),
    }
    yT = nc.dram_tensor("yT", [D, T], F32, kind="ExternalOutput").ap()
    with ExitStack() as stack:
        c = Ctx(nc, stack)
        sem = c.sem("SS")
        block = stack.enter_context(nc.Block())
        p = Prog(nc)
        setup_consts(p, c)
        p.stage()
        emit_ffn0_phase(p, c, xT, yT, wL)
        p.emit(block, sem)
    return nc



RMS_EPS = 1.1920929e-07


def emit_attn0_prep(p, c):
    c.eye_bf = c.sb([128, 128], BF16, "eye_bf")
    c.eye_f = c.sb([128, 128], F32, "eye_f")
    c.mask_bf = c.sb([128, 128], BF16, "mask_bf")
    it = c.sb([128, 128], F32)
    p.stage()
    p.g(lambda e: e.iota(it[:, :], [[1, 128]], base=0, channel_multiplier=-1,
                         allow_small_or_imprecise_dtypes=True))
    p.stage()
    p.v(lambda e: e.tensor_scalar(c.eye_bf[:, :], it[:, :], 0.0, None,
                                  ALU.is_equal))
    p.v(lambda e: e.tensor_scalar(c.eye_f[:, :], it[:, :], 0.0, None,
                                  ALU.is_equal))
    p.v(lambda e: e.tensor_scalar(c.mask_bf[:, :], it[:, :], 0.0, None,
                                  ALU.is_ge))


def attn_bufs(c):
    B = {}
    if getattr(c, "PSG", None) is not None:
        B["PS"] = c.PSG
        B["PT"] = c.PTB
    else:
        B["PS"] = [c.ps([128, 512], F32, f"APS{i}") for i in range(7)]
        B["PT"] = c.ps([128, 256], BF16, "APT")
    B["rstd_row"] = c.sb([1, T], F32, "rstd_row")
    B["mr_row"] = c.sb([1, T], F32, "mr_row")
    B["xnh_f"] = c.sb([64, 512], F32)
    B["xnh"] = c.sb([64, T], BF16, "xnh")
    B["cq"] = [c.sb([128, 512], BF16) for _ in range(4)]
    B["ckv"] = [c.sb([128, 512], BF16) for _ in range(8)]
    B["qc_f"] = c.sb([64, 512], F32)
    B["kc_f"] = c.sb([64, 512], F32)
    B["v_bf"] = c.sb([64, 512], BF16)
    B["sq"] = c.sb([64, 512], F32)
    B["sk"] = c.sb([64, 512], F32)
    B["msq"] = c.sb([1, 512], F32)
    B["msk"] = c.sb([1, 512], F32)
    B["Q"] = c.sb([128, T], BF16, "Qt")
    B["K"] = c.sb([128, T], BF16, "Kt")
    B["v_tok"] = c.sb([128, 65 * 32], BF16, "v_tok")
    B["qs_bf"] = c.sb([128, 512], BF16)
    B["ks_bf"] = c.sb([128, 512], BF16)
    B["ones_bfc"] = c.sb([128, 1], BF16)
    B["inv64"] = c.sb([64, 1], F32)
    B["t11a"] = c.sb([1, 1], F32)
    B["t11b"] = c.sb([1, 1], F32)
    B["qmax"] = c.sb([1, 1], F32)
    B["kmax"] = c.sb([1, 1], F32)
    B["mh_col"] = c.sb([128, 1], F32)
    B["p_bf"] = [c.sb([128, 512], BF16) for _ in range(2)]
    B["rc"] = c.sb([128, 4], F32)
    B["ctx_bf"] = c.sb([128, 256], BF16)
    B["ctxT_bf"] = c.sb([64, 512], BF16)
    return B


def emit_attn0_head(p, c, h, xh, w, B, Xrows_add):
    tcq = 512
    PS = B["PS"]
    PT = B["PT"]
    nchunk = T // tcq
    for ci in range(nchunk):
        cs = slice(ci * tcq, (ci + 1) * tcq)
        p.stage()
        p.t(lambda e, cs=cs: e.matmul(PS[0][0:64, :], c.ones_f[0:1, 0:64],
                                      B["rstd_row"][0:1, cs], start=True,
                                      stop=True))
        p.t(lambda e, cs=cs: e.matmul(PS[1][0:64, :], c.ones_f[0:1, 0:64],
                                      B["mr_row"][0:1, cs], start=True,
                                      stop=True))
        p.stage()
        p.v(lambda e, cs=cs: e.tensor_tensor(B["xnh_f"][:, :], xh[:, cs],
                                             PS[0][0:64, :], ALU.mult))
        p.stage()
        p.v(lambda e, cs=cs: e.tensor_tensor(B["xnh"][:, cs], B["xnh_f"][:, :],
                                             PS[1][0:64, :], ALU.subtract))
        p.stage()
        for m in range(4):
            p.t(lambda e, m=m, cs=cs: e.matmul(
                PS[2][:, :], w["dq"][0:64, m * 128:(m + 1) * 128],
                B["xnh"][:, cs], start=True, stop=True))
            p.stage()
            if m % 2 == 0:
                p.s(lambda e, m=m: e.copy(B["cq"][m][:, :], PS[2][:, :]))
            else:
                p.v(lambda e, m=m: e.tensor_copy(B["cq"][m][:, :], PS[2][:, :]))
            p.stage()
        for m in range(8):
            p.t(lambda e, m=m, cs=cs: e.matmul(
                PS[3][:, :], w["dkv"][0:64, m * 128:(m + 1) * 128],
                B["xnh"][:, cs], start=True, stop=True))
            p.stage()
            if m % 2 == 0:
                p.s(lambda e, m=m: e.copy(B["ckv"][m][:, :], PS[3][:, :]))
            else:
                p.v(lambda e, m=m: e.tensor_copy(B["ckv"][m][:, :],
                                                 PS[3][:, :]))
            p.stage()
        for k in range(4):
            p.t(lambda e, k=k: e.matmul(PS[4][0:64, :], w["uq"][k][:, 0:64],
                                        B["cq"][k][:, :],
                                        start=(k == 0), stop=(k == 3)))
        for k in range(8):
            p.t(lambda e, k=k: e.matmul(PS[5][:, :], w["ukv"][k][:, :],
                                        B["ckv"][k][:, :],
                                        start=(k == 0), stop=(k == 7)))
        p.stage()
        p.s(lambda e: e.copy(B["qc_f"][:, :], PS[4][0:64, :]))
        p.s(lambda e: e.copy(B["kc_f"][:, :], PS[5][0:64, :]))
        p.v(lambda e: e.tensor_copy(B["v_bf"][:, :], PS[5][64:128, :]))
        p.stage()
        p.v(lambda e: e.tensor_tensor(B["sq"][:, :], B["qc_f"][:, :],
                                      B["qc_f"][:, :], ALU.mult))
        p.v(lambda e: e.tensor_tensor(B["sk"][:, :], B["kc_f"][:, :],
                                      B["kc_f"][:, :], ALU.mult))
        for j in range(tcq // 128):
            jt = ci * 4 + j
            p.stage()
            p.t(lambda e, j=j: e.transpose(
                PT[:, 0:64], B["v_bf"][:, j * 128:(j + 1) * 128],
                c.eye_bf[0:64, 0:64]))
            p.stage()
            p.v(lambda e, jt=jt: e.tensor_copy(
                B["v_tok"][:, jt * 65:jt * 65 + 64], PT[:, 0:64]))
        p.stage()
        p.t(lambda e: e.matmul(PS[6][0:1, :], B["inv64"][:, :],
                               B["sq"][:, :], start=True, stop=True))
        p.t(lambda e: e.matmul(PS[2][0:1, :], B["inv64"][:, :],
                               B["sk"][:, :], start=True, stop=True))
        p.stage()
        epsc = constcol(c, RMS_EPS, 1)
        p.s(lambda e, epsc=epsc: e.activation(B["msq"][0:1, :], PS[6][0:1, :],
                                              AF.Ln, bias=epsc[:, :]))
        p.s(lambda e, epsc=epsc: e.activation(B["msk"][0:1, :], PS[2][0:1, :],
                                              AF.Ln, bias=epsc[:, :]))
        p.stage()
        p.s(lambda e: e.activation(B["msq"][0:1, :], B["msq"][0:1, :],
                                   AF.Exp, scale=-0.5))
        p.s(lambda e: e.activation(B["msk"][0:1, :], B["msk"][0:1, :],
                                   AF.Exp, scale=-0.5))
        p.stage()
        p.t(lambda e: e.matmul(PS[6][0:64, :], c.ones_f[0:1, 0:64],
                               B["msq"][0:1, :], start=True, stop=True))
        p.t(lambda e: e.matmul(PS[2][0:64, :], c.ones_f[0:1, 0:64],
                               B["msk"][0:1, :], start=True, stop=True))
        p.stage()
        p.v(lambda e, cs=cs: e.scalar_tensor_tensor(
            B["Q"][0:64, cs], B["qc_f"][:, :], w["qnw"][:, :],
            PS[6][0:64, :], ALU.mult, ALU.mult))
        p.v(lambda e, cs=cs: e.scalar_tensor_tensor(
            B["K"][0:64, cs], B["kc_f"][:, :], w["knw"][:, :],
            PS[2][0:64, :], ALU.mult, ALU.mult))
        p.stage()
        p.t(lambda e, cs=cs: e.matmul(PS[4][0:64, :], w["qr"][0:64, 0:64],
                                      B["Q"][0:64, cs], start=True, stop=True))
        p.t(lambda e, cs=cs: e.matmul(PS[5][0:64, :], w["kr"][0:64, 0:64],
                                      B["xnh"][:, cs], start=True, stop=True))
        p.stage()
        p.s(lambda e, cs=cs: e.copy(B["Q"][64:128, cs], PS[4][0:64, :]))
        p.s(lambda e, cs=cs: e.copy(B["K"][64:128, cs], PS[5][0:64, :]))
        p.stage()
        p.v(lambda e, cs=cs: e.tensor_tensor(B["qs_bf"][:, :], B["Q"][:, cs],
                                             B["Q"][:, cs], ALU.mult))
        p.v(lambda e, cs=cs: e.tensor_tensor(B["ks_bf"][:, :], B["K"][:, cs],
                                             B["K"][:, cs], ALU.mult))
        p.stage()
        p.t(lambda e: e.matmul(PS[4][0:1, :], B["ones_bfc"][:, :],
                               B["qs_bf"][:, :], start=True, stop=True))
        p.t(lambda e: e.matmul(PS[5][0:1, :], B["ones_bfc"][:, :],
                               B["ks_bf"][:, :], start=True, stop=True))
        p.stage()
        p.v(lambda e: e.tensor_reduce(B["t11a"][0:1, 0:1], PS[4][0:1, :],
                                      AX.X, ALU.max))
        p.v(lambda e: e.tensor_reduce(B["t11b"][0:1, 0:1], PS[5][0:1, :],
                                      AX.X, ALU.max))
        p.stage()
        p.v(lambda e, ci=ci: e.tensor_tensor(
            B["qmax"][0:1, 0:1],
            B["qmax"][0:1, 0:1] if ci else B["t11a"][0:1, 0:1],
            B["t11a"][0:1, 0:1], ALU.max))
        p.v(lambda e, ci=ci: e.tensor_tensor(
            B["kmax"][0:1, 0:1],
            B["kmax"][0:1, 0:1] if ci else B["t11b"][0:1, 0:1],
            B["t11b"][0:1, 0:1], ALU.max))
    p.stage()
    p.v(lambda e: e.tensor_tensor(B["t11a"][0:1, 0:1], B["qmax"][0:1, 0:1],
                                  B["kmax"][0:1, 0:1], ALU.mult))
    p.stage()
    p.s(lambda e: e.activation(B["t11a"][0:1, 0:1], B["t11a"][0:1, 0:1],
                               AF.Sqrt))
    p.stage()
    p.v(lambda e: e.tensor_scalar(B["t11a"][0:1, 0:1], B["t11a"][0:1, 0:1],
                                  0.125, 80.0, ALU.mult, op1=ALU.min))
    p.stage()
    p.v(lambda e: e.tensor_scalar(B["t11a"][0:1, 0:1], B["t11a"][0:1, 0:1],
                                  -1.0, None, ALU.mult))
    p.stage()
    p.t(lambda e: e.matmul(PS[0][:, 0:1], c.ones_f[0:1, :],
                           B["t11a"][0:1, 0:1], start=True, stop=True))
    p.stage()
    p.v(lambda e: e.tensor_copy(B["mh_col"][:, :], PS[0][:, 0:1]))
    p.stage()
    sbank = [PS[2], PS[6]]
    for qs_i in range(T // 512):
        span = slice(qs_i * 512, (qs_i + 1) * 512)
        nk = 4 * qs_i + 4

        def emit_S(e, j, qs_i=qs_i):
            return e.matmul(sbank[j % 2][:, :],
                            B["K"][:, j * 128:(j + 1) * 128],
                            B["Q"][:, qs_i * 512:(qs_i + 1) * 512],
                            start=True, stop=True)

        def emit_exp(e, j):
            return e.activation(B["p_bf"][j % 2][:, :], sbank[j % 2][:, :],
                                AF.Exp, scale=0.125, bias=B["mh_col"][:, :])

        def emit_PVs(j, qs_i=qs_i):
            smin = max(0, j - 4 * qs_i)
            for sub in range(smin, 4):
                def f(e, j=j, sub=sub, qs_i=qs_i):
                    return e.matmul(
                        PS[3][:, sub * 65:(sub + 1) * 65],
                        B["p_bf"][j % 2][:, sub * 128:(sub + 1) * 128],
                        B["v_tok"][:, j * 65:(j + 1) * 65],
                        start=(j == 0), stop=(j == 4 * qs_i + sub))
                p.t(f)

        p.stage()
        p.t(lambda e: emit_S(e, 0))
        p.stage()
        p.s(lambda e: emit_exp(e, 0))
        if nk > 1:
            p.t(lambda e: emit_S(e, 1))
        for j in range(nk):
            if j >= 4 * qs_i:
                sub = j - 4 * qs_i
                p.stage()
                p.v(lambda e, j=j, sub=sub: e.tensor_tensor(
                    B["p_bf"][j % 2][:, sub * 128:(sub + 1) * 128],
                    B["p_bf"][j % 2][:, sub * 128:(sub + 1) * 128],
                    c.mask_bf[:, :], ALU.mult))
            p.stage()
            emit_PVs(j)
            if j + 1 < nk:
                p.s(lambda e, j=j: emit_exp(e, j + 1))
            if j + 2 < nk:
                p.t(lambda e, j=j: emit_S(e, j + 2))
        p.stage()
        for sub in range(4):
            p.v(lambda e, sub=sub: e.reciprocal(
                B["rc"][:, sub:sub + 1], PS[3][:, sub * 65 + 64:sub * 65 + 65]))
        p.stage()
        for sub in range(4):
            p.v(lambda e, sub=sub: e.tensor_scalar(
                B["ctx_bf"][:, sub * 64:(sub + 1) * 64],
                PS[3][:, sub * 65:sub * 65 + 64],
                B["rc"][:, sub:sub + 1], None, ALU.mult))
        for half in range(2):
            p.stage()
            for q in range(2):
                sub = half * 2 + q
                p.t(lambda e, sub=sub, q=q: e.transpose(
                    PT[0:64, q * 128:(q + 1) * 128],
                    B["ctx_bf"][:, sub * 64:(sub + 1) * 64],
                    c.eye_bf[:, :]))
            p.stage()
            for q in range(2):
                sub = half * 2 + q
                p.v(lambda e, sub=sub, q=q: e.tensor_copy(
                    B["ctxT_bf"][:, sub * 128:(sub + 1) * 128],
                    PT[0:64, q * 128:(q + 1) * 128]))
        p.stage()
        for sub in range(4):
            p.t(lambda e, sub=sub: e.matmul(
                PS[4][0:64, sub * 128:(sub + 1) * 128], w["wo"][0:64, 0:64],
                B["ctxT_bf"][:, sub * 128:(sub + 1) * 128],
                start=True, stop=True))
        p.stage()
        p.v(lambda e, span=span: e.tensor_tensor(
            Xrows_add[:, span], Xrows_add[:, span], PS[4][0:64, 0:512],
            ALU.add))

def emit_attn_weights_load(p, c, w, aps):
    (wdqT, wuqT, wdkvT, wukvT, wqrT, wkrT, woT, qnw, knw) = aps
    w["dq"] = c.sb([64, 512], BF16, "wdq")
    w["uq"] = [c.sb([128, 64], BF16, f"wuq{k}") for k in range(4)]
    w["dkv"] = c.sb([64, 1024], BF16, "wdkv")
    w["ukv"] = [c.sb([128, 128], BF16, f"wukv{k}") for k in range(8)]
    w["qr"] = c.sb([64, 64], BF16, "wqr")
    w["kr"] = c.sb([64, 64], BF16, "wkr")
    w["wo"] = c.sb([64, 64], BF16, "wwo")
    w["qnw"] = c.sb([64, 1], F32, "wqnw")
    w["knw"] = c.sb([64, 1], F32, "wknw")
    p.dg(lambda e: e.dma_start(w["dq"][:, :], wdqT))
    for k in range(4):
        p.dg(lambda e, k=k: e.dma_start(w["uq"][k][:, :],
                                        wuqT[k * 128:(k + 1) * 128, :]))
    p.dg(lambda e: e.dma_start(w["dkv"][:, :], wdkvT))
    for k in range(8):
        p.dg(lambda e, k=k: e.dma_start(w["ukv"][k][:, :],
                                        wukvT[k * 128:(k + 1) * 128, :]))
    p.dg(lambda e: e.dma_start(w["qr"][:, :], wqrT))
    p.dg(lambda e: e.dma_start(w["kr"][:, :], wkrT))
    p.dg(lambda e: e.dma_start(w["wo"][:, :], woT))
    p.d(lambda e: e.dma_start(w["qnw"][:, :], qnw))
    p.d(lambda e: e.dma_start(w["knw"][:, :], knw))


def emit_ln_stats_stream(p, c, xT, B, lnb, xbufs):
    tc = lnb.tc
    mean_row = lnb.var
    for ci in range(T // tc):
        cs = slice(ci * tc, (ci + 1) * tc)
        for i in range(NT):
            buf = xbufs[i % 2]
            p.stage()
            p.d(lambda e, buf=buf, i=i, cs=cs: e.dma_start(
                buf[:, :], xT[i * 128:(i + 1) * 128, cs]))
            p.stage()
            p.s(lambda e, buf=buf, i=i: e.activation(
                lnb.sq[i % 2][:, :], buf[:, :], AF.Square))
            p.stage()
            p.t(lambda e, i=i, buf=buf: e.matmul(
                lnb.psA[:, :], c.invD_col[:, :], buf[:, :],
                start=(i == 0), stop=(i == NT - 1)))
            p.t(lambda e, i=i: e.matmul(
                lnb.psB[:, :], c.invD_col[:, :], lnb.sq[i % 2][:, :],
                start=(i == 0), stop=(i == NT - 1)))
        p.stage()
        p.v(lambda e, cs=cs: e.tensor_copy(B["mr_row"][0:1, cs],
                                           lnb.psA[:, :]))
        p.stage()
        p.v(lambda e, cs=cs: e.tensor_tensor(lnb.var[0:1, :],
                                             B["mr_row"][0:1, cs],
                                             B["mr_row"][0:1, cs], ALU.mult))
        p.stage()
        p.v(lambda e: e.tensor_tensor(lnb.var[0:1, :], lnb.psB[:, :],
                                      lnb.var[0:1, :], ALU.subtract))
        p.stage()
        epsc = constcol(c, LN_EPS, 1)
        p.s(lambda e, cs=cs, epsc=epsc: e.activation(
            B["rstd_row"][0:1, cs], lnb.var[0:1, :], AF.Ln, bias=epsc[:, :]))
        p.stage()
        p.s(lambda e, cs=cs: e.activation(B["rstd_row"][0:1, cs],
                                          B["rstd_row"][0:1, cs],
                                          AF.Exp, scale=-0.5))
        p.stage()
        p.v(lambda e, cs=cs: e.tensor_tensor(B["mr_row"][0:1, cs],
                                             B["mr_row"][0:1, cs],
                                             B["rstd_row"][0:1, cs],
                                             ALU.mult))


def emit_attn0_phase(p, c, xT, yT, aps):
    B = attn_bufs(c)
    lnb = LNBufs(c, 512, psA=B["PS"][0][0:1, :], psB=B["PS"][1][0:1, :])
    w = {}
    emit_attn_weights_load(p, c, w, aps)
    p.g(lambda e: e.memset(B["ones_bfc"][:, :], 1.0))
    p.g(lambda e: e.memset(B["inv64"][:, :], 1.0 / 64))
    p.g(lambda e: e.memset(B["v_tok"][:, :], 1.0))
    xbufs = [c.sb([128, 512], F32), c.sb([128, 512], F32)]
    emit_ln_stats_stream(p, c, xT, B, lnb, xbufs)
    xh = c.sb([64, T], F32)
    for h in range(16):
        i, r0 = h // 2, (h % 2) * 64
        p.stage()
        p.d(lambda e, i=i, r0=r0: e.dma_start(
            xh[:, :], xT[i * 128 + r0:i * 128 + r0 + 64, :]))
        p.stage()
        emit_attn0_head(p, c, h, xh, w, B, xh)
        p.stage()
        p.d(lambda e, i=i, r0=r0: e.dma_start(
            yT[i * 128 + r0:i * 128 + r0 + 64, :], xh[:, :]))
        p.stage()


def make_attn0_nc():
    nc = bass.Bass("TRN2", target_bir_lowering=False)
    xT = nc.dram_tensor("xT", [D, T], F32, kind="ExternalInput").ap()
    aps = tuple(
        nc.dram_tensor(nm, sh, F32, kind="ExternalInput").ap()
        for nm, sh in (
            ("wdqT", [64, 512]), ("wuqT", [512, 64]), ("wdkvT", [64, 1024]),
            ("wukvT", [1024, 128]), ("wqrT", [64, 64]), ("wkrT", [64, 64]),
            ("woT", [64, 64]), ("qnw", [64, 1]), ("knw", [64, 1])))
    yT = nc.dram_tensor("yT", [D, T], F32, kind="ExternalOutput").ap()
    with ExitStack() as stack:
        c = Ctx(nc, stack)
        sem = c.sem("SS")
        block = stack.enter_context(nc.Block())
        p = Prog(nc)
        setup_consts(p, c)
        emit_attn0_prep(p, c)
        emit_attn0_phase(p, c, xT, yT, aps)
        p.emit(block, sem)
    return nc



NUM_HEAVY = 1024
LOGK = float(np.log(np.float32(1152.0)))
BIGPOS = 1.0e9


def ffn1_bufs(c):
    B = {}
    if getattr(c, "PSG", None) is not None:
        B["PS"] = c.PSG[0:6]
        B["PTf"] = c.PSG[6][:, 0:128]
        B["PTb"] = c.PTB
    else:
        B["PS"] = [c.ps([128, 512], F32, f"FPS{i}") for i in range(6)]
        B["PTf"] = c.ps([128, 128], F32, "FPTf")
        B["PTb"] = c.ps([128, 256], BF16, "FPTb")
    B["xn"] = [c.sb([128, 512], F32) for _ in range(NT)]
    B["mean"] = c.sb([1, 512], F32)
    B["rstd"] = c.sb([1, 512], F32)
    B["mr"] = c.sb([1, 512], F32)
    B["sn"] = c.sb([1, 512], F32)
    B["s_chunk"] = c.sb([1, 512], F32)
    B["li_bf"] = [c.sb([128, 512], BF16) for _ in range(NT)]
    B["h1l_bf"] = [c.sb([128, 512], BF16) for _ in range(4)]
    B["ladd"] = c.sb([128, 512], F32)
    B["big"] = c.sb([128, 1024], F32, "bigbuf")
    for nm in ("s_pm", "b_pm", "u_pm", "e_pm", "q_pm", "i_all", "cnt_gt",
               "cnt_eq", "sel_pm", "pfx_pm", "ones_pm", "d_pm", "t_pm"):
        B[nm] = c.sb([128, 32], F32, nm)
    B["colA"] = c.sb([128, 1], F32)
    B["colB"] = c.sb([128, 1], F32)
    B["t11"] = [c.sb([1, 1], F32) for _ in range(4)]
    B["trow"] = c.sb([1, 128], F32)
    B["prow"] = c.sb([1, 128], F32)
    B["orow"] = c.sb([1, 128], F32)
    B["q_rep"] = c.sb([128, T], F32, "q_rep")
    B["junk_bf"] = c.sb([128, T], BF16, "junk_bf")
    B["mask_bf"] = c.sb([128, T], BF16, "maskT")
    B["iotaT"] = c.sb([128, T], F32, "iotaT")
    B["d_u32"] = c.sb([128, 32], U32)
    B["iota_src"] = c.sb([128, 32], U32, persist=True)
    B["offs"] = c.sb([128, 1], U32, persist=True)
    B["idx"] = [c.sb([128, 1], U32, persist=True) for _ in range(8)]
    B["hin_bf"] = c.sb([128, 1024], BF16, "hin_tok")
    B["hinT"] = [c.sb([128, 1024], BF16) for _ in range(NT)]
    B["h1h_bf"] = [c.sb([128, 256], BF16) for _ in range(32)]
    B["hoT"] = [c.sb([128, 256], F32) for _ in range(NT)]
    B["w1h"] = c.sb([128, 1024], BF16, "w1hbuf")
    B["w2h"] = c.sb([128, 2048], BF16, "w2hbuf")
    B["srms_col"] = c.sb([128, 1], F32)
    B["sq_col"] = c.sb([128, 1], F32)
    return B


def emit_ffn1(p, c, B, w, getX, putX, dram):
    PS = B["PS"]
    tc = 512
    lnb = w["lnb"]
    for ci in range(T // tc):
        cs = slice(ci * tc, (ci + 1) * tc)
        p.stage()
        Xv = [getX(i, cs) for i in range(NT)]
        p.stage()
        emit_ln_stats_chunk(p, c, Xv, slice(0, tc), lnb, B["mean"], B["rstd"],
                            B["mr"])
        p.stage()
        p.t(lambda e: e.matmul(PS[0][:, :], c.ones_f[:, :], B["rstd"][0:1, :],
                               start=True, stop=True))
        p.t(lambda e: e.matmul(PS[1][:, :], c.ones_f[:, :], B["mr"][0:1, :],
                               start=True, stop=True))
        p.stage()
        for i in range(NT):
            p.v(lambda e, i=i: e.tensor_tensor(B["xn"][i][:, :], Xv[i][:, :],
                                               PS[0][:, :], ALU.mult))
        p.stage()
        for i in range(NT):
            p.v(lambda e, i=i: e.tensor_tensor(B["xn"][i][:, :],
                                               B["xn"][i][:, :], PS[1][:, :],
                                               ALU.subtract))
        p.stage()
        for i in range(NT):
            p.t(lambda e, i=i: e.matmul(PS[2][0:1, :], w["rt"][i][:, :],
                                        B["xn"][i][:, :], start=(i == 0),
                                        stop=(i == NT - 1)))
            p.s(lambda e, i=i: e.activation(lnb.sq[i % 2][:, :],
                                            B["xn"][i][:, :], AF.Square))
            p.stage()
            p.t(lambda e, i=i: e.matmul(PS[3][0:1, :], c.onescol_f[:, :],
                                        lnb.sq[i % 2][:, :], start=(i == 0),
                                        stop=(i == NT - 1)))
        p.stage()
        p.v(lambda e: e.tensor_copy(B["s_chunk"][0:1, :], PS[2][0:1, :]))
        p.s(lambda e: e.activation(B["sn"][0:1, :], PS[3][0:1, :], AF.Sqrt))
        p.stage()
        p.d(lambda e, ci=ci: e.dma_start(dram["qrow"][ci * tc:(ci + 1) * tc],
                                         B["s_chunk"][0:1, :]))
        small = constcol(c, 1e-12, 1)
        p.v(lambda e, small=small: e.tensor_scalar(
            B["sn"][0:1, :], B["sn"][0:1, :], small[:, :], None, ALU.max))
        p.stage()
        p.v(lambda e: e.reciprocal(B["sn"][0:1, :], B["sn"][0:1, :]))
        p.stage()
        p.v(lambda e: e.tensor_scalar(B["sn"][0:1, :], B["sn"][0:1, :],
                                      float(np.sqrt(1024.0)), None, ALU.mult))
        p.stage()
        p.t(lambda e: e.matmul(PS[4][:, :], c.ones_f[:, :], B["sn"][0:1, :],
                               start=True, stop=True))
        p.stage()
        for i in range(NT):
            p.v(lambda e, i=i: e.tensor_tensor(B["li_bf"][i][:, :],
                                               B["xn"][i][:, :], PS[4][:, :],
                                               ALU.mult))
        for tb in range(tc // 128):
            for i in range(NT):
                p.stage()
                p.t(lambda e, i=i, tb=tb: e.transpose(
                    B["PTf"][:, :], B["xn"][i][:, tb * 128:(tb + 1) * 128],
                    c.eye_f[:, :]))
                p.stage()
                p.v(lambda e, i=i: e.tensor_copy(
                    B["big"][:, i * 128:(i + 1) * 128], B["PTf"][:, :]))
            p.stage()
            t0 = ci * tc + tb * 128
            p.d(lambda e, t0=t0: e.dma_start(
                dram["xn_tok"][t0:t0 + 128, :], B["big"][:, :]))
        p.stage()
        for m in range(4):
            p.stage()
            for k in range(NT):
                p.t(lambda e, m=m, k=k: e.matmul(
                    PS[5][:, :], w["w1l"][k][:, m * 128:(m + 1) * 128],
                    B["li_bf"][k][:, :], start=(k == 0), stop=(k == NT - 1)))
            p.stage()
            p.s(lambda e, m=m: e.activation(B["h1l_bf"][m][:, :], PS[5][:, :],
                                            AF.Gelu, bias=w["b1l"][m][:, :]))
        for m in range(NT):
            p.stage()
            for k in range(4):
                p.t(lambda e, m=m, k=k: e.matmul(
                    PS[5][:, :], w["w2l"][k][:, m * 128:(m + 1) * 128],
                    B["h1l_bf"][k][:, :], start=(k == 0), stop=(k == 3)))
            p.stage()
            p.v(lambda e, m=m: e.tensor_scalar(
                B["ladd"][:, :], PS[5][:, :], 0.5, w["b2l_half"][m][:, :],
                ALU.mult, op1=ALU.add))
            p.stage()
            p.v(lambda e, m=m: e.tensor_tensor(Xv[m][:, :], Xv[m][:, :],
                                               B["ladd"][:, :], ALU.add))
        p.stage()
        for i in range(NT):
            putX(i, cs, Xv[i])
        p.stage()
    p.stage()
    p.d(lambda e: e.dma_start(
        B["s_pm"][:, :], bass.AP(dram["qrow"].tensor, 0, [[32, 128], [1, 32]])))
    p.stage()
    p.v(lambda e: e.tensor_copy(B["b_pm"][:, :], B["s_pm"][:, :]))
    p.stage()
    for it in range(50):
        p.stage()
        p.v(lambda e: e.tensor_tensor(B["u_pm"][:, :], B["s_pm"][:, :],
                                      B["b_pm"][:, :], ALU.subtract))
        p.stage()
        p.v(lambda e: e.tensor_reduce(B["colA"][:, :], B["u_pm"][:, :], AX.X,
                                      ALU.max))
        p.stage()
        p.t(lambda e: e.transpose(B["PTf"][0:1, :], B["colA"][:, :],
                                  c.eye_f[:, :]))
        p.stage()
        p.v(lambda e: e.tensor_reduce(B["t11"][0][0:1, :], B["PTf"][0:1, :],
                                      AX.X, ALU.max))
        p.stage()
        p.v(lambda e: e.tensor_scalar(B["t11"][1][0:1, :], B["t11"][0][0:1, :],
                                      -1.0, None, ALU.mult))
        p.stage()
        p.t(lambda e: e.matmul(PS[0][:, 0:1], c.ones_f[0:1, :],
                               B["t11"][1][0:1, :], start=True, stop=True))
        p.stage()
        p.v(lambda e: e.tensor_copy(B["colB"][:, :], PS[0][:, 0:1]))
        p.stage()
        p.s(lambda e: e.activation(B["e_pm"][:, :], B["u_pm"][:, :], AF.Exp,
                                   bias=B["colB"][:, :]))
        p.stage()
        p.v(lambda e: e.tensor_reduce(B["colA"][:, :], B["e_pm"][:, :], AX.X,
                                      ALU.add))
        p.stage()
        p.t(lambda e: e.transpose(B["PTf"][0:1, :], B["colA"][:, :],
                                  c.eye_f[:, :]))
        p.stage()
        p.v(lambda e: e.tensor_reduce(B["t11"][2][0:1, :], B["PTf"][0:1, :],
                                      AX.X, ALU.add))
        p.stage()
        p.s(lambda e: e.activation(B["t11"][2][0:1, :], B["t11"][2][0:1, :],
                                   AF.Ln))
        p.stage()
        p.v(lambda e: e.tensor_tensor(B["t11"][2][0:1, :], B["t11"][2][0:1, :],
                                      B["t11"][0][0:1, :], ALU.add))
        p.stage()
        p.v(lambda e: e.tensor_scalar(B["t11"][3][0:1, :], B["t11"][2][0:1, :],
                                      -1.0, LOGK, ALU.mult, op1=ALU.add))
        p.stage()
        p.t(lambda e: e.matmul(PS[0][:, 0:1], c.ones_f[0:1, :],
                               B["t11"][3][0:1, :], start=True, stop=True))
        p.stage()
        p.v(lambda e: e.tensor_copy(B["colA"][:, :], PS[0][:, 0:1]))
        p.stage()
        p.s(lambda e: e.activation(B["b_pm"][:, :], B["s_pm"][:, :], AF.Relu,
                                   bias=B["colA"][:, :]))
    p.stage()
    p.v(lambda e: e.tensor_scalar(B["q_pm"][:, :], B["s_pm"][:, :],
                                  B["colA"][:, :], 0.0, ALU.add, op1=ALU.min))
    p.stage()
    p.d(lambda e: e.dma_start(
        bass.AP(dram["qrow"].tensor, 0, [[32, 128], [1, 32]]), B["q_pm"][:, :]))
    p.stage()
    p.d(lambda e: e.dma_start(
        B["q_rep"][:, :], bass.AP(dram["qrow"].tensor, 0, [[0, 128], [1, T]])))
    p.g(lambda e: e.iota(B["iotaT"][:, :], [[1, T]], base=0,
                         channel_multiplier=0,
                         allow_small_or_imprecise_dtypes=True))
    p.g(lambda e: e.iota(B["i_all"][:, :], [[1, 32]], base=0,
                         channel_multiplier=32,
                         allow_small_or_imprecise_dtypes=True))
    p.g(lambda e: e.iota(B["iota_src"][:, :], [[128, 32]], base=0,
                         channel_multiplier=1))
    p.g(lambda e: e.memset(B["ones_pm"][:, :], 1.0))
    p.stage()
    for k in range(32):
        p.stage()
        p.v(lambda e, k=k: e.tensor_scalar(
            B["junk_bf"][:, :], B["q_rep"][:, :], B["q_pm"][:, k:k + 1], 0.0,
            ALU.is_gt, op1=ALU.add, accum_out=B["cnt_gt"][:, k:k + 1]))
        p.v(lambda e, k=k: e.tensor_scalar(
            B["mask_bf"][:, :], B["iotaT"][:, :], B["i_all"][:, k:k + 1], None,
            ALU.is_lt))
        p.stage()
        p.v(lambda e, k=k: e.scalar_tensor_tensor(
            B["junk_bf"][:, :], B["q_rep"][:, :], B["q_pm"][:, k:k + 1],
            B["mask_bf"][:, :], ALU.is_equal, ALU.mult,
            accum_out=B["cnt_eq"][:, k:k + 1]))
    p.stage()
    p.v(lambda e: e.tensor_tensor(B["sel_pm"][:, :], B["cnt_gt"][:, :],
                                  B["cnt_eq"][:, :], ALU.add))
    p.stage()
    p.v(lambda e: e.tensor_scalar(B["sel_pm"][:, :], B["sel_pm"][:, :],
                                  float(NUM_HEAVY), None, ALU.is_lt))
    p.stage()
    p.v(lambda e: e.tensor_tensor_scan(B["pfx_pm"][:, :], B["ones_pm"][:, :],
                                       B["sel_pm"][:, :], 0.0, ALU.mult,
                                       ALU.add))
    p.stage()
    p.t(lambda e: e.transpose(B["PTf"][0:1, :], B["pfx_pm"][:, 31:32],
                              c.eye_f[:, :]))
    p.stage()
    p.v(lambda e: e.tensor_copy(B["trow"][0:1, :], B["PTf"][0:1, :]))
    p.g(lambda e: e.memset(B["orow"][0:1, :], 1.0))
    p.stage()
    p.v(lambda e: e.tensor_tensor_scan(B["prow"][0:1, :], B["orow"][0:1, :],
                                       B["trow"][0:1, :], 0.0, ALU.mult,
                                       ALU.add))
    p.stage()
    p.v(lambda e: e.tensor_tensor(B["prow"][0:1, :], B["prow"][0:1, :],
                                  B["trow"][0:1, :], ALU.subtract))
    p.stage()
    p.t(lambda e: e.transpose(B["PTf"][:, 0:1], B["prow"][0:1, :],
                              c.eye_f[0:1, 0:1]))
    p.stage()
    p.v(lambda e: e.tensor_copy(B["colA"][:, :], B["PTf"][:, 0:1]))
    p.stage()
    p.v(lambda e: e.tensor_tensor(B["d_pm"][:, :], B["pfx_pm"][:, :],
                                  B["sel_pm"][:, :], ALU.subtract))
    p.stage()
    p.v(lambda e: e.tensor_scalar(B["d_pm"][:, :], B["d_pm"][:, :],
                                  B["colA"][:, :], None, ALU.add))
    p.stage()
    p.v(lambda e: e.tensor_tensor(B["d_pm"][:, :], B["d_pm"][:, :],
                                  B["sel_pm"][:, :], ALU.mult))
    p.stage()
    p.v(lambda e: e.tensor_scalar(B["t_pm"][:, :], B["sel_pm"][:, :],
                                  -1.0, 1.0, ALU.mult, op1=ALU.add))
    p.v(lambda e: e.tensor_scalar(B["u_pm"][:, :], B["i_all"][:, :],
                                  1.0, float(NUM_HEAVY), ALU.mult,
                                  op1=ALU.add))
    p.stage()
    p.v(lambda e: e.tensor_tensor(B["t_pm"][:, :], B["t_pm"][:, :],
                                  B["u_pm"][:, :], ALU.mult))
    p.stage()
    p.v(lambda e: e.tensor_tensor(B["d_pm"][:, :], B["d_pm"][:, :],
                                  B["t_pm"][:, :], ALU.add))
    p.stage()
    p.v(lambda e: e.tensor_copy(B["d_u32"][:, :], B["d_pm"][:, :]))
    p.stage()
    p.d(lambda e: e.dma_start(
        bass.AP(dram["drow"].tensor, 0, [[32, 128], [1, 32]]), B["d_u32"][:, :]))
    p.stage()
    p.g(lambda e: e.memset(B["big"][:, :], 0.0))
    p.stage()
    for k in range(32):
        p.stage()
        p.d(lambda e, k=k: e.dma_start(
            B["offs"][:, :],
            bass.AP(dram["drow"].tensor, k * 128, [[1, 128], [1, 1]])))
        p.stage()
        p.dg(lambda e, k=k: e.indirect_dma_start(
            dram["idx"],
            bass.IndirectOffsetOnAxis(ap=B["offs"][:, :], axis=0),
            B["iota_src"][:, k:k + 1], None))
    p.stage()
    for tb in range(T // 128):
        p.d(lambda e, tb=tb: e.dma_start(
            dram["hfull"][tb * 128:(tb + 1) * 128, :], B["big"][:, :]))
    p.stage()
    for r in range(8):
        p.d(lambda e, r=r: e.dma_start(
            B["idx"][r][:, :], dram["idx"][r * 128:(r + 1) * 128, :]))
    p.stage()
    for r in range(8):
        p.stage()
        p.dg(lambda e, r=r: e.indirect_dma_start(
            B["big"][:, :], None, dram["xn_tok"],
            bass.IndirectOffsetOnAxis(ap=B["idx"][r][:, :], axis=0)))
        p.stage()
        p.v(lambda e: e.tensor_tensor(B["junk_bf"].bitcast(F32)[:, 0:1024],
                                      B["big"][:, :], B["big"][:, :],
                                      ALU.mult))
        p.stage()
        p.v(lambda e: e.tensor_reduce(B["sq_col"][:, :],
                                      B["junk_bf"].bitcast(F32)[:, 0:1024],
                                      AX.X, ALU.add))
        p.stage()
        p.s(lambda e: e.activation(B["sq_col"][:, :], B["sq_col"][:, :],
                                   AF.Sqrt))
        p.stage()
        small = constcol(c, 1e-12, 128)
        p.v(lambda e, small=small: e.tensor_scalar(
            B["sq_col"][:, :], B["sq_col"][:, :], small[:, :], None, ALU.max))
        p.stage()
        p.v(lambda e: e.reciprocal(B["srms_col"][:, :], B["sq_col"][:, :]))
        p.stage()
        p.v(lambda e: e.tensor_scalar(B["srms_col"][:, :], B["srms_col"][:, :],
                                      float(np.sqrt(1024.0)), None, ALU.mult))
        p.stage()
        p.v(lambda e: e.tensor_scalar(B["hin_bf"][:, :], B["big"][:, :],
                                      B["srms_col"][:, :], None, ALU.mult))
        p.stage()
        for i in range(NT):
            p.t(lambda e, i=i: e.transpose(
                B["PTb"][:, 0:128], B["hin_bf"][:, i * 128:(i + 1) * 128],
                c.eye_bf[:, :]))
            p.stage()
            p.v(lambda e, i=i, r=r: e.tensor_copy(
                B["hinT"][i][:, r * 128:(r + 1) * 128], B["PTb"][:, 0:128]))
            p.stage()
    for rc in range(4):
        rs = slice(rc * 256, (rc + 1) * 256)
        for m in range(32):
            p.stage()
            p.dg(lambda e, m=m: e.dma_start(
                B["w1h"][:, :],
                bass.AP(w["w1hT"].tensor, m * 128,
                        [[4096, 128], [4096 * 128, 8], [1, 128]])))
            p.stage()
            for k in range(NT):
                p.v(lambda e, k=k: e.tensor_scalar(
                    B["w1h"][:, k * 128:(k + 1) * 128],
                    B["w1h"][:, k * 128:(k + 1) * 128],
                    w["gh"][k][:, :], None, ALU.mult))
            p.stage()
            for k in range(NT):
                p.t(lambda e, m=m, k=k, rs=rs: e.matmul(
                    PS[5][:, 0:256], B["w1h"][:, k * 128:(k + 1) * 128],
                    B["hinT"][k][:, rs], start=(k == 0), stop=(k == NT - 1)))
            p.stage()
            p.s(lambda e, m=m: e.activation(B["h1h_bf"][m][:, :], PS[5][:, 0:256],
                                            AF.Gelu, bias=w["b1h"][m][:, :]))
        for m in range(NT):
            for half in range(2):
                p.stage()
                p.dg(lambda e, m=m, half=half: e.dma_start(
                    B["w2h"][:, :],
                    bass.AP(w["w2hT"].tensor, m * 128 + half * 16 * 128 * 1024,
                            [[1024, 128], [1024 * 128, 16], [1, 128]])))
                p.stage()
                for kk in range(16):
                    k = half * 16 + kk
                    p.t(lambda e, m=m, k=k, kk=kk: e.matmul(
                        PS[5][:, 0:256], B["w2h"][:, kk * 128:(kk + 1) * 128],
                        B["h1h_bf"][k][:, :], start=(k == 0), stop=(k == 31)))
            p.stage()
            p.v(lambda e, m=m: e.tensor_scalar(
                B["hoT"][m][:, :], PS[5][:, 0:256], 0.5, w["b2h_half"][m][:, :],
                ALU.mult, op1=ALU.add))
        for rt_ in range(2):
            r = rc * 2 + rt_
            for i in range(NT):
                p.stage()
                p.t(lambda e, i=i, rt_=rt_: e.transpose(
                    B["PTf"][:, :], B["hoT"][i][:, rt_ * 128:(rt_ + 1) * 128],
                    c.eye_f[:, :]))
                p.stage()
                p.v(lambda e, i=i: e.tensor_copy(
                    B["big"][:, i * 128:(i + 1) * 128], B["PTf"][:, :]))
            p.stage()
            p.dg(lambda e, r=r: e.indirect_dma_start(
                dram["hfull"],
                bass.IndirectOffsetOnAxis(ap=B["idx"][r][:, :], axis=0),
                B["big"][:, :], None))
            p.stage()


def emit_hfull_add(p, c, B, dram, getXb, putXb):
    for g in range(T // 512):
        p.stage()
        Xv = [getXb(i, g) for i in range(NT)]
        for tb in range(4):
            t0 = g * 512 + tb * 128
            p.stage()
            p.d(lambda e, t0=t0: e.dma_start(B["big"][:, :],
                                             dram["hfull"][t0:t0 + 128, :]))
            p.stage()
            for i in range(NT):
                p.t(lambda e, i=i: e.transpose(
                    B["PTf"][:, :], B["big"][:, i * 128:(i + 1) * 128],
                    c.eye_f[:, :]))
                p.stage()
                p.v(lambda e, i=i, tb=tb: e.tensor_tensor(
                    Xv[i][:, tb * 128:(tb + 1) * 128],
                    Xv[i][:, tb * 128:(tb + 1) * 128],
                    B["PTf"][:, :], ALU.add))
                p.stage()
        p.stage()
        for i in range(NT):
            putXb(i, g, Xv[i])
        p.stage()


def emit_ffn1_weights_load(p, c, w, aps):
    (w1lT, b1l, w2lT, b2l, w1hT, b1h, w2hT, b2h, rt, gl, gh) = aps
    w["w1l"] = load_lhsT_bf16(p, c, w1lT, 1024, 512, "w1l")
    w["w2l"] = load_lhsT_bf16(p, c, w2lT, 512, 1024, "w2l")
    w["b1l"] = load_col_f32(p, c, b1l, 512, "b1l")
    b2l_t = load_col_f32(p, c, b2l, 1024, "b2l")
    w["b1h"] = load_col_f32(p, c, b1h, 4096, "b1h")
    b2h_t = load_col_f32(p, c, b2h, 1024, "b2h")
    w["rt"] = load_col_f32(p, c, rt, 1024, "rt")
    gl_t = load_col_f32(p, c, gl, 1024, "gl")
    w["gh"] = load_col_f32(p, c, gh, 1024, "gh")
    w["w1hT"] = w1hT
    w["w2hT"] = w2hT
    p.stage()
    for i in range(NT):
        p.v(lambda e, i=i: e.tensor_scalar(b2l_t[i][:, :], b2l_t[i][:, :],
                                           0.5, None, ALU.mult))
        p.v(lambda e, i=i: e.tensor_scalar(b2h_t[i][:, :], b2h_t[i][:, :],
                                           0.5, None, ALU.mult))
        p.v(lambda e, i=i: e.tensor_scalar(w["w1l"][i][:, :], w["w1l"][i][:, :],
                                           gl_t[i][:, :], None, ALU.mult))
    w["b2l_half"] = b2l_t
    w["b2h_half"] = b2h_t


def emit_ffn1_phase(p, c, xT, yT, aps, dram):
    B = ffn1_bufs(c)
    w = {"lnb": LNBufs(c, 512, psA=B["PS"][0][0:1, :],
                       psB=B["PS"][1][0:1, :])}
    emit_ffn1_weights_load(p, c, w, aps)
    xbufs = [c.sb([128, 512], F32) for _ in range(NT)]

    def getX(i, cs):
        p.on("sync", lambda e, i=i, cs=cs: e.dma_start(
            xbufs[i][:, :], xT[i * 128:(i + 1) * 128, cs]), dma=True)
        return xbufs[i]

    def putX(i, cs, buf):
        p.on("sync", lambda e, i=i, cs=cs, buf=buf: e.dma_start(
            yT[i * 128:(i + 1) * 128, cs], buf[:, :]), dma=True)

    emit_ffn1(p, c, B, w, getX, putX, dram)

    def getXb(i, g):
        cs = slice(g * 512, (g + 1) * 512)
        p.on("sync", lambda e, i=i, cs=cs: e.dma_start(
            xbufs[i][:, :], yT[i * 128:(i + 1) * 128, cs]), dma=True)
        return xbufs[i]

    def putXb(i, g, buf):
        cs = slice(g * 512, (g + 1) * 512)
        p.on("sync", lambda e, i=i, cs=cs, buf=buf: e.dma_start(
            yT[i * 128:(i + 1) * 128, cs], buf[:, :]), dma=True)

    emit_hfull_add(p, c, B, dram, getXb, putXb)


def ffn1_dram(nc, tag=""):
    return {
        "xn_tok": nc.dram_tensor(f"xn_tok{tag}", [T, 1024], F32,
                                 kind="Internal").ap(),
        "qrow": nc.dram_tensor(f"qrow{tag}", [T], F32, kind="Internal").ap(),
        "drow": nc.dram_tensor(f"drow{tag}", [T], U32, kind="Internal").ap(),
        "idx": nc.dram_tensor(f"idxl{tag}", [NUM_HEAVY + T, 1], U32,
                              kind="Internal").ap(),
        "hfull": nc.dram_tensor(f"hfull{tag}", [T, 1024], F32,
                                kind="Internal").ap(),
    }


def make_ffn1_nc():
    nc = bass.Bass("TRN2", target_bir_lowering=False)
    xT = nc.dram_tensor("xT", [D, T], F32, kind="ExternalInput").ap()
    aps = tuple(
        nc.dram_tensor(nm, sh, F32, kind="ExternalInput").ap()
        for nm, sh in (
            ("w1lT", [1024, 512]), ("b1l", [512, 1]), ("w2lT", [512, 1024]),
            ("b2l", [1024, 1]), ("w1hT", [1024, 4096]), ("b1h", [4096, 1]),
            ("w2hT", [4096, 1024]), ("b2h", [1024, 1]), ("rt", [1024, 1]),
            ("gl", [1024, 1]), ("gh", [1024, 1])))
    yT = nc.dram_tensor("yT", [D, T], F32, kind="ExternalOutput").ap()
    dram = ffn1_dram(nc)
    with ExitStack() as stack:
        c = Ctx(nc, stack)
        sem = c.sem("SS")
        block = stack.enter_context(nc.Block())
        p = Prog(nc)
        setup_consts(p, c)
        emit_attn0_prep(p, c)
        emit_ffn1_phase(p, c, xT, yT, aps, dram)
        p.emit(block, sem)
    return nc



DIN = 2048
NDT = DIN // 128
NST = 16
TCM = 128
SEG = TCM + 1


def mamba_bufs(c):
    B = {}
    if getattr(c, "PSG", None) is not None:
        B["PS"] = c.PSG
    else:
        B["PS"] = [c.ps([128, 512], F32, f"MPS{i}") for i in range(7)]
    B["mean"] = c.sb([1, 512], F32)
    B["rstd"] = c.sb([1, 512], F32)
    B["mr"] = c.sb([1, 512], F32)
    B["xtile"] = [c.sb([128, 512], F32) for _ in range(2)]
    B["xnf"] = c.sb([128, 512], F32)
    B["stg"] = [c.sb([128, 512], F32) for _ in range(2)]
    B["xnc"] = [c.sb([128, 512], BF16) for _ in range(NT)]
    B["wblk"] = [c.sb([128, 1024], BF16) for _ in range(NT)]
    B["P_x"] = [c.sb([128, TCM + 3], F32) for _ in range(NDT)]
    B["P_u"] = [c.sb([128, TCM], F32) for _ in range(NDT)]
    B["P_d"] = [c.sb([128, TCM], F32) for _ in range(NDT)]
    B["P_y"] = [c.sb([128, TCM], F32) for _ in range(NDT)]
    B["u_bf"] = [c.sb([128, TCM], BF16) for _ in range(NDT)]
    B["scanA"] = c.sb([128, NST * SEG], F32, "scanA")
    B["scanB"] = c.sb([128, NST * SEG], F32, "scanB")
    B["scanH"] = c.sb([128, NST * SEG], F32, "scanH")
    B["hC"] = c.sb([128, NST * TCM], F32, "hCil")
    B["yred"] = c.sb([128, TCM], F32)
    B["B_rep"] = c.sb([128, NST * TCM], F32, "B_rep")
    B["C_rep"] = c.sb([128, NST * TCM], F32, "C_rep")
    B["carry"] = [c.sb([128, NST], F32) for _ in range(NDT)]
    B["draw_bf"] = c.sb([64, TCM], BF16)
    B["B_t"] = c.sb([16, TCM], F32)
    B["C_t"] = c.sb([16, TCM], F32)
    B["tmp1"] = c.sb([128, TCM], F32)
    B["tmp2"] = c.sb([128, TCM], F32)
    B["y_bf"] = [c.sb([128, 256], BF16) for _ in range(NDT)]
    B["xres"] = [c.sb([128, 256], F32) for _ in range(NT)]
    B["xa_t"] = [B["scanA"][:, i * 512:(i + 1) * 512] for i in range(4)]
    B["xa_t"] += [B["scanB"][:, i * 512:(i + 1) * 512] for i in range(4)]
    return B


def emit_mamba_weights_load(p, c, w, aps):
    (ng, nb, inwT, convw, convb, xprojT, dtwT, dtb, alog, dvec, outwT) = aps
    w["ng"] = load_col_f32(p, c, ng, D, "mng")
    w["nb"] = load_col_f32(p, c, nb, D, "mnb")
    w["inwT"] = inwT
    w["convw"] = []
    for i in range(NDT):
        t = c.sb([128, 4], F32, f"cw{i}")
        w["convw"].append(t)
        p.d(lambda e, t=t, i=i: e.dma_start(t[:, :],
                                            convw[i * 128:(i + 1) * 128, :]))
    w["convb"] = load_col_f32(p, c, convb, DIN, "mcb")
    w["xproj"] = []
    for i in range(NDT):
        t = c.sb([128, 128], BF16, f"xpj{i}")
        w["xproj"].append(t)
        p.dg(lambda e, t=t, i=i: e.dma_start(t[:, :],
                                             xprojT[i * 128:(i + 1) * 128, :]))
    w["dtw"] = c.sb([64, 2048], BF16, "dtw")
    p.dg(lambda e: e.dma_start(w["dtw"][:, :], dtwT))
    w["dtb"] = load_col_f32(p, c, dtb, DIN, "mdtb")
    w["A"] = []
    for i in range(NDT):
        t = c.sb([128, NST], F32, f"mA{i}")
        w["A"].append(t)
        p.d(lambda e, t=t, i=i: e.dma_start(t[:, :],
                                            alog[i * 128:(i + 1) * 128, :]))
    w["D"] = load_col_f32(p, c, dvec, DIN, "mD")
    w["outwT"] = []
    for k in range(NDT):
        t = c.sb([128, 1024], BF16, f"mow{k}")
        w["outwT"].append(t)
        p.dg(lambda e, t=t, k=k: e.dma_start(t[:, :],
                                             outwT[k * 128:(k + 1) * 128, :]))
    p.stage()
    for i in range(NDT):
        p.s(lambda e, i=i: e.activation(w["A"][i][:, :], w["A"][i][:, :],
                                        AF.Exp))
    p.stage()
    for i in range(NDT):
        p.v(lambda e, i=i: e.tensor_scalar(w["A"][i][:, :], w["A"][i][:, :],
                                           -1.0, None, ALU.mult))


def emit_mamba(p, c, B, w, xT, yT, dram):
    PS = B["PS"]
    lnb = w["lnb"]
    epsc = constcol(c, LN_EPS, 1)
    for ci in range(T // 512):
        cs = slice(ci * 512, (ci + 1) * 512)
        p.stage()
        for i in range(NT):
            buf = B["xtile"][i % 2]
            p.stage()
            p.d(lambda e, buf=buf, i=i, cs=cs: e.dma_start(
                buf[:, :], xT[i * 128:(i + 1) * 128, cs]))
            p.stage()
            p.s(lambda e, buf=buf, i=i: e.activation(
                lnb.sq[i % 2][:, :], buf[:, :], AF.Square))
            p.stage()
            p.t(lambda e, i=i, buf=buf: e.matmul(
                lnb.psA[:, :], c.invD_col[:, :], buf[:, :],
                start=(i == 0), stop=(i == NT - 1)))
            p.t(lambda e, i=i: e.matmul(
                lnb.psB[:, :], c.invD_col[:, :], lnb.sq[i % 2][:, :],
                start=(i == 0), stop=(i == NT - 1)))
        p.stage()
        p.v(lambda e: e.tensor_copy(B["mean"][0:1, :], lnb.psA[:, :]))
        p.stage()
        p.v(lambda e: e.tensor_tensor(lnb.var[0:1, :], B["mean"][0:1, :],
                                      B["mean"][0:1, :], ALU.mult))
        p.stage()
        p.v(lambda e: e.tensor_tensor(lnb.var[0:1, :], lnb.psB[:, :],
                                      lnb.var[0:1, :], ALU.subtract))
        p.stage()
        p.s(lambda e, epsc=epsc: e.activation(B["rstd"][0:1, :],
                                              lnb.var[0:1, :], AF.Ln,
                                              bias=epsc[:, :]))
        p.stage()
        p.s(lambda e: e.activation(B["rstd"][0:1, :], B["rstd"][0:1, :],
                                   AF.Exp, scale=-0.5))
        p.stage()
        p.v(lambda e: e.tensor_tensor(B["mr"][0:1, :], B["mean"][0:1, :],
                                      B["rstd"][0:1, :], ALU.mult))
        p.stage()
        p.t(lambda e: e.matmul(PS[0][:, :], c.ones_f[:, :], B["rstd"][0:1, :],
                               start=True, stop=True))
        p.t(lambda e: e.matmul(PS[1][:, :], c.ones_f[:, :], B["mr"][0:1, :],
                               start=True, stop=True))
        p.stage()
        for i in range(NT):
            buf = B["xtile"][i % 2]
            p.stage()
            p.d(lambda e, buf=buf, i=i, cs=cs: e.dma_start(
                buf[:, :], xT[i * 128:(i + 1) * 128, cs]))
            p.stage()
            p.v(lambda e, buf=buf, i=i: e.tensor_tensor(
                B["xa_t"][i][:, :], buf[:, :], PS[0][:, :], ALU.mult))
            p.stage()
            p.v(lambda e, i=i: e.tensor_tensor(B["xa_t"][i][:, :],
                                               B["xa_t"][i][:, :],
                                               PS[1][:, :], ALU.subtract))
            p.stage()
            p.d(lambda e, i=i, cs=cs: e.dma_start(
                dram["xa"][i * 128:(i + 1) * 128, cs], B["xa_t"][i][:, :]))
        for i in range(NT):
            p.stage()
            p.s(lambda e, i=i: e.activation(lnb.sq[i % 2][:, :],
                                            B["xa_t"][i][:, :], AF.Square))
            p.stage()
            p.t(lambda e, i=i: e.matmul(
                PS[4][0:1, :], c.invD_col[:, :], B["xa_t"][i][:, :],
                start=(i == 0), stop=(i == NT - 1)))
            p.t(lambda e, i=i: e.matmul(
                PS[5][0:1, :], c.invD_col[:, :], lnb.sq[i % 2][:, :],
                start=(i == 0), stop=(i == NT - 1)))
        p.stage()
        p.v(lambda e: e.tensor_copy(B["mean"][0:1, :], PS[4][0:1, :]))
        p.stage()
        p.v(lambda e: e.tensor_tensor(lnb.var[0:1, :], B["mean"][0:1, :],
                                      B["mean"][0:1, :], ALU.mult))
        p.stage()
        p.v(lambda e: e.tensor_tensor(lnb.var[0:1, :], PS[5][0:1, :],
                                      lnb.var[0:1, :], ALU.subtract))
        p.stage()
        p.s(lambda e, epsc=epsc: e.activation(B["rstd"][0:1, :],
                                              lnb.var[0:1, :], AF.Ln,
                                              bias=epsc[:, :]))
        p.stage()
        p.s(lambda e: e.activation(B["rstd"][0:1, :], B["rstd"][0:1, :],
                                   AF.Exp, scale=-0.5))
        p.stage()
        p.v(lambda e: e.tensor_tensor(B["mr"][0:1, :], B["mean"][0:1, :],
                                      B["rstd"][0:1, :], ALU.mult))
        p.stage()
        p.t(lambda e: e.matmul(PS[0][:, :], c.ones_f[:, :], B["rstd"][0:1, :],
                               start=True, stop=True))
        p.t(lambda e: e.matmul(PS[1][:, :], c.ones_f[:, :], B["mr"][0:1, :],
                               start=True, stop=True))
        p.stage()
        for i in range(NT):
            p.stage()
            p.v(lambda e, i=i: e.tensor_tensor(B["xnf"][:, :],
                                               B["xa_t"][i][:, :],
                                               PS[0][:, :], ALU.mult))
            p.stage()
            p.v(lambda e: e.tensor_tensor(B["xnf"][:, :], B["xnf"][:, :],
                                          PS[1][:, :], ALU.subtract))
            p.stage()
            p.v(lambda e, i=i: e.tensor_scalar(
                B["xnc"][i][:, :], B["xnf"][:, :], w["ng"][i][:, :],
                w["nb"][i][:, :], ALU.mult, op1=ALU.add))
            p.stage()
            p.d(lambda e, i=i, cs=cs: e.dma_start(
                dram["xnb"][i * 128:(i + 1) * 128, cs], B["xnc"][i][:, :]))
        p.stage()
    for mb in range(4):
        p.stage()
        for k in range(NT):
            p.dg(lambda e, mb=mb, k=k: e.dma_start(
                B["wblk"][k][:, :],
                w["inwT"][k * 128:(k + 1) * 128,
                          mb * 1024:(mb + 1) * 1024]))
        p.stage()
        for tcx in range(T // 512):
            cs = slice(tcx * 512, (tcx + 1) * 512)
            p.stage()
            for i in range(NT):
                p.d(lambda e, i=i, cs=cs: e.dma_start(
                    B["xnc"][i][:, :], dram["xnb"][i * 128:(i + 1) * 128, cs]))
            p.stage()
            for m in range(8):
                par = m % 2
                p.stage()
                for k in range(NT):
                    p.t(lambda e, m=m, k=k, par=par: e.matmul(
                        PS[2 + par][:, :],
                        B["wblk"][k][:, m * 128:(m + 1) * 128],
                        B["xnc"][k][:, :], start=(k == 0), stop=(k == NT - 1)))
                p.stage()
                if par == 0:
                    p.s(lambda e, par=par: e.copy(B["stg"][par][:, :],
                                                  PS[2 + par][:, :]))
                else:
                    p.v(lambda e, par=par: e.tensor_copy(B["stg"][par][:, :],
                                                         PS[2 + par][:, :]))
                p.stage()
                row = mb * 8 + m
                p.d(lambda e, row=row, cs=cs, par=par: e.dma_start(
                    dram["xz"][row * 128:(row + 1) * 128, cs],
                    B["stg"][par][:, :]))
    p.stage()
    for i in range(NDT):
        p.g(lambda e, i=i: e.memset(B["carry"][i][:, :], 0.0))
    for ci in range(T // TCM):
        t0 = ci * TCM
        p.stage()
        for i in range(NDT):
            if ci == 0:
                p.g(lambda e, i=i: e.memset(B["P_x"][i][:, 0:3], 0.0))
                p.d(lambda e, i=i, t0=t0: e.dma_start(
                    B["P_x"][i][:, 3:TCM + 3],
                    dram["xz"][i * 128:(i + 1) * 128, t0:t0 + TCM]))
            else:
                p.d(lambda e, i=i, t0=t0: e.dma_start(
                    B["P_x"][i][:, 0:TCM + 3],
                    dram["xz"][i * 128:(i + 1) * 128, t0 - 3:t0 + TCM]))
        p.stage()
        for i in range(NDT):
            p.v(lambda e, i=i: e.tensor_scalar(
                B["tmp1"][:, :], B["P_x"][i][:, 0:TCM], w["convw"][i][:, 0:1],
                w["convb"][i][:, :], ALU.mult, op1=ALU.add))
            for kk in range(1, 4):
                p.v(lambda e, i=i, kk=kk: e.scalar_tensor_tensor(
                    B["tmp1"][:, :], B["P_x"][i][:, kk:kk + TCM],
                    w["convw"][i][:, kk:kk + 1], B["tmp1"][:, :],
                    ALU.mult, ALU.add))
            p.stage()
            p.s(lambda e, i=i: e.activation(B["P_u"][i][:, :], B["tmp1"][:, :],
                                            AF.Silu))
            p.stage()
            p.v(lambda e, i=i: e.tensor_copy(B["u_bf"][i][:, :],
                                             B["P_u"][i][:, :]))
            p.stage()
        p.stage()
        for k in range(NDT):
            p.t(lambda e, k=k: e.matmul(PS[2][:, 0:TCM], w["xproj"][k][:, :],
                                        B["u_bf"][k][:, :], start=(k == 0),
                                        stop=(k == NDT - 1)))
        p.stage()
        p.s(lambda e: e.copy(B["draw_bf"][:, :], PS[2][0:64, 0:TCM]))
        p.v(lambda e: e.tensor_copy(B["B_t"][:, :], PS[2][64:80, 0:TCM]))
        p.v(lambda e: e.tensor_copy(B["C_t"][:, :], PS[2][96:112, 0:TCM]))
        p.stage()
        p.d(lambda e: e.dma_start(
            bass.AP(dram["bcf"].tensor, 0, [[TCM, 16], [1, TCM]]),
            B["B_t"][:, :]))
        p.d(lambda e: e.dma_start(
            bass.AP(dram["bcf"].tensor, NST * TCM, [[TCM, 16], [1, TCM]]),
            B["C_t"][:, :]))
        p.stage()
        p.d(lambda e: e.dma_start(
            B["B_rep"][:, :],
            bass.AP(dram["bcf"].tensor, 0, [[0, 128], [1, NST * TCM]])))
        p.d(lambda e: e.dma_start(
            B["C_rep"][:, :],
            bass.AP(dram["bcf"].tensor, NST * TCM, [[0, 128], [1, NST * TCM]])))
        for mi in range(NDT):
            p.stage()
            p.t(lambda e, mi=mi: e.matmul(
                PS[3][:, 0:TCM], w["dtw"][:, mi * 128:(mi + 1) * 128],
                B["draw_bf"][:, :], start=True, stop=True))
            p.stage()
            p.s(lambda e, mi=mi: e.activation(B["tmp1"][:, :], PS[3][:, 0:TCM],
                                              AF.Abs, bias=w["dtb"][mi][:, :]))
            p.s(lambda e, mi=mi: e.activation(B["tmp2"][:, :], PS[3][:, 0:TCM],
                                              AF.Relu, bias=w["dtb"][mi][:, :]))
            p.stage()
            p.s(lambda e: e.activation(B["tmp1"][:, :], B["tmp1"][:, :],
                                       AF.Exp, scale=-1.0))
            p.stage()
            p.s(lambda e: e.activation(B["tmp1"][:, :], B["tmp1"][:, :],
                                       AF.Ln, bias=1.0))
            p.stage()
            p.v(lambda e, mi=mi: e.tensor_tensor(B["P_d"][mi][:, :],
                                                 B["tmp1"][:, :],
                                                 B["tmp2"][:, :], ALU.add))
        p.stage()
        for i in range(NDT):
            p.v(lambda e, i=i: e.tensor_scalar(B["P_y"][i][:, :],
                                               B["P_u"][i][:, :],
                                               w["D"][i][:, :], None, ALU.mult))
        p.stage()
        for i in range(NDT):
            p.v(lambda e, i=i: e.tensor_tensor(B["P_u"][i][:, :],
                                               B["P_u"][i][:, :],
                                               B["P_d"][i][:, :], ALU.mult))
        p.stage()
        for i in range(NDT):
            p.stage()
            p.g(lambda e: e.memset(
                apx(B["scanA"], 0, [[SEG, NST], [1, 1]]), 0.0))
            for n in range(NST):
                p.s(lambda e, i=i, n=n: e.activation(
                    B["scanA"][:, n * SEG + 1:(n + 1) * SEG],
                    B["P_d"][i][:, :], AF.Exp, scale=w["A"][i][:, n:n + 1]))
            p.v(lambda e, i=i: e.tensor_copy(
                apx(B["scanB"], 0, [[SEG, NST], [1, 1]]),
                B["carry"][i][:, :]))
            p.stage()
            p.v(lambda e, i=i: e.tensor_tensor(
                apx(B["scanB"], 1, [[SEG, NST], [1, TCM]]),
                apx(B["P_u"][i], 0, [[0, NST], [1, TCM]]),
                B["B_rep"][:, :], ALU.mult))
            p.stage()
            p.v(lambda e: e.tensor_tensor_scan(
                B["scanH"][:, :], B["scanA"][:, :], B["scanB"][:, :], 0.0,
                ALU.mult, ALU.add))
            p.stage()
            p.v(lambda e, i=i: e.tensor_copy(
                B["carry"][i][:, :],
                apx(B["scanH"], TCM, [[SEG, NST], [1, 1]])))
            p.v(lambda e: e.tensor_tensor(
                apx(B["hC"], 0, [[1, NST], [NST, TCM]]),
                apx(B["scanH"], 1, [[SEG, NST], [1, TCM]]),
                B["C_rep"][:, :], ALU.mult))
            p.stage()
            p.v(lambda e: e.tensor_reduce(
                B["yred"][:, :],
                apx(B["hC"], 0, [[NST, TCM], [1, NST]]),
                AX.X, ALU.add))
            p.stage()
            p.v(lambda e, i=i: e.tensor_tensor(B["P_y"][i][:, :],
                                               B["P_y"][i][:, :],
                                               B["yred"][:, :], ALU.add))
            p.stage()
        p.stage()
        for i in range(NDT):
            p.d(lambda e, i=i, t0=t0: e.dma_start(
                B["P_x"][i][:, 0:TCM],
                dram["xz"][(NDT + i) * 128:(NDT + i + 1) * 128, t0:t0 + TCM]))
        p.stage()
        for i in range(NDT):
            p.s(lambda e, i=i: e.activation(B["P_d"][i][:, :],
                                            B["P_x"][i][:, 0:TCM], AF.Silu))
        p.stage()
        for i in range(NDT):
            p.v(lambda e, i=i: e.tensor_tensor(B["P_y"][i][:, :],
                                               B["P_y"][i][:, :],
                                               B["P_d"][i][:, :], ALU.mult))
        p.stage()
        for i in range(NDT):
            p.d(lambda e, i=i, t0=t0: e.dma_start(
                dram["ysp"][i * 128:(i + 1) * 128, t0:t0 + TCM],
                B["P_y"][i][:, :]))
    for tcx in range(T // 256):
        cs = slice(tcx * 256, (tcx + 1) * 256)
        p.stage()
        for k in range(NDT):
            p.dg(lambda e, k=k, cs=cs: e.dma_start(
                B["y_bf"][k][:, :], dram["ysp"][k * 128:(k + 1) * 128, cs]))
        for i in range(NT):
            p.d(lambda e, i=i, cs=cs: e.dma_start(
                B["xres"][i][:, :], xT[i * 128:(i + 1) * 128, cs]))
        p.stage()
        for m in range(NT):
            par = m % 2
            p.stage()
            for k in range(NDT):
                p.t(lambda e, m=m, k=k, par=par: e.matmul(
                    PS[2 + par][:, 0:256],
                    w["outwT"][k][:, m * 128:(m + 1) * 128],
                    B["y_bf"][k][:, :], start=(k == 0), stop=(k == NDT - 1)))
            p.d(lambda e, m=m, cs=cs, par=par: e.dma_start(
                B["stg"][par][:, 0:256], dram["xa"][m * 128:(m + 1) * 128, cs]))
            p.stage()
            p.v(lambda e, m=m, par=par: e.tensor_tensor(
                B["xres"][m][:, :], B["xres"][m][:, :], PS[2 + par][:, 0:256],
                ALU.add))
            p.stage()
            p.v(lambda e, m=m, par=par: e.tensor_tensor(
                B["xres"][m][:, :], B["xres"][m][:, :], B["stg"][par][:, 0:256],
                ALU.add))
            p.stage()
            p.d(lambda e, m=m, cs=cs: e.dma_start(
                yT[m * 128:(m + 1) * 128, cs], B["xres"][m][:, :]))


def emit_mamba_phase(p, c, xT, yT, aps, dram):
    B = mamba_bufs(c)
    w = {"lnb": LNBufs(c, 512, psA=B["PS"][0][0:1, :],
                       psB=B["PS"][1][0:1, :])}
    emit_mamba_weights_load(p, c, w, aps)
    p.stage()
    emit_mamba(p, c, B, w, xT, yT, dram)


def mamba_dram(nc, dbg=False):
    kd = "ExternalOutput" if dbg else "Internal"
    return {
        "xz": nc.dram_tensor("xz_sp", [2 * DIN, T], F32, kind=kd).ap(),
        "ysp": nc.dram_tensor("y_sp", [DIN, T], F32, kind=kd).ap(),
        "xnb": nc.dram_tensor("xn_bf", [D, T], BF16, kind=kd).ap(),
        "xa": nc.dram_tensor("xa_sp", [D, T], F32, kind=kd).ap(),
        "bcf": nc.dram_tensor("bc_flat", [2 * NST * TCM], F32,
                              kind="Internal").ap(),
    }


def make_mamba_nc(dbg=False):
    nc = bass.Bass("TRN2", target_bir_lowering=False)
    xT = nc.dram_tensor("xT", [D, T], F32, kind="ExternalInput").ap()
    aps = tuple(
        nc.dram_tensor(nm, sh, F32, kind="ExternalInput").ap()
        for nm, sh in (
            ("ng", [D, 1]), ("nb", [D, 1]), ("inwT", [1024, 4096]),
            ("convw", [DIN, 4]), ("convb", [DIN, 1]), ("xprojT", [DIN, 128]),
            ("dtwT", [64, DIN]), ("dtb", [DIN, 1]), ("alog", [DIN, NST]),
            ("dvec", [DIN, 1]), ("outwT", [DIN, 1024])))
    yT = nc.dram_tensor("yT", [D, T], F32, kind="ExternalOutput").ap()
    dram = mamba_dram(nc, dbg)
    with ExitStack() as stack:
        c = Ctx(nc, stack)
        sem = c.sem("SS")
        block = stack.enter_context(nc.Block())
        p = Prog(nc)
        setup_consts(p, c)
        emit_mamba_phase(p, c, xT, yT, aps, dram)
        p.emit(block, sem)
    return nc



ARENA_N = 52224


def make_fused_nc():
    nc = bass.Bass("TRN2", target_bir_lowering=False)
    xT = nc.dram_tensor("xT", [D, T], F32, kind="ExternalInput").ap()

    def din(nm, sh):
        return nc.dram_tensor(nm, sh, F32, kind="ExternalInput").ap()

    ffn0_aps = {k: din(k, sh) for k, sh in (
        ("wrT", [D, D]), ("wkT", [D, D]), ("wvT", [D, D]),
        ("mur", [D, 1]), ("muk", [D, 1]))}
    attn_aps = tuple(din(nm, sh) for nm, sh in (
        ("wdqT", [64, 512]), ("wuqT", [512, 64]), ("wdkvT", [64, 1024]),
        ("wukvT", [1024, 128]), ("wqrT", [64, 64]), ("wkrT", [64, 64]),
        ("woT", [64, 64]), ("qnw", [64, 1]), ("knw", [64, 1])))
    ffn1_aps = tuple(din(nm, sh) for nm, sh in (
        ("w1lT", [1024, 512]), ("b1l", [512, 1]), ("w2lT", [512, 1024]),
        ("b2l", [1024, 1]), ("w1hT", [1024, 4096]), ("b1h", [4096, 1]),
        ("w2hT", [4096, 1024]), ("b2h", [1024, 1]), ("rt", [1024, 1]),
        ("gl", [1024, 1]), ("gh", [1024, 1])))
    mamba_aps = tuple(din(nm, sh) for nm, sh in (
        ("ng", [D, 1]), ("nb", [D, 1]), ("inwT", [1024, 4096]),
        ("convw", [DIN, 4]), ("convb", [DIN, 1]), ("xprojT", [DIN, 128]),
        ("dtwT", [64, DIN]), ("dtb", [DIN, 1]), ("alog", [DIN, NST]),
        ("dvec", [DIN, 1]), ("outwT", [DIN, 1024])))
    yT = nc.dram_tensor("yT", [D, T], F32, kind="ExternalOutput").ap()
    xs = [nc.dram_tensor(f"xc{i}", [D, T], F32, kind="Internal").ap()
          for i in range(5)]
    dramF = ffn1_dram(nc)
    dramM = mamba_dram(nc)
    with ExitStack() as stack:
        c = Ctx(nc, stack)
        sem = c.sem("SS")
        block = stack.enter_context(nc.Block())
        p = Prog(nc)
        setup_consts(p, c)
        emit_attn0_prep(p, c)
        c.PSG = [c.ps([128, 512], F32, f"G{i}") for i in range(7)]
        c.PTB = c.ps([128, 256], BF16, "GT")
        ar = Arena(c, ARENA_N)
        c.arena = ar
        wf0 = (ffn0_aps["wrT"], ffn0_aps["wkT"], ffn0_aps["wvT"],
               ffn0_aps["mur"], ffn0_aps["muk"])
        ar.reset()
        p.stage()
        emit_ffn0_phase(p, c, xT, xs[0], ffn0_aps)
        ar.reset()
        p.stage()
        emit_attn0_phase(p, c, xs[0], xs[1], attn_aps)
        ar.reset()
        p.stage()
        emit_ffn0_phase(p, c, xs[1], xs[2], ffn0_aps)
        ar.reset()
        p.stage()
        emit_ffn1_phase(p, c, xs[2], xs[3], ffn1_aps, dramF)
        ar.reset()
        p.stage()
        emit_mamba_phase(p, c, xs[3], xs[4], mamba_aps, dramM)
        ar.reset()
        p.stage()
        emit_ffn1_phase(p, c, xs[4], yT, ffn1_aps, dramF)
        p.emit(block, sem)
    return nc



def _ct(a):
    return np.ascontiguousarray(a)


def kernel(**inputs):
    ncores = 8
    x = inputs["L0_attn_x"]
    cur = [_ct(x[b].T) for b in range(ncores)]

    ffn0_w = {
        "wrT": _ct(inputs["L0_ffn_W_r"].T),
        "wkT": _ct(inputs["L0_ffn_W_k"].T),
        "wvT": _ct(inputs["L0_ffn_W_v"].T),
        "mur": _ct(inputs["L0_ffn_mu_r"].reshape(-1, 1)),
        "muk": _ct(inputs["L0_ffn_mu_k"].reshape(-1, 1)),
    }
    attn_w = {
        "wdqT": _ct(inputs["L0_attn_W_DQ"].T),
        "wuqT": _ct(inputs["L0_attn_W_UQ"].T),
        "wdkvT": _ct(inputs["L0_attn_W_DKV"].T),
        "wukvT": _ct(inputs["L0_attn_W_UKV"].T),
        "wqrT": _ct(inputs["L0_attn_W_QR"].T),
        "wkrT": _ct(inputs["L0_attn_W_KR"].T),
        "woT": _ct(inputs["L0_attn_W_O"].T),
        "qnw": _ct(inputs["L0_attn_q_norm_w"].reshape(-1, 1)),
        "knw": _ct(inputs["L0_attn_k_norm_w"].reshape(-1, 1)),
    }
    ffn1_w = {
        "w1lT": _ct(inputs["L1_ffn_w1_l"].T),
        "b1l": _ct(inputs["L1_ffn_b1_l"].reshape(-1, 1)),
        "w2lT": _ct(inputs["L1_ffn_w2_l"].T),
        "b2l": _ct(inputs["L1_ffn_b2_l"].reshape(-1, 1)),
        "w1hT": _ct(inputs["L1_ffn_w1_h"].T),
        "b1h": _ct(inputs["L1_ffn_b1_h"].reshape(-1, 1)),
        "w2hT": _ct(inputs["L1_ffn_w2_h"].T),
        "b2h": _ct(inputs["L1_ffn_b2_h"].reshape(-1, 1)),
        "rt": _ct(inputs["L1_ffn_routing_token"].reshape(-1, 1)),
        "gl": _ct(inputs["L1_ffn_gamma_l"].reshape(-1, 1)),
        "gh": _ct(inputs["L1_ffn_gamma_h"].reshape(-1, 1)),
    }
    xp = inputs["L1_attn_xproj_w"]
    xprojT_pad = np.concatenate(
        [_ct(xp[:64].T), _ct(xp[64:80].T), np.zeros((DIN, 16), np.float32),
         _ct(xp[80:96].T), np.zeros((DIN, 16), np.float32)], axis=1)
    mamba_w = {
        "ng": _ct(inputs["L1_attn_norm_g"].reshape(-1, 1)),
        "nb": _ct(inputs["L1_attn_norm_b"].reshape(-1, 1)),
        "inwT": _ct(inputs["L1_attn_in_w"].T),
        "convw": _ct(inputs["L1_attn_conv_w"][:, 0, :]),
        "convb": _ct(inputs["L1_attn_conv_b"].reshape(-1, 1)),
        "xprojT": _ct(xprojT_pad),
        "dtwT": _ct(inputs["L1_attn_dt_w"].T),
        "dtb": _ct(inputs["L1_attn_dt_b"].reshape(-1, 1)),
        "alog": _ct(inputs["L1_attn_A_log"]),
        "dvec": _ct(inputs["L1_attn_D"].reshape(-1, 1)),
        "outwT": _ct(inputs["L1_attn_out_w"].T),
    }

    wall = {**ffn0_w, **attn_w, **ffn1_w, **mamba_w}
    nc = make_fused_nc()
    in_maps = [{**wall, "xT": cur[b]} for b in range(ncores)]
    res = run_bass_kernel_spmd(nc, in_maps, core_ids=list(range(ncores)))
    cur = [np.asarray(res.results[b]["yT"]) for b in range(ncores)]
    out = np.stack([cur[b].T for b in range(ncores)], axis=0)
    return np.ascontiguousarray(out).astype(np.float32)


# version 06
# speedup vs baseline: 1.4137x; 1.1817x over previous
import math
import numpy as np
from contextlib import ExitStack

import concourse.bass as bass
import concourse.mybir as mybir
from concourse.bass_utils import run_bass_kernel_spmd

F32 = mybir.dt.float32
BF16 = mybir.dt.bfloat16
U32 = mybir.dt.uint32
AF = mybir.ActivationFunctionType
ALU = mybir.AluOpType
AX = mybir.AxisListType

T = 4096
D = 1024
NT = D // 128
LN_EPS = 1e-5

ENGINES = ("sync", "vector", "scalar", "tensor", "gpsimd")


class Prog:

    def __init__(self, nc):
        self.nc = nc
        self.stages = []
        self.nlate = 0
        self.stage()

    def stage(self, dep=None, lwait=False):
        self.stages.append({"ops": {e: [] for e in ENGINES}, "dep": dep,
                            "lw": (self.nlate if lwait else None)})
        return len(self.stages) - 1

    def lwait_here(self):
        self.stages[-1]["lw"] = self.nlate

    def late(self, eng, fn):
        self.stages[-1]["ops"][eng].append((fn, "late"))
        self.nlate += 1

    def on(self, eng, fn, dma=False, stage_idx=None):
        st = self.stages[-1 if stage_idx is None else stage_idx]
        st["ops"][eng].append((fn, dma))

    def t(self, fn):
        self.on("tensor", fn)

    def v(self, fn):
        self.on("vector", fn)

    def s(self, fn):
        self.on("scalar", fn)

    def g(self, fn):
        self.on("gpsimd", fn)

    def d(self, fn):
        self.on("sync", fn, dma=True)

    def dg(self, fn):
        self.on("gpsimd", fn, dma=True)

    def ds(self, fn):
        self.on("scalar", fn, dma=True)

    def emit(self, block, sem, semL=None):
        Tt = []
        tot = 0
        for st in self.stages:
            n_dma = sum(1 for e in ENGINES for f, dd in st["ops"][e]
                        if dd is True)
            n_ce = sum(
                1 for e in ENGINES
                if any(dd is False for f, dd in st["ops"][e])
            )
            tot += 16 * n_dma + n_ce
            Tt.append(tot)
        assert self.nlate == 0 or semL is not None

        def dep_of(i):
            d = self.stages[i]["dep"]
            return (i - 1) if d is None else d

        def mk_body(engine_name):
            def body(e):
                hw = 0
                hwL = 0
                for i, st in enumerate(self.stages):
                    ops = st["ops"][engine_name]
                    if not ops:
                        continue
                    dep = dep_of(i)
                    if dep >= 0 and Tt[dep] > hw:
                        e.wait_ge(sem, Tt[dep])
                        hw = Tt[dep]
                    lw = st["lw"]
                    if lw is not None and 16 * lw > hwL:
                        e.wait_ge(semL, 16 * lw)
                        hwL = 16 * lw
                    last_comp = None
                    for j, (fn, dd) in enumerate(ops):
                        if dd is False:
                            last_comp = j
                    for j, (fn, dd) in enumerate(ops):
                        inst = fn(e)
                        if dd is True:
                            inst.then_inc(sem, 16)
                        elif dd == "late":
                            inst.then_inc(semL, 16)
                        elif j == last_comp:
                            inst.then_inc(sem, 1)
                if engine_name == "sync":
                    e.wait_ge(sem, Tt[-1])
                    if self.nlate:
                        e.wait_ge(semL, 16 * self.nlate)
            return body

        block.sync(mk_body("sync"))
        block.vector(mk_body("vector"))
        block.scalar(mk_body("scalar"))
        block.tensor(mk_body("tensor"))
        block.gpsimd(mk_body("gpsimd"))


def apx(buf, off, freepat):
    if isinstance(buf, bass.AP):
        return bass.AP(buf.tensor, buf.offset + off,
                       [list(buf.ap[0])] + [list(x) for x in freepat])
    base = buf[:, :]
    return bass.AP(base.tensor, off,
                   [list(base.ap[0])] + [list(x) for x in freepat])


class Arena:

    def __init__(self, c, nelems):
        self.t = c.sb([128, nelems], F32, "arena", persist=True)
        self.n = nelems
        self.off = 0
        self.hi = 0

    def reset(self):
        self.hi = max(self.hi, self.off)
        self.off = 0

    def alloc(self, shape, dtype):
        nbytes = shape[1] * mybir.dt.size(dtype)
        cells = (nbytes + 3) // 4
        cells = (cells + 7) & ~7
        off = self.off
        self.off += cells
        assert self.off <= self.n, f"arena overflow {self.off}>{self.n}"
        return self.view(off, shape, dtype)

    def view(self, off, shape, dtype):
        if dtype == F32:
            return self.t[0:shape[0], off:off + shape[1]]
        ratio = 4 // mybir.dt.size(dtype)
        return self.t.bitcast(dtype)[0:shape[0],
                                     off * ratio:off * ratio + shape[1]]


class Ctx:

    def __init__(self, nc, stack):
        self.nc = nc
        self.stack = stack
        self.uid = 0
        self.arena = None

    def sb(self, shape, dtype, name=None, persist=False):
        if self.arena is not None and not persist:
            return self.arena.alloc(shape, dtype)
        self.uid += 1
        nm = name or f"sb{self.uid}"
        return self.stack.enter_context(self.nc.sbuf_tensor(nm, shape, dtype))

    def ps(self, shape, dtype=F32, name=None):
        self.uid += 1
        nm = name or f"ps{self.uid}"
        return self.stack.enter_context(self.nc.psum_tensor(nm, shape, dtype))

    def sem(self, name=None):
        self.uid += 1
        return self.stack.enter_context(
            self.nc.semaphore(name or f"sem{self.uid}")
        )


def setup_consts(p, c):
    c.ones_f = c.sb([1, 128], F32, "ones_f")
    c.onescol_f = c.sb([128, 1], F32, "onescol_f")
    c.invD_col = c.sb([128, 1], F32, "invD_col")
    p.g(lambda e: e.memset(c.ones_f[:, :], 1.0))
    p.g(lambda e: e.memset(c.onescol_f[:, :], 1.0))
    p.g(lambda e: e.memset(c.invD_col[:, :], 1.0 / D))
    c._constcols = {}
    c._constprog = p


def constcol(c, val, parts=1):
    key = (float(val), parts)
    if key not in c._constcols:
        tile = c.sb([parts, 1], F32, persist=True)
        c._constprog.on("gpsimd",
                        lambda e, tile=tile, val=val: e.memset(tile[:, :],
                                                               float(val)),
                        stage_idx=0)
        c._constcols[key] = tile
    return c._constcols[key]


class LNBufs:
    def __init__(self, c, tc, psA=None, psB=None):
        self.tc = tc
        self.sq = [c.sb([128, tc], F32), c.sb([128, tc], F32)]
        self.psA = c.ps([1, tc], F32) if psA is None else psA
        self.psB = c.ps([1, tc], F32) if psB is None else psB
        self.var = c.sb([1, tc], F32)


def emit_ln_stats_chunk(p, c, X, cs, b, mean_row, rstd_row, mr_row=None,
                        eps=LN_EPS, width=None):
    w = width if width is not None else b.tc
    for i in range(NT):
        buf = b.sq[i % 2]

        def f(e, i=i, buf=buf, cs=cs, w=w):
            return e.activation(buf[:, 0:w], X[i][:, cs], AF.Square)
        p.s(f)
        p.stage()

        def fA(e, i=i, cs=cs, w=w):
            return e.matmul(b.psA[:, 0:w], c.invD_col[:, :], X[i][:, cs],
                            start=(i == 0), stop=(i == NT - 1))
        p.t(fA)

        def fB(e, i=i, buf=buf, w=w):
            return e.matmul(b.psB[:, 0:w], c.invD_col[:, :], buf[:, 0:w],
                            start=(i == 0), stop=(i == NT - 1))
        p.t(fB)
    p.stage()

    def fm(e, w=w):
        return e.tensor_copy(mean_row[0:1, 0:w], b.psA[:, 0:w])
    p.v(fm)
    p.stage()

    def fv(e, w=w):
        return e.tensor_tensor(b.var[0:1, 0:w], mean_row[0:1, 0:w],
                               mean_row[0:1, 0:w], ALU.mult)
    p.v(fv)
    p.stage()

    def fv2(e, w=w):
        return e.tensor_tensor(b.var[0:1, 0:w], b.psB[:, 0:w], b.var[0:1, 0:w],
                               ALU.subtract)
    p.v(fv2)
    p.stage()

    epscol = constcol(c, eps, 1)

    def fr1(e, w=w, epscol=epscol):
        return e.activation(rstd_row[0:1, 0:w], b.var[0:1, 0:w], AF.Ln,
                            bias=epscol[:, :])
    p.s(fr1)
    p.stage()

    def fr2(e, w=w):
        return e.activation(rstd_row[0:1, 0:w], rstd_row[0:1, 0:w], AF.Exp,
                            scale=-0.5)
    p.s(fr2)
    if mr_row is not None:
        p.stage()

        def fmr(e, w=w):
            return e.tensor_tensor(mr_row[0:1, 0:w], mean_row[0:1, 0:w],
                                   rstd_row[0:1, 0:w], ALU.mult)
        p.v(fmr)


def load_lhsT_bf16(p, c, dram_ap, K, M, name=None):
    nk = (K + 127) // 128
    tiles = []
    for k in range(nk):
        kk = min(128, K - k * 128)
        tile = c.sb([kk, M], BF16, name and f"{name}_{k}")
        tiles.append(tile)

        def f(e, tile=tile, k=k, kk=kk):
            return e.dma_start(tile[:, :], dram_ap[k * 128:k * 128 + kk, :])
        p.dg(f)
    return tiles


def load_col_f32(p, c, dram_ap, n, name=None):
    tiles = []
    for i in range((n + 127) // 128):
        kk = min(128, n - i * 128)
        tile = c.sb([kk, 1], F32, name and f"{name}_{i}")
        tiles.append(tile)

        def f(e, tile=tile, i=i, kk=kk):
            return e.dma_start(tile[:, :], dram_ap[i * 128:i * 128 + kk, :])
        p.d(f)
    return tiles



def emit_ffn0(p, c, getX, putX, wrT, wkT, wvT, mur, muk):
    tc = 256
    PSG = getattr(c, "PSG", None)
    if PSG is not None:
        ps_g, ps_mr = PSG[0][:, 0:tc], PSG[1][:, 0:tc]
        ps_mm = [PSG[2][:, 0:tc], PSG[3][:, 0:tc]]
        ps_v = [PSG[4][:, 0:tc], PSG[5][:, 0:tc]]
        lnb = LNBufs(c, tc, psA=PSG[2][0:1, 0:tc], psB=PSG[3][0:1, 0:tc])
    else:
        ps_g = c.ps([128, tc], F32)
        ps_mr = c.ps([128, tc], F32)
        ps_mm = [c.ps([128, tc], F32), c.ps([128, tc], F32)]
        ps_v = [c.ps([128, tc], F32), c.ps([128, tc], F32)]
        lnb = LNBufs(c, tc)
    mean_row = c.sb([1, tc], F32)
    rstd_row = c.sb([1, tc], F32)
    mr_row = c.sb([1, tc], F32)

    om_r = [c.sb([128, 1], F32) for _ in range(NT)]
    om_k = [c.sb([128, 1], F32) for _ in range(NT)]
    p.stage()
    for i in range(NT):
        def f1(e, i=i):
            return e.tensor_scalar(om_r[i][:, :], mur[i][:, :], -1.0, 1.0,
                                   ALU.mult, op1=ALU.add)
        p.v(f1)

        def f2(e, i=i):
            return e.tensor_scalar(om_k[i][:, :], muk[i][:, :], -1.0, 1.0,
                                   ALU.mult, op1=ALU.add)
        p.v(f2)

    xn = [c.sb([128, tc], F32) for _ in range(NT)]
    stash = c.sb([128, NT], F32)
    zcol = c.sb([128, 1], F32)
    tmp = [c.sb([128, tc], F32)]
    xr_bf = [c.sb([128, tc], BF16) for _ in range(NT)]
    xk_bf = [c.sb([128, tc], BF16) for _ in range(NT)]
    r_sig1 = c.sb([128, tc], F32)
    k2_bf = [c.sb([128, tc], BF16) for _ in range(NT)]
    krelu = c.sb([128, tc], F32)
    vtmp = c.sb([128, tc], F32)
    p.stage()
    p.g(lambda e: e.memset(zcol[:, :], 0.0))
    p.g(lambda e: e.memset(stash[:, :], 0.0))

    for ci in range(T // tc):
        cs = slice(ci * tc, (ci + 1) * tc)
        p.stage()
        Xv = [getX(i, cs) for i in range(NT)]
        p.stage()
        emit_ln_stats_chunk(p, c, Xv, slice(0, tc), lnb, mean_row, rstd_row,
                            mr_row)
        p.stage()

        def fb1(e):
            return e.matmul(ps_g[:, :], c.ones_f[:, :], rstd_row[0:1, :],
                            start=True, stop=True)
        p.t(fb1)

        def fb2(e):
            return e.matmul(ps_mr[:, :], c.ones_f[:, :], mr_row[0:1, :],
                            start=True, stop=True)
        p.t(fb2)
        p.stage()
        for i in range(NT):
            def fx1(e, i=i):
                return e.tensor_tensor(xn[i][:, :], Xv[i][:, :], ps_g[:, :],
                                       ALU.mult)
            p.v(fx1)

            def fx2(e, i=i):
                return e.tensor_tensor(xn[i][:, :], xn[i][:, :], ps_mr[:, :],
                                       ALU.subtract)
            p.v(fx2)
        p.stage()
        for i in range(NT):
            for (om, mu, dst) in ((om_r, mur, xr_bf), (om_k, muk, xk_bf)):
                def fs1(e, i=i, om=om):
                    return e.tensor_scalar(tmp[0][:, 1:tc], xn[i][:, 0:tc - 1],
                                           om[i][:, :], None, ALU.mult)
                p.v(fs1)

                def fs2(e, i=i, om=om, ci=ci):
                    src = stash[:, i:i + 1] if ci > 0 else zcol[:, :]
                    return e.tensor_scalar(tmp[0][:, 0:1], src, om[i][:, :],
                                           None, ALU.mult)
                p.v(fs2)

                def fs3(e, i=i, mu=mu, dst=dst):
                    return e.scalar_tensor_tensor(dst[i][:, :], xn[i][:, :],
                                                  mu[i][:, :], tmp[0][:, :],
                                                  ALU.mult, ALU.add)
                p.v(fs3)
        p.stage()
        for i in range(NT):
            def fst(e, i=i):
                return e.tensor_copy(stash[:, i:i + 1], xn[i][:, tc - 1:tc])
            p.v(fst)

        for m in range(NT):
            p.stage()
            for k in range(NT):
                def fm(e, m=m, k=k):
                    return e.matmul(ps_mm[m % 2][:, :],
                                    wkT[k][:, m * 128:(m + 1) * 128],
                                    xk_bf[k][:, :],
                                    start=(k == 0), stop=(k == NT - 1))
                p.t(fm)
            p.stage()

            def fr(e, m=m):
                return e.activation(krelu[:, :], ps_mm[m % 2][:, :], AF.Relu)
            p.s(fr)
            p.stage()

            def fq(e, m=m):
                return e.tensor_tensor(k2_bf[m][:, :], krelu[:, :],
                                       krelu[:, :], ALU.mult)
            p.v(fq)
        for m in range(NT):
            p.stage()
            for k in range(NT):
                def fm(e, m=m, k=k):
                    return e.matmul(ps_mm[m % 2][:, :],
                                    wrT[k][:, m * 128:(m + 1) * 128],
                                    xr_bf[k][:, :],
                                    start=(k == 0), stop=(k == NT - 1))
                p.t(fm)
            p.stage()
            for k in range(NT):
                def fm(e, m=m, k=k):
                    return e.matmul(ps_v[m % 2][:, :],
                                    wvT[k][:, m * 128:(m + 1) * 128],
                                    k2_bf[k][:, :],
                                    start=(k == 0), stop=(k == NT - 1))
                p.t(fm)

            def fe(e, m=m):
                return e.activation(r_sig1[:, :], ps_mm[m % 2][:, :],
                                    AF.Sigmoid)
            p.s(fe)
            p.stage()

            def fv1(e, m=m):
                return e.scalar_tensor_tensor(vtmp[:, :], r_sig1[:, :],
                                              0.5, ps_v[m % 2][:, :],
                                              ALU.mult, ALU.mult)
            p.v(fv1)
            p.stage()

            def fv2(e, m=m):
                return e.tensor_tensor(Xv[m][:, :], Xv[m][:, :], vtmp[:, :],
                                       ALU.add)
            p.v(fv2)
        p.stage()
        for i in range(NT):
            putX(i, cs, Xv[i])
        p.stage()


def emit_ffn0_phase(p, c, xT, yT, wL):
    wr = load_lhsT_bf16(p, c, wL["wrT"], D, D)
    wk = load_lhsT_bf16(p, c, wL["wkT"], D, D)
    wv = load_lhsT_bf16(p, c, wL["wvT"], D, D)
    mur_t = load_col_f32(p, c, wL["mur"], D)
    muk_t = load_col_f32(p, c, wL["muk"], D)
    xbufs = [c.sb([128, 256], F32) for _ in range(NT)]

    def getX(i, cs):
        p.on("sync", lambda e, i=i, cs=cs: e.dma_start(
            xbufs[i][:, :], xT[i * 128:(i + 1) * 128, cs]), dma=True)
        return xbufs[i]

    def putX(i, cs, buf):
        p.on("sync", lambda e, i=i, cs=cs, buf=buf: e.dma_start(
            yT[i * 128:(i + 1) * 128, cs], buf[:, :]), dma=True)

    emit_ffn0(p, c, getX, putX, wr, wk, wv, mur_t, muk_t)


def make_ffn0_nc():
    nc = bass.Bass("TRN2", target_bir_lowering=False)
    xT = nc.dram_tensor("xT", [D, T], F32, kind="ExternalInput").ap()
    wL = {
        "wrT": nc.dram_tensor("wrT", [D, D], F32, kind="ExternalInput").ap(),
        "wkT": nc.dram_tensor("wkT", [D, D], F32, kind="ExternalInput").ap(),
        "wvT": nc.dram_tensor("wvT", [D, D], F32, kind="ExternalInput").ap(),
        "mur": nc.dram_tensor("mur", [D, 1], F32, kind="ExternalInput").ap(),
        "muk": nc.dram_tensor("muk", [D, 1], F32, kind="ExternalInput").ap(),
    }
    yT = nc.dram_tensor("yT", [D, T], F32, kind="ExternalOutput").ap()
    with ExitStack() as stack:
        c = Ctx(nc, stack)
        sem = c.sem("SS")
        semL = c.sem("SL")
        block = stack.enter_context(nc.Block())
        p = Prog(nc)
        setup_consts(p, c)
        p.stage()
        emit_ffn0_phase(p, c, xT, yT, wL)
        p.emit(block, sem, semL)
    return nc



RMS_EPS = 1.1920929e-07


def emit_attn0_prep(p, c):
    c.eye_bf = c.sb([128, 128], BF16, "eye_bf")
    c.eye_f = c.sb([128, 128], F32, "eye_f")
    c.mask_bf = c.sb([128, 128], BF16, "mask_bf")
    it = c.sb([128, 128], F32)
    p.stage()
    p.g(lambda e: e.iota(it[:, :], [[1, 128]], base=0, channel_multiplier=-1,
                         allow_small_or_imprecise_dtypes=True))
    p.stage()
    p.v(lambda e: e.tensor_scalar(c.eye_bf[:, :], it[:, :], 0.0, None,
                                  ALU.is_equal))
    p.v(lambda e: e.tensor_scalar(c.eye_f[:, :], it[:, :], 0.0, None,
                                  ALU.is_equal))
    p.v(lambda e: e.tensor_scalar(c.mask_bf[:, :], it[:, :], 0.0, None,
                                  ALU.is_ge))


def attn_bufs(c):
    B = {}
    if getattr(c, "PSG", None) is not None:
        B["PS"] = c.PSG
        B["PT"] = c.PTB
    else:
        B["PS"] = [c.ps([128, 512], F32, f"APS{i}") for i in range(7)]
        B["PT"] = c.ps([128, 256], BF16, "APT")
    B["rstd_row"] = c.sb([1, T], F32, "rstd_row")
    B["mr_row"] = c.sb([1, T], F32, "mr_row")
    B["xnh_f"] = c.sb([64, 512], F32)
    B["xnh"] = c.sb([64, T], BF16, "xnh")
    B["cq"] = [c.sb([128, 512], BF16) for _ in range(4)]
    B["ckv"] = [c.sb([128, 512], BF16) for _ in range(8)]
    B["qc_f"] = c.sb([64, 512], F32)
    B["kc_f"] = c.sb([64, 512], F32)
    B["v_bf"] = c.sb([64, 512], BF16)
    B["sq"] = c.sb([64, 512], F32)
    B["sk"] = c.sb([64, 512], F32)
    B["msq"] = c.sb([1, 512], F32)
    B["msk"] = c.sb([1, 512], F32)
    B["Q"] = c.sb([128, T], BF16, "Qt")
    B["K"] = c.sb([128, T], BF16, "Kt")
    B["v_tok"] = c.sb([128, 65 * 32], BF16, "v_tok")
    B["qs_bf"] = c.sb([128, 512], BF16)
    B["ks_bf"] = c.sb([128, 512], BF16)
    B["ones_bfc"] = c.sb([128, 1], BF16)
    B["inv64"] = c.sb([64, 1], F32)
    B["t11a"] = c.sb([1, 1], F32)
    B["t11b"] = c.sb([1, 1], F32)
    B["qmax"] = c.sb([1, 1], F32)
    B["kmax"] = c.sb([1, 1], F32)
    B["mh_col"] = c.sb([128, 1], F32)
    B["p_bf"] = [c.sb([128, 512], BF16) for _ in range(2)]
    B["rc"] = c.sb([128, 4], F32)
    B["ctx_bf"] = c.sb([128, 256], BF16)
    B["ctxT_bf"] = c.sb([64, 512], BF16)
    return B


def emit_attn0_head(p, c, h, xh, w, B, Xrows_add):
    tcq = 512
    PS = B["PS"]
    PT = B["PT"]
    nchunk = T // tcq
    for ci in range(nchunk):
        cs = slice(ci * tcq, (ci + 1) * tcq)
        p.stage()
        p.t(lambda e, cs=cs: e.matmul(PS[0][0:64, :], c.ones_f[0:1, 0:64],
                                      B["rstd_row"][0:1, cs], start=True,
                                      stop=True))
        p.t(lambda e, cs=cs: e.matmul(PS[1][0:64, :], c.ones_f[0:1, 0:64],
                                      B["mr_row"][0:1, cs], start=True,
                                      stop=True))
        p.stage()
        p.v(lambda e, cs=cs: e.tensor_tensor(B["xnh_f"][:, :], xh[:, cs],
                                             PS[0][0:64, :], ALU.mult))
        p.stage()
        p.v(lambda e, cs=cs: e.tensor_tensor(B["xnh"][:, cs], B["xnh_f"][:, :],
                                             PS[1][0:64, :], ALU.subtract))
        p.stage()
        for m in range(4):
            p.t(lambda e, m=m, cs=cs: e.matmul(
                PS[2][:, :], w["dq"][0:64, m * 128:(m + 1) * 128],
                B["xnh"][:, cs], start=True, stop=True))
            p.stage()
            if m % 2 == 0:
                p.s(lambda e, m=m: e.copy(B["cq"][m][:, :], PS[2][:, :]))
            else:
                p.v(lambda e, m=m: e.tensor_copy(B["cq"][m][:, :], PS[2][:, :]))
            p.stage()
        for m in range(8):
            p.t(lambda e, m=m, cs=cs: e.matmul(
                PS[3][:, :], w["dkv"][0:64, m * 128:(m + 1) * 128],
                B["xnh"][:, cs], start=True, stop=True))
            p.stage()
            if m % 2 == 0:
                p.s(lambda e, m=m: e.copy(B["ckv"][m][:, :], PS[3][:, :]))
            else:
                p.v(lambda e, m=m: e.tensor_copy(B["ckv"][m][:, :],
                                                 PS[3][:, :]))
            p.stage()
        for k in range(4):
            p.t(lambda e, k=k: e.matmul(PS[4][0:64, :], w["uq"][k][:, 0:64],
                                        B["cq"][k][:, :],
                                        start=(k == 0), stop=(k == 3)))
        for k in range(8):
            p.t(lambda e, k=k: e.matmul(PS[5][:, :], w["ukv"][k][:, :],
                                        B["ckv"][k][:, :],
                                        start=(k == 0), stop=(k == 7)))
        p.stage()
        p.s(lambda e: e.copy(B["qc_f"][:, :], PS[4][0:64, :]))
        p.s(lambda e: e.copy(B["kc_f"][:, :], PS[5][0:64, :]))
        p.v(lambda e: e.tensor_copy(B["v_bf"][:, :], PS[5][64:128, :]))
        p.stage()
        p.v(lambda e: e.tensor_tensor(B["sq"][:, :], B["qc_f"][:, :],
                                      B["qc_f"][:, :], ALU.mult))
        p.v(lambda e: e.tensor_tensor(B["sk"][:, :], B["kc_f"][:, :],
                                      B["kc_f"][:, :], ALU.mult))
        for j in range(tcq // 128):
            jt = ci * 4 + j
            p.stage()
            p.t(lambda e, j=j: e.transpose(
                PT[:, 0:64], B["v_bf"][:, j * 128:(j + 1) * 128],
                c.eye_bf[0:64, 0:64]))
            p.stage()
            p.v(lambda e, jt=jt: e.tensor_copy(
                B["v_tok"][:, jt * 65:jt * 65 + 64], PT[:, 0:64]))
        p.stage()
        p.t(lambda e: e.matmul(PS[6][0:1, :], B["inv64"][:, :],
                               B["sq"][:, :], start=True, stop=True))
        p.t(lambda e: e.matmul(PS[2][0:1, :], B["inv64"][:, :],
                               B["sk"][:, :], start=True, stop=True))
        p.stage()
        epsc = constcol(c, RMS_EPS, 1)
        p.s(lambda e, epsc=epsc: e.activation(B["msq"][0:1, :], PS[6][0:1, :],
                                              AF.Ln, bias=epsc[:, :]))
        p.s(lambda e, epsc=epsc: e.activation(B["msk"][0:1, :], PS[2][0:1, :],
                                              AF.Ln, bias=epsc[:, :]))
        p.stage()
        p.s(lambda e: e.activation(B["msq"][0:1, :], B["msq"][0:1, :],
                                   AF.Exp, scale=-0.5))
        p.s(lambda e: e.activation(B["msk"][0:1, :], B["msk"][0:1, :],
                                   AF.Exp, scale=-0.5))
        p.stage()
        p.t(lambda e: e.matmul(PS[6][0:64, :], c.ones_f[0:1, 0:64],
                               B["msq"][0:1, :], start=True, stop=True))
        p.t(lambda e: e.matmul(PS[2][0:64, :], c.ones_f[0:1, 0:64],
                               B["msk"][0:1, :], start=True, stop=True))
        p.stage()
        p.v(lambda e, cs=cs: e.scalar_tensor_tensor(
            B["Q"][0:64, cs], B["qc_f"][:, :], w["qnw"][:, :],
            PS[6][0:64, :], ALU.mult, ALU.mult))
        p.v(lambda e, cs=cs: e.scalar_tensor_tensor(
            B["K"][0:64, cs], B["kc_f"][:, :], w["knw"][:, :],
            PS[2][0:64, :], ALU.mult, ALU.mult))
        p.stage()
        p.t(lambda e, cs=cs: e.matmul(PS[4][0:64, :], w["qr"][0:64, 0:64],
                                      B["Q"][0:64, cs], start=True, stop=True))
        p.t(lambda e, cs=cs: e.matmul(PS[5][0:64, :], w["kr"][0:64, 0:64],
                                      B["xnh"][:, cs], start=True, stop=True))
        p.stage()
        p.s(lambda e, cs=cs: e.copy(B["Q"][64:128, cs], PS[4][0:64, :]))
        p.s(lambda e, cs=cs: e.copy(B["K"][64:128, cs], PS[5][0:64, :]))
        p.stage()
        p.v(lambda e, cs=cs: e.tensor_tensor(B["qs_bf"][:, :], B["Q"][:, cs],
                                             B["Q"][:, cs], ALU.mult))
        p.v(lambda e, cs=cs: e.tensor_tensor(B["ks_bf"][:, :], B["K"][:, cs],
                                             B["K"][:, cs], ALU.mult))
        p.stage()
        p.t(lambda e: e.matmul(PS[4][0:1, :], B["ones_bfc"][:, :],
                               B["qs_bf"][:, :], start=True, stop=True))
        p.t(lambda e: e.matmul(PS[5][0:1, :], B["ones_bfc"][:, :],
                               B["ks_bf"][:, :], start=True, stop=True))
        p.stage()
        p.v(lambda e: e.tensor_reduce(B["t11a"][0:1, 0:1], PS[4][0:1, :],
                                      AX.X, ALU.max))
        p.v(lambda e: e.tensor_reduce(B["t11b"][0:1, 0:1], PS[5][0:1, :],
                                      AX.X, ALU.max))
        p.stage()
        p.v(lambda e, ci=ci: e.tensor_tensor(
            B["qmax"][0:1, 0:1],
            B["qmax"][0:1, 0:1] if ci else B["t11a"][0:1, 0:1],
            B["t11a"][0:1, 0:1], ALU.max))
        p.v(lambda e, ci=ci: e.tensor_tensor(
            B["kmax"][0:1, 0:1],
            B["kmax"][0:1, 0:1] if ci else B["t11b"][0:1, 0:1],
            B["t11b"][0:1, 0:1], ALU.max))
    p.stage()
    p.v(lambda e: e.tensor_tensor(B["t11a"][0:1, 0:1], B["qmax"][0:1, 0:1],
                                  B["kmax"][0:1, 0:1], ALU.mult))
    p.stage()
    p.s(lambda e: e.activation(B["t11a"][0:1, 0:1], B["t11a"][0:1, 0:1],
                               AF.Sqrt))
    p.stage()
    p.v(lambda e: e.tensor_scalar(B["t11a"][0:1, 0:1], B["t11a"][0:1, 0:1],
                                  0.125, 80.0, ALU.mult, op1=ALU.min))
    p.stage()
    p.v(lambda e: e.tensor_scalar(B["t11a"][0:1, 0:1], B["t11a"][0:1, 0:1],
                                  -1.0, None, ALU.mult))
    p.stage()
    p.t(lambda e: e.matmul(PS[0][:, 0:1], c.ones_f[0:1, :],
                           B["t11a"][0:1, 0:1], start=True, stop=True))
    p.stage()
    p.v(lambda e: e.tensor_copy(B["mh_col"][:, :], PS[0][:, 0:1]))
    p.stage()
    sbank = [PS[2], PS[6]]
    for qs_i in range(T // 512):
        span = slice(qs_i * 512, (qs_i + 1) * 512)
        nk = 4 * qs_i + 4

        def emit_S(e, j, qs_i=qs_i):
            return e.matmul(sbank[j % 2][:, :],
                            B["K"][:, j * 128:(j + 1) * 128],
                            B["Q"][:, qs_i * 512:(qs_i + 1) * 512],
                            start=True, stop=True)

        def emit_exp(e, j):
            return e.activation(B["p_bf"][j % 2][:, :], sbank[j % 2][:, :],
                                AF.Exp, scale=0.125, bias=B["mh_col"][:, :])

        def emit_PVs(j, qs_i=qs_i):
            smin = max(0, j - 4 * qs_i)
            for sub in range(smin, 4):
                def f(e, j=j, sub=sub, qs_i=qs_i):
                    return e.matmul(
                        PS[3][:, sub * 65:(sub + 1) * 65],
                        B["p_bf"][j % 2][:, sub * 128:(sub + 1) * 128],
                        B["v_tok"][:, j * 65:(j + 1) * 65],
                        start=(j == 0), stop=(j == 4 * qs_i + sub))
                p.t(f)

        p.stage()
        p.t(lambda e: emit_S(e, 0))
        p.stage()
        p.s(lambda e: emit_exp(e, 0))
        if nk > 1:
            p.t(lambda e: emit_S(e, 1))
        for j in range(nk):
            if j >= 4 * qs_i:
                sub = j - 4 * qs_i
                p.stage()
                p.v(lambda e, j=j, sub=sub: e.tensor_tensor(
                    B["p_bf"][j % 2][:, sub * 128:(sub + 1) * 128],
                    B["p_bf"][j % 2][:, sub * 128:(sub + 1) * 128],
                    c.mask_bf[:, :], ALU.mult))
            p.stage()
            emit_PVs(j)
            if j + 1 < nk:
                p.s(lambda e, j=j: emit_exp(e, j + 1))
            if j + 2 < nk:
                p.t(lambda e, j=j: emit_S(e, j + 2))
        p.stage()
        for sub in range(4):
            p.v(lambda e, sub=sub: e.reciprocal(
                B["rc"][:, sub:sub + 1], PS[3][:, sub * 65 + 64:sub * 65 + 65]))
        p.stage()
        for sub in range(4):
            p.v(lambda e, sub=sub: e.tensor_scalar(
                B["ctx_bf"][:, sub * 64:(sub + 1) * 64],
                PS[3][:, sub * 65:sub * 65 + 64],
                B["rc"][:, sub:sub + 1], None, ALU.mult))
        for half in range(2):
            p.stage()
            for q in range(2):
                sub = half * 2 + q
                p.t(lambda e, sub=sub, q=q: e.transpose(
                    PT[0:64, q * 128:(q + 1) * 128],
                    B["ctx_bf"][:, sub * 64:(sub + 1) * 64],
                    c.eye_bf[:, :]))
            p.stage()
            for q in range(2):
                sub = half * 2 + q
                p.v(lambda e, sub=sub, q=q: e.tensor_copy(
                    B["ctxT_bf"][:, sub * 128:(sub + 1) * 128],
                    PT[0:64, q * 128:(q + 1) * 128]))
        p.stage()
        for sub in range(4):
            p.t(lambda e, sub=sub: e.matmul(
                PS[4][0:64, sub * 128:(sub + 1) * 128], w["wo"][0:64, 0:64],
                B["ctxT_bf"][:, sub * 128:(sub + 1) * 128],
                start=True, stop=True))
        p.stage()
        p.v(lambda e, span=span: e.tensor_tensor(
            Xrows_add[:, span], Xrows_add[:, span], PS[4][0:64, 0:512],
            ALU.add))

def emit_attn_weights_load(p, c, w, aps):
    (wdqT, wuqT, wdkvT, wukvT, wqrT, wkrT, woT, qnw, knw) = aps
    w["dq"] = c.sb([64, 512], BF16, "wdq")
    w["uq"] = [c.sb([128, 64], BF16, f"wuq{k}") for k in range(4)]
    w["dkv"] = c.sb([64, 1024], BF16, "wdkv")
    w["ukv"] = [c.sb([128, 128], BF16, f"wukv{k}") for k in range(8)]
    w["qr"] = c.sb([64, 64], BF16, "wqr")
    w["kr"] = c.sb([64, 64], BF16, "wkr")
    w["wo"] = c.sb([64, 64], BF16, "wwo")
    w["qnw"] = c.sb([64, 1], F32, "wqnw")
    w["knw"] = c.sb([64, 1], F32, "wknw")
    p.dg(lambda e: e.dma_start(w["dq"][:, :], wdqT))
    for k in range(4):
        p.dg(lambda e, k=k: e.dma_start(w["uq"][k][:, :],
                                        wuqT[k * 128:(k + 1) * 128, :]))
    p.dg(lambda e: e.dma_start(w["dkv"][:, :], wdkvT))
    for k in range(8):
        p.dg(lambda e, k=k: e.dma_start(w["ukv"][k][:, :],
                                        wukvT[k * 128:(k + 1) * 128, :]))
    p.dg(lambda e: e.dma_start(w["qr"][:, :], wqrT))
    p.dg(lambda e: e.dma_start(w["kr"][:, :], wkrT))
    p.dg(lambda e: e.dma_start(w["wo"][:, :], woT))
    p.d(lambda e: e.dma_start(w["qnw"][:, :], qnw))
    p.d(lambda e: e.dma_start(w["knw"][:, :], knw))


def emit_ln_stats_stream(p, c, xT, B, lnb, xbufs):
    tc = lnb.tc
    mean_row = lnb.var
    for ci in range(T // tc):
        cs = slice(ci * tc, (ci + 1) * tc)
        for i in range(NT):
            buf = xbufs[i % 2]
            p.stage()
            p.d(lambda e, buf=buf, i=i, cs=cs: e.dma_start(
                buf[:, :], xT[i * 128:(i + 1) * 128, cs]))
            p.stage()
            p.s(lambda e, buf=buf, i=i: e.activation(
                lnb.sq[i % 2][:, :], buf[:, :], AF.Square))
            p.stage()
            p.t(lambda e, i=i, buf=buf: e.matmul(
                lnb.psA[:, :], c.invD_col[:, :], buf[:, :],
                start=(i == 0), stop=(i == NT - 1)))
            p.t(lambda e, i=i: e.matmul(
                lnb.psB[:, :], c.invD_col[:, :], lnb.sq[i % 2][:, :],
                start=(i == 0), stop=(i == NT - 1)))
        p.stage()
        p.v(lambda e, cs=cs: e.tensor_copy(B["mr_row"][0:1, cs],
                                           lnb.psA[:, :]))
        p.stage()
        p.v(lambda e, cs=cs: e.tensor_tensor(lnb.var[0:1, :],
                                             B["mr_row"][0:1, cs],
                                             B["mr_row"][0:1, cs], ALU.mult))
        p.stage()
        p.v(lambda e: e.tensor_tensor(lnb.var[0:1, :], lnb.psB[:, :],
                                      lnb.var[0:1, :], ALU.subtract))
        p.stage()
        epsc = constcol(c, LN_EPS, 1)
        p.s(lambda e, cs=cs, epsc=epsc: e.activation(
            B["rstd_row"][0:1, cs], lnb.var[0:1, :], AF.Ln, bias=epsc[:, :]))
        p.stage()
        p.s(lambda e, cs=cs: e.activation(B["rstd_row"][0:1, cs],
                                          B["rstd_row"][0:1, cs],
                                          AF.Exp, scale=-0.5))
        p.stage()
        p.v(lambda e, cs=cs: e.tensor_tensor(B["mr_row"][0:1, cs],
                                             B["mr_row"][0:1, cs],
                                             B["rstd_row"][0:1, cs],
                                             ALU.mult))


def emit_attn0_phase(p, c, xT, yT, aps):
    B = attn_bufs(c)
    lnb = LNBufs(c, 512, psA=B["PS"][0][0:1, :], psB=B["PS"][1][0:1, :])
    w = {}
    emit_attn_weights_load(p, c, w, aps)
    p.g(lambda e: e.memset(B["ones_bfc"][:, :], 1.0))
    p.g(lambda e: e.memset(B["inv64"][:, :], 1.0 / 64))
    p.g(lambda e: e.memset(B["v_tok"][:, :], 1.0))
    xbufs = [c.sb([128, 512], F32), c.sb([128, 512], F32)]
    emit_ln_stats_stream(p, c, xT, B, lnb, xbufs)
    xh = c.sb([64, T], F32)
    for h in range(16):
        i, r0 = h // 2, (h % 2) * 64
        p.stage()
        p.d(lambda e, i=i, r0=r0: e.dma_start(
            xh[:, :], xT[i * 128 + r0:i * 128 + r0 + 64, :]))
        p.stage()
        emit_attn0_head(p, c, h, xh, w, B, xh)
        p.stage()
        p.d(lambda e, i=i, r0=r0: e.dma_start(
            yT[i * 128 + r0:i * 128 + r0 + 64, :], xh[:, :]))
        p.stage()


def make_attn0_nc():
    nc = bass.Bass("TRN2", target_bir_lowering=False)
    xT = nc.dram_tensor("xT", [D, T], F32, kind="ExternalInput").ap()
    aps = tuple(
        nc.dram_tensor(nm, sh, F32, kind="ExternalInput").ap()
        for nm, sh in (
            ("wdqT", [64, 512]), ("wuqT", [512, 64]), ("wdkvT", [64, 1024]),
            ("wukvT", [1024, 128]), ("wqrT", [64, 64]), ("wkrT", [64, 64]),
            ("woT", [64, 64]), ("qnw", [64, 1]), ("knw", [64, 1])))
    yT = nc.dram_tensor("yT", [D, T], F32, kind="ExternalOutput").ap()
    with ExitStack() as stack:
        c = Ctx(nc, stack)
        sem = c.sem("SS")
        semL = c.sem("SL")
        block = stack.enter_context(nc.Block())
        p = Prog(nc)
        setup_consts(p, c)
        emit_attn0_prep(p, c)
        emit_attn0_phase(p, c, xT, yT, aps)
        p.emit(block, sem, semL)
    return nc



NUM_HEAVY = 1024
LOGK = float(np.log(np.float32(1152.0)))
BIGPOS = 1.0e9


def ffn1_bufs(c):
    B = {}
    if getattr(c, "PSG", None) is not None:
        B["PS"] = c.PSG[0:6]
        B["PTf"] = c.PSG[6][:, 0:128]
        B["PTb"] = c.PTB
    else:
        B["PS"] = [c.ps([128, 512], F32, f"FPS{i}") for i in range(6)]
        B["PTf"] = c.ps([128, 128], F32, "FPTf")
        B["PTb"] = c.ps([128, 256], BF16, "FPTb")
    B["xn"] = [c.sb([128, 512], F32) for _ in range(NT)]
    B["mean"] = c.sb([1, 512], F32)
    B["rstd"] = c.sb([1, 512], F32)
    B["mr"] = c.sb([1, 512], F32)
    B["sn"] = c.sb([1, 512], F32)
    B["s_chunk"] = c.sb([1, 512], F32)
    B["li_bf"] = [c.sb([128, 512], BF16) for _ in range(NT)]
    B["h1l_bf"] = [c.sb([128, 512], BF16) for _ in range(4)]
    B["ladd"] = c.sb([128, 512], F32)
    B["big"] = c.sb([128, 1024], F32, "bigbuf")
    for nm in ("s_pm", "b_pm", "u_pm", "e_pm", "q_pm", "i_all", "cnt_gt",
               "cnt_eq", "sel_pm", "pfx_pm", "ones_pm", "d_pm", "t_pm"):
        B[nm] = c.sb([128, 32], F32, nm)
    B["colA"] = c.sb([128, 1], F32)
    B["colB"] = c.sb([128, 1], F32)
    B["t11"] = [c.sb([1, 1], F32) for _ in range(4)]
    B["trow"] = c.sb([1, 128], F32)
    B["prow"] = c.sb([1, 128], F32)
    B["orow"] = c.sb([1, 128], F32)
    B["q_rep"] = c.sb([128, T], F32, "q_rep")
    B["junk_bf"] = c.sb([128, T], BF16, "junk_bf")
    B["mask_bf"] = c.sb([128, T], BF16, "maskT")
    B["iotaT"] = c.sb([128, T], F32, "iotaT")
    B["d_u32"] = c.sb([128, 32], U32)
    B["iota_src"] = c.sb([128, 32], U32, persist=True)
    B["offs"] = c.sb([128, 1], U32, persist=True)
    B["idx"] = [c.sb([128, 1], U32, persist=True) for _ in range(8)]
    B["hin_bf"] = c.sb([128, 1024], BF16, "hin_tok")
    B["hinT"] = [c.sb([128, 1024], BF16) for _ in range(NT)]
    B["h1h_bf"] = [c.sb([128, 256], BF16) for _ in range(32)]
    B["hoT"] = [c.sb([128, 256], F32) for _ in range(NT)]
    B["w1h"] = c.sb([128, 1024], BF16, "w1hbuf")
    B["w2h"] = c.sb([128, 2048], BF16, "w2hbuf")
    B["srms_col"] = c.sb([128, 1], F32)
    B["sq_col"] = c.sb([128, 1], F32)
    return B


def emit_ffn1(p, c, B, w, getX, putX, dram):
    PS = B["PS"]
    tc = 512
    lnb = w["lnb"]
    for ci in range(T // tc):
        cs = slice(ci * tc, (ci + 1) * tc)
        p.stage()
        Xv = [getX(i, cs) for i in range(NT)]
        p.stage()
        emit_ln_stats_chunk(p, c, Xv, slice(0, tc), lnb, B["mean"], B["rstd"],
                            B["mr"])
        p.stage()
        p.t(lambda e: e.matmul(PS[0][:, :], c.ones_f[:, :], B["rstd"][0:1, :],
                               start=True, stop=True))
        p.t(lambda e: e.matmul(PS[1][:, :], c.ones_f[:, :], B["mr"][0:1, :],
                               start=True, stop=True))
        p.stage()
        for i in range(NT):
            p.v(lambda e, i=i: e.tensor_tensor(B["xn"][i][:, :], Xv[i][:, :],
                                               PS[0][:, :], ALU.mult))
        p.stage()
        for i in range(NT):
            p.v(lambda e, i=i: e.tensor_tensor(B["xn"][i][:, :],
                                               B["xn"][i][:, :], PS[1][:, :],
                                               ALU.subtract))
        p.stage()
        for i in range(NT):
            p.t(lambda e, i=i: e.matmul(PS[2][0:1, :], w["rt"][i][:, :],
                                        B["xn"][i][:, :], start=(i == 0),
                                        stop=(i == NT - 1)))
            p.s(lambda e, i=i: e.activation(lnb.sq[i % 2][:, :],
                                            B["xn"][i][:, :], AF.Square))
            p.stage()
            p.t(lambda e, i=i: e.matmul(PS[3][0:1, :], c.onescol_f[:, :],
                                        lnb.sq[i % 2][:, :], start=(i == 0),
                                        stop=(i == NT - 1)))
        p.stage()
        p.v(lambda e: e.tensor_copy(B["s_chunk"][0:1, :], PS[2][0:1, :]))
        p.s(lambda e: e.activation(B["sn"][0:1, :], PS[3][0:1, :], AF.Sqrt))
        p.stage()
        p.d(lambda e, ci=ci: e.dma_start(dram["qrow"][ci * tc:(ci + 1) * tc],
                                         B["s_chunk"][0:1, :]))
        small = constcol(c, 1e-12, 1)
        p.v(lambda e, small=small: e.tensor_scalar(
            B["sn"][0:1, :], B["sn"][0:1, :], small[:, :], None, ALU.max))
        p.stage()
        p.v(lambda e: e.reciprocal(B["sn"][0:1, :], B["sn"][0:1, :]))
        p.stage()
        p.v(lambda e: e.tensor_scalar(B["sn"][0:1, :], B["sn"][0:1, :],
                                      float(np.sqrt(1024.0)), None, ALU.mult))
        p.stage()
        p.t(lambda e: e.matmul(PS[4][:, :], c.ones_f[:, :], B["sn"][0:1, :],
                               start=True, stop=True))
        p.stage()
        for i in range(NT):
            p.v(lambda e, i=i: e.tensor_tensor(B["li_bf"][i][:, :],
                                               B["xn"][i][:, :], PS[4][:, :],
                                               ALU.mult))
        for tb in range(tc // 128):
            for i in range(NT):
                p.stage()
                p.t(lambda e, i=i, tb=tb: e.transpose(
                    B["PTf"][:, :], B["xn"][i][:, tb * 128:(tb + 1) * 128],
                    c.eye_f[:, :]))
                p.stage()
                p.v(lambda e, i=i: e.tensor_copy(
                    B["big"][:, i * 128:(i + 1) * 128], B["PTf"][:, :]))
            p.stage()
            t0 = ci * tc + tb * 128
            p.d(lambda e, t0=t0: e.dma_start(
                dram["xn_tok"][t0:t0 + 128, :], B["big"][:, :]))
        p.stage()
        for m in range(4):
            p.stage()
            for k in range(NT):
                p.t(lambda e, m=m, k=k: e.matmul(
                    PS[5][:, :], w["w1l"][k][:, m * 128:(m + 1) * 128],
                    B["li_bf"][k][:, :], start=(k == 0), stop=(k == NT - 1)))
            p.stage()
            p.s(lambda e, m=m: e.activation(B["h1l_bf"][m][:, :], PS[5][:, :],
                                            AF.Gelu, bias=w["b1l"][m][:, :]))
        for m in range(NT):
            p.stage()
            for k in range(4):
                p.t(lambda e, m=m, k=k: e.matmul(
                    PS[5][:, :], w["w2l"][k][:, m * 128:(m + 1) * 128],
                    B["h1l_bf"][k][:, :], start=(k == 0), stop=(k == 3)))
            p.stage()
            p.v(lambda e, m=m: e.tensor_scalar(
                B["ladd"][:, :], PS[5][:, :], 0.5, w["b2l_half"][m][:, :],
                ALU.mult, op1=ALU.add))
            p.stage()
            p.v(lambda e, m=m: e.tensor_tensor(Xv[m][:, :], Xv[m][:, :],
                                               B["ladd"][:, :], ALU.add))
        p.stage()
        for i in range(NT):
            putX(i, cs, Xv[i])
        p.stage()
    p.stage()
    p.d(lambda e: e.dma_start(
        B["s_pm"][:, :], bass.AP(dram["qrow"].tensor, 0, [[32, 128], [1, 32]])))
    p.stage()
    p.v(lambda e: e.tensor_copy(B["b_pm"][:, :], B["s_pm"][:, :]))
    p.stage()
    for it in range(50):
        p.stage()
        p.v(lambda e: e.tensor_tensor(B["u_pm"][:, :], B["s_pm"][:, :],
                                      B["b_pm"][:, :], ALU.subtract))
        p.stage()
        p.v(lambda e: e.tensor_reduce(B["colA"][:, :], B["u_pm"][:, :], AX.X,
                                      ALU.max))
        p.stage()
        p.t(lambda e: e.transpose(B["PTf"][0:1, :], B["colA"][:, :],
                                  c.eye_f[:, :]))
        p.stage()
        p.v(lambda e: e.tensor_reduce(B["t11"][0][0:1, :], B["PTf"][0:1, :],
                                      AX.X, ALU.max))
        p.stage()
        p.v(lambda e: e.tensor_scalar(B["t11"][1][0:1, :], B["t11"][0][0:1, :],
                                      -1.0, None, ALU.mult))
        p.stage()
        p.t(lambda e: e.matmul(PS[0][:, 0:1], c.ones_f[0:1, :],
                               B["t11"][1][0:1, :], start=True, stop=True))
        p.stage()
        p.v(lambda e: e.tensor_copy(B["colB"][:, :], PS[0][:, 0:1]))
        p.stage()
        p.s(lambda e: e.activation(B["e_pm"][:, :], B["u_pm"][:, :], AF.Exp,
                                   bias=B["colB"][:, :]))
        p.stage()
        p.v(lambda e: e.tensor_reduce(B["colA"][:, :], B["e_pm"][:, :], AX.X,
                                      ALU.add))
        p.stage()
        p.t(lambda e: e.transpose(B["PTf"][0:1, :], B["colA"][:, :],
                                  c.eye_f[:, :]))
        p.stage()
        p.v(lambda e: e.tensor_reduce(B["t11"][2][0:1, :], B["PTf"][0:1, :],
                                      AX.X, ALU.add))
        p.stage()
        p.s(lambda e: e.activation(B["t11"][2][0:1, :], B["t11"][2][0:1, :],
                                   AF.Ln))
        p.stage()
        p.v(lambda e: e.tensor_tensor(B["t11"][2][0:1, :], B["t11"][2][0:1, :],
                                      B["t11"][0][0:1, :], ALU.add))
        p.stage()
        p.v(lambda e: e.tensor_scalar(B["t11"][3][0:1, :], B["t11"][2][0:1, :],
                                      -1.0, LOGK, ALU.mult, op1=ALU.add))
        p.stage()
        p.t(lambda e: e.matmul(PS[0][:, 0:1], c.ones_f[0:1, :],
                               B["t11"][3][0:1, :], start=True, stop=True))
        p.stage()
        p.v(lambda e: e.tensor_copy(B["colA"][:, :], PS[0][:, 0:1]))
        p.stage()
        p.s(lambda e: e.activation(B["b_pm"][:, :], B["s_pm"][:, :], AF.Relu,
                                   bias=B["colA"][:, :]))
    p.stage()
    p.v(lambda e: e.tensor_scalar(B["q_pm"][:, :], B["s_pm"][:, :],
                                  B["colA"][:, :], 0.0, ALU.add, op1=ALU.min))
    p.stage()
    p.d(lambda e: e.dma_start(
        bass.AP(dram["qrow"].tensor, 0, [[32, 128], [1, 32]]), B["q_pm"][:, :]))
    p.stage()
    p.d(lambda e: e.dma_start(
        B["q_rep"][:, :], bass.AP(dram["qrow"].tensor, 0, [[0, 128], [1, T]])))
    p.g(lambda e: e.iota(B["iotaT"][:, :], [[1, T]], base=0,
                         channel_multiplier=0,
                         allow_small_or_imprecise_dtypes=True))
    p.g(lambda e: e.iota(B["i_all"][:, :], [[1, 32]], base=0,
                         channel_multiplier=32,
                         allow_small_or_imprecise_dtypes=True))
    p.g(lambda e: e.iota(B["iota_src"][:, :], [[128, 32]], base=0,
                         channel_multiplier=1))
    p.g(lambda e: e.memset(B["ones_pm"][:, :], 1.0))
    p.stage()
    for k in range(32):
        p.stage()
        p.v(lambda e, k=k: e.tensor_scalar(
            B["junk_bf"][:, :], B["q_rep"][:, :], B["q_pm"][:, k:k + 1], 0.0,
            ALU.is_gt, op1=ALU.add, accum_out=B["cnt_gt"][:, k:k + 1]))
        p.v(lambda e, k=k: e.tensor_scalar(
            B["mask_bf"][:, :], B["iotaT"][:, :], B["i_all"][:, k:k + 1], None,
            ALU.is_lt))
        p.stage()
        p.v(lambda e, k=k: e.scalar_tensor_tensor(
            B["junk_bf"][:, :], B["q_rep"][:, :], B["q_pm"][:, k:k + 1],
            B["mask_bf"][:, :], ALU.is_equal, ALU.mult,
            accum_out=B["cnt_eq"][:, k:k + 1]))
    p.stage()
    p.v(lambda e: e.tensor_tensor(B["sel_pm"][:, :], B["cnt_gt"][:, :],
                                  B["cnt_eq"][:, :], ALU.add))
    p.stage()
    p.v(lambda e: e.tensor_scalar(B["sel_pm"][:, :], B["sel_pm"][:, :],
                                  float(NUM_HEAVY), None, ALU.is_lt))
    p.stage()
    p.v(lambda e: e.tensor_tensor_scan(B["pfx_pm"][:, :], B["ones_pm"][:, :],
                                       B["sel_pm"][:, :], 0.0, ALU.mult,
                                       ALU.add))
    p.stage()
    p.t(lambda e: e.transpose(B["PTf"][0:1, :], B["pfx_pm"][:, 31:32],
                              c.eye_f[:, :]))
    p.stage()
    p.v(lambda e: e.tensor_copy(B["trow"][0:1, :], B["PTf"][0:1, :]))
    p.g(lambda e: e.memset(B["orow"][0:1, :], 1.0))
    p.stage()
    p.v(lambda e: e.tensor_tensor_scan(B["prow"][0:1, :], B["orow"][0:1, :],
                                       B["trow"][0:1, :], 0.0, ALU.mult,
                                       ALU.add))
    p.stage()
    p.v(lambda e: e.tensor_tensor(B["prow"][0:1, :], B["prow"][0:1, :],
                                  B["trow"][0:1, :], ALU.subtract))
    p.stage()
    p.t(lambda e: e.transpose(B["PTf"][:, 0:1], B["prow"][0:1, :],
                              c.eye_f[0:1, 0:1]))
    p.stage()
    p.v(lambda e: e.tensor_copy(B["colA"][:, :], B["PTf"][:, 0:1]))
    p.stage()
    p.v(lambda e: e.tensor_tensor(B["d_pm"][:, :], B["pfx_pm"][:, :],
                                  B["sel_pm"][:, :], ALU.subtract))
    p.stage()
    p.v(lambda e: e.tensor_scalar(B["d_pm"][:, :], B["d_pm"][:, :],
                                  B["colA"][:, :], None, ALU.add))
    p.stage()
    p.v(lambda e: e.tensor_tensor(B["d_pm"][:, :], B["d_pm"][:, :],
                                  B["sel_pm"][:, :], ALU.mult))
    p.stage()
    p.v(lambda e: e.tensor_scalar(B["t_pm"][:, :], B["sel_pm"][:, :],
                                  -1.0, 1.0, ALU.mult, op1=ALU.add))
    p.v(lambda e: e.tensor_scalar(B["u_pm"][:, :], B["i_all"][:, :],
                                  1.0, float(NUM_HEAVY), ALU.mult,
                                  op1=ALU.add))
    p.stage()
    p.v(lambda e: e.tensor_tensor(B["t_pm"][:, :], B["t_pm"][:, :],
                                  B["u_pm"][:, :], ALU.mult))
    p.stage()
    p.v(lambda e: e.tensor_tensor(B["d_pm"][:, :], B["d_pm"][:, :],
                                  B["t_pm"][:, :], ALU.add))
    p.stage()
    p.v(lambda e: e.tensor_copy(B["d_u32"][:, :], B["d_pm"][:, :]))
    p.stage()
    p.d(lambda e: e.dma_start(
        bass.AP(dram["drow"].tensor, 0, [[32, 128], [1, 32]]), B["d_u32"][:, :]))
    p.stage()
    p.g(lambda e: e.memset(B["big"][:, :], 0.0))
    p.stage()
    for k in range(32):
        p.stage()
        p.d(lambda e, k=k: e.dma_start(
            B["offs"][:, :],
            bass.AP(dram["drow"].tensor, k * 128, [[1, 128], [1, 1]])))
        p.stage()
        p.dg(lambda e, k=k: e.indirect_dma_start(
            dram["idx"],
            bass.IndirectOffsetOnAxis(ap=B["offs"][:, :], axis=0),
            B["iota_src"][:, k:k + 1], None))
    p.stage()
    for tb in range(T // 128):
        p.d(lambda e, tb=tb: e.dma_start(
            dram["hfull"][tb * 128:(tb + 1) * 128, :], B["big"][:, :]))
    p.stage()
    for r in range(8):
        p.d(lambda e, r=r: e.dma_start(
            B["idx"][r][:, :], dram["idx"][r * 128:(r + 1) * 128, :]))
    p.stage()
    for r in range(8):
        p.stage()
        p.dg(lambda e, r=r: e.indirect_dma_start(
            B["big"][:, :], None, dram["xn_tok"],
            bass.IndirectOffsetOnAxis(ap=B["idx"][r][:, :], axis=0)))
        p.stage()
        p.v(lambda e: e.tensor_tensor(B["junk_bf"].bitcast(F32)[:, 0:1024],
                                      B["big"][:, :], B["big"][:, :],
                                      ALU.mult))
        p.stage()
        p.v(lambda e: e.tensor_reduce(B["sq_col"][:, :],
                                      B["junk_bf"].bitcast(F32)[:, 0:1024],
                                      AX.X, ALU.add))
        p.stage()
        p.s(lambda e: e.activation(B["sq_col"][:, :], B["sq_col"][:, :],
                                   AF.Sqrt))
        p.stage()
        small = constcol(c, 1e-12, 128)
        p.v(lambda e, small=small: e.tensor_scalar(
            B["sq_col"][:, :], B["sq_col"][:, :], small[:, :], None, ALU.max))
        p.stage()
        p.v(lambda e: e.reciprocal(B["srms_col"][:, :], B["sq_col"][:, :]))
        p.stage()
        p.v(lambda e: e.tensor_scalar(B["srms_col"][:, :], B["srms_col"][:, :],
                                      float(np.sqrt(1024.0)), None, ALU.mult))
        p.stage()
        p.v(lambda e: e.tensor_scalar(B["hin_bf"][:, :], B["big"][:, :],
                                      B["srms_col"][:, :], None, ALU.mult))
        p.stage()
        for i in range(NT):
            p.t(lambda e, i=i: e.transpose(
                B["PTb"][:, 0:128], B["hin_bf"][:, i * 128:(i + 1) * 128],
                c.eye_bf[:, :]))
            p.stage()
            p.v(lambda e, i=i, r=r: e.tensor_copy(
                B["hinT"][i][:, r * 128:(r + 1) * 128], B["PTb"][:, 0:128]))
            p.stage()
    for rc in range(4):
        rs = slice(rc * 256, (rc + 1) * 256)
        for m in range(32):
            p.stage()
            p.dg(lambda e, m=m: e.dma_start(
                B["w1h"][:, :],
                bass.AP(w["w1hT"].tensor, m * 128,
                        [[4096, 128], [4096 * 128, 8], [1, 128]])))
            p.stage()
            for k in range(NT):
                p.v(lambda e, k=k: e.tensor_scalar(
                    B["w1h"][:, k * 128:(k + 1) * 128],
                    B["w1h"][:, k * 128:(k + 1) * 128],
                    w["gh"][k][:, :], None, ALU.mult))
            p.stage()
            for k in range(NT):
                p.t(lambda e, m=m, k=k, rs=rs: e.matmul(
                    PS[5][:, 0:256], B["w1h"][:, k * 128:(k + 1) * 128],
                    B["hinT"][k][:, rs], start=(k == 0), stop=(k == NT - 1)))
            p.stage()
            p.s(lambda e, m=m: e.activation(B["h1h_bf"][m][:, :], PS[5][:, 0:256],
                                            AF.Gelu, bias=w["b1h"][m][:, :]))
        for m in range(NT):
            for half in range(2):
                p.stage()
                p.dg(lambda e, m=m, half=half: e.dma_start(
                    B["w2h"][:, :],
                    bass.AP(w["w2hT"].tensor, m * 128 + half * 16 * 128 * 1024,
                            [[1024, 128], [1024 * 128, 16], [1, 128]])))
                p.stage()
                for kk in range(16):
                    k = half * 16 + kk
                    p.t(lambda e, m=m, k=k, kk=kk: e.matmul(
                        PS[5][:, 0:256], B["w2h"][:, kk * 128:(kk + 1) * 128],
                        B["h1h_bf"][k][:, :], start=(k == 0), stop=(k == 31)))
            p.stage()
            p.v(lambda e, m=m: e.tensor_scalar(
                B["hoT"][m][:, :], PS[5][:, 0:256], 0.5, w["b2h_half"][m][:, :],
                ALU.mult, op1=ALU.add))
        for rt_ in range(2):
            r = rc * 2 + rt_
            for i in range(NT):
                p.stage()
                p.t(lambda e, i=i, rt_=rt_: e.transpose(
                    B["PTf"][:, :], B["hoT"][i][:, rt_ * 128:(rt_ + 1) * 128],
                    c.eye_f[:, :]))
                p.stage()
                p.v(lambda e, i=i: e.tensor_copy(
                    B["big"][:, i * 128:(i + 1) * 128], B["PTf"][:, :]))
            p.stage()
            p.dg(lambda e, r=r: e.indirect_dma_start(
                dram["hfull"],
                bass.IndirectOffsetOnAxis(ap=B["idx"][r][:, :], axis=0),
                B["big"][:, :], None))
            p.stage()


def emit_hfull_add(p, c, B, dram, getXb, putXb):
    for g in range(T // 512):
        p.stage()
        Xv = [getXb(i, g) for i in range(NT)]
        for tb in range(4):
            t0 = g * 512 + tb * 128
            p.stage()
            p.d(lambda e, t0=t0: e.dma_start(B["big"][:, :],
                                             dram["hfull"][t0:t0 + 128, :]))
            p.stage()
            for i in range(NT):
                p.t(lambda e, i=i: e.transpose(
                    B["PTf"][:, :], B["big"][:, i * 128:(i + 1) * 128],
                    c.eye_f[:, :]))
                p.stage()
                p.v(lambda e, i=i, tb=tb: e.tensor_tensor(
                    Xv[i][:, tb * 128:(tb + 1) * 128],
                    Xv[i][:, tb * 128:(tb + 1) * 128],
                    B["PTf"][:, :], ALU.add))
                p.stage()
        p.stage()
        for i in range(NT):
            putXb(i, g, Xv[i])
        p.stage()


def emit_ffn1_weights_load(p, c, w, aps):
    (w1lT, b1l, w2lT, b2l, w1hT, b1h, w2hT, b2h, rt, gl, gh) = aps
    w["w1l"] = load_lhsT_bf16(p, c, w1lT, 1024, 512, "w1l")
    w["w2l"] = load_lhsT_bf16(p, c, w2lT, 512, 1024, "w2l")
    w["b1l"] = load_col_f32(p, c, b1l, 512, "b1l")
    b2l_t = load_col_f32(p, c, b2l, 1024, "b2l")
    w["b1h"] = load_col_f32(p, c, b1h, 4096, "b1h")
    b2h_t = load_col_f32(p, c, b2h, 1024, "b2h")
    w["rt"] = load_col_f32(p, c, rt, 1024, "rt")
    gl_t = load_col_f32(p, c, gl, 1024, "gl")
    w["gh"] = load_col_f32(p, c, gh, 1024, "gh")
    w["w1hT"] = w1hT
    w["w2hT"] = w2hT
    p.stage()
    for i in range(NT):
        p.v(lambda e, i=i: e.tensor_scalar(b2l_t[i][:, :], b2l_t[i][:, :],
                                           0.5, None, ALU.mult))
        p.v(lambda e, i=i: e.tensor_scalar(b2h_t[i][:, :], b2h_t[i][:, :],
                                           0.5, None, ALU.mult))
        p.v(lambda e, i=i: e.tensor_scalar(w["w1l"][i][:, :], w["w1l"][i][:, :],
                                           gl_t[i][:, :], None, ALU.mult))
    w["b2l_half"] = b2l_t
    w["b2h_half"] = b2h_t


def emit_ffn1_phase(p, c, xT, yT, aps, dram):
    B = ffn1_bufs(c)
    w = {"lnb": LNBufs(c, 512, psA=B["PS"][0][0:1, :],
                       psB=B["PS"][1][0:1, :])}
    emit_ffn1_weights_load(p, c, w, aps)
    xbufs = [c.sb([128, 512], F32) for _ in range(NT)]

    def getX(i, cs):
        p.on("sync", lambda e, i=i, cs=cs: e.dma_start(
            xbufs[i][:, :], xT[i * 128:(i + 1) * 128, cs]), dma=True)
        return xbufs[i]

    def putX(i, cs, buf):
        p.on("sync", lambda e, i=i, cs=cs, buf=buf: e.dma_start(
            yT[i * 128:(i + 1) * 128, cs], buf[:, :]), dma=True)

    emit_ffn1(p, c, B, w, getX, putX, dram)

    def getXb(i, g):
        cs = slice(g * 512, (g + 1) * 512)
        p.on("sync", lambda e, i=i, cs=cs: e.dma_start(
            xbufs[i][:, :], yT[i * 128:(i + 1) * 128, cs]), dma=True)
        return xbufs[i]

    def putXb(i, g, buf):
        cs = slice(g * 512, (g + 1) * 512)
        p.on("sync", lambda e, i=i, cs=cs, buf=buf: e.dma_start(
            yT[i * 128:(i + 1) * 128, cs], buf[:, :]), dma=True)

    emit_hfull_add(p, c, B, dram, getXb, putXb)


def ffn1_dram(nc, tag=""):
    return {
        "xn_tok": nc.dram_tensor(f"xn_tok{tag}", [T, 1024], F32,
                                 kind="Internal").ap(),
        "qrow": nc.dram_tensor(f"qrow{tag}", [T], F32, kind="Internal").ap(),
        "drow": nc.dram_tensor(f"drow{tag}", [T], U32, kind="Internal").ap(),
        "idx": nc.dram_tensor(f"idxl{tag}", [NUM_HEAVY + T, 1], U32,
                              kind="Internal").ap(),
        "hfull": nc.dram_tensor(f"hfull{tag}", [T, 1024], F32,
                                kind="Internal").ap(),
    }


def make_ffn1_nc():
    nc = bass.Bass("TRN2", target_bir_lowering=False)
    xT = nc.dram_tensor("xT", [D, T], F32, kind="ExternalInput").ap()
    aps = tuple(
        nc.dram_tensor(nm, sh, F32, kind="ExternalInput").ap()
        for nm, sh in (
            ("w1lT", [1024, 512]), ("b1l", [512, 1]), ("w2lT", [512, 1024]),
            ("b2l", [1024, 1]), ("w1hT", [1024, 4096]), ("b1h", [4096, 1]),
            ("w2hT", [4096, 1024]), ("b2h", [1024, 1]), ("rt", [1024, 1]),
            ("gl", [1024, 1]), ("gh", [1024, 1])))
    yT = nc.dram_tensor("yT", [D, T], F32, kind="ExternalOutput").ap()
    dram = ffn1_dram(nc)
    with ExitStack() as stack:
        c = Ctx(nc, stack)
        sem = c.sem("SS")
        semL = c.sem("SL")
        block = stack.enter_context(nc.Block())
        p = Prog(nc)
        setup_consts(p, c)
        emit_attn0_prep(p, c)
        emit_ffn1_phase(p, c, xT, yT, aps, dram)
        p.emit(block, sem, semL)
    return nc



DIN = 2048
NDT = DIN // 128
NST = 16
TCM = 128
SEG = TCM + 1


def mamba_bufs(c):
    ar = c.arena
    B = {}
    if getattr(c, "PSG", None) is not None:
        B["PS"] = c.PSG
    else:
        B["PS"] = [c.ps([128, 512], F32, f"MPS{i}") for i in range(7)]
    B["XZW"] = [c.sb([128, NDT * (TCM + 3)], F32) for _ in range(2)]
    B["ZW"] = [c.sb([128, DIN], F32) for _ in range(2)]
    B["UW"] = c.sb([128, DIN], F32)
    B["UBF"] = c.sb([128, DIN], BF16)
    B["DW"] = c.sb([128, DIN], F32)
    B["YW"] = [c.sb([128, DIN], F32) for _ in range(2)]
    o_brep = ar.off
    B["B_rep"] = c.sb([128, DIN], F32)
    o_crep = ar.off
    B["C_rep"] = c.sb([128, DIN], F32)
    o_scan = ar.off
    B["scanA"] = [c.sb([128, NST * SEG], F32) for _ in range(2)]
    B["scanB"] = [c.sb([128, NST * SEG], F32) for _ in range(2)]
    B["scanH"] = [c.sb([128, NST * SEG], F32) for _ in range(2)]
    o_hc = ar.off
    B["hC"] = [c.sb([128, DIN], F32) for _ in range(2)]
    B["TRED"] = c.sb([128, TCM], F32)
    B["CARR"] = c.sb([128, NDT * NST], F32)
    B["draw_bf"] = c.sb([64, TCM], BF16)
    B["B_t"] = c.sb([16, TCM], F32)
    B["C_t"] = c.sb([16, TCM], F32)
    B["XB"] = B["hC"][0]
    B["T1"] = B["hC"][1]
    B["xtile"] = [B["B_rep"][:, 0:512], B["B_rep"][:, 512:1024]]
    B["xnf"] = B["B_rep"][:, 1024:1536]
    B["stg"] = [B["C_rep"][:, j * 512:(j + 1) * 512] for j in range(4)]
    B["lnsq"] = [B["YW"][0][:, 0:512], B["YW"][0][:, 512:1024]]
    B["lnvar"] = B["YW"][0][0:1, 1024:1536]
    B["mean"] = B["YW"][1][0:1, 0:512]
    B["rstd"] = B["YW"][1][0:1, 512:1024]
    B["mr"] = B["YW"][1][0:1, 1024:1536]
    B["xa_t"] = ([B["scanA"][0][:, j * 512:(j + 1) * 512] for j in range(4)]
                 + [B["scanA"][1][:, j * 512:(j + 1) * 512] for j in range(4)])
    o_sb0 = o_scan + 2 * 2064
    o_sb1 = o_scan + 3 * 2064
    B["xnc"] = [[ar.view(o_sb0 + j * 256, [128, 512], BF16) for j in range(8)],
                [ar.view(o_sb1 + j * 256, [128, 512], BF16) for j in range(8)]]
    o_sh0 = o_scan + 4 * 2064
    o_sh1 = o_scan + 5 * 2064
    B["wblk"] = [[ar.view(o_sh0 + j * 512, [128, 1024], BF16) for j in range(4)]
                 + [ar.view(o_sh1 + j * 512, [128, 1024], BF16) for j in range(4)],
                 [ar.view(o_hc + j * 512, [128, 1024], BF16) for j in range(4)]
                 + [ar.view(o_hc + 2048 + j * 512, [128, 1024], BF16)
                    for j in range(4)]]
    B["y_bf"] = [ar.view(o_scan + j * 128, [128, 256], BF16) for j in range(16)]
    B["xres"] = [B["scanB"][0][:, j * 256:(j + 1) * 256] for j in range(8)]
    B["xastg"] = [B["scanB"][1][:, j * 256:(j + 1) * 256] for j in range(8)]
    B["outwT"] = ([ar.view(o_sh0 + j * 512, [128, 1024], BF16) for j in range(4)]
                  + [ar.view(o_sh1 + j * 512, [128, 1024], BF16) for j in range(4)]
                  + [ar.view(o_hc + j * 512, [128, 1024], BF16) for j in range(4)]
                  + [ar.view(o_hc + 2048 + j * 512, [128, 1024], BF16)
                     for j in range(4)])
    return B


def emit_mamba_weights_load(p, c, w, aps):
    (ng, nb, inwT, convw, convb, xprojT, dtwT, dtb, alog, dvec, outwT) = aps
    w["ng"] = load_col_f32(p, c, ng, D, "mng")
    w["nb"] = load_col_f32(p, c, nb, D, "mnb")
    w["inwT"] = inwT
    w["outwT_d"] = outwT
    w["convw"] = []
    for i in range(NDT):
        t = c.sb([128, 4], F32, f"cw{i}")
        w["convw"].append(t)
        p.d(lambda e, t=t, i=i: e.dma_start(t[:, :],
                                            convw[i * 128:(i + 1) * 128, :]))
    w["convb"] = load_col_f32(p, c, convb, DIN, "mcb")
    w["xproj"] = []
    for i in range(NDT):
        t = c.sb([128, 128], BF16, f"xpj{i}")
        w["xproj"].append(t)
        p.dg(lambda e, t=t, i=i: e.dma_start(t[:, :],
                                             xprojT[i * 128:(i + 1) * 128, :]))
    w["dtw"] = c.sb([64, 2048], BF16, "dtw")
    p.dg(lambda e: e.dma_start(w["dtw"][:, :], dtwT))
    dtb_cols = load_col_f32(p, c, dtb, DIN, "mdtb")
    w["A"] = []
    for i in range(NDT):
        t = c.sb([128, NST], F32, f"mA{i}")
        w["A"].append(t)
        p.d(lambda e, t=t, i=i: e.dma_start(t[:, :],
                                            alog[i * 128:(i + 1) * 128, :]))
    d_cols = load_col_f32(p, c, dvec, DIN, "mD")
    w["dtb_pm"] = c.sb([128, NDT], F32, "dtbpm")
    w["D_pm"] = c.sb([128, NDT], F32, "Dpm")
    p.stage()
    for i in range(NDT):
        p.s(lambda e, i=i: e.activation(w["A"][i][:, :], w["A"][i][:, :],
                                        AF.Exp))
        p.v(lambda e, i=i, t=dtb_cols[i]: e.tensor_copy(
            w["dtb_pm"][:, i:i + 1], t[:, :]))
        p.v(lambda e, i=i, t=d_cols[i]: e.tensor_copy(
            w["D_pm"][:, i:i + 1], t[:, :]))
    p.stage()
    for i in range(NDT):
        p.v(lambda e, i=i: e.tensor_scalar(w["A"][i][:, :], w["A"][i][:, :],
                                           -1.0, None, ALU.mult))


def emit_mamba_ph0(p, c, B, w, xT, dram):
    PS = B["PS"]
    epsc = constcol(c, LN_EPS, 1)
    sq = B["lnsq"]
    var = B["lnvar"]
    psA, psB = PS[0][0:1, :], PS[1][0:1, :]
    for ci in range(T // 512):
        cs = slice(ci * 512, (ci + 1) * 512)
        p.stage()
        for i in range(NT):
            buf = B["xtile"][i % 2]
            p.stage()
            p.d(lambda e, buf=buf, i=i, cs=cs: e.dma_start(
                buf[:, :], xT[i * 128:(i + 1) * 128, cs]))
            p.stage()
            p.s(lambda e, buf=buf, i=i: e.activation(
                sq[i % 2][:, :], buf[:, :], AF.Square))
            p.stage()
            p.t(lambda e, i=i, buf=buf: e.matmul(
                psA, c.invD_col[:, :], buf[:, :],
                start=(i == 0), stop=(i == NT - 1)))
            p.t(lambda e, i=i: e.matmul(
                psB, c.invD_col[:, :], sq[i % 2][:, :],
                start=(i == 0), stop=(i == NT - 1)))
        p.stage()
        p.v(lambda e: e.tensor_copy(B["mean"], psA))
        p.stage()
        p.v(lambda e: e.tensor_tensor(var, B["mean"], B["mean"], ALU.mult))
        p.stage()
        p.v(lambda e: e.tensor_tensor(var, psB, var, ALU.subtract))
        p.stage()
        p.s(lambda e, epsc=epsc: e.activation(B["rstd"], var, AF.Ln,
                                              bias=epsc[:, :]))
        p.stage()
        p.s(lambda e: e.activation(B["rstd"], B["rstd"], AF.Exp, scale=-0.5))
        p.stage()
        p.v(lambda e: e.tensor_tensor(B["mr"], B["mean"], B["rstd"],
                                      ALU.mult))
        p.stage()
        p.t(lambda e: e.matmul(PS[2][:, :], c.ones_f[:, :], B["rstd"],
                               start=True, stop=True))
        p.t(lambda e: e.matmul(PS[3][:, :], c.ones_f[:, :], B["mr"],
                               start=True, stop=True))
        p.stage()
        for i in range(NT):
            buf = B["xtile"][i % 2]
            p.stage()
            p.d(lambda e, buf=buf, i=i, cs=cs: e.dma_start(
                buf[:, :], xT[i * 128:(i + 1) * 128, cs]))
            p.stage()
            p.v(lambda e, buf=buf, i=i: e.tensor_tensor(
                B["xa_t"][i][:, :], buf[:, :], PS[2][:, :], ALU.mult))
            p.stage()
            p.v(lambda e, i=i: e.tensor_tensor(B["xa_t"][i][:, :],
                                               B["xa_t"][i][:, :],
                                               PS[3][:, :], ALU.subtract))
            p.stage()
            p.d(lambda e, i=i, cs=cs: e.dma_start(
                dram["xa"][i * 128:(i + 1) * 128, cs], B["xa_t"][i][:, :]))
        for i in range(NT):
            p.stage()
            p.s(lambda e, i=i: e.activation(sq[i % 2][:, :],
                                            B["xa_t"][i][:, :], AF.Square))
            p.stage()
            p.t(lambda e, i=i: e.matmul(
                PS[4][0:1, :], c.invD_col[:, :], B["xa_t"][i][:, :],
                start=(i == 0), stop=(i == NT - 1)))
            p.t(lambda e, i=i: e.matmul(
                PS[5][0:1, :], c.invD_col[:, :], sq[i % 2][:, :],
                start=(i == 0), stop=(i == NT - 1)))
        p.stage()
        p.v(lambda e: e.tensor_copy(B["mean"], PS[4][0:1, :]))
        p.stage()
        p.v(lambda e: e.tensor_tensor(var, B["mean"], B["mean"], ALU.mult))
        p.stage()
        p.v(lambda e: e.tensor_tensor(var, PS[5][0:1, :], var, ALU.subtract))
        p.stage()
        p.s(lambda e, epsc=epsc: e.activation(B["rstd"], var, AF.Ln,
                                              bias=epsc[:, :]))
        p.stage()
        p.s(lambda e: e.activation(B["rstd"], B["rstd"], AF.Exp, scale=-0.5))
        p.stage()
        p.v(lambda e: e.tensor_tensor(B["mr"], B["mean"], B["rstd"],
                                      ALU.mult))
        p.stage()
        p.t(lambda e: e.matmul(PS[2][:, :], c.ones_f[:, :], B["rstd"],
                               start=True, stop=True))
        p.t(lambda e: e.matmul(PS[3][:, :], c.ones_f[:, :], B["mr"],
                               start=True, stop=True))
        p.stage()
        xnc = B["xnc"][0]
        for i in range(NT):
            p.stage()
            p.v(lambda e, i=i: e.tensor_tensor(B["xnf"][:, :],
                                               B["xa_t"][i][:, :],
                                               PS[2][:, :], ALU.mult))
            p.stage()
            p.v(lambda e: e.tensor_tensor(B["xnf"][:, :], B["xnf"][:, :],
                                          PS[3][:, :], ALU.subtract))
            p.stage()
            p.v(lambda e, i=i, xnc=xnc: e.tensor_scalar(
                xnc[i][:, :], B["xnf"][:, :], w["ng"][i][:, :],
                w["nb"][i][:, :], ALU.mult, op1=ALU.add))
            p.stage()
            p.d(lambda e, i=i, cs=cs, xnc=xnc: e.dma_start(
                dram["xnb"][i * 128:(i + 1) * 128, cs], xnc[i][:, :]))
        p.stage()


def emit_mamba_phA(p, c, B, w, dram):
    PS = B["PS"]

    def load_wblk(mb, par):
        for k in range(NT):
            p.late("gpsimd", lambda e, mb=mb, k=k, par=par: e.dma_start(
                B["wblk"][par][k][:, :],
                w["inwT"][k * 128:(k + 1) * 128,
                          mb * 1024:(mb + 1) * 1024]))

    def load_xnc(tcx, par):
        cs = slice(tcx * 512, (tcx + 1) * 512)
        for i in range(NT):
            p.late("sync", lambda e, i=i, cs=cs, par=par: e.dma_start(
                B["xnc"][par][i][:, :],
                dram["xnb"][i * 128:(i + 1) * 128, cs]))

    p.stage()
    load_wblk(0, 0)
    load_xnc(0, 0)
    for mb in range(4):
        wpar = mb % 2
        for tcx in range(T // 512):
            xpar = tcx % 2
            cs = slice(tcx * 512, (tcx + 1) * 512)
            first = p.stage(lwait=True)
            if tcx == 0 and mb < 3:
                load_wblk(mb + 1, wpar ^ 1)
            if tcx < 7:
                load_xnc(tcx + 1, xpar ^ 1)
            elif mb < 3:
                load_xnc(0, xpar ^ 1)
            for m in range(10):
                if m < 8:
                    for k in range(NT):
                        p.t(lambda e, m=m, k=k, wpar=wpar, xpar=xpar: e.matmul(
                            PS[2 + m % 2][:, :],
                            B["wblk"][wpar][k][:, m * 128:(m + 1) * 128],
                            B["xnc"][xpar][k][:, :],
                            start=(k == 0), stop=(k == NT - 1)))
                if 1 <= m <= 8:
                    mm = m - 1
                    if mm % 2 == 0:
                        p.v(lambda e, mm=mm: e.tensor_copy(
                            B["stg"][mm % 4][:, :], PS[2 + mm % 2][:, :]))
                    else:
                        p.s(lambda e, mm=mm: e.copy(
                            B["stg"][mm % 4][:, :], PS[2 + mm % 2][:, :]))
                if 2 <= m:
                    ms = m - 2
                    row = mb * 8 + ms
                    p.late("scalar", lambda e, ms=ms, row=row, cs=cs:
                           e.dma_start(dram["xz"][row * 128:(row + 1) * 128,
                                                  cs],
                                       B["stg"][ms % 4][:, :]))
                p.stage()


def emit_mamba_phB(p, c, B, w, dram):
    PS = B["PS"]
    NCH = T // TCM

    def c1_loads(ci):
        q = ci % 2
        t0 = ci * TCM
        for i in range(NDT):
            if ci == 0:
                p.late("sync", lambda e, i=i, q=q, t0=t0: e.dma_start(
                    B["XZW"][q][:, i * 131 + 3:i * 131 + 131],
                    dram["xz"][i * 128:(i + 1) * 128, t0:t0 + TCM]))
            else:
                p.late("sync", lambda e, i=i, q=q, t0=t0: e.dma_start(
                    B["XZW"][q][:, i * 131:i * 131 + 131],
                    dram["xz"][i * 128:(i + 1) * 128, t0 - 3:t0 + TCM]))
            p.late("scalar", lambda e, i=i, q=q, t0=t0: e.dma_start(
                B["ZW"][q][:, i * TCM:(i + 1) * TCM],
                dram["xz"][(NDT + i) * 128:(NDT + i + 1) * 128,
                           t0:t0 + TCM]))

    p.stage()
    for q in range(2):
        p.g(lambda e, q=q: e.memset(
            apx(B["scanA"][q], 0, [[SEG, NST], [1, 1]]), 0.0))
    p.g(lambda e: e.memset(B["CARR"][:, :], 0.0))
    p.g(lambda e: e.memset(
        apx(B["XZW"][0], 0, [[131, NDT], [1, 3]]), 0.0))
    p.stage(lwait=True)
    c1_loads(0)

    for ci in range(NCH):
        q = ci % 2
        t0 = ci * TCM
        p.stage(lwait=True)
        for i in range(NDT):
            emit = p.v
            seg = slice(i * TCM, (i + 1) * TCM)

            def f0(e, i=i, q=q, seg=seg):
                return e.tensor_scalar(
                    B["DW"][:, seg], B["XZW"][q][:, i * 131:i * 131 + TCM],
                    w["convw"][i][:, 0:1], None, ALU.mult)
            emit(f0)
            for k in range(1, 4):
                def fk(e, i=i, k=k, q=q, seg=seg):
                    return e.scalar_tensor_tensor(
                        B["DW"][:, seg],
                        B["XZW"][q][:, i * 131 + k:i * 131 + k + TCM],
                        w["convw"][i][:, k:k + 1], B["DW"][:, seg],
                        ALU.mult, ALU.add)
                emit(fk)
        p.stage()
        for i in range(NDT):
            p.s(lambda e, i=i: e.activation(
                B["UW"][:, i * TCM:(i + 1) * TCM],
                B["DW"][:, i * TCM:(i + 1) * TCM],
                AF.Silu, bias=w["convb"][i][:, :]))
        p.stage()
        p.v(lambda e: e.tensor_copy(B["UBF"][:, :], B["UW"][:, :]))
        p.stage()
        for k in range(NDT):
            p.t(lambda e, k=k: e.matmul(
                PS[6][:, 0:TCM], w["xproj"][k][:, :],
                B["UBF"][:, k * 128:(k + 1) * 128],
                start=(k == 0), stop=(k == NDT - 1)))
        p.stage()
        p.s(lambda e: e.copy(B["draw_bf"][:, :], PS[6][0:64, 0:TCM]))
        p.v(lambda e: e.tensor_copy(B["B_t"][:, :], PS[6][64:80, 0:TCM]))
        p.v(lambda e: e.tensor_copy(B["C_t"][:, :], PS[6][96:112, 0:TCM]))
        p.stage()
        p.late("gpsimd", lambda e: e.dma_start(
            bass.AP(dram["bcf"].tensor, 0, [[TCM, 16], [1, TCM]]),
            B["B_t"][:, :]))
        p.late("gpsimd", lambda e: e.dma_start(
            bass.AP(dram["bcf"].tensor, NST * TCM, [[TCM, 16], [1, TCM]]),
            B["C_t"][:, :]))
        p.stage()
        for mi in range(NDT):
            p.t(lambda e, mi=mi: e.matmul(
                PS[2 + mi // 4][:, (mi % 4) * 128:(mi % 4 + 1) * 128],
                w["dtw"][:, mi * 128:(mi + 1) * 128],
                B["draw_bf"][:, :], start=True, stop=True))
        p.stage()
        for b in range(4):
            p.v(lambda e, b=b: e.tensor_tensor(
                B["XB"][:, b * 512:(b + 1) * 512], PS[2 + b][:, :],
                apx(w["dtb_pm"], 4 * b, [[1, 4], [0, 128]]), ALU.add))
        p.stage(lwait=True)
        p.late("sync", lambda e: e.dma_start(
            B["B_rep"][:, :],
            bass.AP(dram["bcf"].tensor, 0, [[0, 128], [1, NST * TCM]])))
        p.late("sync", lambda e: e.dma_start(
            B["C_rep"][:, :],
            bass.AP(dram["bcf"].tensor, NST * TCM,
                    [[0, 128], [1, NST * TCM]])))
        p.v(lambda e: e.tensor_scalar(B["DW"][:, :], B["XB"][:, :], 0.0,
                                      None, ALU.max))
        p.s(lambda e: e.activation(B["T1"][:, :], B["XB"][:, :], AF.Abs))
        p.s(lambda e: e.activation(B["T1"][:, :], B["T1"][:, :], AF.Exp,
                                   scale=-1.0))
        p.s(lambda e: e.activation(B["T1"][:, :], B["T1"][:, :], AF.Ln,
                                   bias=1.0))
        p.stage()
        p.v(lambda e: e.tensor_tensor(B["DW"][:, :], B["DW"][:, :],
                                      B["T1"][:, :], ALU.add))
        p.stage(lwait=True)
        p.v(lambda e, q=q: e.tensor_tensor(
            B["YW"][q][:, :], B["UW"][:, :],
            apx(w["D_pm"], 0, [[1, NST], [0, TCM]]), ALU.mult))
        p.v(lambda e: e.tensor_tensor(B["UW"][:, :], B["UW"][:, :],
                                      B["DW"][:, :], ALU.mult))
        for s in range(NDT + 3):
            if s == 0:
                p.stage(lwait=True)
            else:
                p.stage()
            if s == 2 and ci + 1 < NCH:
                c1_loads(ci + 1)
            i_x = s
            if i_x < NDT:
                qi = i_x % 2
                for n in range(NST):
                    p.s(lambda e, i=i_x, n=n, qi=qi: e.activation(
                        B["scanA"][qi][:, n * SEG + 1:n * SEG + 1 + TCM],
                        B["DW"][:, i * TCM:(i + 1) * TCM],
                        AF.Exp, scale=w["A"][i][:, n:n + 1]))
                p.v(lambda e, i=i_x, qi=qi: e.tensor_copy(
                    apx(B["scanB"][qi], 0, [[SEG, NST], [1, 1]]),
                    B["CARR"][:, i * NST:(i + 1) * NST]))
                p.g(lambda e, i=i_x, qi=qi: e.tensor_tensor(
                    apx(B["scanB"][qi], 1, [[SEG, NST], [1, TCM]]),
                    apx(B["UW"], i * TCM, [[0, NST], [1, TCM]]),
                    B["B_rep"][:, :], ALU.mult))
            i_s = s - 1
            if 0 <= i_s < NDT:
                qi = i_s % 2
                p.v(lambda e, qi=qi: e.tensor_tensor_scan(
                    B["scanH"][qi][:, :], B["scanA"][qi][:, :],
                    B["scanB"][qi][:, :], 0.0, ALU.mult, ALU.add))
            i_h = s - 2
            if 0 <= i_h < NDT:
                qi = i_h % 2
                p.s(lambda e, i=i_h, qi=qi: e.copy(
                    B["CARR"][:, i * NST:(i + 1) * NST],
                    apx(B["scanH"][qi], TCM, [[SEG, NST], [1, 1]])))
                p.v(lambda e, qi=qi: e.tensor_tensor(
                    B["hC"][qi][:, :],
                    apx(B["scanH"][qi], 1, [[SEG, NST], [1, TCM]]),
                    B["C_rep"][:, :], ALU.mult))
            i_y = s - 3
            if 0 <= i_y < NDT:
                qi = i_y % 2
                p.v(lambda e, qi=qi: e.tensor_reduce(
                    B["TRED"][:, :],
                    apx(B["hC"][qi], 0, [[1, TCM], [TCM, NST]]),
                    AX.X, ALU.add))
                p.v(lambda e, i=i_y, q=q: e.tensor_tensor(
                    B["YW"][q][:, i * TCM:(i + 1) * TCM],
                    B["YW"][q][:, i * TCM:(i + 1) * TCM],
                    B["TRED"][:, :], ALU.add))
        p.stage()
        p.s(lambda e, q=q: e.activation(B["T1"][:, :], B["ZW"][q][:, :],
                                        AF.Silu))
        p.stage()
        p.v(lambda e, q=q: e.tensor_tensor(B["YW"][q][:, :], B["YW"][q][:, :],
                                           B["T1"][:, :], ALU.mult))
        p.stage()
        for i in range(NDT):
            eng = ("sync", "scalar", "gpsimd")[i % 3]
            p.late(eng, lambda e, i=i, q=q, t0=t0: e.dma_start(
                dram["ysp"][i * 128:(i + 1) * 128, t0:t0 + TCM],
                B["YW"][q][:, i * TCM:(i + 1) * TCM]))


def emit_mamba_phC(p, c, B, w, xT, yT, dram):
    PS = B["PS"]
    p.stage()
    for k in range(NDT):
        p.late("gpsimd", lambda e, k=k: e.dma_start(
            B["outwT"][k][:, :],
            w["outwT_d"][k * 128:(k + 1) * 128, :]))
    for tcx in range(T // 256):
        cs = slice(tcx * 256, (tcx + 1) * 256)
        p.stage(lwait=True)
        for k in range(NDT):
            p.dg(lambda e, k=k, cs=cs: e.dma_start(
                B["y_bf"][k][:, :], dram["ysp"][k * 128:(k + 1) * 128, cs]))
        for i in range(NT):
            p.d(lambda e, i=i, cs=cs: e.dma_start(
                B["xres"][i][:, :], xT[i * 128:(i + 1) * 128, cs]))
            p.ds(lambda e, i=i, cs=cs: e.dma_start(
                B["xastg"][i][:, :], dram["xa"][i * 128:(i + 1) * 128, cs]))
        p.stage(lwait=True)
        for m in range(10):
            if m < 8:
                for k in range(NDT):
                    p.t(lambda e, m=m, k=k: e.matmul(
                        PS[2 + m % 2][:, 0:256],
                        B["outwT"][k][:, m * 128:(m + 1) * 128],
                        B["y_bf"][k][:, :], start=(k == 0),
                        stop=(k == NDT - 1)))
            if 1 <= m <= 8:
                mm = m - 1
                p.v(lambda e, mm=mm: e.tensor_tensor(
                    B["xres"][mm][:, :], B["xres"][mm][:, :],
                    PS[2 + mm % 2][:, 0:256], ALU.add))
                p.v(lambda e, mm=mm: e.tensor_tensor(
                    B["xres"][mm][:, :], B["xres"][mm][:, :],
                    B["xastg"][mm][:, :], ALU.add))
            if 2 <= m:
                ms = m - 2
                p.d(lambda e, ms=ms, cs=cs: e.dma_start(
                    yT[ms * 128:(ms + 1) * 128, cs], B["xres"][ms][:, :]))
            p.stage()


def emit_mamba(p, c, B, w, xT, yT, dram):
    emit_mamba_ph0(p, c, B, w, xT, dram)
    emit_mamba_phA(p, c, B, w, dram)
    emit_mamba_phB(p, c, B, w, dram)
    emit_mamba_phC(p, c, B, w, xT, yT, dram)


def emit_mamba_phase(p, c, xT, yT, aps, dram):
    B = mamba_bufs(c)
    w = {}
    emit_mamba_weights_load(p, c, w, aps)
    p.stage()
    emit_mamba(p, c, B, w, xT, yT, dram)


def mamba_dram(nc, dbg=False):
    kd = "ExternalOutput" if dbg else "Internal"
    return {
        "xz": nc.dram_tensor("xz_sp", [2 * DIN, T], F32, kind=kd).ap(),
        "ysp": nc.dram_tensor("y_sp", [DIN, T], F32, kind=kd).ap(),
        "xnb": nc.dram_tensor("xn_bf", [D, T], BF16, kind=kd).ap(),
        "xa": nc.dram_tensor("xa_sp", [D, T], F32, kind=kd).ap(),
        "bcf": nc.dram_tensor("bc_flat", [2 * NST * TCM], F32,
                              kind="Internal").ap(),
    }


def make_mamba_nc(dbg=False):
    nc = bass.Bass("TRN2", target_bir_lowering=False)
    xT = nc.dram_tensor("xT", [D, T], F32, kind="ExternalInput").ap()
    aps = tuple(
        nc.dram_tensor(nm, sh, F32, kind="ExternalInput").ap()
        for nm, sh in (
            ("ng", [D, 1]), ("nb", [D, 1]), ("inwT", [1024, 4096]),
            ("convw", [DIN, 4]), ("convb", [DIN, 1]), ("xprojT", [DIN, 128]),
            ("dtwT", [64, DIN]), ("dtb", [DIN, 1]), ("alog", [DIN, NST]),
            ("dvec", [DIN, 1]), ("outwT", [DIN, 1024])))
    yT = nc.dram_tensor("yT", [D, T], F32, kind="ExternalOutput").ap()
    dram = mamba_dram(nc, dbg)
    with ExitStack() as stack:
        c = Ctx(nc, stack)
        sem = c.sem("SS")
        semL = c.sem("SL")
        block = stack.enter_context(nc.Block())
        p = Prog(nc)
        setup_consts(p, c)
        emit_attn0_prep(p, c)
        c.PSG = [c.ps([128, 512], F32, f"G{i}") for i in range(7)]
        c.arena = Arena(c, ARENA_N)
        p.stage()
        emit_mamba_phase(p, c, xT, yT, aps, dram)
        p.emit(block, sem, semL)
    return nc



ARENA_N = 52224


def make_fused_nc():
    nc = bass.Bass("TRN2", target_bir_lowering=False)
    xT = nc.dram_tensor("xT", [D, T], F32, kind="ExternalInput").ap()

    def din(nm, sh):
        return nc.dram_tensor(nm, sh, F32, kind="ExternalInput").ap()

    ffn0_aps = {k: din(k, sh) for k, sh in (
        ("wrT", [D, D]), ("wkT", [D, D]), ("wvT", [D, D]),
        ("mur", [D, 1]), ("muk", [D, 1]))}
    attn_aps = tuple(din(nm, sh) for nm, sh in (
        ("wdqT", [64, 512]), ("wuqT", [512, 64]), ("wdkvT", [64, 1024]),
        ("wukvT", [1024, 128]), ("wqrT", [64, 64]), ("wkrT", [64, 64]),
        ("woT", [64, 64]), ("qnw", [64, 1]), ("knw", [64, 1])))
    ffn1_aps = tuple(din(nm, sh) for nm, sh in (
        ("w1lT", [1024, 512]), ("b1l", [512, 1]), ("w2lT", [512, 1024]),
        ("b2l", [1024, 1]), ("w1hT", [1024, 4096]), ("b1h", [4096, 1]),
        ("w2hT", [4096, 1024]), ("b2h", [1024, 1]), ("rt", [1024, 1]),
        ("gl", [1024, 1]), ("gh", [1024, 1])))
    mamba_aps = tuple(din(nm, sh) for nm, sh in (
        ("ng", [D, 1]), ("nb", [D, 1]), ("inwT", [1024, 4096]),
        ("convw", [DIN, 4]), ("convb", [DIN, 1]), ("xprojT", [DIN, 128]),
        ("dtwT", [64, DIN]), ("dtb", [DIN, 1]), ("alog", [DIN, NST]),
        ("dvec", [DIN, 1]), ("outwT", [DIN, 1024])))
    yT = nc.dram_tensor("yT", [D, T], F32, kind="ExternalOutput").ap()
    xs = [nc.dram_tensor(f"xc{i}", [D, T], F32, kind="Internal").ap()
          for i in range(5)]
    dramF = ffn1_dram(nc)
    dramM = mamba_dram(nc)
    with ExitStack() as stack:
        c = Ctx(nc, stack)
        sem = c.sem("SS")
        semL = c.sem("SL")
        block = stack.enter_context(nc.Block())
        p = Prog(nc)
        setup_consts(p, c)
        emit_attn0_prep(p, c)
        c.PSG = [c.ps([128, 512], F32, f"G{i}") for i in range(7)]
        c.PTB = c.ps([128, 256], BF16, "GT")
        ar = Arena(c, ARENA_N)
        c.arena = ar
        wf0 = (ffn0_aps["wrT"], ffn0_aps["wkT"], ffn0_aps["wvT"],
               ffn0_aps["mur"], ffn0_aps["muk"])
        ar.reset()
        p.stage()
        emit_ffn0_phase(p, c, xT, xs[0], ffn0_aps)
        ar.reset()
        p.stage()
        emit_attn0_phase(p, c, xs[0], xs[1], attn_aps)
        ar.reset()
        p.stage()
        emit_ffn0_phase(p, c, xs[1], xs[2], ffn0_aps)
        ar.reset()
        p.stage()
        emit_ffn1_phase(p, c, xs[2], xs[3], ffn1_aps, dramF)
        ar.reset()
        p.stage()
        emit_mamba_phase(p, c, xs[3], xs[4], mamba_aps, dramM)
        ar.reset()
        p.stage()
        emit_ffn1_phase(p, c, xs[4], yT, ffn1_aps, dramF)
        p.emit(block, sem, semL)
    return nc



def _ct(a):
    return np.ascontiguousarray(a)


def kernel(**inputs):
    ncores = 8
    x = inputs["L0_attn_x"]
    cur = [_ct(x[b].T) for b in range(ncores)]

    ffn0_w = {
        "wrT": _ct(inputs["L0_ffn_W_r"].T),
        "wkT": _ct(inputs["L0_ffn_W_k"].T),
        "wvT": _ct(inputs["L0_ffn_W_v"].T),
        "mur": _ct(inputs["L0_ffn_mu_r"].reshape(-1, 1)),
        "muk": _ct(inputs["L0_ffn_mu_k"].reshape(-1, 1)),
    }
    attn_w = {
        "wdqT": _ct(inputs["L0_attn_W_DQ"].T),
        "wuqT": _ct(inputs["L0_attn_W_UQ"].T),
        "wdkvT": _ct(inputs["L0_attn_W_DKV"].T),
        "wukvT": _ct(inputs["L0_attn_W_UKV"].T),
        "wqrT": _ct(inputs["L0_attn_W_QR"].T),
        "wkrT": _ct(inputs["L0_attn_W_KR"].T),
        "woT": _ct(inputs["L0_attn_W_O"].T),
        "qnw": _ct(inputs["L0_attn_q_norm_w"].reshape(-1, 1)),
        "knw": _ct(inputs["L0_attn_k_norm_w"].reshape(-1, 1)),
    }
    ffn1_w = {
        "w1lT": _ct(inputs["L1_ffn_w1_l"].T),
        "b1l": _ct(inputs["L1_ffn_b1_l"].reshape(-1, 1)),
        "w2lT": _ct(inputs["L1_ffn_w2_l"].T),
        "b2l": _ct(inputs["L1_ffn_b2_l"].reshape(-1, 1)),
        "w1hT": _ct(inputs["L1_ffn_w1_h"].T),
        "b1h": _ct(inputs["L1_ffn_b1_h"].reshape(-1, 1)),
        "w2hT": _ct(inputs["L1_ffn_w2_h"].T),
        "b2h": _ct(inputs["L1_ffn_b2_h"].reshape(-1, 1)),
        "rt": _ct(inputs["L1_ffn_routing_token"].reshape(-1, 1)),
        "gl": _ct(inputs["L1_ffn_gamma_l"].reshape(-1, 1)),
        "gh": _ct(inputs["L1_ffn_gamma_h"].reshape(-1, 1)),
    }
    xp = inputs["L1_attn_xproj_w"]
    xprojT_pad = np.concatenate(
        [_ct(xp[:64].T), _ct(xp[64:80].T), np.zeros((DIN, 16), np.float32),
         _ct(xp[80:96].T), np.zeros((DIN, 16), np.float32)], axis=1)
    mamba_w = {
        "ng": _ct(inputs["L1_attn_norm_g"].reshape(-1, 1)),
        "nb": _ct(inputs["L1_attn_norm_b"].reshape(-1, 1)),
        "inwT": _ct(inputs["L1_attn_in_w"].T),
        "convw": _ct(inputs["L1_attn_conv_w"][:, 0, :]),
        "convb": _ct(inputs["L1_attn_conv_b"].reshape(-1, 1)),
        "xprojT": _ct(xprojT_pad),
        "dtwT": _ct(inputs["L1_attn_dt_w"].T),
        "dtb": _ct(inputs["L1_attn_dt_b"].reshape(-1, 1)),
        "alog": _ct(inputs["L1_attn_A_log"]),
        "dvec": _ct(inputs["L1_attn_D"].reshape(-1, 1)),
        "outwT": _ct(inputs["L1_attn_out_w"].T),
    }

    wall = {**ffn0_w, **attn_w, **ffn1_w, **mamba_w}
    nc = make_fused_nc()
    in_maps = [{**wall, "xT": cur[b]} for b in range(ncores)]
    res = run_bass_kernel_spmd(nc, in_maps, core_ids=list(range(ncores)))
    cur = [np.asarray(res.results[b]["yT"]) for b in range(ncores)]
    out = np.stack([cur[b].T for b in range(ncores)], axis=0)
    return np.ascontiguousarray(out).astype(np.float32)


# version 09
# speedup vs baseline: 1.5991x; 1.1312x over previous
import math
import numpy as np
from contextlib import ExitStack

import concourse.bass as bass
import concourse.mybir as mybir
from concourse.bass_utils import run_bass_kernel_spmd

F32 = mybir.dt.float32
BF16 = mybir.dt.bfloat16
U32 = mybir.dt.uint32
AF = mybir.ActivationFunctionType
ALU = mybir.AluOpType
AX = mybir.AxisListType

T = 4096
D = 1024
NT = D // 128
LN_EPS = 1e-5

ENGINES = ("sync", "vector", "scalar", "tensor", "gpsimd")
FFN1_STOP = None


class Prog:

    def __init__(self, nc):
        self.nc = nc
        self.stages = []
        self.nlate = 0
        self.stage()

    def stage(self, dep=None, lwait=False):
        self.stages.append({"ops": {e: [] for e in ENGINES}, "dep": dep,
                            "lw": (self.nlate if lwait else None)})
        return len(self.stages) - 1

    def lwait_here(self):
        self.stages[-1]["lw"] = self.nlate

    def late(self, eng, fn):
        self.stages[-1]["ops"][eng].append((fn, "late"))
        self.nlate += 1

    def on(self, eng, fn, dma=False, stage_idx=None):
        st = self.stages[-1 if stage_idx is None else stage_idx]
        st["ops"][eng].append((fn, dma))

    def t(self, fn):
        self.on("tensor", fn)

    def v(self, fn):
        self.on("vector", fn)

    def s(self, fn):
        self.on("scalar", fn)

    def g(self, fn):
        self.on("gpsimd", fn)

    def d(self, fn):
        self.on("sync", fn, dma=True)

    def dg(self, fn):
        self.on("gpsimd", fn, dma=True)

    def ds(self, fn):
        self.on("scalar", fn, dma=True)

    def emit(self, block, sem, semL=None):
        Tt = []
        tot = 0
        for st in self.stages:
            n_dma = sum(1 for e in ENGINES for f, dd in st["ops"][e]
                        if dd is True)
            n_ce = sum(
                1 for e in ENGINES
                if any(dd is False for f, dd in st["ops"][e])
            )
            tot += 16 * n_dma + n_ce
            Tt.append(tot)
        assert self.nlate == 0 or semL is not None

        def dep_of(i):
            d = self.stages[i]["dep"]
            return (i - 1) if d is None else d

        def mk_body(engine_name):
            def body(e):
                hw = 0
                hwL = 0
                for i, st in enumerate(self.stages):
                    ops = st["ops"][engine_name]
                    if not ops:
                        continue
                    dep = dep_of(i)
                    if dep >= 0 and Tt[dep] > hw:
                        e.wait_ge(sem, Tt[dep])
                        hw = Tt[dep]
                    lw = st["lw"]
                    if lw is not None and 16 * lw > hwL:
                        e.wait_ge(semL, 16 * lw)
                        hwL = 16 * lw
                    last_comp = None
                    for j, (fn, dd) in enumerate(ops):
                        if dd is False:
                            last_comp = j
                    for j, (fn, dd) in enumerate(ops):
                        inst = fn(e)
                        if dd is True:
                            inst.then_inc(sem, 16)
                        elif dd == "late":
                            inst.then_inc(semL, 16)
                        elif j == last_comp:
                            inst.then_inc(sem, 1)
                if engine_name == "sync":
                    e.wait_ge(sem, Tt[-1])
                    if self.nlate:
                        e.wait_ge(semL, 16 * self.nlate)
            return body

        block.sync(mk_body("sync"))
        block.vector(mk_body("vector"))
        block.scalar(mk_body("scalar"))
        block.tensor(mk_body("tensor"))
        block.gpsimd(mk_body("gpsimd"))


def apx(buf, off, freepat):
    if isinstance(buf, bass.AP):
        return bass.AP(buf.tensor, buf.offset + off,
                       [list(buf.ap[0])] + [list(x) for x in freepat])
    base = buf[:, :]
    return bass.AP(base.tensor, off,
                   [list(base.ap[0])] + [list(x) for x in freepat])


class Arena:

    def __init__(self, c, nelems):
        self.t = c.sb([128, nelems], F32, "arena", persist=True)
        self.n = nelems
        self.off = 0
        self.hi = 0

    def reset(self):
        self.hi = max(self.hi, self.off)
        self.off = 0

    def alloc(self, shape, dtype):
        nbytes = shape[1] * mybir.dt.size(dtype)
        cells = (nbytes + 3) // 4
        cells = (cells + 7) & ~7
        off = self.off
        self.off += cells
        assert self.off <= self.n, f"arena overflow {self.off}>{self.n}"
        return self.view(off, shape, dtype)

    def view(self, off, shape, dtype):
        if dtype == F32:
            return self.t[0:shape[0], off:off + shape[1]]
        ratio = 4 // mybir.dt.size(dtype)
        return self.t.bitcast(dtype)[0:shape[0],
                                     off * ratio:off * ratio + shape[1]]


class Ctx:

    def __init__(self, nc, stack):
        self.nc = nc
        self.stack = stack
        self.uid = 0
        self.arena = None

    def sb(self, shape, dtype, name=None, persist=False):
        if self.arena is not None and not persist:
            return self.arena.alloc(shape, dtype)
        self.uid += 1
        nm = name or f"sb{self.uid}"
        return self.stack.enter_context(self.nc.sbuf_tensor(nm, shape, dtype))

    def ps(self, shape, dtype=F32, name=None):
        self.uid += 1
        nm = name or f"ps{self.uid}"
        return self.stack.enter_context(self.nc.psum_tensor(nm, shape, dtype))

    def sem(self, name=None):
        self.uid += 1
        return self.stack.enter_context(
            self.nc.semaphore(name or f"sem{self.uid}")
        )


def setup_consts(p, c):
    c.ones_f = c.sb([1, 128], F32, "ones_f")
    c.onescol_f = c.sb([128, 1], F32, "onescol_f")
    c.invD_col = c.sb([128, 1], F32, "invD_col")
    p.g(lambda e: e.memset(c.ones_f[:, :], 1.0))
    p.g(lambda e: e.memset(c.onescol_f[:, :], 1.0))
    p.g(lambda e: e.memset(c.invD_col[:, :], 1.0 / D))
    c._constcols = {}
    c._constprog = p


def constcol(c, val, parts=1):
    key = (float(val), parts)
    if key not in c._constcols:
        tile = c.sb([parts, 1], F32, persist=True)
        c._constprog.on("gpsimd",
                        lambda e, tile=tile, val=val: e.memset(tile[:, :],
                                                               float(val)),
                        stage_idx=0)
        c._constcols[key] = tile
    return c._constcols[key]


class LNBufs:
    def __init__(self, c, tc, psA=None, psB=None):
        self.tc = tc
        self.sq = [c.sb([128, tc], F32), c.sb([128, tc], F32)]
        self.psA = c.ps([1, tc], F32) if psA is None else psA
        self.psB = c.ps([1, tc], F32) if psB is None else psB
        self.var = c.sb([1, tc], F32)


def emit_ln_stats_chunk(p, c, X, cs, b, mean_row, rstd_row, mr_row=None,
                        eps=LN_EPS, width=None):
    w = width if width is not None else b.tc
    for i in range(NT):
        buf = b.sq[i % 2]

        def f(e, i=i, buf=buf, cs=cs, w=w):
            return e.activation(buf[:, 0:w], X[i][:, cs], AF.Square)
        p.s(f)
        p.stage()

        def fA(e, i=i, cs=cs, w=w):
            return e.matmul(b.psA[:, 0:w], c.invD_col[:, :], X[i][:, cs],
                            start=(i == 0), stop=(i == NT - 1))
        p.t(fA)

        def fB(e, i=i, buf=buf, w=w):
            return e.matmul(b.psB[:, 0:w], c.invD_col[:, :], buf[:, 0:w],
                            start=(i == 0), stop=(i == NT - 1))
        p.t(fB)
    p.stage()

    def fm(e, w=w):
        return e.tensor_copy(mean_row[0:1, 0:w], b.psA[:, 0:w])
    p.v(fm)
    p.stage()

    def fv(e, w=w):
        return e.tensor_tensor(b.var[0:1, 0:w], mean_row[0:1, 0:w],
                               mean_row[0:1, 0:w], ALU.mult)
    p.v(fv)
    p.stage()

    def fv2(e, w=w):
        return e.tensor_tensor(b.var[0:1, 0:w], b.psB[:, 0:w], b.var[0:1, 0:w],
                               ALU.subtract)
    p.v(fv2)
    p.stage()

    epscol = constcol(c, eps, 1)

    def fr1(e, w=w, epscol=epscol):
        return e.activation(rstd_row[0:1, 0:w], b.var[0:1, 0:w], AF.Ln,
                            bias=epscol[:, :])
    p.s(fr1)
    p.stage()

    def fr2(e, w=w):
        return e.activation(rstd_row[0:1, 0:w], rstd_row[0:1, 0:w], AF.Exp,
                            scale=-0.5)
    p.s(fr2)
    if mr_row is not None:
        p.stage()

        def fmr(e, w=w):
            return e.tensor_tensor(mr_row[0:1, 0:w], mean_row[0:1, 0:w],
                                   rstd_row[0:1, 0:w], ALU.mult)
        p.v(fmr)


def load_lhsT_bf16(p, c, dram_ap, K, M, name=None):
    nk = (K + 127) // 128
    tiles = []
    for k in range(nk):
        kk = min(128, K - k * 128)
        tile = c.sb([kk, M], BF16, name and f"{name}_{k}")
        tiles.append(tile)

        def f(e, tile=tile, k=k, kk=kk):
            return e.dma_start(tile[:, :], dram_ap[k * 128:k * 128 + kk, :])
        p.dg(f)
    return tiles


def load_col_f32(p, c, dram_ap, n, name=None):
    tiles = []
    for i in range((n + 127) // 128):
        kk = min(128, n - i * 128)
        tile = c.sb([kk, 1], F32, name and f"{name}_{i}")
        tiles.append(tile)

        def f(e, tile=tile, i=i, kk=kk):
            return e.dma_start(tile[:, :], dram_ap[i * 128:i * 128 + kk, :])
        p.d(f)
    return tiles



def emit_ffn0(p, c, getX, putX, wrT, wkT, wvT, mur, muk):
    tc = 256
    PSG = getattr(c, "PSG", None)
    if PSG is not None:
        ps_g, ps_mr = PSG[0][:, 0:tc], PSG[1][:, 0:tc]
        ps_mm = [PSG[2][:, 0:tc], PSG[3][:, 0:tc]]
        ps_v = [PSG[4][:, 0:tc], PSG[5][:, 0:tc]]
        lnb = LNBufs(c, tc, psA=PSG[2][0:1, 0:tc], psB=PSG[3][0:1, 0:tc])
    else:
        ps_g = c.ps([128, tc], F32)
        ps_mr = c.ps([128, tc], F32)
        ps_mm = [c.ps([128, tc], F32), c.ps([128, tc], F32)]
        ps_v = [c.ps([128, tc], F32), c.ps([128, tc], F32)]
        lnb = LNBufs(c, tc)
    mean_row = c.sb([1, tc], F32)
    rstd_row = c.sb([1, tc], F32)
    mr_row = c.sb([1, tc], F32)

    om_r = [c.sb([128, 1], F32) for _ in range(NT)]
    om_k = [c.sb([128, 1], F32) for _ in range(NT)]
    p.stage()
    for i in range(NT):
        def f1(e, i=i):
            return e.tensor_scalar(om_r[i][:, :], mur[i][:, :], -1.0, 1.0,
                                   ALU.mult, op1=ALU.add)
        p.v(f1)

        def f2(e, i=i):
            return e.tensor_scalar(om_k[i][:, :], muk[i][:, :], -1.0, 1.0,
                                   ALU.mult, op1=ALU.add)
        p.v(f2)

    xn = [c.sb([128, tc], F32) for _ in range(NT)]
    stash = c.sb([128, NT], F32)
    zcol = c.sb([128, 1], F32)
    tmp = [c.sb([128, tc], F32)]
    xr_bf = [c.sb([128, tc], BF16) for _ in range(NT)]
    xk_bf = [c.sb([128, tc], BF16) for _ in range(NT)]
    r_sig1 = c.sb([128, tc], F32)
    k2_bf = [c.sb([128, tc], BF16) for _ in range(NT)]
    krelu = c.sb([128, tc], F32)
    vtmp = c.sb([128, tc], F32)
    p.stage()
    p.g(lambda e: e.memset(zcol[:, :], 0.0))
    p.g(lambda e: e.memset(stash[:, :], 0.0))

    for ci in range(T // tc):
        cs = slice(ci * tc, (ci + 1) * tc)
        p.stage()
        Xv = [getX(i, cs) for i in range(NT)]
        p.stage()
        emit_ln_stats_chunk(p, c, Xv, slice(0, tc), lnb, mean_row, rstd_row,
                            mr_row)
        p.stage()

        def fb1(e):
            return e.matmul(ps_g[:, :], c.ones_f[:, :], rstd_row[0:1, :],
                            start=True, stop=True)
        p.t(fb1)

        def fb2(e):
            return e.matmul(ps_mr[:, :], c.ones_f[:, :], mr_row[0:1, :],
                            start=True, stop=True)
        p.t(fb2)
        p.stage()
        for i in range(NT):
            def fx1(e, i=i):
                return e.tensor_tensor(xn[i][:, :], Xv[i][:, :], ps_g[:, :],
                                       ALU.mult)
            p.v(fx1)

            def fx2(e, i=i):
                return e.tensor_tensor(xn[i][:, :], xn[i][:, :], ps_mr[:, :],
                                       ALU.subtract)
            p.v(fx2)
        p.stage()
        for i in range(NT):
            for (om, mu, dst) in ((om_r, mur, xr_bf), (om_k, muk, xk_bf)):
                def fs1(e, i=i, om=om):
                    return e.tensor_scalar(tmp[0][:, 1:tc], xn[i][:, 0:tc - 1],
                                           om[i][:, :], None, ALU.mult)
                p.v(fs1)

                def fs2(e, i=i, om=om, ci=ci):
                    src = stash[:, i:i + 1] if ci > 0 else zcol[:, :]
                    return e.tensor_scalar(tmp[0][:, 0:1], src, om[i][:, :],
                                           None, ALU.mult)
                p.v(fs2)

                def fs3(e, i=i, mu=mu, dst=dst):
                    return e.scalar_tensor_tensor(dst[i][:, :], xn[i][:, :],
                                                  mu[i][:, :], tmp[0][:, :],
                                                  ALU.mult, ALU.add)
                p.v(fs3)
        p.stage()
        for i in range(NT):
            def fst(e, i=i):
                return e.tensor_copy(stash[:, i:i + 1], xn[i][:, tc - 1:tc])
            p.v(fst)

        for m in range(NT):
            p.stage()
            for k in range(NT):
                def fm(e, m=m, k=k):
                    return e.matmul(ps_mm[m % 2][:, :],
                                    wkT[k][:, m * 128:(m + 1) * 128],
                                    xk_bf[k][:, :],
                                    start=(k == 0), stop=(k == NT - 1))
                p.t(fm)
            p.stage()

            def fr(e, m=m):
                return e.activation(krelu[:, :], ps_mm[m % 2][:, :], AF.Relu)
            p.s(fr)
            p.stage()

            def fq(e, m=m):
                return e.tensor_tensor(k2_bf[m][:, :], krelu[:, :],
                                       krelu[:, :], ALU.mult)
            p.v(fq)
        for m in range(NT):
            p.stage()
            for k in range(NT):
                def fm(e, m=m, k=k):
                    return e.matmul(ps_mm[m % 2][:, :],
                                    wrT[k][:, m * 128:(m + 1) * 128],
                                    xr_bf[k][:, :],
                                    start=(k == 0), stop=(k == NT - 1))
                p.t(fm)
            p.stage()
            for k in range(NT):
                def fm(e, m=m, k=k):
                    return e.matmul(ps_v[m % 2][:, :],
                                    wvT[k][:, m * 128:(m + 1) * 128],
                                    k2_bf[k][:, :],
                                    start=(k == 0), stop=(k == NT - 1))
                p.t(fm)

            def fe(e, m=m):
                return e.activation(r_sig1[:, :], ps_mm[m % 2][:, :],
                                    AF.Sigmoid)
            p.s(fe)
            p.stage()

            def fv1(e, m=m):
                return e.scalar_tensor_tensor(vtmp[:, :], r_sig1[:, :],
                                              0.5, ps_v[m % 2][:, :],
                                              ALU.mult, ALU.mult)
            p.v(fv1)
            p.stage()

            def fv2(e, m=m):
                return e.tensor_tensor(Xv[m][:, :], Xv[m][:, :], vtmp[:, :],
                                       ALU.add)
            p.v(fv2)
        p.stage()
        for i in range(NT):
            putX(i, cs, Xv[i])
        p.stage()


def emit_ffn0_phase(p, c, xT, yT, wL):
    wr = load_lhsT_bf16(p, c, wL["wrT"], D, D)
    wk = load_lhsT_bf16(p, c, wL["wkT"], D, D)
    wv = load_lhsT_bf16(p, c, wL["wvT"], D, D)
    mur_t = load_col_f32(p, c, wL["mur"], D)
    muk_t = load_col_f32(p, c, wL["muk"], D)
    xbufs = [c.sb([128, 256], F32) for _ in range(NT)]

    def getX(i, cs):
        p.on("sync", lambda e, i=i, cs=cs: e.dma_start(
            xbufs[i][:, :], xT[i * 128:(i + 1) * 128, cs]), dma=True)
        return xbufs[i]

    def putX(i, cs, buf):
        p.on("sync", lambda e, i=i, cs=cs, buf=buf: e.dma_start(
            yT[i * 128:(i + 1) * 128, cs], buf[:, :]), dma=True)

    emit_ffn0(p, c, getX, putX, wr, wk, wv, mur_t, muk_t)


def make_ffn0_nc():
    nc = bass.Bass("TRN2", target_bir_lowering=False)
    xT = nc.dram_tensor("xT", [D, T], F32, kind="ExternalInput").ap()
    wL = {
        "wrT": nc.dram_tensor("wrT", [D, D], F32, kind="ExternalInput").ap(),
        "wkT": nc.dram_tensor("wkT", [D, D], F32, kind="ExternalInput").ap(),
        "wvT": nc.dram_tensor("wvT", [D, D], F32, kind="ExternalInput").ap(),
        "mur": nc.dram_tensor("mur", [D, 1], F32, kind="ExternalInput").ap(),
        "muk": nc.dram_tensor("muk", [D, 1], F32, kind="ExternalInput").ap(),
    }
    yT = nc.dram_tensor("yT", [D, T], F32, kind="ExternalOutput").ap()
    with ExitStack() as stack:
        c = Ctx(nc, stack)
        sem = c.sem("SS")
        semL = c.sem("SL")
        block = stack.enter_context(nc.Block())
        p = Prog(nc)
        setup_consts(p, c)
        p.stage()
        emit_ffn0_phase(p, c, xT, yT, wL)
        p.emit(block, sem, semL)
    return nc



RMS_EPS = 1.1920929e-07


def emit_attn0_prep(p, c):
    c.eye_bf = c.sb([128, 128], BF16, "eye_bf")
    c.eye_f = c.sb([128, 128], F32, "eye_f")
    c.mask_bf = c.sb([128, 128], BF16, "mask_bf")
    it = c.sb([128, 128], F32)
    p.stage()
    p.g(lambda e: e.iota(it[:, :], [[1, 128]], base=0, channel_multiplier=-1,
                         allow_small_or_imprecise_dtypes=True))
    p.stage()
    p.v(lambda e: e.tensor_scalar(c.eye_bf[:, :], it[:, :], 0.0, None,
                                  ALU.is_equal))
    p.v(lambda e: e.tensor_scalar(c.eye_f[:, :], it[:, :], 0.0, None,
                                  ALU.is_equal))
    p.v(lambda e: e.tensor_scalar(c.mask_bf[:, :], it[:, :], 0.0, None,
                                  ALU.is_ge))


def attn_bufs(c):
    B = {}
    if getattr(c, "PSG", None) is not None:
        B["PS"] = c.PSG
        B["PT"] = c.PTB
    else:
        B["PS"] = [c.ps([128, 512], F32, f"APS{i}") for i in range(7)]
        B["PT"] = c.ps([128, 256], BF16, "APT")
    B["rstd_row"] = c.sb([1, T], F32, "rstd_row")
    B["mr_row"] = c.sb([1, T], F32, "mr_row")
    B["xnh_f"] = c.sb([64, 512], F32)
    B["xnh"] = c.sb([64, T], BF16, "xnh")
    B["cq"] = [c.sb([128, 512], BF16) for _ in range(4)]
    B["ckv"] = [c.sb([128, 512], BF16) for _ in range(8)]
    B["qc_f"] = c.sb([64, 512], F32)
    B["kc_f"] = c.sb([64, 512], F32)
    B["v_bf"] = c.sb([64, 512], BF16)
    B["sq"] = c.sb([64, 512], F32)
    B["sk"] = c.sb([64, 512], F32)
    B["msq"] = c.sb([1, 512], F32)
    B["msk"] = c.sb([1, 512], F32)
    B["Q"] = c.sb([128, T], BF16, "Qt")
    B["K"] = c.sb([128, T], BF16, "Kt")
    B["v_tok"] = c.sb([128, 65 * 32], BF16, "v_tok")
    B["qs_bf"] = c.sb([128, 512], BF16)
    B["ks_bf"] = c.sb([128, 512], BF16)
    B["ones_bfc"] = c.sb([128, 1], BF16)
    B["inv64"] = c.sb([64, 1], F32)
    B["t11a"] = c.sb([1, 1], F32)
    B["t11b"] = c.sb([1, 1], F32)
    B["qmax"] = c.sb([1, 1], F32)
    B["kmax"] = c.sb([1, 1], F32)
    B["mh_col"] = c.sb([128, 1], F32)
    B["p_bf"] = [c.sb([128, 512], BF16) for _ in range(2)]
    B["rc"] = c.sb([128, 4], F32)
    B["ctx_bf"] = c.sb([128, 256], BF16)
    B["ctxT_bf"] = c.sb([64, 512], BF16)
    return B


def emit_attn0_head(p, c, h, xh, w, B, Xrows_add):
    tcq = 512
    PS = B["PS"]
    PT = B["PT"]
    nchunk = T // tcq
    for ci in range(nchunk):
        cs = slice(ci * tcq, (ci + 1) * tcq)
        p.stage()
        p.t(lambda e, cs=cs: e.matmul(PS[0][0:64, :], c.ones_f[0:1, 0:64],
                                      B["rstd_row"][0:1, cs], start=True,
                                      stop=True))
        p.t(lambda e, cs=cs: e.matmul(PS[1][0:64, :], c.ones_f[0:1, 0:64],
                                      B["mr_row"][0:1, cs], start=True,
                                      stop=True))
        p.stage()
        p.v(lambda e, cs=cs: e.tensor_tensor(B["xnh_f"][:, :], xh[:, cs],
                                             PS[0][0:64, :], ALU.mult))
        p.stage()
        p.v(lambda e, cs=cs: e.tensor_tensor(B["xnh"][:, cs], B["xnh_f"][:, :],
                                             PS[1][0:64, :], ALU.subtract))
        p.stage()
        for m in range(4):
            p.t(lambda e, m=m, cs=cs: e.matmul(
                PS[2][:, :], w["dq"][0:64, m * 128:(m + 1) * 128],
                B["xnh"][:, cs], start=True, stop=True))
            p.stage()
            if m % 2 == 0:
                p.s(lambda e, m=m: e.copy(B["cq"][m][:, :], PS[2][:, :]))
            else:
                p.v(lambda e, m=m: e.tensor_copy(B["cq"][m][:, :], PS[2][:, :]))
            p.stage()
        for m in range(8):
            p.t(lambda e, m=m, cs=cs: e.matmul(
                PS[3][:, :], w["dkv"][0:64, m * 128:(m + 1) * 128],
                B["xnh"][:, cs], start=True, stop=True))
            p.stage()
            if m % 2 == 0:
                p.s(lambda e, m=m: e.copy(B["ckv"][m][:, :], PS[3][:, :]))
            else:
                p.v(lambda e, m=m: e.tensor_copy(B["ckv"][m][:, :],
                                                 PS[3][:, :]))
            p.stage()
        for k in range(4):
            p.t(lambda e, k=k: e.matmul(PS[4][0:64, :], w["uq"][k][:, 0:64],
                                        B["cq"][k][:, :],
                                        start=(k == 0), stop=(k == 3)))
        for k in range(8):
            p.t(lambda e, k=k: e.matmul(PS[5][:, :], w["ukv"][k][:, :],
                                        B["ckv"][k][:, :],
                                        start=(k == 0), stop=(k == 7)))
        p.stage()
        p.s(lambda e: e.copy(B["qc_f"][:, :], PS[4][0:64, :]))
        p.s(lambda e: e.copy(B["kc_f"][:, :], PS[5][0:64, :]))
        p.v(lambda e: e.tensor_copy(B["v_bf"][:, :], PS[5][64:128, :]))
        p.stage()
        p.v(lambda e: e.tensor_tensor(B["sq"][:, :], B["qc_f"][:, :],
                                      B["qc_f"][:, :], ALU.mult))
        p.v(lambda e: e.tensor_tensor(B["sk"][:, :], B["kc_f"][:, :],
                                      B["kc_f"][:, :], ALU.mult))
        for j in range(tcq // 128):
            jt = ci * 4 + j
            p.stage()
            p.t(lambda e, j=j: e.transpose(
                PT[:, 0:64], B["v_bf"][:, j * 128:(j + 1) * 128],
                c.eye_bf[0:64, 0:64]))
            p.stage()
            p.v(lambda e, jt=jt: e.tensor_copy(
                B["v_tok"][:, jt * 65:jt * 65 + 64], PT[:, 0:64]))
        p.stage()
        p.t(lambda e: e.matmul(PS[6][0:1, :], B["inv64"][:, :],
                               B["sq"][:, :], start=True, stop=True))
        p.t(lambda e: e.matmul(PS[2][0:1, :], B["inv64"][:, :],
                               B["sk"][:, :], start=True, stop=True))
        p.stage()
        epsc = constcol(c, RMS_EPS, 1)
        p.s(lambda e, epsc=epsc: e.activation(B["msq"][0:1, :], PS[6][0:1, :],
                                              AF.Ln, bias=epsc[:, :]))
        p.s(lambda e, epsc=epsc: e.activation(B["msk"][0:1, :], PS[2][0:1, :],
                                              AF.Ln, bias=epsc[:, :]))
        p.stage()
        p.s(lambda e: e.activation(B["msq"][0:1, :], B["msq"][0:1, :],
                                   AF.Exp, scale=-0.5))
        p.s(lambda e: e.activation(B["msk"][0:1, :], B["msk"][0:1, :],
                                   AF.Exp, scale=-0.5))
        p.stage()
        p.t(lambda e: e.matmul(PS[6][0:64, :], c.ones_f[0:1, 0:64],
                               B["msq"][0:1, :], start=True, stop=True))
        p.t(lambda e: e.matmul(PS[2][0:64, :], c.ones_f[0:1, 0:64],
                               B["msk"][0:1, :], start=True, stop=True))
        p.stage()
        p.v(lambda e, cs=cs: e.scalar_tensor_tensor(
            B["Q"][0:64, cs], B["qc_f"][:, :], w["qnw"][:, :],
            PS[6][0:64, :], ALU.mult, ALU.mult))
        p.v(lambda e, cs=cs: e.scalar_tensor_tensor(
            B["K"][0:64, cs], B["kc_f"][:, :], w["knw"][:, :],
            PS[2][0:64, :], ALU.mult, ALU.mult))
        p.stage()
        p.t(lambda e, cs=cs: e.matmul(PS[4][0:64, :], w["qr"][0:64, 0:64],
                                      B["Q"][0:64, cs], start=True, stop=True))
        p.t(lambda e, cs=cs: e.matmul(PS[5][0:64, :], w["kr"][0:64, 0:64],
                                      B["xnh"][:, cs], start=True, stop=True))
        p.stage()
        p.s(lambda e, cs=cs: e.copy(B["Q"][64:128, cs], PS[4][0:64, :]))
        p.s(lambda e, cs=cs: e.copy(B["K"][64:128, cs], PS[5][0:64, :]))
        p.stage()
        p.v(lambda e, cs=cs: e.tensor_tensor(B["qs_bf"][:, :], B["Q"][:, cs],
                                             B["Q"][:, cs], ALU.mult))
        p.v(lambda e, cs=cs: e.tensor_tensor(B["ks_bf"][:, :], B["K"][:, cs],
                                             B["K"][:, cs], ALU.mult))
        p.stage()
        p.t(lambda e: e.matmul(PS[4][0:1, :], B["ones_bfc"][:, :],
                               B["qs_bf"][:, :], start=True, stop=True))
        p.t(lambda e: e.matmul(PS[5][0:1, :], B["ones_bfc"][:, :],
                               B["ks_bf"][:, :], start=True, stop=True))
        p.stage()
        p.v(lambda e: e.tensor_reduce(B["t11a"][0:1, 0:1], PS[4][0:1, :],
                                      AX.X, ALU.max))
        p.v(lambda e: e.tensor_reduce(B["t11b"][0:1, 0:1], PS[5][0:1, :],
                                      AX.X, ALU.max))
        p.stage()
        p.v(lambda e, ci=ci: e.tensor_tensor(
            B["qmax"][0:1, 0:1],
            B["qmax"][0:1, 0:1] if ci else B["t11a"][0:1, 0:1],
            B["t11a"][0:1, 0:1], ALU.max))
        p.v(lambda e, ci=ci: e.tensor_tensor(
            B["kmax"][0:1, 0:1],
            B["kmax"][0:1, 0:1] if ci else B["t11b"][0:1, 0:1],
            B["t11b"][0:1, 0:1], ALU.max))
    p.stage()
    p.v(lambda e: e.tensor_tensor(B["t11a"][0:1, 0:1], B["qmax"][0:1, 0:1],
                                  B["kmax"][0:1, 0:1], ALU.mult))
    p.stage()
    p.s(lambda e: e.activation(B["t11a"][0:1, 0:1], B["t11a"][0:1, 0:1],
                               AF.Sqrt))
    p.stage()
    p.v(lambda e: e.tensor_scalar(B["t11a"][0:1, 0:1], B["t11a"][0:1, 0:1],
                                  0.125, 80.0, ALU.mult, op1=ALU.min))
    p.stage()
    p.v(lambda e: e.tensor_scalar(B["t11a"][0:1, 0:1], B["t11a"][0:1, 0:1],
                                  -1.0, None, ALU.mult))
    p.stage()
    p.t(lambda e: e.matmul(PS[0][:, 0:1], c.ones_f[0:1, :],
                           B["t11a"][0:1, 0:1], start=True, stop=True))
    p.stage()
    p.v(lambda e: e.tensor_copy(B["mh_col"][:, :], PS[0][:, 0:1]))
    p.stage()
    sbank = [PS[2], PS[6]]
    for qs_i in range(T // 512):
        span = slice(qs_i * 512, (qs_i + 1) * 512)
        nk = 4 * qs_i + 4

        def emit_S(e, j, qs_i=qs_i):
            return e.matmul(sbank[j % 2][:, :],
                            B["K"][:, j * 128:(j + 1) * 128],
                            B["Q"][:, qs_i * 512:(qs_i + 1) * 512],
                            start=True, stop=True)

        def emit_exp(e, j):
            return e.activation(B["p_bf"][j % 2][:, :], sbank[j % 2][:, :],
                                AF.Exp, scale=0.125, bias=B["mh_col"][:, :])

        def emit_PVs(j, qs_i=qs_i):
            smin = max(0, j - 4 * qs_i)
            for sub in range(smin, 4):
                def f(e, j=j, sub=sub, qs_i=qs_i):
                    return e.matmul(
                        PS[3][:, sub * 65:(sub + 1) * 65],
                        B["p_bf"][j % 2][:, sub * 128:(sub + 1) * 128],
                        B["v_tok"][:, j * 65:(j + 1) * 65],
                        start=(j == 0), stop=(j == 4 * qs_i + sub))
                p.t(f)

        p.stage()
        p.t(lambda e: emit_S(e, 0))
        p.stage()
        p.s(lambda e: emit_exp(e, 0))
        if nk > 1:
            p.t(lambda e: emit_S(e, 1))
        for j in range(nk):
            if j >= 4 * qs_i:
                sub = j - 4 * qs_i
                p.stage()
                p.v(lambda e, j=j, sub=sub: e.tensor_tensor(
                    B["p_bf"][j % 2][:, sub * 128:(sub + 1) * 128],
                    B["p_bf"][j % 2][:, sub * 128:(sub + 1) * 128],
                    c.mask_bf[:, :], ALU.mult))
            p.stage()
            emit_PVs(j)
            if j + 1 < nk:
                p.s(lambda e, j=j: emit_exp(e, j + 1))
            if j + 2 < nk:
                p.t(lambda e, j=j: emit_S(e, j + 2))
        p.stage()
        for sub in range(4):
            p.v(lambda e, sub=sub: e.reciprocal(
                B["rc"][:, sub:sub + 1], PS[3][:, sub * 65 + 64:sub * 65 + 65]))
        p.stage()
        for sub in range(4):
            p.v(lambda e, sub=sub: e.tensor_scalar(
                B["ctx_bf"][:, sub * 64:(sub + 1) * 64],
                PS[3][:, sub * 65:sub * 65 + 64],
                B["rc"][:, sub:sub + 1], None, ALU.mult))
        for half in range(2):
            p.stage()
            for q in range(2):
                sub = half * 2 + q
                p.t(lambda e, sub=sub, q=q: e.transpose(
                    PT[0:64, q * 128:(q + 1) * 128],
                    B["ctx_bf"][:, sub * 64:(sub + 1) * 64],
                    c.eye_bf[:, :]))
            p.stage()
            for q in range(2):
                sub = half * 2 + q
                p.v(lambda e, sub=sub, q=q: e.tensor_copy(
                    B["ctxT_bf"][:, sub * 128:(sub + 1) * 128],
                    PT[0:64, q * 128:(q + 1) * 128]))
        p.stage()
        for sub in range(4):
            p.t(lambda e, sub=sub: e.matmul(
                PS[4][0:64, sub * 128:(sub + 1) * 128], w["wo"][0:64, 0:64],
                B["ctxT_bf"][:, sub * 128:(sub + 1) * 128],
                start=True, stop=True))
        p.stage()
        p.v(lambda e, span=span: e.tensor_tensor(
            Xrows_add[:, span], Xrows_add[:, span], PS[4][0:64, 0:512],
            ALU.add))

def emit_attn_weights_load(p, c, w, aps):
    (wdqT, wuqT, wdkvT, wukvT, wqrT, wkrT, woT, qnw, knw) = aps
    w["dq"] = c.sb([64, 512], BF16, "wdq")
    w["uq"] = [c.sb([128, 64], BF16, f"wuq{k}") for k in range(4)]
    w["dkv"] = c.sb([64, 1024], BF16, "wdkv")
    w["ukv"] = [c.sb([128, 128], BF16, f"wukv{k}") for k in range(8)]
    w["qr"] = c.sb([64, 64], BF16, "wqr")
    w["kr"] = c.sb([64, 64], BF16, "wkr")
    w["wo"] = c.sb([64, 64], BF16, "wwo")
    w["qnw"] = c.sb([64, 1], F32, "wqnw")
    w["knw"] = c.sb([64, 1], F32, "wknw")
    p.dg(lambda e: e.dma_start(w["dq"][:, :], wdqT))
    for k in range(4):
        p.dg(lambda e, k=k: e.dma_start(w["uq"][k][:, :],
                                        wuqT[k * 128:(k + 1) * 128, :]))
    p.dg(lambda e: e.dma_start(w["dkv"][:, :], wdkvT))
    for k in range(8):
        p.dg(lambda e, k=k: e.dma_start(w["ukv"][k][:, :],
                                        wukvT[k * 128:(k + 1) * 128, :]))
    p.dg(lambda e: e.dma_start(w["qr"][:, :], wqrT))
    p.dg(lambda e: e.dma_start(w["kr"][:, :], wkrT))
    p.dg(lambda e: e.dma_start(w["wo"][:, :], woT))
    p.d(lambda e: e.dma_start(w["qnw"][:, :], qnw))
    p.d(lambda e: e.dma_start(w["knw"][:, :], knw))


def emit_ln_stats_stream(p, c, xT, B, lnb, xbufs):
    tc = lnb.tc
    mean_row = lnb.var
    for ci in range(T // tc):
        cs = slice(ci * tc, (ci + 1) * tc)
        for i in range(NT):
            buf = xbufs[i % 2]
            p.stage()
            p.d(lambda e, buf=buf, i=i, cs=cs: e.dma_start(
                buf[:, :], xT[i * 128:(i + 1) * 128, cs]))
            p.stage()
            p.s(lambda e, buf=buf, i=i: e.activation(
                lnb.sq[i % 2][:, :], buf[:, :], AF.Square))
            p.stage()
            p.t(lambda e, i=i, buf=buf: e.matmul(
                lnb.psA[:, :], c.invD_col[:, :], buf[:, :],
                start=(i == 0), stop=(i == NT - 1)))
            p.t(lambda e, i=i: e.matmul(
                lnb.psB[:, :], c.invD_col[:, :], lnb.sq[i % 2][:, :],
                start=(i == 0), stop=(i == NT - 1)))
        p.stage()
        p.v(lambda e, cs=cs: e.tensor_copy(B["mr_row"][0:1, cs],
                                           lnb.psA[:, :]))
        p.stage()
        p.v(lambda e, cs=cs: e.tensor_tensor(lnb.var[0:1, :],
                                             B["mr_row"][0:1, cs],
                                             B["mr_row"][0:1, cs], ALU.mult))
        p.stage()
        p.v(lambda e: e.tensor_tensor(lnb.var[0:1, :], lnb.psB[:, :],
                                      lnb.var[0:1, :], ALU.subtract))
        p.stage()
        epsc = constcol(c, LN_EPS, 1)
        p.s(lambda e, cs=cs, epsc=epsc: e.activation(
            B["rstd_row"][0:1, cs], lnb.var[0:1, :], AF.Ln, bias=epsc[:, :]))
        p.stage()
        p.s(lambda e, cs=cs: e.activation(B["rstd_row"][0:1, cs],
                                          B["rstd_row"][0:1, cs],
                                          AF.Exp, scale=-0.5))
        p.stage()
        p.v(lambda e, cs=cs: e.tensor_tensor(B["mr_row"][0:1, cs],
                                             B["mr_row"][0:1, cs],
                                             B["rstd_row"][0:1, cs],
                                             ALU.mult))


def emit_attn0_phase(p, c, xT, yT, aps):
    B = attn_bufs(c)
    lnb = LNBufs(c, 512, psA=B["PS"][0][0:1, :], psB=B["PS"][1][0:1, :])
    w = {}
    emit_attn_weights_load(p, c, w, aps)
    p.g(lambda e: e.memset(B["ones_bfc"][:, :], 1.0))
    p.g(lambda e: e.memset(B["inv64"][:, :], 1.0 / 64))
    p.g(lambda e: e.memset(B["v_tok"][:, :], 1.0))
    xbufs = [c.sb([128, 512], F32), c.sb([128, 512], F32)]
    emit_ln_stats_stream(p, c, xT, B, lnb, xbufs)
    xh = c.sb([64, T], F32)
    for h in range(16):
        i, r0 = h // 2, (h % 2) * 64
        p.stage()
        p.d(lambda e, i=i, r0=r0: e.dma_start(
            xh[:, :], xT[i * 128 + r0:i * 128 + r0 + 64, :]))
        p.stage()
        emit_attn0_head(p, c, h, xh, w, B, xh)
        p.stage()
        p.d(lambda e, i=i, r0=r0: e.dma_start(
            yT[i * 128 + r0:i * 128 + r0 + 64, :], xh[:, :]))
        p.stage()


def make_attn0_nc():
    nc = bass.Bass("TRN2", target_bir_lowering=False)
    xT = nc.dram_tensor("xT", [D, T], F32, kind="ExternalInput").ap()
    aps = tuple(
        nc.dram_tensor(nm, sh, F32, kind="ExternalInput").ap()
        for nm, sh in (
            ("wdqT", [64, 512]), ("wuqT", [512, 64]), ("wdkvT", [64, 1024]),
            ("wukvT", [1024, 128]), ("wqrT", [64, 64]), ("wkrT", [64, 64]),
            ("woT", [64, 64]), ("qnw", [64, 1]), ("knw", [64, 1])))
    yT = nc.dram_tensor("yT", [D, T], F32, kind="ExternalOutput").ap()
    with ExitStack() as stack:
        c = Ctx(nc, stack)
        sem = c.sem("SS")
        semL = c.sem("SL")
        block = stack.enter_context(nc.Block())
        p = Prog(nc)
        setup_consts(p, c)
        emit_attn0_prep(p, c)
        emit_attn0_phase(p, c, xT, yT, aps)
        p.emit(block, sem, semL)
    return nc



NUM_HEAVY = 1024
LOGK = float(np.log(np.float32(1152.0)))
BIGPOS = 1.0e9


def ffn1_bufs(c):
    B = {}
    if getattr(c, "PSG", None) is not None:
        B["PS"] = c.PSG[0:6]
        B["PT6"] = c.PSG[6]
        B["PTf"] = c.PSG[6][:, 0:128]
        B["PTb"] = c.PTB
    else:
        B["PS"] = [c.ps([128, 512], F32, f"FPS{i}") for i in range(6)]
        B["PT6"] = c.ps([128, 512], F32, "FPT6")
        B["PTf"] = B["PT6"][:, 0:128]
        B["PTb"] = c.ps([128, 512], BF16, "FPTb")
    o_xn = c.arena.off if c.arena is not None else None
    B["xn"] = [c.sb([128, 512], F32) for _ in range(NT)]
    B["mean"] = c.sb([1, 512], F32)
    B["rstd"] = c.sb([1, 512], F32)
    B["mr"] = c.sb([1, 512], F32)
    B["sn"] = c.sb([1, 512], F32)
    B["s_chunk"] = c.sb([1, 512], F32)
    B["li_bf"] = [c.sb([128, 512], BF16) for _ in range(NT)]
    B["h1l_bf"] = [c.sb([128, 512], BF16) for _ in range(4)]
    B["ladd"] = c.sb([128, 512], F32)
    B["big"] = c.sb([128, 1024], F32, "bigbuf")
    B["big2"] = c.sb([128, 1024], F32, "bigbuf2")
    for nm in ("s_pm", "b_pm", "u_pm", "e_pm", "q_pm", "i_all", "cnt_gt",
               "cnt_eq", "sel_pm", "pfx_pm", "ones_pm", "d_pm", "t_pm"):
        B[nm] = c.sb([128, 32], F32, nm)
    B["colA"] = c.sb([128, 1], F32)
    B["colB"] = c.sb([128, 1], F32)
    B["t11"] = [c.sb([1, 1], F32) for _ in range(4)]
    B["trow"] = c.sb([1, 128], F32)
    B["prow"] = c.sb([1, 128], F32)
    B["orow"] = c.sb([1, 128], F32)
    o_route = c.arena.off if c.arena is not None else None
    B["q_rep"] = c.sb([128, T], F32, "q_rep")
    B["junk_bf"] = c.sb([128, T], BF16, "junk_bf")
    B["mask_bf"] = c.sb([128, T], BF16, "maskT")
    B["iotaT"] = c.sb([128, T], F32, "iotaT")
    B["d_u32"] = c.sb([128, 32], U32)
    B["iota_src"] = c.sb([128, 32], U32, persist=True)
    B["offs"] = c.sb([128, 1], U32, persist=True)
    B["idx"] = [c.sb([128, 1], U32, persist=True) for _ in range(8)]
    B["hin_bf"] = c.sb([128, 1024], BF16, "hin_tok")
    B["hinT"] = [c.sb([128, 1024], BF16) for _ in range(NT)]
    ar = c.arena
    B["h1h_bf"] = ([ar.view(o_route + j * 512, [128, 1024], BF16)
                    for j in range(24)]
                   + [ar.view(o_xn + j * 512, [128, 1024], BF16)
                      for j in range(8)])
    B["hoT"] = [c.sb([128, 1024], F32) for _ in range(NT)]
    B["w1h"] = [c.sb([128, 1024], BF16) for _ in range(3)]
    B["w2h"] = [c.sb([128, 2048], BF16) for _ in range(2)]
    B["srms_col"] = c.sb([128, 1], F32)
    B["sq_col"] = c.sb([128, 1], F32)
    return B


def emit_ffn1(p, c, B, w, getX, putX, dram):
    PS = B["PS"]
    tc = 512
    lnb = w["lnb"]
    for ci in range(T // tc):
        cs = slice(ci * tc, (ci + 1) * tc)
        p.stage()
        Xv = [getX(i, cs) for i in range(NT)]
        p.stage()
        emit_ln_stats_chunk(p, c, Xv, slice(0, tc), lnb, B["mean"], B["rstd"],
                            B["mr"])
        p.stage()
        p.t(lambda e: e.matmul(PS[0][:, :], c.ones_f[:, :], B["rstd"][0:1, :],
                               start=True, stop=True))
        p.t(lambda e: e.matmul(PS[1][:, :], c.ones_f[:, :], B["mr"][0:1, :],
                               start=True, stop=True))
        p.stage()
        for i in range(NT):
            p.v(lambda e, i=i: e.tensor_tensor(B["xn"][i][:, :], Xv[i][:, :],
                                               PS[0][:, :], ALU.mult))
        p.stage()
        for i in range(NT):
            p.v(lambda e, i=i: e.tensor_tensor(B["xn"][i][:, :],
                                               B["xn"][i][:, :], PS[1][:, :],
                                               ALU.subtract))
        p.stage()
        for i in range(NT):
            p.t(lambda e, i=i: e.matmul(PS[2][0:1, :], w["rt"][i][:, :],
                                        B["xn"][i][:, :], start=(i == 0),
                                        stop=(i == NT - 1)))
            p.s(lambda e, i=i: e.activation(lnb.sq[i % 2][:, :],
                                            B["xn"][i][:, :], AF.Square))
            p.stage()
            p.t(lambda e, i=i: e.matmul(PS[3][0:1, :], c.onescol_f[:, :],
                                        lnb.sq[i % 2][:, :], start=(i == 0),
                                        stop=(i == NT - 1)))
        p.stage()
        p.v(lambda e: e.tensor_copy(B["s_chunk"][0:1, :], PS[2][0:1, :]))
        p.s(lambda e: e.activation(B["sn"][0:1, :], PS[3][0:1, :], AF.Sqrt))
        p.stage()
        p.d(lambda e, ci=ci: e.dma_start(dram["qrow"][ci * tc:(ci + 1) * tc],
                                         B["s_chunk"][0:1, :]))
        small = constcol(c, 1e-12, 1)
        p.v(lambda e, small=small: e.tensor_scalar(
            B["sn"][0:1, :], B["sn"][0:1, :], small[:, :], None, ALU.max))
        p.stage()
        p.v(lambda e: e.reciprocal(B["sn"][0:1, :], B["sn"][0:1, :]))
        p.stage()
        p.v(lambda e: e.tensor_scalar(B["sn"][0:1, :], B["sn"][0:1, :],
                                      float(np.sqrt(1024.0)), None, ALU.mult))
        p.stage()
        p.t(lambda e: e.matmul(PS[4][:, :], c.ones_f[:, :], B["sn"][0:1, :],
                               start=True, stop=True))
        p.stage()
        for i in range(NT):
            p.v(lambda e, i=i: e.tensor_tensor(B["li_bf"][i][:, :],
                                               B["xn"][i][:, :], PS[4][:, :],
                                               ALU.mult))
        for tb in range(tc // 128):
            bigb = B["big"] if tb % 2 == 0 else B["big2"]
            banks = (B["PT6"], PS[5])
            for j in range(5):
                p.stage()
                for h in range(2):
                    if j < 4:
                        i = 2 * j + h
                        bk = banks[j % 2]
                        p.t(lambda e, i=i, tb=tb, bk=bk, h=h: e.transpose(
                            bk[:, h * 128:(h + 1) * 128],
                            B["xn"][i][:, tb * 128:(tb + 1) * 128],
                            c.eye_f[:, :]))
                    if j >= 1:
                        i = 2 * (j - 1) + h
                        bk = banks[(j - 1) % 2]
                        p.v(lambda e, i=i, bigb=bigb, bk=bk, h=h:
                            e.tensor_copy(bigb[:, i * 128:(i + 1) * 128],
                                          bk[:, h * 128:(h + 1) * 128]))
            p.stage()
            t0 = ci * tc + tb * 128
            p.late("sync", lambda e, t0=t0, bigb=bigb: e.dma_start(
                dram["xn_tok"][t0:t0 + 128, :], bigb[:, :]))
        p.stage()
        for m in range(4):
            p.stage()
            for k in range(NT):
                p.t(lambda e, m=m, k=k: e.matmul(
                    PS[5][:, :], w["w1l"][k][:, m * 128:(m + 1) * 128],
                    B["li_bf"][k][:, :], start=(k == 0), stop=(k == NT - 1)))
            p.stage()
            p.s(lambda e, m=m: e.activation(B["h1l_bf"][m][:, :], PS[5][:, :],
                                            AF.Gelu, bias=w["b1l"][m][:, :]))
        for m in range(NT):
            p.stage()
            for k in range(4):
                p.t(lambda e, m=m, k=k: e.matmul(
                    PS[5][:, :], w["w2l"][k][:, m * 128:(m + 1) * 128],
                    B["h1l_bf"][k][:, :], start=(k == 0), stop=(k == 3)))
            p.stage()
            p.v(lambda e, m=m: e.tensor_scalar(
                B["ladd"][:, :], PS[5][:, :], 0.5, w["b2l_half"][m][:, :],
                ALU.mult, op1=ALU.add))
            p.stage()
            p.v(lambda e, m=m: e.tensor_tensor(Xv[m][:, :], Xv[m][:, :],
                                               B["ladd"][:, :], ALU.add))
        p.stage()
        for i in range(NT):
            putX(i, cs, Xv[i])
        p.stage()
    if FFN1_STOP == "light":
        return
    p.stage()
    p.d(lambda e: e.dma_start(
        B["s_pm"][:, :], bass.AP(dram["qrow"].tensor, 0, [[32, 128], [1, 32]])))
    p.stage()
    p.v(lambda e: e.tensor_copy(B["b_pm"][:, :], B["s_pm"][:, :]))
    p.stage()
    for it in range(50):
        p.stage()
        p.v(lambda e: e.tensor_tensor(B["u_pm"][:, :], B["s_pm"][:, :],
                                      B["b_pm"][:, :], ALU.subtract))
        p.stage()
        p.v(lambda e: e.tensor_reduce(B["colA"][:, :], B["u_pm"][:, :], AX.X,
                                      ALU.max))
        p.stage()
        p.t(lambda e: e.transpose(B["PTf"][0:1, :], B["colA"][:, :],
                                  c.eye_f[:, :]))
        p.stage()
        p.v(lambda e: e.tensor_reduce(B["t11"][0][0:1, :], B["PTf"][0:1, :],
                                      AX.X, ALU.max))
        p.stage()
        p.v(lambda e: e.tensor_scalar(B["t11"][1][0:1, :], B["t11"][0][0:1, :],
                                      -1.0, None, ALU.mult))
        p.stage()
        p.t(lambda e: e.matmul(PS[0][:, 0:1], c.ones_f[0:1, :],
                               B["t11"][1][0:1, :], start=True, stop=True))
        p.stage()
        p.v(lambda e: e.tensor_copy(B["colB"][:, :], PS[0][:, 0:1]))
        p.stage()
        p.s(lambda e: e.activation(B["e_pm"][:, :], B["u_pm"][:, :], AF.Exp,
                                   bias=B["colB"][:, :]))
        p.stage()
        p.v(lambda e: e.tensor_reduce(B["colA"][:, :], B["e_pm"][:, :], AX.X,
                                      ALU.add))
        p.stage()
        p.t(lambda e: e.transpose(B["PTf"][0:1, :], B["colA"][:, :],
                                  c.eye_f[:, :]))
        p.stage()
        p.v(lambda e: e.tensor_reduce(B["t11"][2][0:1, :], B["PTf"][0:1, :],
                                      AX.X, ALU.add))
        p.stage()
        p.s(lambda e: e.activation(B["t11"][2][0:1, :], B["t11"][2][0:1, :],
                                   AF.Ln))
        p.stage()
        p.v(lambda e: e.tensor_tensor(B["t11"][2][0:1, :], B["t11"][2][0:1, :],
                                      B["t11"][0][0:1, :], ALU.add))
        p.stage()
        p.v(lambda e: e.tensor_scalar(B["t11"][3][0:1, :], B["t11"][2][0:1, :],
                                      -1.0, LOGK, ALU.mult, op1=ALU.add))
        p.stage()
        p.t(lambda e: e.matmul(PS[0][:, 0:1], c.ones_f[0:1, :],
                               B["t11"][3][0:1, :], start=True, stop=True))
        p.stage()
        p.v(lambda e: e.tensor_copy(B["colA"][:, :], PS[0][:, 0:1]))
        p.stage()
        p.s(lambda e: e.activation(B["b_pm"][:, :], B["s_pm"][:, :], AF.Relu,
                                   bias=B["colA"][:, :]))
    p.stage()
    p.v(lambda e: e.tensor_scalar(B["q_pm"][:, :], B["s_pm"][:, :],
                                  B["colA"][:, :], 0.0, ALU.add, op1=ALU.min))
    p.stage()
    p.d(lambda e: e.dma_start(
        bass.AP(dram["qrow"].tensor, 0, [[32, 128], [1, 32]]), B["q_pm"][:, :]))
    p.stage()
    p.d(lambda e: e.dma_start(
        B["q_rep"][:, :], bass.AP(dram["qrow"].tensor, 0, [[0, 128], [1, T]])))
    p.g(lambda e: e.iota(B["iotaT"][:, :], [[1, T]], base=0,
                         channel_multiplier=0,
                         allow_small_or_imprecise_dtypes=True))
    p.g(lambda e: e.iota(B["i_all"][:, :], [[1, 32]], base=0,
                         channel_multiplier=32,
                         allow_small_or_imprecise_dtypes=True))
    p.g(lambda e: e.iota(B["iota_src"][:, :], [[128, 32]], base=0,
                         channel_multiplier=1))
    p.g(lambda e: e.memset(B["ones_pm"][:, :], 1.0))
    p.stage()
    for k in range(32):
        p.stage()
        p.v(lambda e, k=k: e.tensor_scalar(
            B["junk_bf"][:, :], B["q_rep"][:, :], B["q_pm"][:, k:k + 1], 0.0,
            ALU.is_gt, op1=ALU.add, accum_out=B["cnt_gt"][:, k:k + 1]))
        p.v(lambda e, k=k: e.tensor_scalar(
            B["mask_bf"][:, :], B["iotaT"][:, :], B["i_all"][:, k:k + 1], None,
            ALU.is_lt))
        p.stage()
        p.v(lambda e, k=k: e.scalar_tensor_tensor(
            B["junk_bf"][:, :], B["q_rep"][:, :], B["q_pm"][:, k:k + 1],
            B["mask_bf"][:, :], ALU.is_equal, ALU.mult,
            accum_out=B["cnt_eq"][:, k:k + 1]))
    p.stage()
    p.v(lambda e: e.tensor_tensor(B["sel_pm"][:, :], B["cnt_gt"][:, :],
                                  B["cnt_eq"][:, :], ALU.add))
    p.stage()
    p.v(lambda e: e.tensor_scalar(B["sel_pm"][:, :], B["sel_pm"][:, :],
                                  float(NUM_HEAVY), None, ALU.is_lt))
    p.stage()
    p.v(lambda e: e.tensor_tensor_scan(B["pfx_pm"][:, :], B["ones_pm"][:, :],
                                       B["sel_pm"][:, :], 0.0, ALU.mult,
                                       ALU.add))
    p.stage()
    p.t(lambda e: e.transpose(B["PTf"][0:1, :], B["pfx_pm"][:, 31:32],
                              c.eye_f[:, :]))
    p.stage()
    p.v(lambda e: e.tensor_copy(B["trow"][0:1, :], B["PTf"][0:1, :]))
    p.g(lambda e: e.memset(B["orow"][0:1, :], 1.0))
    p.stage()
    p.v(lambda e: e.tensor_tensor_scan(B["prow"][0:1, :], B["orow"][0:1, :],
                                       B["trow"][0:1, :], 0.0, ALU.mult,
                                       ALU.add))
    p.stage()
    p.v(lambda e: e.tensor_tensor(B["prow"][0:1, :], B["prow"][0:1, :],
                                  B["trow"][0:1, :], ALU.subtract))
    p.stage()
    p.t(lambda e: e.transpose(B["PTf"][:, 0:1], B["prow"][0:1, :],
                              c.eye_f[0:1, 0:1]))
    p.stage()
    p.v(lambda e: e.tensor_copy(B["colA"][:, :], B["PTf"][:, 0:1]))
    p.stage()
    p.v(lambda e: e.tensor_tensor(B["d_pm"][:, :], B["pfx_pm"][:, :],
                                  B["sel_pm"][:, :], ALU.subtract))
    p.stage()
    p.v(lambda e: e.tensor_scalar(B["d_pm"][:, :], B["d_pm"][:, :],
                                  B["colA"][:, :], None, ALU.add))
    p.stage()
    p.v(lambda e: e.tensor_tensor(B["d_pm"][:, :], B["d_pm"][:, :],
                                  B["sel_pm"][:, :], ALU.mult))
    p.stage()
    p.v(lambda e: e.tensor_scalar(B["t_pm"][:, :], B["sel_pm"][:, :],
                                  -1.0, 1.0, ALU.mult, op1=ALU.add))
    p.v(lambda e: e.tensor_scalar(B["u_pm"][:, :], B["i_all"][:, :],
                                  1.0, float(NUM_HEAVY), ALU.mult,
                                  op1=ALU.add))
    p.stage()
    p.v(lambda e: e.tensor_tensor(B["t_pm"][:, :], B["t_pm"][:, :],
                                  B["u_pm"][:, :], ALU.mult))
    p.stage()
    p.v(lambda e: e.tensor_tensor(B["d_pm"][:, :], B["d_pm"][:, :],
                                  B["t_pm"][:, :], ALU.add))
    p.stage()
    p.v(lambda e: e.tensor_copy(B["d_u32"][:, :], B["d_pm"][:, :]))
    p.stage()
    p.d(lambda e: e.dma_start(
        bass.AP(dram["drow"].tensor, 0, [[32, 128], [1, 32]]), B["d_u32"][:, :]))
    p.stage(lwait=True)
    p.g(lambda e: e.memset(B["big"][:, :], 0.0))
    p.stage()
    for k in range(32):
        p.stage()
        p.d(lambda e, k=k: e.dma_start(
            B["offs"][:, :],
            bass.AP(dram["drow"].tensor, k * 128, [[1, 128], [1, 1]])))
        p.stage()
        p.dg(lambda e, k=k: e.indirect_dma_start(
            dram["idx"],
            bass.IndirectOffsetOnAxis(ap=B["offs"][:, :], axis=0),
            B["iota_src"][:, k:k + 1], None))
    p.stage()
    for tb in range(T // 128):
        eng = ("sync", "scalar")[tb % 2]
        p.late(eng, lambda e, tb=tb: e.dma_start(
            dram["hfull"][tb * 128:(tb + 1) * 128, :], B["big"][:, :]))
    for r in range(8):
        p.d(lambda e, r=r: e.dma_start(
            B["idx"][r][:, :], dram["idx"][r * 128:(r + 1) * 128, :]))
    p.stage()
    if FFN1_STOP == "routing":
        return
    small = constcol(c, 1e-12, 128)
    for r in range(8):
        p.stage(lwait=True) if r == 0 else p.stage()
        p.dg(lambda e, r=r: e.indirect_dma_start(
            B["big"][:, :], None, dram["xn_tok"],
            bass.IndirectOffsetOnAxis(ap=B["idx"][r][:, :], axis=0)))
        p.stage()
        p.v(lambda e: e.tensor_tensor(B["big2"][:, :], B["big"][:, :],
                                      B["big"][:, :], ALU.mult))
        p.v(lambda e: e.tensor_reduce(B["sq_col"][:, :], B["big2"][:, :],
                                      AX.X, ALU.add))
        p.stage()
        p.s(lambda e: e.activation(B["sq_col"][:, :], B["sq_col"][:, :],
                                   AF.Sqrt))
        p.stage()
        p.v(lambda e, small=small: e.tensor_scalar(
            B["sq_col"][:, :], B["sq_col"][:, :], small[:, :], None, ALU.max))
        p.v(lambda e: e.reciprocal(B["srms_col"][:, :], B["sq_col"][:, :]))
        p.stage()
        p.v(lambda e: e.tensor_scalar(B["hin_bf"][:, :], B["big"][:, :],
                                      B["srms_col"][:, :], None, ALU.mult))
        for g4 in range(2):
            for jj in range(2):
                p.stage()
                for h in range(2):
                    i = g4 * 4 + jj * 2 + h
                    s = jj * 2 + h
                    p.t(lambda e, i=i, s=s: e.transpose(
                        B["PTb"][:, s * 128:(s + 1) * 128],
                        B["hin_bf"][:, i * 128:(i + 1) * 128],
                        c.eye_bf[:, :]))
            p.stage()
            for s in range(4):
                i = g4 * 4 + s
                p.v(lambda e, i=i, r=r, s=s: e.tensor_scalar(
                    B["hinT"][i][:, r * 128:(r + 1) * 128],
                    B["PTb"][:, s * 128:(s + 1) * 128],
                    w["gh32"][i][:, :], None, ALU.mult))
    if FFN1_STOP == "gather":
        p.stage()
        for i in range(NT):
            p.dg(lambda e, i=i: e.dma_start(
                w["dbg_yT"][i * 128:(i + 1) * 128, 0:1024], B["hinT"][i][:, :]))
        return
    def w1_load(m):
        p.late("gpsimd", lambda e, m=m: e.dma_start(
            B["w1h"][m % 3][:, :],
            bass.AP(w["w1hT"].tensor, m * 128,
                    [[4096, 128], [4096 * 128, 8], [1, 128]])))

    def w2_load(u):
        m2, half = u // 2, u % 2
        p.late("gpsimd", lambda e, m2=m2, half=half, u=u: e.dma_start(
            B["w2h"][u % 2][:, :],
            bass.AP(w["w2hT"].tensor, m2 * 128 + half * 16 * 128 * 1024,
                    [[1024, 128], [1024 * 128, 16], [1, 128]])))

    def gelu_emit(m):
        b0, b1 = (4, 5) if m % 2 == 0 else (2, 3)
        for h, b in ((0, b0), (1, b1)):
            p.s(lambda e, m=m, h=h, b=b: e.activation(
                B["h1h_bf"][m][:, h * 512:(h + 1) * 512], PS[b][:, :],
                AF.Gelu, bias=w["b1h"][m][:, :]))

    p.stage()
    w1_load(0)
    w1_load(1)
    for m in range(32):
        b0, b1 = (4, 5) if m % 2 == 0 else (2, 3)
        p.stage(lwait=True)
        if m + 2 < 32:
            w1_load(m + 2)
        for k in range(NT):
            p.t(lambda e, m=m, k=k, b0=b0: e.matmul(
                PS[b0][:, :], B["w1h"][m % 3][:, k * 128:(k + 1) * 128],
                B["hinT"][k][:, 0:512], start=(k == 0), stop=(k == NT - 1)))
            p.t(lambda e, m=m, k=k, b1=b1: e.matmul(
                PS[b1][:, :], B["w1h"][m % 3][:, k * 128:(k + 1) * 128],
                B["hinT"][k][:, 512:1024], start=(k == 0), stop=(k == NT - 1)))
        if m >= 1:
            gelu_emit(m - 1)
    p.stage()
    gelu_emit(31)
    if FFN1_STOP == "w1":
        p.stage()
        for k in range(NT):
            p.dg(lambda e, k=k: e.dma_start(
                w["dbg_yT"][k * 128:(k + 1) * 128, 0:1024],
                B["h1h_bf"][k][:, :]))
        return
    w2_load(0)

    def ho_evac(m2):
        b0, b1 = (4, 5) if m2 % 2 == 0 else (2, 3)
        for h, b in ((0, b0), (1, b1)):
            p.v(lambda e, m2=m2, h=h, b=b: e.tensor_scalar(
                B["hoT"][m2][:, h * 512:(h + 1) * 512], PS[b][:, :],
                0.5, w["b2h_half"][m2][:, :], ALU.mult, op1=ALU.add))

    for u in range(16):
        m2, half = u // 2, u % 2
        b0, b1 = (4, 5) if m2 % 2 == 0 else (2, 3)
        p.stage(lwait=True)
        if u + 1 < 16:
            w2_load(u + 1)
        for kk in range(16):
            k = half * 16 + kk
            p.t(lambda e, k=k, kk=kk, u=u, b0=b0: e.matmul(
                PS[b0][:, :], B["w2h"][u % 2][:, kk * 128:(kk + 1) * 128],
                B["h1h_bf"][k][:, 0:512], start=(k == 0), stop=(k == 31)))
            p.t(lambda e, k=k, kk=kk, u=u, b1=b1: e.matmul(
                PS[b1][:, :], B["w2h"][u % 2][:, kk * 128:(kk + 1) * 128],
                B["h1h_bf"][k][:, 512:1024], start=(k == 0), stop=(k == 31)))
        if half == 1 and m2 >= 1:
            ho_evac(m2 - 1)
    p.stage()
    ho_evac(7)
    for r in range(8):
        bigb = B["big"] if r % 2 == 0 else B["big2"]
        banks = (B["PT6"], PS[5])
        for j in range(5):
            p.stage()
            for h in range(2):
                if j < 4:
                    i = 2 * j + h
                    bk = banks[j % 2]
                    p.t(lambda e, i=i, r=r, bk=bk, h=h: e.transpose(
                        bk[:, h * 128:(h + 1) * 128],
                        B["hoT"][i][:, r * 128:(r + 1) * 128],
                        c.eye_f[:, :]))
                if j >= 1:
                    i = 2 * (j - 1) + h
                    bk = banks[(j - 1) % 2]
                    p.v(lambda e, i=i, bigb=bigb, bk=bk, h=h: e.tensor_copy(
                        bigb[:, i * 128:(i + 1) * 128],
                        bk[:, h * 128:(h + 1) * 128]))
        p.stage()
        p.dg(lambda e, r=r, bigb=bigb: e.indirect_dma_start(
            dram["hfull"],
            bass.IndirectOffsetOnAxis(ap=B["idx"][r][:, :], axis=0),
            bigb[:, :], None))


def emit_hfull_add(p, c, B, dram, getXb, putXb):
    p.stage(lwait=True)
    p.late("sync", lambda e: e.dma_start(B["big"][:, :],
                                         dram["hfull"][0:128, :]))
    for g in range(T // 512):
        p.stage()
        Xv = [getXb(i, g) for i in range(NT)]
        for tb in range(4):
            bigb = B["big"] if tb % 2 == 0 else B["big2"]
            nb = B["big2"] if tb % 2 == 0 else B["big"]
            t1 = g * 512 + (tb + 1) * 128
            banks = (B["PT6"], B["PS"][5])
            for j in range(5):
                p.stage(lwait=(j == 0))
                if j == 0 and t1 < T:
                    p.late("scalar", lambda e, t1=t1, nb=nb: e.dma_start(
                        nb[:, :], dram["hfull"][t1:t1 + 128, :]))
                for h in range(2):
                    if j < 4:
                        i = 2 * j + h
                        bk = banks[j % 2]
                        p.t(lambda e, i=i, bigb=bigb, bk=bk, h=h: e.transpose(
                            bk[:, h * 128:(h + 1) * 128],
                            bigb[:, i * 128:(i + 1) * 128],
                            c.eye_f[:, :]))
                    if j >= 1:
                        i = 2 * (j - 1) + h
                        bk = banks[(j - 1) % 2]
                        p.v(lambda e, i=i, tb=tb, bk=bk, h=h: e.tensor_tensor(
                            Xv[i][:, tb * 128:(tb + 1) * 128],
                            Xv[i][:, tb * 128:(tb + 1) * 128],
                            bk[:, h * 128:(h + 1) * 128], ALU.add))
        p.stage()
        for i in range(NT):
            putXb(i, g, Xv[i])
        p.stage()


def emit_ffn1_weights_load(p, c, w, aps):
    (w1lT, b1l, w2lT, b2l, w1hT, b1h, w2hT, b2h, rt, gl, gh) = aps
    w["w1l"] = load_lhsT_bf16(p, c, w1lT, 1024, 512, "w1l")
    w["w2l"] = load_lhsT_bf16(p, c, w2lT, 512, 1024, "w2l")
    w["b1l"] = load_col_f32(p, c, b1l, 512, "b1l")
    b2l_t = load_col_f32(p, c, b2l, 1024, "b2l")
    w["b1h"] = load_col_f32(p, c, b1h, 4096, "b1h")
    b2h_t = load_col_f32(p, c, b2h, 1024, "b2h")
    w["rt"] = load_col_f32(p, c, rt, 1024, "rt")
    gl_t = load_col_f32(p, c, gl, 1024, "gl")
    w["gh"] = load_col_f32(p, c, gh, 1024, "gh")
    w["gh32"] = w["gh"]
    w["w1hT"] = w1hT
    w["w2hT"] = w2hT
    p.stage()
    for i in range(NT):
        p.v(lambda e, i=i: e.tensor_scalar(
            w["gh32"][i][:, :], w["gh32"][i][:, :],
            float(np.sqrt(1024.0)), None, ALU.mult))
        p.v(lambda e, i=i: e.tensor_scalar(b2l_t[i][:, :], b2l_t[i][:, :],
                                           0.5, None, ALU.mult))
        p.v(lambda e, i=i: e.tensor_scalar(b2h_t[i][:, :], b2h_t[i][:, :],
                                           0.5, None, ALU.mult))
        p.v(lambda e, i=i: e.tensor_scalar(w["w1l"][i][:, :], w["w1l"][i][:, :],
                                           gl_t[i][:, :], None, ALU.mult))
    w["b2l_half"] = b2l_t
    w["b2h_half"] = b2h_t


def emit_ffn1_phase(p, c, xT, yT, aps, dram):
    B = ffn1_bufs(c)
    w = {"lnb": LNBufs(c, 512, psA=B["PS"][0][0:1, :],
                       psB=B["PS"][1][0:1, :])}
    emit_ffn1_weights_load(p, c, w, aps)
    w["dbg_yT"] = yT
    xbufs = [c.sb([128, 512], F32) for _ in range(NT)]

    def getX(i, cs):
        p.on("sync", lambda e, i=i, cs=cs: e.dma_start(
            xbufs[i][:, :], xT[i * 128:(i + 1) * 128, cs]), dma=True)
        return xbufs[i]

    def putX(i, cs, buf):
        p.on("sync", lambda e, i=i, cs=cs, buf=buf: e.dma_start(
            yT[i * 128:(i + 1) * 128, cs], buf[:, :]), dma=True)

    emit_ffn1(p, c, B, w, getX, putX, dram)
    if FFN1_STOP is not None:
        return

    def getXb(i, g):
        cs = slice(g * 512, (g + 1) * 512)
        p.on("sync", lambda e, i=i, cs=cs: e.dma_start(
            xbufs[i][:, :], yT[i * 128:(i + 1) * 128, cs]), dma=True)
        return xbufs[i]

    def putXb(i, g, buf):
        cs = slice(g * 512, (g + 1) * 512)
        p.on("sync", lambda e, i=i, cs=cs, buf=buf: e.dma_start(
            yT[i * 128:(i + 1) * 128, cs], buf[:, :]), dma=True)

    emit_hfull_add(p, c, B, dram, getXb, putXb)


def ffn1_dram(nc, tag=""):
    return {
        "xn_tok": nc.dram_tensor(f"xn_tok{tag}", [T, 1024], F32,
                                 kind="Internal").ap(),
        "qrow": nc.dram_tensor(f"qrow{tag}", [T], F32, kind="Internal").ap(),
        "drow": nc.dram_tensor(f"drow{tag}", [T], U32, kind="Internal").ap(),
        "idx": nc.dram_tensor(f"idxl{tag}", [NUM_HEAVY + T, 1], U32,
                              kind="Internal").ap(),
        "hfull": nc.dram_tensor(f"hfull{tag}", [T, 1024], F32,
                                kind="Internal").ap(),
    }


def make_ffn1_nc():
    nc = bass.Bass("TRN2", target_bir_lowering=False)
    xT = nc.dram_tensor("xT", [D, T], F32, kind="ExternalInput").ap()
    aps = tuple(
        nc.dram_tensor(nm, sh, F32, kind="ExternalInput").ap()
        for nm, sh in (
            ("w1lT", [1024, 512]), ("b1l", [512, 1]), ("w2lT", [512, 1024]),
            ("b2l", [1024, 1]), ("w1hT", [1024, 4096]), ("b1h", [4096, 1]),
            ("w2hT", [4096, 1024]), ("b2h", [1024, 1]), ("rt", [1024, 1]),
            ("gl", [1024, 1]), ("gh", [1024, 1])))
    yT = nc.dram_tensor("yT", [D, T], F32, kind="ExternalOutput").ap()
    dram = ffn1_dram(nc)
    with ExitStack() as stack:
        c = Ctx(nc, stack)
        sem = c.sem("SS")
        semL = c.sem("SL")
        block = stack.enter_context(nc.Block())
        p = Prog(nc)
        setup_consts(p, c)
        emit_attn0_prep(p, c)
        c.PSG = [c.ps([128, 512], F32, f"G{i}") for i in range(7)]
        c.PTB = c.ps([128, 512], BF16, "GT")
        c.arena = Arena(c, ARENA_N)
        p.stage()
        emit_ffn1_phase(p, c, xT, yT, aps, dram)
        p.emit(block, sem, semL)
    return nc



DIN = 2048
NDT = DIN // 128
NST = 16
TCM = 128
SEG = TCM + 1


def mamba_bufs(c):
    ar = c.arena
    B = {}
    if getattr(c, "PSG", None) is not None:
        B["PS"] = c.PSG
    else:
        B["PS"] = [c.ps([128, 512], F32, f"MPS{i}") for i in range(7)]
    B["XZW"] = [c.sb([128, NDT * (TCM + 3)], F32) for _ in range(2)]
    B["ZW"] = [c.sb([128, DIN], F32) for _ in range(2)]
    B["UW"] = c.sb([128, DIN], F32)
    B["UBF"] = c.sb([128, DIN], BF16)
    B["DW"] = c.sb([128, DIN], F32)
    B["YW"] = [c.sb([128, DIN], F32) for _ in range(2)]
    o_brep = ar.off
    B["B_rep"] = c.sb([128, DIN], F32)
    o_crep = ar.off
    B["C_rep"] = c.sb([128, DIN], F32)
    o_scan = ar.off
    B["scanA"] = [c.sb([128, NST * SEG], F32) for _ in range(2)]
    B["scanB"] = [c.sb([128, NST * SEG], F32) for _ in range(2)]
    B["scanH"] = [c.sb([128, NST * SEG], F32) for _ in range(2)]
    o_hc = ar.off
    B["hC"] = [c.sb([128, DIN], F32) for _ in range(2)]
    B["TRED"] = c.sb([128, TCM], F32)
    B["CARR"] = c.sb([128, NDT * NST], F32)
    B["draw_bf"] = c.sb([64, TCM], BF16)
    B["B_t"] = c.sb([16, TCM], F32)
    B["C_t"] = c.sb([16, TCM], F32)
    B["XB"] = B["hC"][0]
    B["T1"] = B["hC"][1]
    B["xtile"] = [B["B_rep"][:, 0:512], B["B_rep"][:, 512:1024]]
    B["xnf"] = B["B_rep"][:, 1024:1536]
    B["stg"] = [B["C_rep"][:, j * 512:(j + 1) * 512] for j in range(4)]
    B["lnsq"] = [B["YW"][0][:, 0:512], B["YW"][0][:, 512:1024]]
    B["lnvar"] = B["YW"][0][0:1, 1024:1536]
    B["mean"] = B["YW"][1][0:1, 0:512]
    B["rstd"] = B["YW"][1][0:1, 512:1024]
    B["mr"] = B["YW"][1][0:1, 1024:1536]
    B["xa_t"] = ([B["scanA"][0][:, j * 512:(j + 1) * 512] for j in range(4)]
                 + [B["scanA"][1][:, j * 512:(j + 1) * 512] for j in range(4)])
    o_sb0 = o_scan + 2 * 2064
    o_sb1 = o_scan + 3 * 2064
    B["xnc"] = [[ar.view(o_sb0 + j * 256, [128, 512], BF16) for j in range(8)],
                [ar.view(o_sb1 + j * 256, [128, 512], BF16) for j in range(8)]]
    o_sh0 = o_scan + 4 * 2064
    o_sh1 = o_scan + 5 * 2064
    B["wblk"] = [[ar.view(o_sh0 + j * 512, [128, 1024], BF16) for j in range(4)]
                 + [ar.view(o_sh1 + j * 512, [128, 1024], BF16) for j in range(4)],
                 [ar.view(o_hc + j * 512, [128, 1024], BF16) for j in range(4)]
                 + [ar.view(o_hc + 2048 + j * 512, [128, 1024], BF16)
                    for j in range(4)]]
    B["y_bf"] = [ar.view(o_scan + j * 128, [128, 256], BF16) for j in range(16)]
    B["xres"] = [B["scanB"][0][:, j * 256:(j + 1) * 256] for j in range(8)]
    B["xastg"] = [B["scanB"][1][:, j * 256:(j + 1) * 256] for j in range(8)]
    B["outwT"] = ([ar.view(o_sh0 + j * 512, [128, 1024], BF16) for j in range(4)]
                  + [ar.view(o_sh1 + j * 512, [128, 1024], BF16) for j in range(4)]
                  + [ar.view(o_hc + j * 512, [128, 1024], BF16) for j in range(4)]
                  + [ar.view(o_hc + 2048 + j * 512, [128, 1024], BF16)
                     for j in range(4)])
    return B


def emit_mamba_weights_load(p, c, w, aps):
    (ng, nb, inwT, convw, convb, xprojT, dtwT, dtb, alog, dvec, outwT) = aps
    w["ng"] = load_col_f32(p, c, ng, D, "mng")
    w["nb"] = load_col_f32(p, c, nb, D, "mnb")
    w["inwT"] = inwT
    w["outwT_d"] = outwT
    w["convw"] = []
    for i in range(NDT):
        t = c.sb([128, 4], F32, f"cw{i}")
        w["convw"].append(t)
        p.d(lambda e, t=t, i=i: e.dma_start(t[:, :],
                                            convw[i * 128:(i + 1) * 128, :]))
    w["convb"] = load_col_f32(p, c, convb, DIN, "mcb")
    w["xproj"] = []
    for i in range(NDT):
        t = c.sb([128, 128], BF16, f"xpj{i}")
        w["xproj"].append(t)
        p.dg(lambda e, t=t, i=i: e.dma_start(t[:, :],
                                             xprojT[i * 128:(i + 1) * 128, :]))
    w["dtw"] = c.sb([64, 2048], BF16, "dtw")
    p.dg(lambda e: e.dma_start(w["dtw"][:, :], dtwT))
    dtb_cols = load_col_f32(p, c, dtb, DIN, "mdtb")
    w["A"] = []
    for i in range(NDT):
        t = c.sb([128, NST], F32, f"mA{i}")
        w["A"].append(t)
        p.d(lambda e, t=t, i=i: e.dma_start(t[:, :],
                                            alog[i * 128:(i + 1) * 128, :]))
    d_cols = load_col_f32(p, c, dvec, DIN, "mD")
    w["dtb_pm"] = c.sb([128, NDT], F32, "dtbpm")
    w["D_pm"] = c.sb([128, NDT], F32, "Dpm")
    p.stage()
    for i in range(NDT):
        p.s(lambda e, i=i: e.activation(w["A"][i][:, :], w["A"][i][:, :],
                                        AF.Exp))
        p.v(lambda e, i=i, t=dtb_cols[i]: e.tensor_copy(
            w["dtb_pm"][:, i:i + 1], t[:, :]))
        p.v(lambda e, i=i, t=d_cols[i]: e.tensor_copy(
            w["D_pm"][:, i:i + 1], t[:, :]))
    p.stage()
    for i in range(NDT):
        p.v(lambda e, i=i: e.tensor_scalar(w["A"][i][:, :], w["A"][i][:, :],
                                           -1.0, None, ALU.mult))


def emit_mamba_ph0(p, c, B, w, xT, dram):
    PS = B["PS"]
    epsc = constcol(c, LN_EPS, 1)
    sq = B["lnsq"]
    var = B["lnvar"]
    psA, psB = PS[0][0:1, :], PS[1][0:1, :]
    for ci in range(T // 512):
        cs = slice(ci * 512, (ci + 1) * 512)
        p.stage()
        for i in range(NT):
            buf = B["xtile"][i % 2]
            p.stage()
            p.d(lambda e, buf=buf, i=i, cs=cs: e.dma_start(
                buf[:, :], xT[i * 128:(i + 1) * 128, cs]))
            p.stage()
            p.s(lambda e, buf=buf, i=i: e.activation(
                sq[i % 2][:, :], buf[:, :], AF.Square))
            p.stage()
            p.t(lambda e, i=i, buf=buf: e.matmul(
                psA, c.invD_col[:, :], buf[:, :],
                start=(i == 0), stop=(i == NT - 1)))
            p.t(lambda e, i=i: e.matmul(
                psB, c.invD_col[:, :], sq[i % 2][:, :],
                start=(i == 0), stop=(i == NT - 1)))
        p.stage()
        p.v(lambda e: e.tensor_copy(B["mean"], psA))
        p.stage()
        p.v(lambda e: e.tensor_tensor(var, B["mean"], B["mean"], ALU.mult))
        p.stage()
        p.v(lambda e: e.tensor_tensor(var, psB, var, ALU.subtract))
        p.stage()
        p.s(lambda e, epsc=epsc: e.activation(B["rstd"], var, AF.Ln,
                                              bias=epsc[:, :]))
        p.stage()
        p.s(lambda e: e.activation(B["rstd"], B["rstd"], AF.Exp, scale=-0.5))
        p.stage()
        p.v(lambda e: e.tensor_tensor(B["mr"], B["mean"], B["rstd"],
                                      ALU.mult))
        p.stage()
        p.t(lambda e: e.matmul(PS[2][:, :], c.ones_f[:, :], B["rstd"],
                               start=True, stop=True))
        p.t(lambda e: e.matmul(PS[3][:, :], c.ones_f[:, :], B["mr"],
                               start=True, stop=True))
        p.stage()
        for i in range(NT):
            buf = B["xtile"][i % 2]
            p.stage()
            p.d(lambda e, buf=buf, i=i, cs=cs: e.dma_start(
                buf[:, :], xT[i * 128:(i + 1) * 128, cs]))
            p.stage()
            p.v(lambda e, buf=buf, i=i: e.tensor_tensor(
                B["xa_t"][i][:, :], buf[:, :], PS[2][:, :], ALU.mult))
            p.stage()
            p.v(lambda e, i=i: e.tensor_tensor(B["xa_t"][i][:, :],
                                               B["xa_t"][i][:, :],
                                               PS[3][:, :], ALU.subtract))
            p.stage()
            p.d(lambda e, i=i, cs=cs: e.dma_start(
                dram["xa"][i * 128:(i + 1) * 128, cs], B["xa_t"][i][:, :]))
        for i in range(NT):
            p.stage()
            p.s(lambda e, i=i: e.activation(sq[i % 2][:, :],
                                            B["xa_t"][i][:, :], AF.Square))
            p.stage()
            p.t(lambda e, i=i: e.matmul(
                PS[4][0:1, :], c.invD_col[:, :], B["xa_t"][i][:, :],
                start=(i == 0), stop=(i == NT - 1)))
            p.t(lambda e, i=i: e.matmul(
                PS[5][0:1, :], c.invD_col[:, :], sq[i % 2][:, :],
                start=(i == 0), stop=(i == NT - 1)))
        p.stage()
        p.v(lambda e: e.tensor_copy(B["mean"], PS[4][0:1, :]))
        p.stage()
        p.v(lambda e: e.tensor_tensor(var, B["mean"], B["mean"], ALU.mult))
        p.stage()
        p.v(lambda e: e.tensor_tensor(var, PS[5][0:1, :], var, ALU.subtract))
        p.stage()
        p.s(lambda e, epsc=epsc: e.activation(B["rstd"], var, AF.Ln,
                                              bias=epsc[:, :]))
        p.stage()
        p.s(lambda e: e.activation(B["rstd"], B["rstd"], AF.Exp, scale=-0.5))
        p.stage()
        p.v(lambda e: e.tensor_tensor(B["mr"], B["mean"], B["rstd"],
                                      ALU.mult))
        p.stage()
        p.t(lambda e: e.matmul(PS[2][:, :], c.ones_f[:, :], B["rstd"],
                               start=True, stop=True))
        p.t(lambda e: e.matmul(PS[3][:, :], c.ones_f[:, :], B["mr"],
                               start=True, stop=True))
        p.stage()
        xnc = B["xnc"][0]
        for i in range(NT):
            p.stage()
            p.v(lambda e, i=i: e.tensor_tensor(B["xnf"][:, :],
                                               B["xa_t"][i][:, :],
                                               PS[2][:, :], ALU.mult))
            p.stage()
            p.v(lambda e: e.tensor_tensor(B["xnf"][:, :], B["xnf"][:, :],
                                          PS[3][:, :], ALU.subtract))
            p.stage()
            p.v(lambda e, i=i, xnc=xnc: e.tensor_scalar(
                xnc[i][:, :], B["xnf"][:, :], w["ng"][i][:, :],
                w["nb"][i][:, :], ALU.mult, op1=ALU.add))
            p.stage()
            p.d(lambda e, i=i, cs=cs, xnc=xnc: e.dma_start(
                dram["xnb"][i * 128:(i + 1) * 128, cs], xnc[i][:, :]))
        p.stage()


def emit_mamba_phA(p, c, B, w, dram):
    PS = B["PS"]

    def load_wblk(mb, par):
        for k in range(NT):
            p.late("gpsimd", lambda e, mb=mb, k=k, par=par: e.dma_start(
                B["wblk"][par][k][:, :],
                w["inwT"][k * 128:(k + 1) * 128,
                          mb * 1024:(mb + 1) * 1024]))

    def load_xnc(tcx, par):
        cs = slice(tcx * 512, (tcx + 1) * 512)
        for i in range(NT):
            p.late("sync", lambda e, i=i, cs=cs, par=par: e.dma_start(
                B["xnc"][par][i][:, :],
                dram["xnb"][i * 128:(i + 1) * 128, cs]))

    p.stage()
    load_wblk(0, 0)
    load_xnc(0, 0)
    for mb in range(4):
        wpar = mb % 2
        for tcx in range(T // 512):
            xpar = tcx % 2
            cs = slice(tcx * 512, (tcx + 1) * 512)
            first = p.stage(lwait=True)
            if tcx == 0 and mb < 3:
                load_wblk(mb + 1, wpar ^ 1)
            if tcx < 7:
                load_xnc(tcx + 1, xpar ^ 1)
            elif mb < 3:
                load_xnc(0, xpar ^ 1)
            for m in range(10):
                if m < 8:
                    for k in range(NT):
                        p.t(lambda e, m=m, k=k, wpar=wpar, xpar=xpar: e.matmul(
                            PS[2 + m % 2][:, :],
                            B["wblk"][wpar][k][:, m * 128:(m + 1) * 128],
                            B["xnc"][xpar][k][:, :],
                            start=(k == 0), stop=(k == NT - 1)))
                if 1 <= m <= 8:
                    mm = m - 1
                    if mm % 2 == 0:
                        p.v(lambda e, mm=mm: e.tensor_copy(
                            B["stg"][mm % 4][:, :], PS[2 + mm % 2][:, :]))
                    else:
                        p.s(lambda e, mm=mm: e.copy(
                            B["stg"][mm % 4][:, :], PS[2 + mm % 2][:, :]))
                if 2 <= m:
                    ms = m - 2
                    row = mb * 8 + ms
                    p.late("scalar", lambda e, ms=ms, row=row, cs=cs:
                           e.dma_start(dram["xz"][row * 128:(row + 1) * 128,
                                                  cs],
                                       B["stg"][ms % 4][:, :]))
                p.stage()


def emit_mamba_phB(p, c, B, w, dram):
    PS = B["PS"]
    NCH = T // TCM

    def c1_loads(ci):
        q = ci % 2
        t0 = ci * TCM
        for i in range(NDT):
            if ci == 0:
                p.late("sync", lambda e, i=i, q=q, t0=t0: e.dma_start(
                    B["XZW"][q][:, i * 131 + 3:i * 131 + 131],
                    dram["xz"][i * 128:(i + 1) * 128, t0:t0 + TCM]))
            else:
                p.late("sync", lambda e, i=i, q=q, t0=t0: e.dma_start(
                    B["XZW"][q][:, i * 131:i * 131 + 131],
                    dram["xz"][i * 128:(i + 1) * 128, t0 - 3:t0 + TCM]))
            p.late("scalar", lambda e, i=i, q=q, t0=t0: e.dma_start(
                B["ZW"][q][:, i * TCM:(i + 1) * TCM],
                dram["xz"][(NDT + i) * 128:(NDT + i + 1) * 128,
                           t0:t0 + TCM]))

    p.stage()
    for q in range(2):
        p.g(lambda e, q=q: e.memset(
            apx(B["scanA"][q], 0, [[SEG, NST], [1, 1]]), 0.0))
    p.g(lambda e: e.memset(B["CARR"][:, :], 0.0))
    p.g(lambda e: e.memset(
        apx(B["XZW"][0], 0, [[131, NDT], [1, 3]]), 0.0))
    p.stage(lwait=True)
    c1_loads(0)

    for ci in range(NCH):
        q = ci % 2
        t0 = ci * TCM
        p.stage(lwait=True)
        for i in range(NDT):
            emit = p.v
            seg = slice(i * TCM, (i + 1) * TCM)

            def f0(e, i=i, q=q, seg=seg):
                return e.tensor_scalar(
                    B["DW"][:, seg], B["XZW"][q][:, i * 131:i * 131 + TCM],
                    w["convw"][i][:, 0:1], None, ALU.mult)
            emit(f0)
            for k in range(1, 4):
                def fk(e, i=i, k=k, q=q, seg=seg):
                    return e.scalar_tensor_tensor(
                        B["DW"][:, seg],
                        B["XZW"][q][:, i * 131 + k:i * 131 + k + TCM],
                        w["convw"][i][:, k:k + 1], B["DW"][:, seg],
                        ALU.mult, ALU.add)
                emit(fk)
        p.stage()
        for i in range(NDT):
            p.s(lambda e, i=i: e.activation(
                B["UW"][:, i * TCM:(i + 1) * TCM],
                B["DW"][:, i * TCM:(i + 1) * TCM],
                AF.Silu, bias=w["convb"][i][:, :]))
        p.stage()
        p.v(lambda e: e.tensor_copy(B["UBF"][:, :], B["UW"][:, :]))
        p.stage()
        for k in range(NDT):
            p.t(lambda e, k=k: e.matmul(
                PS[6][:, 0:TCM], w["xproj"][k][:, :],
                B["UBF"][:, k * 128:(k + 1) * 128],
                start=(k == 0), stop=(k == NDT - 1)))
        p.stage()
        p.s(lambda e: e.copy(B["draw_bf"][:, :], PS[6][0:64, 0:TCM]))
        p.v(lambda e: e.tensor_copy(B["B_t"][:, :], PS[6][64:80, 0:TCM]))
        p.v(lambda e: e.tensor_copy(B["C_t"][:, :], PS[6][96:112, 0:TCM]))
        p.stage()
        p.late("gpsimd", lambda e: e.dma_start(
            bass.AP(dram["bcf"].tensor, 0, [[TCM, 16], [1, TCM]]),
            B["B_t"][:, :]))
        p.late("gpsimd", lambda e: e.dma_start(
            bass.AP(dram["bcf"].tensor, NST * TCM, [[TCM, 16], [1, TCM]]),
            B["C_t"][:, :]))
        p.stage()
        for mi in range(NDT):
            p.t(lambda e, mi=mi: e.matmul(
                PS[2 + mi // 4][:, (mi % 4) * 128:(mi % 4 + 1) * 128],
                w["dtw"][:, mi * 128:(mi + 1) * 128],
                B["draw_bf"][:, :], start=True, stop=True))
        p.stage()
        for b in range(4):
            p.v(lambda e, b=b: e.tensor_tensor(
                B["XB"][:, b * 512:(b + 1) * 512], PS[2 + b][:, :],
                apx(w["dtb_pm"], 4 * b, [[1, 4], [0, 128]]), ALU.add))
        p.stage(lwait=True)
        p.late("sync", lambda e: e.dma_start(
            B["B_rep"][:, :],
            bass.AP(dram["bcf"].tensor, 0, [[0, 128], [1, NST * TCM]])))
        p.late("sync", lambda e: e.dma_start(
            B["C_rep"][:, :],
            bass.AP(dram["bcf"].tensor, NST * TCM,
                    [[0, 128], [1, NST * TCM]])))
        p.v(lambda e: e.tensor_scalar(B["DW"][:, :], B["XB"][:, :], 0.0,
                                      None, ALU.max))
        p.s(lambda e: e.activation(B["T1"][:, :], B["XB"][:, :], AF.Abs))
        p.s(lambda e: e.activation(B["T1"][:, :], B["T1"][:, :], AF.Exp,
                                   scale=-1.0))
        p.s(lambda e: e.activation(B["T1"][:, :], B["T1"][:, :], AF.Ln,
                                   bias=1.0))
        p.stage()
        p.v(lambda e: e.tensor_tensor(B["DW"][:, :], B["DW"][:, :],
                                      B["T1"][:, :], ALU.add))
        p.stage(lwait=True)
        p.v(lambda e, q=q: e.tensor_tensor(
            B["YW"][q][:, :], B["UW"][:, :],
            apx(w["D_pm"], 0, [[1, NST], [0, TCM]]), ALU.mult))
        p.v(lambda e: e.tensor_tensor(B["UW"][:, :], B["UW"][:, :],
                                      B["DW"][:, :], ALU.mult))
        for s in range(NDT + 3):
            if s == 0:
                p.stage(lwait=True)
            else:
                p.stage()
            if s == 2 and ci + 1 < NCH:
                c1_loads(ci + 1)
            i_x = s
            if i_x < NDT:
                qi = i_x % 2
                for n in range(NST):
                    p.s(lambda e, i=i_x, n=n, qi=qi: e.activation(
                        B["scanA"][qi][:, n * SEG + 1:n * SEG + 1 + TCM],
                        B["DW"][:, i * TCM:(i + 1) * TCM],
                        AF.Exp, scale=w["A"][i][:, n:n + 1]))
                p.v(lambda e, i=i_x, qi=qi: e.tensor_copy(
                    apx(B["scanB"][qi], 0, [[SEG, NST], [1, 1]]),
                    B["CARR"][:, i * NST:(i + 1) * NST]))
                p.g(lambda e, i=i_x, qi=qi: e.tensor_tensor(
                    apx(B["scanB"][qi], 1, [[SEG, NST], [1, TCM]]),
                    apx(B["UW"], i * TCM, [[0, NST], [1, TCM]]),
                    B["B_rep"][:, :], ALU.mult))
            i_s = s - 1
            if 0 <= i_s < NDT:
                qi = i_s % 2
                p.v(lambda e, qi=qi: e.tensor_tensor_scan(
                    B["scanH"][qi][:, :], B["scanA"][qi][:, :],
                    B["scanB"][qi][:, :], 0.0, ALU.mult, ALU.add))
            i_h = s - 2
            if 0 <= i_h < NDT:
                qi = i_h % 2
                p.s(lambda e, i=i_h, qi=qi: e.copy(
                    B["CARR"][:, i * NST:(i + 1) * NST],
                    apx(B["scanH"][qi], TCM, [[SEG, NST], [1, 1]])))
                p.v(lambda e, qi=qi: e.tensor_tensor(
                    B["hC"][qi][:, :],
                    apx(B["scanH"][qi], 1, [[SEG, NST], [1, TCM]]),
                    B["C_rep"][:, :], ALU.mult))
            i_y = s - 3
            if 0 <= i_y < NDT:
                qi = i_y % 2
                p.v(lambda e, qi=qi: e.tensor_reduce(
                    B["TRED"][:, :],
                    apx(B["hC"][qi], 0, [[1, TCM], [TCM, NST]]),
                    AX.X, ALU.add))
                p.v(lambda e, i=i_y, q=q: e.tensor_tensor(
                    B["YW"][q][:, i * TCM:(i + 1) * TCM],
                    B["YW"][q][:, i * TCM:(i + 1) * TCM],
                    B["TRED"][:, :], ALU.add))
        p.stage()
        p.s(lambda e, q=q: e.activation(B["T1"][:, :], B["ZW"][q][:, :],
                                        AF.Silu))
        p.stage()
        p.v(lambda e, q=q: e.tensor_tensor(B["YW"][q][:, :], B["YW"][q][:, :],
                                           B["T1"][:, :], ALU.mult))
        p.stage()
        for i in range(NDT):
            eng = ("sync", "scalar", "gpsimd")[i % 3]
            p.late(eng, lambda e, i=i, q=q, t0=t0: e.dma_start(
                dram["ysp"][i * 128:(i + 1) * 128, t0:t0 + TCM],
                B["YW"][q][:, i * TCM:(i + 1) * TCM]))


def emit_mamba_phC(p, c, B, w, xT, yT, dram):
    PS = B["PS"]
    p.stage()
    for k in range(NDT):
        p.late("gpsimd", lambda e, k=k: e.dma_start(
            B["outwT"][k][:, :],
            w["outwT_d"][k * 128:(k + 1) * 128, :]))
    for tcx in range(T // 256):
        cs = slice(tcx * 256, (tcx + 1) * 256)
        p.stage(lwait=True)
        for k in range(NDT):
            p.dg(lambda e, k=k, cs=cs: e.dma_start(
                B["y_bf"][k][:, :], dram["ysp"][k * 128:(k + 1) * 128, cs]))
        for i in range(NT):
            p.d(lambda e, i=i, cs=cs: e.dma_start(
                B["xres"][i][:, :], xT[i * 128:(i + 1) * 128, cs]))
            p.ds(lambda e, i=i, cs=cs: e.dma_start(
                B["xastg"][i][:, :], dram["xa"][i * 128:(i + 1) * 128, cs]))
        p.stage(lwait=True)
        for m in range(10):
            if m < 8:
                for k in range(NDT):
                    p.t(lambda e, m=m, k=k: e.matmul(
                        PS[2 + m % 2][:, 0:256],
                        B["outwT"][k][:, m * 128:(m + 1) * 128],
                        B["y_bf"][k][:, :], start=(k == 0),
                        stop=(k == NDT - 1)))
            if 1 <= m <= 8:
                mm = m - 1
                p.v(lambda e, mm=mm: e.tensor_tensor(
                    B["xres"][mm][:, :], B["xres"][mm][:, :],
                    PS[2 + mm % 2][:, 0:256], ALU.add))
                p.v(lambda e, mm=mm: e.tensor_tensor(
                    B["xres"][mm][:, :], B["xres"][mm][:, :],
                    B["xastg"][mm][:, :], ALU.add))
            if 2 <= m:
                ms = m - 2
                p.d(lambda e, ms=ms, cs=cs: e.dma_start(
                    yT[ms * 128:(ms + 1) * 128, cs], B["xres"][ms][:, :]))
            p.stage()


def emit_mamba(p, c, B, w, xT, yT, dram):
    emit_mamba_ph0(p, c, B, w, xT, dram)
    emit_mamba_phA(p, c, B, w, dram)
    emit_mamba_phB(p, c, B, w, dram)
    emit_mamba_phC(p, c, B, w, xT, yT, dram)


def emit_mamba_phase(p, c, xT, yT, aps, dram):
    B = mamba_bufs(c)
    w = {}
    emit_mamba_weights_load(p, c, w, aps)
    p.stage()
    emit_mamba(p, c, B, w, xT, yT, dram)


def mamba_dram(nc, dbg=False):
    kd = "ExternalOutput" if dbg else "Internal"
    return {
        "xz": nc.dram_tensor("xz_sp", [2 * DIN, T], F32, kind=kd).ap(),
        "ysp": nc.dram_tensor("y_sp", [DIN, T], F32, kind=kd).ap(),
        "xnb": nc.dram_tensor("xn_bf", [D, T], BF16, kind=kd).ap(),
        "xa": nc.dram_tensor("xa_sp", [D, T], F32, kind=kd).ap(),
        "bcf": nc.dram_tensor("bc_flat", [2 * NST * TCM], F32,
                              kind="Internal").ap(),
    }


def make_mamba_nc(dbg=False):
    nc = bass.Bass("TRN2", target_bir_lowering=False)
    xT = nc.dram_tensor("xT", [D, T], F32, kind="ExternalInput").ap()
    aps = tuple(
        nc.dram_tensor(nm, sh, F32, kind="ExternalInput").ap()
        for nm, sh in (
            ("ng", [D, 1]), ("nb", [D, 1]), ("inwT", [1024, 4096]),
            ("convw", [DIN, 4]), ("convb", [DIN, 1]), ("xprojT", [DIN, 128]),
            ("dtwT", [64, DIN]), ("dtb", [DIN, 1]), ("alog", [DIN, NST]),
            ("dvec", [DIN, 1]), ("outwT", [DIN, 1024])))
    yT = nc.dram_tensor("yT", [D, T], F32, kind="ExternalOutput").ap()
    dram = mamba_dram(nc, dbg)
    with ExitStack() as stack:
        c = Ctx(nc, stack)
        sem = c.sem("SS")
        semL = c.sem("SL")
        block = stack.enter_context(nc.Block())
        p = Prog(nc)
        setup_consts(p, c)
        emit_attn0_prep(p, c)
        c.PSG = [c.ps([128, 512], F32, f"G{i}") for i in range(7)]
        c.arena = Arena(c, ARENA_N)
        p.stage()
        emit_mamba_phase(p, c, xT, yT, aps, dram)
        p.emit(block, sem, semL)
    return nc



ARENA_N = 52224


def make_fused_nc():
    nc = bass.Bass("TRN2", target_bir_lowering=False)
    xT = nc.dram_tensor("xT", [D, T], F32, kind="ExternalInput").ap()

    def din(nm, sh):
        return nc.dram_tensor(nm, sh, F32, kind="ExternalInput").ap()

    ffn0_aps = {k: din(k, sh) for k, sh in (
        ("wrT", [D, D]), ("wkT", [D, D]), ("wvT", [D, D]),
        ("mur", [D, 1]), ("muk", [D, 1]))}
    attn_aps = tuple(din(nm, sh) for nm, sh in (
        ("wdqT", [64, 512]), ("wuqT", [512, 64]), ("wdkvT", [64, 1024]),
        ("wukvT", [1024, 128]), ("wqrT", [64, 64]), ("wkrT", [64, 64]),
        ("woT", [64, 64]), ("qnw", [64, 1]), ("knw", [64, 1])))
    ffn1_aps = tuple(din(nm, sh) for nm, sh in (
        ("w1lT", [1024, 512]), ("b1l", [512, 1]), ("w2lT", [512, 1024]),
        ("b2l", [1024, 1]), ("w1hT", [1024, 4096]), ("b1h", [4096, 1]),
        ("w2hT", [4096, 1024]), ("b2h", [1024, 1]), ("rt", [1024, 1]),
        ("gl", [1024, 1]), ("gh", [1024, 1])))
    mamba_aps = tuple(din(nm, sh) for nm, sh in (
        ("ng", [D, 1]), ("nb", [D, 1]), ("inwT", [1024, 4096]),
        ("convw", [DIN, 4]), ("convb", [DIN, 1]), ("xprojT", [DIN, 128]),
        ("dtwT", [64, DIN]), ("dtb", [DIN, 1]), ("alog", [DIN, NST]),
        ("dvec", [DIN, 1]), ("outwT", [DIN, 1024])))
    yT = nc.dram_tensor("yT", [D, T], F32, kind="ExternalOutput").ap()
    xs = [nc.dram_tensor(f"xc{i}", [D, T], F32, kind="Internal").ap()
          for i in range(5)]
    dramF = ffn1_dram(nc)
    dramM = mamba_dram(nc)
    with ExitStack() as stack:
        c = Ctx(nc, stack)
        sem = c.sem("SS")
        semL = c.sem("SL")
        block = stack.enter_context(nc.Block())
        p = Prog(nc)
        setup_consts(p, c)
        emit_attn0_prep(p, c)
        c.PSG = [c.ps([128, 512], F32, f"G{i}") for i in range(7)]
        c.PTB = c.ps([128, 512], BF16, "GT")
        ar = Arena(c, ARENA_N)
        c.arena = ar
        wf0 = (ffn0_aps["wrT"], ffn0_aps["wkT"], ffn0_aps["wvT"],
               ffn0_aps["mur"], ffn0_aps["muk"])
        ar.reset()
        p.stage()
        emit_ffn0_phase(p, c, xT, xs[0], ffn0_aps)
        ar.reset()
        p.stage()
        emit_attn0_phase(p, c, xs[0], xs[1], attn_aps)
        ar.reset()
        p.stage()
        emit_ffn0_phase(p, c, xs[1], xs[2], ffn0_aps)
        ar.reset()
        p.stage()
        emit_ffn1_phase(p, c, xs[2], xs[3], ffn1_aps, dramF)
        ar.reset()
        p.stage()
        emit_mamba_phase(p, c, xs[3], xs[4], mamba_aps, dramM)
        ar.reset()
        p.stage()
        emit_ffn1_phase(p, c, xs[4], yT, ffn1_aps, dramF)
        p.emit(block, sem, semL)
    return nc



def _ct(a):
    return np.ascontiguousarray(a)


def kernel(**inputs):
    ncores = 8
    x = inputs["L0_attn_x"]
    cur = [_ct(x[b].T) for b in range(ncores)]

    ffn0_w = {
        "wrT": _ct(inputs["L0_ffn_W_r"].T),
        "wkT": _ct(inputs["L0_ffn_W_k"].T),
        "wvT": _ct(inputs["L0_ffn_W_v"].T),
        "mur": _ct(inputs["L0_ffn_mu_r"].reshape(-1, 1)),
        "muk": _ct(inputs["L0_ffn_mu_k"].reshape(-1, 1)),
    }
    attn_w = {
        "wdqT": _ct(inputs["L0_attn_W_DQ"].T),
        "wuqT": _ct(inputs["L0_attn_W_UQ"].T),
        "wdkvT": _ct(inputs["L0_attn_W_DKV"].T),
        "wukvT": _ct(inputs["L0_attn_W_UKV"].T),
        "wqrT": _ct(inputs["L0_attn_W_QR"].T),
        "wkrT": _ct(inputs["L0_attn_W_KR"].T),
        "woT": _ct(inputs["L0_attn_W_O"].T),
        "qnw": _ct(inputs["L0_attn_q_norm_w"].reshape(-1, 1)),
        "knw": _ct(inputs["L0_attn_k_norm_w"].reshape(-1, 1)),
    }
    ffn1_w = {
        "w1lT": _ct(inputs["L1_ffn_w1_l"].T),
        "b1l": _ct(inputs["L1_ffn_b1_l"].reshape(-1, 1)),
        "w2lT": _ct(inputs["L1_ffn_w2_l"].T),
        "b2l": _ct(inputs["L1_ffn_b2_l"].reshape(-1, 1)),
        "w1hT": _ct(inputs["L1_ffn_w1_h"].T),
        "b1h": _ct(inputs["L1_ffn_b1_h"].reshape(-1, 1)),
        "w2hT": _ct(inputs["L1_ffn_w2_h"].T),
        "b2h": _ct(inputs["L1_ffn_b2_h"].reshape(-1, 1)),
        "rt": _ct(inputs["L1_ffn_routing_token"].reshape(-1, 1)),
        "gl": _ct(inputs["L1_ffn_gamma_l"].reshape(-1, 1)),
        "gh": _ct(inputs["L1_ffn_gamma_h"].reshape(-1, 1)),
    }
    xp = inputs["L1_attn_xproj_w"]
    xprojT_pad = np.concatenate(
        [_ct(xp[:64].T), _ct(xp[64:80].T), np.zeros((DIN, 16), np.float32),
         _ct(xp[80:96].T), np.zeros((DIN, 16), np.float32)], axis=1)
    mamba_w = {
        "ng": _ct(inputs["L1_attn_norm_g"].reshape(-1, 1)),
        "nb": _ct(inputs["L1_attn_norm_b"].reshape(-1, 1)),
        "inwT": _ct(inputs["L1_attn_in_w"].T),
        "convw": _ct(inputs["L1_attn_conv_w"][:, 0, :]),
        "convb": _ct(inputs["L1_attn_conv_b"].reshape(-1, 1)),
        "xprojT": _ct(xprojT_pad),
        "dtwT": _ct(inputs["L1_attn_dt_w"].T),
        "dtb": _ct(inputs["L1_attn_dt_b"].reshape(-1, 1)),
        "alog": _ct(inputs["L1_attn_A_log"]),
        "dvec": _ct(inputs["L1_attn_D"].reshape(-1, 1)),
        "outwT": _ct(inputs["L1_attn_out_w"].T),
    }

    wall = {**ffn0_w, **attn_w, **ffn1_w, **mamba_w}
    nc = make_fused_nc()
    in_maps = [{**wall, "xT": cur[b]} for b in range(ncores)]
    res = run_bass_kernel_spmd(nc, in_maps, core_ids=list(range(ncores)))
    cur = [np.asarray(res.results[b]["yT"]) for b in range(ncores)]
    out = np.stack([cur[b].T for b in range(ncores)], axis=0)
    return np.ascontiguousarray(out).astype(np.float32)
